# version 10
# baseline (speedup 1.0000x reference)
import math
import contextlib
import numpy as np
import ml_dtypes
import concourse.bass as bass
import concourse.mybir as mybir
from concourse.bass_utils import run_bass_kernel_spmd

F32 = mybir.dt.float32
BF16 = mybir.dt.bfloat16
AF = mybir.ActivationFunctionType
ALU = mybir.AluOpType
AXX = mybir.AxisListType.X

B, S, D = 4, 4096, 2048
H = 16
TC = 2048
NT = TC // 128
INTER0 = 5632
NH0 = INTER0 // 128
H1 = 5120
NH1 = H1 // 128
CHUNK = 256
ND = D // 128
SCALE0 = 1.0 / math.sqrt(128.0)
C_SHIFT = 20.0
PAIRSWAP = [i ^ 1 for i in range(32)]
SPANS = {0: [0, 3], 1: [1, 2]}
CORES = list(range(8))


def own_ranges(p):
    return [(sp * 1024, (sp + 1) * 1024) for sp in SPANS[p]]


def own_chunks(p):
    return [sp * 4 + i for sp in SPANS[p] for i in range(4)]



def _reduce_pm_pi(theta):
    k = np.round(theta / (2 * np.pi))
    return np.clip(theta - k * (2 * np.pi), -np.pi + 1e-7, np.pi - 1e-7)


def _hi_lo(a):
    hi = a.astype(np.float32)
    return hi, (a - hi.astype(np.float64)).astype(np.float32)


def make_rope_tables(layer):
    j = np.arange(128)
    t = np.arange(S, dtype=np.float64)
    inv = 10000.0 ** (-(np.arange(64, dtype=np.float64) / 64.0))
    theta = (inv[j // 2] if layer == 0 else inv[j % 64])[:, None] * t[None, :]
    sign = np.where(j % 2 == 0, -1.0, 1.0)[:, None]
    sh, sl = _hi_lo(_reduce_pm_pi(theta * sign))
    ch, cl = _hi_lo(_reduce_pm_pi(theta + np.pi / 2))
    return sh, sl, ch, cl


def tile_lhsT(W, noc, nic):
    W4 = W.reshape(noc, 128, nic, 128)
    return np.ascontiguousarray(W4.transpose(0, 2, 3, 1))


def col_layout(v):
    n = v.shape[0] // 128
    return np.ascontiguousarray(v.reshape(n, 128).T)



ENGINES = ("sync", "tensor", "vector", "scalar", "gpsimd")


class Sem:
    __slots__ = ("name", "count", "handle")

    def __init__(self, name):
        self.name, self.count, self.handle = name, 0, None


STRICT = ("vector", "scalar", "gpsimd")


class Prog:

    def __init__(self, nc):
        self.nc = nc
        self.q = {e: [] for e in ENGINES}
        self.sems = []
        self.last_wait = {e: {} for e in ENGINES}
        self.serial = {e: self.sem("ser_" + e) for e in STRICT}
        self.last_done = {}
        self.all_waits = []
        self.dma_sem_owner = {}
        self.named = {}
        self.barsem = None

    def sem(self, name=None):
        s = Sem(name or f"s{len(self.sems)}")
        self.sems.append(s)
        return s

    def nsem(self, name):
        if name not in self.named:
            self.named[name] = self.sem(name)
        return self.named[name]

    def wait(self, eng, sem, val):
        if sem is None or val is None or val <= 0:
            return
        lw = self.last_wait[eng]
        if lw.get(sem.name, 0) >= val:
            return
        lw[sem.name] = val
        self.all_waits.append((eng, sem, val))
        self.q[eng].append(lambda e, s=sem, v=val: e.wait_ge(s.handle, v))

    def op(self, eng, fn, inc=None, inc_n=1):
        if eng in STRICT:
            ld = self.last_done.get(eng)
            if ld is not None:
                self.wait(eng, ld[0], ld[1])
            if inc is not None:
                inc.count += inc_n
                self.q[eng].append(
                    lambda e, f=fn, s=inc, n=inc_n: f(e).then_inc(s.handle, n))
                self.last_done[eng] = (inc, inc.count)
                return inc.count
            ser = self.serial[eng]
            ser.count += 1
            self.q[eng].append(
                lambda e, f=fn, sr=ser: f(e).then_inc(sr.handle, 1))
            self.last_done[eng] = (ser, ser.count)
            return None
        if inc is not None:
            inc.count += inc_n
            self.q[eng].append(
                lambda e, f=fn, s=inc, n=inc_n: f(e).then_inc(s.handle, n))
            return inc.count
        self.q[eng].append(fn)
        return None

    def rawop(self, eng, fn):
        self.q[eng].append(fn)

    def edone(self, waiter, producer):
        ld = self.last_done.get(producer)
        if ld is not None:
            self.wait(waiter, ld[0], ld[1])

    def dma(self, eng, out, in_, sem, **kw):
        owner = self.dma_sem_owner.setdefault(sem.name, eng)
        assert owner == eng, f"DMA sem {sem.name} used from {eng} and {owner}"
        if self.last_wait[eng].get(sem.name, 0) < sem.count:
            self.wait(eng, sem, sem.count)
            self.q[eng].append(lambda e: e.nop())
        sem.count += 16
        self.q[eng].append(
            lambda e, o=out, i=in_, k=kw, s=sem: e.dma_start(out=o, in_=i, **k)
            .then_inc(s.handle, 16))
        return sem.count

    def barrier(self):
        if self.barsem is None:
            self.barsem = self.sem("bar")
        bs = self.barsem
        for e in ENGINES:
            if e == "sync":
                self.q[e].append(lambda en, s=bs: en.nop().then_inc(s.handle, 1))
            else:
                self.q[e].append(lambda en, s=bs: en.drain().then_inc(s.handle, 1))
            bs.count += 1
        tgt = bs.count
        for e in ENGINES:
            self.wait(e, bs, tgt)

    def finalize(self, es):
        for eng, sem, val in self.all_waits:
            if val > sem.count:
                raise RuntimeError(
                    f"unsatisfiable wait: {eng} waits {sem.name} >= {val}, "
                    f"final count {sem.count}")
        for s in self.sems:
            s.handle = es.enter_context(self.nc.semaphore(s.name))
        block = es.enter_context(self.nc.Block())
        for eng in ENGINES:
            ops = self.q[eng]
            if ops:
                getattr(block, eng)(lambda e, ops=ops: [f(e) for f in ops])



def emit_consts(P, nc, es):
    c = {}
    c["ones_r_f"] = es.enter_context(nc.sbuf_tensor("ones_r_f", [1, 128], F32))
    c["ones_c_b"] = es.enter_context(nc.sbuf_tensor("ones_c_b", [128, 1], BF16))
    c["ones_c_f"] = es.enter_context(nc.sbuf_tensor("ones_c_f", [128, 1], F32))
    c["ident_b"] = es.enter_context(nc.sbuf_tensor("ident_b", [128, 128], BF16))
    c["negC"] = es.enter_context(nc.sbuf_tensor("negC", [128, 1], F32))
    c["epsc"] = es.enter_context(nc.sbuf_tensor("epsc", [128, 1], F32))
    done = P.sem("consts")
    P.op("gpsimd", lambda e: e.memset(c["ones_r_f"].ap(), 1.0))
    P.op("gpsimd", lambda e: e.memset(c["ones_c_b"].ap(), 1.0))
    P.op("gpsimd", lambda e: e.memset(c["ones_c_f"].ap(), 1.0))
    P.op("gpsimd", lambda e: e.memset(c["negC"].ap(), -C_SHIFT))
    P.op("gpsimd", lambda e: e.memset(c["epsc"].ap(), 1e-5))
    P.op("gpsimd", lambda e: e.memset(c["ident_b"].ap(), 0.0))
    k = P.op("gpsimd", lambda e: e.affine_select(
        out=c["ident_b"].ap(), in_=c["ident_b"].ap(),
        compare_op=ALU.not_equal, fill=1.0, base=0,
        pattern=[[-1, 128]], channel_multiplier=1), inc=done)
    c["done"] = done
    for e in ENGINES:
        P.wait(e, done, k)
    return c


def emit_ln_cols(P, co, xs_ap, scr_ap, cols_ap, vsem):
    nf = xs_ap.shape[-1]
    inv_n = 1.0 / nf
    sumc, sgc = cols_ap[:, 0:1], cols_ap[:, 1:2]
    muc, rsc, nmc = cols_ap[:, 2:3], cols_ap[:, 3:4], cols_ap[:, 4:5]
    P.edone("scalar", "vector")
    t = P.op("scalar", lambda e: e.activation(scr_ap, xs_ap, AF.Square, accum_out=sgc),
             inc=vsem)
    P.op("vector", lambda e: e.reduce_sum(sumc, xs_ap, axis=AXX))
    P.op("vector", lambda e: e.tensor_scalar(muc, sumc, inv_n, None, op0=ALU.mult))
    P.wait("vector", vsem, t)
    P.op("vector", lambda e: e.tensor_scalar(sgc, sgc, inv_n, None, op0=ALU.mult))
    P.op("vector", lambda e: e.tensor_mul(nmc, muc, muc))
    t = P.op("vector", lambda e: e.tensor_sub(sgc, sgc, nmc), inc=vsem)
    P.wait("scalar", vsem, t)
    t = P.op("scalar", lambda e: e.activation(sgc, sgc, AF.Sqrt, bias=co["epsc"].ap()),
             inc=vsem)
    P.wait("vector", vsem, t)
    P.op("vector", lambda e: e.reciprocal(rsc, sgc))
    P.op("vector", lambda e: e.tensor_mul(nmc, muc, rsc))
    return P.op("vector", lambda e: e.tensor_scalar(nmc, nmc, -1.0, None, op0=ALU.mult),
                inc=vsem)


def alloc_xnT_bufs(nc, es, tag, ntt):
    b = {}
    b["xnT"] = es.enter_context(nc.sbuf_tensor(tag + "T", [128, ND * ntt * 128], BF16))
    b["xs"] = es.enter_context(nc.sbuf_tensor(tag + "_xs", [128, D], F32))
    b["xb"] = es.enter_context(nc.sbuf_tensor(tag + "_xb", [128, D], BF16))
    b["scr"] = es.enter_context(nc.sbuf_tensor(tag + "_scr", [128, D], F32))
    b["cols"] = es.enter_context(nc.sbuf_tensor(tag + "_cols", [128, 6], F32))
    b["pt"] = es.enter_context(nc.psum_tensor(tag + "_pt", [128, 512], BF16))
    return b


def emit_xnT(P, nc, es, co, x_dram, tag="xn", ntt=NT, tt0=0, bufs=None):
    ess = contextlib.ExitStack()
    if bufs is None:
        bufs = alloc_xnT_bufs(nc, es, tag, ntt)
    xnT, xs, xb, scr, cols, pt = (bufs["xnT"], bufs["xs"], bufs["xb"],
                                  bufs["scr"], bufs["cols"], bufs["pt"])
    ld = P.sem(); vs = P.sem(); tp = P.sem(); ev = P.sem()
    for i in range(ntt):
        tt = tt0 + i
        P.wait("sync", tp, tp.count)
        t_ld = P.dma("sync", xs.ap(), x_dram[tt * 128:(tt + 1) * 128, :], ld)
        P.wait("scalar", ld, t_ld)
        P.wait("vector", ld, t_ld)
        t_n = emit_ln_cols(P, co, xs.ap(), scr.ap(), cols.ap(), vs)
        P.wait("vector", vs, t_n)
        t_c = P.op("vector", lambda e: e.tensor_scalar(
            xb.ap(), xs.ap(), cols.ap()[:, 3:4], cols.ap()[:, 4:5],
            op0=ALU.mult, op1=ALU.add), inc=vs)
        for g in range(4):
            P.wait("tensor", vs, t_c)
            P.wait("tensor", ev, ev.count)
            tl = None
            for f in range(4):
                ic = g * 4 + f
                tl = P.op("tensor", lambda e, f=f, ic=ic:
                          e.transpose(pt.ap()[:, f * 128:(f + 1) * 128],
                                      xb.ap()[:, ic * 128:(ic + 1) * 128],
                                      co["ident_b"].ap()),
                          inc=tp if f == 3 else None)
            P.wait("scalar", tp, tl)

            def ev_fn(e, g=g, i=i):
                dst = xnT.ap().rearrange("p (ic t) -> p ic t", ic=ND)[
                    :, g * 4:(g + 1) * 4, i * 128:(i + 1) * 128]
                src = pt.ap().rearrange("p (f t) -> p f t", f=4)
                return e.activation(dst, src, AF.Identity)
            P.op("scalar", ev_fn, inc=ev)
    P.barrier()
    ess.close()
    return xnT


def emit_rope_tables(P, nc, es, io, pref):
    sinT = es.enter_context(nc.sbuf_tensor(pref + "sinT", [128, TC], F32))
    cosT = es.enter_context(nc.sbuf_tensor(pref + "cosT", [128, TC], F32))
    ess = contextlib.ExitStack()
    th = ess.enter_context(nc.sbuf_tensor(pref + "th", [128, 2 * TC], F32))
    ld = P.sem(); ok = P.sem()
    for which, dst in (("s", sinT), ("c", cosT)):
        P.dma("gpsimd", th.ap()[:, 0:TC], io[pref + which + "h"], ld)
        t1 = P.dma("gpsimd", th.ap()[:, TC:], io[pref + which + "l"], ld)
        P.wait("vector", ld, t1)
        t2 = P.op("vector", lambda e: e.tensor_add(
            th.ap()[:, 0:TC], th.ap()[:, 0:TC], th.ap()[:, TC:]), inc=ok)
        P.wait("scalar", ok, t2)
        t3 = P.op("scalar", lambda e, d=dst: e.activation(d.ap(), th.ap()[:, 0:TC], AF.Sin),
                  inc=ok)
        P.wait("gpsimd", ok, t3)
    P.barrier()
    ess.close()
    return sinT, cosT


def emit_proj_featmajor(P, nc, es, co, xnT, w_tiled, noc, out_dram, tag,
                        bias_col_sb=None, bias_off=0, scale=1.0,
                        rope=None, elu1=False, ntok=TC, pswap=None):
    NB = ntok // 512
    ess = contextlib.ExitStack()
    wf = ess.enter_context(nc.sbuf_tensor(tag + "_wf", [128, ND * 128], F32))
    wb = ess.enter_context(nc.sbuf_tensor(tag + "_wb", [128, ND * 128], BF16))
    qr = ess.enter_context(nc.sbuf_tensor(tag + "_qr", [128, 512], F32))
    qs = ess.enter_context(nc.sbuf_tensor(tag + "_qs", [128, 512], F32))
    m1 = ess.enter_context(nc.sbuf_tensor(tag + "_m1", [128, 512], F32))
    ob = ess.enter_context(nc.sbuf_tensor(tag + "_ob", [128, 512], BF16))
    ps = ess.enter_context(nc.psum_tensor(tag + "_ps", [128, 512], F32))
    psQ = ess.enter_context(nc.psum_tensor(tag + "_psQ", [128, 512], F32)) \
        if rope is not None else None
    wld = P.sem(); wc = P.sem(); mm = P.sem(); ev = P.sem(); st = P.sem()
    xv = xnT.ap().rearrange("p (ic t) -> p ic t", ic=ND)
    for oc in range(noc):
        P.wait("sync", mm, mm.count)
        t_w = P.dma("sync", wf.ap().rearrange("p (ic m) -> p ic m", ic=ND),
                    w_tiled[oc].rearrange("ic p m -> p ic m"), wld)
        P.wait("vector", wld, t_w)
        P.wait("vector", mm, mm.count)
        t_c = P.op("vector", lambda e: e.tensor_copy(wb.ap(), wf.ap()), inc=wc)
        for nb in range(NB):
            P.wait("tensor", wc, t_c)
            P.wait("tensor", ev, ev.count)
            t_m = None
            for ic in range(ND):
                t_m = P.op("tensor", lambda e, ic=ic, nb=nb:
                           e.matmul(ps.ap(), wb.ap()[:, ic * 128:(ic + 1) * 128],
                                    xv[:, ic, nb * 512:(nb + 1) * 512],
                                    start=(ic == 0), stop=(ic == ND - 1)),
                           inc=mm if ic == ND - 1 else None)
            P.wait("scalar", mm, t_m)
            P.wait("scalar", st, st.count)
            if bias_col_sb is not None:
                bc = bias_col_sb.ap()[:, bias_off + oc:bias_off + oc + 1]
                t_e = P.op("scalar", lambda e, bc=bc: e.activation(
                    qr.ap(), ps.ap(), AF.Identity, bias=bc, scale=scale), inc=ev)
            else:
                t_e = P.op("scalar", lambda e: e.activation(
                    qr.ap(), ps.ap(), AF.Identity, scale=scale), inc=ev)
            cur = qr
            P.wait("vector", ev, t_e)
            P.wait("vector", st, st.count)
            if rope is not None:
                sinT, cosT = rope
                sv = sinT.ap()[:, nb * 512:(nb + 1) * 512]
                cv = cosT.ap()[:, nb * 512:(nb + 1) * 512]
                P.wait("tensor", ev, t_e)
                t_pm = P.op("tensor", lambda e: e.matmul(
                    psQ.ap(), pswap.ap(), qr.ap(), start=True, stop=True), inc=mm)
                P.op("vector", lambda e, cv=cv: e.tensor_mul(m1.ap(), qr.ap(), cv))
                P.wait("vector", mm, t_pm)
                P.op("vector", lambda e, sv=sv: e.tensor_mul(qs.ap(), psQ.ap(), sv))
                t_e = P.op("vector", lambda e: e.tensor_add(m1.ap(), m1.ap(), qs.ap()),
                           inc=ev)
                cur = m1
            if elu1:
                t_e = P.op("vector", lambda e, c=cur: e.tensor_scalar(
                    qs.ap(), c.ap(), 0.0, None, op0=ALU.min), inc=ev)
                P.wait("scalar", ev, t_e)
                t_e = P.op("scalar", lambda e: e.activation(qs.ap(), qs.ap(), AF.Exp),
                           inc=ev)
                P.wait("vector", ev, t_e)
                P.op("vector", lambda e, c=cur: e.tensor_scalar(
                    c.ap(), c.ap(), 0.0, None, op0=ALU.max))
                t_e = P.op("vector", lambda e, c=cur: e.tensor_add(c.ap(), c.ap(), qs.ap()),
                           inc=ev)
            P.wait("vector", ev, t_e)
            t_b = P.op("vector", lambda e, c=cur: e.tensor_copy(ob.ap(), c.ap()), inc=ev)
            P.wait("gpsimd", ev, t_b)
            P.dma("gpsimd", out_dram[oc * 128:(oc + 1) * 128, nb * 512:(nb + 1) * 512],
                  ob.ap(), st)
    P.wait("gpsimd", st, st.count)
    P.barrier()
    ess.close()


def emit_proj_tokmajor(P, nc, es, co, lhsT_sb, w_rows, nic, out_dram, tag,
                       bias_bcast=None, ntok=TC, nof=D, per_unit_epilogue=None):
    ntt = ntok // 128
    NOB = nof // 512
    ess = contextlib.ExitStack()
    wfs = ess.enter_context(nc.sbuf_tensor(tag + "_wfs", [128, 512], F32))
    wbp = ess.enter_context(nc.sbuf_tensor(tag + "_wbp", [128, nic * 512], BF16))
    obuf = ess.enter_context(nc.sbuf_tensor(tag + "_obf", [128, 512], BF16))
    ps = ess.enter_context(nc.psum_tensor(tag + "_ps", [128, 512], F32))
    wld = P.sem(); wc = P.sem(); mm = P.sem(); ev = P.sem(); st = P.sem()
    xv = lhsT_sb.ap().rearrange("p (ic t) -> p ic t", ic=nic)
    for tt in range(ntt):
        for obk in range(NOB):
            P.wait("sync", mm, mm.count)
            for ic in range(nic):
                t_w = P.dma("sync", wfs.ap(),
                            w_rows[ic * 128:(ic + 1) * 128, obk * 512:(obk + 1) * 512],
                            wld)
                P.wait("vector", wld, t_w)
                t_c = P.op("vector", lambda e, ic=ic: e.tensor_copy(
                    wbp.ap()[:, ic * 512:(ic + 1) * 512], wfs.ap()), inc=wc)
                P.wait("sync", wc, t_c)
            P.wait("tensor", wc, wc.count)
            P.wait("tensor", ev, ev.count)
            t_m = None
            for ic in range(nic):
                t_m = P.op("tensor", lambda e, ic=ic, tt=tt, obk=obk:
                           e.matmul(ps.ap(), xv[:, ic, tt * 128:(tt + 1) * 128],
                                    wbp.ap()[:, ic * 512:(ic + 1) * 512],
                                    start=(ic == 0), stop=(ic == nic - 1)),
                           inc=mm if ic == nic - 1 else None)
            if per_unit_epilogue is not None:
                per_unit_epilogue(tt, obk, ps, mm, t_m, ev)
            else:
                P.wait("vector", mm, t_m)
                P.wait("vector", st, st.count)
                if bias_bcast is not None:
                    t_e = P.op("vector", lambda e, obk=obk: e.tensor_add(
                        obuf.ap(), ps.ap(),
                        bias_bcast.ap()[:, obk * 512:(obk + 1) * 512]), inc=ev)
                else:
                    t_e = P.op("vector", lambda e: e.tensor_copy(obuf.ap(), ps.ap()),
                               inc=ev)
                P.wait("gpsimd", ev, t_e)
                P.dma("gpsimd", out_dram[tt * 128:(tt + 1) * 128,
                                         obk * 512:(obk + 1) * 512], obuf.ap(), st)
                P.wait("vector", st, st.count)
    P.wait("gpsimd", st, st.count)
    P.barrier()
    ess.close()



def build_l0qkv(nc, io):
    es = contextlib.ExitStack()
    with es:
        P = Prog(nc)
        co = emit_consts(P, nc, es)
        xnT = emit_xnT(P, nc, es, co, io["x0"])
        sinT, cosT = emit_rope_tables(P, nc, es, io, "r0")
        pswap = es.enter_context(nc.sbuf_tensor("pswap_s", [128, 128], F32))
        psld = P.sem()
        t = P.dma("sync", pswap.ap(), io["pswap"], psld)
        P.wait("tensor", psld, t)
        bcol = es.enter_context(nc.sbuf_tensor("bcol", [128, 2 * ND], F32))
        bvb = es.enter_context(nc.sbuf_tensor("bvb", [128, D], F32))
        bld = P.sem(); bsc = P.sem()
        P.dma("gpsimd", bcol.ap()[:, 0:ND], io["bq_c"], bld)
        P.dma("gpsimd", bcol.ap()[:, ND:], io["bk_c"], bld)
        t = P.dma("gpsimd", bvb.ap(), io["bv_row"].broadcast_to([128, D]), bld)
        P.wait("vector", bld, t)
        P.op("vector", lambda e: e.tensor_scalar(bcol.ap()[:, 0:ND], bcol.ap()[:, 0:ND],
                                                 SCALE0, None, op0=ALU.mult), inc=bsc)
        P.barrier()
        emit_proj_featmajor(P, nc, es, co, xnT, io["wq_t"], ND, io["qT0"], "q",
                            bias_col_sb=bcol, bias_off=0, scale=SCALE0,
                            rope=(sinT, cosT), pswap=pswap)
        emit_proj_featmajor(P, nc, es, co, xnT, io["wk_t"], ND, io["kT0"], "k",
                            bias_col_sb=bcol, bias_off=ND, scale=1.0,
                            rope=(sinT, cosT), pswap=pswap)
        emit_proj_tokmajor(P, nc, es, co, xnT, io["wv_r"], ND, io["v0"], "v",
                           bias_bcast=bvb)
        P.finalize(es)
    return nc


def build_l0attn(nc, io, p):
    ranges = own_ranges(p)
    kmax = max(r[1] for r in ranges)
    nkt_all = kmax // 128
    es = contextlib.ExitStack()
    with es:
        P = Prog(nc)
        co = emit_consts(P, nc, es)
        masks = es.enter_context(nc.sbuf_tensor("masks", [128, 4 * 512], BF16))
        mk = P.sem()
        t = None
        for r in range(4):
            P.op("gpsimd", lambda e, r=r: e.memset(masks.ap()[:, r * 512:(r + 1) * 512], 0.0))
            t = P.op("gpsimd", lambda e, r=r: e.affine_select(
                out=masks.ap()[:, r * 512:(r + 1) * 512],
                in_=masks.ap()[:, r * 512:(r + 1) * 512],
                compare_op=ALU.is_gt, fill=1.0, base=r * 128,
                pattern=[[-1, 512]], channel_multiplier=1), inc=mk)
        for e in ("tensor", "vector", "scalar"):
            P.wait(e, mk, t)
        qT = es.enter_context(nc.sbuf_tensor("qT", [128, ND * TC], BF16))
        qld = P.sem()
        t = P.dma("sync", qT.ap().rearrange("p (h t) -> p h t", h=ND),
                  io["qT0"].rearrange("(h p) t -> p h t", p=128), qld)
        P.wait("tensor", qld, t)
        kT = es.enter_context(nc.sbuf_tensor("kT_s", [128, nkt_all * 128], BF16))
        vS = es.enter_context(nc.sbuf_tensor("vS", [128, nkt_all * 128], BF16))
        pbf = es.enter_context(nc.sbuf_tensor("pbf", [128, 512], BF16))
        nscr = es.enter_context(nc.sbuf_tensor("nscr", [128, 512], F32))
        rden = es.enter_context(nc.sbuf_tensor("rden", [1, 512], F32))
        aob = es.enter_context(nc.sbuf_tensor("aob", [128, 512], BF16))
        psS = es.enter_context(nc.psum_tensor("psS", [128, 512], F32))
        psA = es.enter_context(nc.psum_tensor("psA", [128, 512], F32))
        psD = es.enter_context(nc.psum_tensor("psD", [1, 512], F32))
        psB = es.enter_context(nc.psum_tensor("psB", [128, 512], F32))
        kld = P.sem(); mmS = P.sem(); exq = P.sem(); kkt = P.sem()
        hdn = P.sem(); evq = P.sem(); st = P.sem()
        qblocks = []
        for li, (g0, g1) in enumerate(ranges):
            for jj in range((g1 - g0) // 512):
                qblocks.append((li * 1024 + jj * 512, g0 + jj * 512))
        qv = qT.ap().rearrange("p (hh t) -> p hh t", hh=ND)
        vv = vS.ap().rearrange("p (kt d) -> p kt d", kt=nkt_all)
        for h in range(H):
            P.wait("sync", hdn, hdn.count)
            t_k = P.dma("sync", kT.ap(), io["kT"][h * 128:(h + 1) * 128, 0:kmax], kld)
            for k8 in range(0, nkt_all, 8):
                k8e = min(k8 + 8, nkt_all)
                t_k = P.dma("sync", vv[:, k8:k8e, :],
                            io["v"].rearrange("(kt p) d -> p kt d", p=128)[
                                :, k8:k8e, h * 128:(h + 1) * 128], kld)
            P.wait("tensor", kld, t_k)
            for (lq, gq) in qblocks:
                nkt = (gq + 512) // 128
                qrhs = qv[:, h, lq:lq + 512]
                for kt in range(nkt):
                    P.wait("tensor", exq, exq.count)
                    t_s = P.op("tensor", lambda e, kt=kt, qrhs=qrhs:
                               e.matmul(psS.ap(), kT.ap()[:, kt * 128:(kt + 1) * 128],
                                        qrhs, start=True, stop=True), inc=mmS)
                    P.wait("scalar", mmS, t_s)
                    P.wait("scalar", kkt, kkt.count)
                    t_x = P.op("scalar", lambda e: e.activation(
                        pbf.ap(), psS.ap(), AF.Exp, bias=co["negC"].ap()), inc=exq)
                    r = kt * 128 - gq
                    if r >= 0:
                        P.wait("vector", exq, t_x)
                        t_x = P.op("vector", lambda e, r=r: e.tensor_mul(
                            pbf.ap(), pbf.ap(),
                            masks.ap()[:, (r // 128) * 512:(r // 128) * 512 + 512]),
                            inc=exq)
                    P.wait("tensor", exq, t_x)
                    P.op("tensor", lambda e, kt=kt, first=(kt == 0), last=(kt == nkt - 1):
                         e.matmul(psA.ap(), vv[:, kt, :], pbf.ap(),
                                  start=first, stop=last))
                    P.op("tensor", lambda e, first=(kt == 0), last=(kt == nkt - 1):
                         e.matmul(psD.ap(), co["ones_c_b"].ap(), pbf.ap(),
                                  start=first, stop=last), inc=kkt)
                P.wait("vector", kkt, kkt.count)
                P.wait("vector", st, st.count)
                t_r = P.op("vector", lambda e: e.reciprocal(rden.ap(), psD.ap()), inc=evq)
                P.wait("tensor", evq, t_r)
                t_b = P.op("tensor", lambda e: e.matmul(
                    psB.ap(), co["ones_r_f"].ap(), rden.ap(), start=True, stop=True),
                    inc=hdn)
                t_n = P.op("vector", lambda e: e.tensor_copy(nscr.ap(), psA.ap()), inc=evq)
                P.wait("vector", hdn, t_b)
                t_o = P.op("vector", lambda e: e.tensor_mul(aob.ap(), nscr.ap(), psB.ap()),
                           inc=evq)
                P.wait("gpsimd", evq, t_o)
                P.dma("gpsimd", io["aoT0"][h * 128:(h + 1) * 128, lq:lq + 512],
                      aob.ap(), st)
                P.wait("tensor", evq, t_o)
        P.wait("sync", st, st.count)
        P.finalize(es)
    return nc


def build_oproj(nc, io):
    es = contextlib.ExitStack()
    with es:
        P = Prog(nc)
        co = emit_consts(P, nc, es)
        aoT = es.enter_context(nc.sbuf_tensor("aoT_s", [128, ND * TC], BF16))
        qld = P.sem()
        t = P.dma("sync", aoT.ap().rearrange("p (c t) -> p c t", c=ND),
                  io["aoT"].rearrange("(c p) t -> p c t", p=128), qld)
        P.wait("tensor", qld, t)
        bob = es.enter_context(nc.sbuf_tensor("bob", [128, D], F32))
        bobld = P.sem()
        t_b = P.dma("gpsimd", bob.ap(), io["bo_row"].broadcast_to([128, D]), bobld)
        raw = es.enter_context(nc.sbuf_tensor("raw", [128, D], F32))
        xs = es.enter_context(nc.sbuf_tensor("oxs", [128, D], F32))
        scr = es.enter_context(nc.sbuf_tensor("oscr", [128, D], F32))
        cols = es.enter_context(nc.sbuf_tensor("ocols", [128, 6], F32))
        ob2 = es.enter_context(nc.sbuf_tensor("oob", [128, D], F32))
        vs = P.sem(); st = P.sem(); ld2 = P.sem()

        def epi(tt, obk, ps, mm, t_m, ev):
            P.wait("vector", mm, t_m)
            P.wait("vector", bobld, t_b)
            t_e = P.op("vector", lambda e, obk=obk: e.tensor_add(
                raw.ap()[:, obk * 512:(obk + 1) * 512], ps.ap(),
                bob.ap()[:, obk * 512:(obk + 1) * 512]), inc=ev)
            P.wait("tensor", ev, t_e)
            if obk == 3:
                P.wait("sync", vs, vs.count)
                t_x = P.dma("sync", xs.ap(), io["x_in"][tt * 128:(tt + 1) * 128, :], ld2)
                P.wait("scalar", ev, t_e)
                t_n = emit_ln_cols(P, co, raw.ap(), scr.ap(), cols.ap(), vs)
                P.wait("vector", vs, t_n)
                P.op("vector", lambda e: e.tensor_scalar(
                    scr.ap(), raw.ap(), cols.ap()[:, 3:4], cols.ap()[:, 4:5],
                    op0=ALU.mult, op1=ALU.add))
                P.wait("vector", ld2, t_x)
                t_s = P.op("vector", lambda e: e.tensor_add(ob2.ap(), xs.ap(), scr.ap()),
                           inc=vs)
                P.wait("gpsimd", vs, t_s)
                P.dma("gpsimd", io["x_out"][tt * 128:(tt + 1) * 128, :], ob2.ap(), st)
                P.wait("vector", st, st.count)
                P.wait("scalar", st, st.count)

        emit_proj_tokmajor(P, nc, es, co, aoT, io["wo_r"], ND, None, "o",
                           per_unit_epilogue=epi)
        P.wait("sync", st, st.count)
        P.finalize(es)
    return nc


def build_l0ffn(nc, io):
    es = contextlib.ExitStack()
    with es:
        P = Prog(nc)
        co = emit_consts(P, nc, es)
        gsc = es.enter_context(nc.sbuf_tensor("gsc", [128, 2 * NH0], F32))
        dsb = es.enter_context(nc.sbuf_tensor("dsb", [128, D], F32))
        bld = P.sem()
        P.dma("gpsimd", gsc.ap()[:, 0:NH0], io["gate_s_c"], bld)
        P.dma("gpsimd", gsc.ap()[:, NH0:], io["up_s_c"], bld)
        P.dma("gpsimd", dsb.ap(), io["down_s_row"].broadcast_to([128, D]), bld)
        NQ = 4
        QT = TC // NQ
        hdnT = es.enter_context(nc.sbuf_tensor("hdnT", [128, NH0 * QT], BF16))
        rawh = es.enter_context(nc.sbuf_tensor("rawh", [128, 4 * D], F32))
        xbufs = alloc_xnT_bufs(nc, es, "fx", QT // 128)
        wf = es.enter_context(nc.sbuf_tensor("fwf", [128, 2 * ND * 128], F32))
        wb = es.enter_context(nc.sbuf_tensor("fwb", [128, 2 * ND * 128], BF16))
        sil = es.enter_context(nc.sbuf_tensor("fsil", [128, 512], F32))
        ups = es.enter_context(nc.sbuf_tensor("fups", [128, 512], F32))
        psG = es.enter_context(nc.psum_tensor("psG", [128, 512], F32))
        psU = es.enter_context(nc.psum_tensor("psU", [128, 512], F32))
        dwf = es.enter_context(nc.sbuf_tensor("dwf", [128, 512], F32))
        dwb = es.enter_context(nc.sbuf_tensor("dwb", [128, 512], BF16))
        psE = [es.enter_context(nc.psum_tensor(f"psE{i}", [128, 512], F32))
               for i in range(4)]
        pxs = es.enter_context(nc.sbuf_tensor("pxs", [128, D], F32))
        pscr = es.enter_context(nc.sbuf_tensor("pscr", [128, D], F32))
        pcols = es.enter_context(nc.sbuf_tensor("pcols", [128, 6], F32))
        pob = es.enter_context(nc.sbuf_tensor("pob", [128, D], F32))
        P.barrier()
        for q in range(NQ):
            tt0 = q * (QT // 128)
            xnT = emit_xnT(P, nc, es, co, io["x1"], tag=f"fx", ntt=QT // 128,
                           tt0=tt0, bufs=xbufs)
            wld = P.sem(); wc = P.sem(); mm = P.sem(); ev = P.sem()
            xv = xnT.ap().rearrange("p (ic t) -> p ic t", ic=ND)
            for hc in range(NH0):
                P.wait("sync", mm, mm.count)
                P.dma("sync", wf.ap().rearrange("p (q2 ic m) -> p q2 ic m", q2=2, ic=ND, m=128)[:, 0],
                      io["gate_t"][hc].rearrange("ic p m -> p ic m"), wld)
                t_w = P.dma("sync", wf.ap().rearrange("p (q2 ic m) -> p q2 ic m", q2=2, ic=ND, m=128)[:, 1],
                            io["up_t"][hc].rearrange("ic p m -> p ic m"), wld)
                P.wait("vector", wld, t_w)
                P.wait("vector", mm, mm.count)
                t_c = P.op("vector", lambda e: e.tensor_copy(wb.ap(), wf.ap()), inc=wc)
                P.wait("tensor", wc, t_c)
                P.wait("tensor", ev, ev.count)
                for ic in range(ND):
                    P.op("tensor", lambda e, ic=ic:
                         e.matmul(psG.ap(), wb.ap()[:, ic * 128:(ic + 1) * 128],
                                  xv[:, ic, 0:512],
                                  start=(ic == 0), stop=(ic == ND - 1)))
                t_m = None
                for ic in range(ND):
                    t_m = P.op("tensor", lambda e, ic=ic:
                               e.matmul(psU.ap(),
                                        wb.ap()[:, (ND + ic) * 128:(ND + ic + 1) * 128],
                                        xv[:, ic, 0:512],
                                        start=(ic == 0), stop=(ic == ND - 1)),
                               inc=mm if ic == ND - 1 else None)
                P.wait("scalar", mm, t_m)
                P.wait("scalar", ev, max(0, ev.count - 1))
                t_s = P.op("scalar", lambda e, hc=hc: e.activation(
                    sil.ap(), psG.ap(), AF.Silu, scale=gsc.ap()[:, hc:hc + 1]), inc=ev)
                P.wait("vector", ev, t_s)
                P.wait("vector", bld, 48)
                t_u = P.op("vector", lambda e, hc=hc: e.tensor_scalar(
                    ups.ap(), psU.ap(), gsc.ap()[:, NH0 + hc:NH0 + hc + 1],
                    None, op0=ALU.mult), inc=ev)
                t_h = P.op("vector", lambda e, hc=hc: e.tensor_mul(
                    hdnT.ap().rearrange("p (hc t) -> p hc t", hc=NH0)[:, hc, :],
                    sil.ap(), ups.ap()), inc=ev)
                P.wait("tensor", ev, t_h)
            P.barrier()
            wld = P.sem(); wc = P.sem(); mm = P.sem(); ev = P.sem()
            hv = hdnT.ap().rearrange("p (hc t) -> p hc t", hc=NH0)
            for obk in range(4):
                for hc in range(NH0):
                    P.wait("sync", mm, mm.count)
                    t_w = P.dma("sync", dwf.ap(),
                                io["down_r"][hc * 128:(hc + 1) * 128,
                                             obk * 512:(obk + 1) * 512], wld)
                    P.wait("vector", wld, t_w)
                    P.wait("vector", mm, mm.count)
                    t_c = P.op("vector", lambda e: e.tensor_copy(dwb.ap(), dwf.ap()),
                               inc=wc)
                    P.wait("tensor", wc, t_c)
                    if hc == 0:
                        P.wait("tensor", ev, ev.count)
                    t_m = None
                    for tt in range(4):
                        t_m = P.op("tensor", lambda e, tt=tt, hc=hc:
                                   e.matmul(psE[tt].ap(),
                                            hv[:, hc, tt * 128:(tt + 1) * 128],
                                            dwb.ap(), start=(hc == 0),
                                            stop=(hc == NH0 - 1)),
                                   inc=mm if tt == 3 else None)
                P.wait("vector", mm, mm.count)
                for tt in range(4):
                    P.op("vector", lambda e, tt=tt, obk=obk: e.tensor_mul(
                        rawh.ap()[:, tt * D + obk * 512: tt * D + obk * 512 + 512],
                        psE[tt].ap(), dsb.ap()[:, obk * 512:(obk + 1) * 512]),
                        inc=ev)
                P.wait("tensor", ev, ev.count)
            P.barrier()
            vs = P.sem(); st = P.sem(); ld2 = P.sem()
            for i in range(4):
                tt = tt0 + i
                raw_ap = rawh.ap()[:, i * D:(i + 1) * D]
                P.wait("sync", vs, vs.count)
                t_x = P.dma("sync", pxs.ap(), io["x1"][tt * 128:(tt + 1) * 128, :], ld2)
                t_n = emit_ln_cols(P, co, raw_ap, pscr.ap(), pcols.ap(), vs)
                P.wait("vector", vs, t_n)
                P.op("vector", lambda e, raw_ap=raw_ap: e.tensor_scalar(
                    pscr.ap(), raw_ap, pcols.ap()[:, 3:4], pcols.ap()[:, 4:5],
                    op0=ALU.mult, op1=ALU.add))
                P.wait("vector", ld2, t_x)
                t_s = P.op("vector", lambda e: e.tensor_add(pob.ap(), pxs.ap(), pscr.ap()),
                           inc=vs)
                P.wait("gpsimd", vs, t_s)
                P.dma("gpsimd", io["x2"][tt * 128:(tt + 1) * 128, :], pob.ap(), st)
                P.wait("vector", st, st.count)
                P.wait("scalar", st, st.count)
            P.wait("sync", st, st.count)
            P.wait("gpsimd", st, st.count)
            P.barrier()
        P.finalize(es)
    return nc


def build_l1qkv(nc, io):
    es = contextlib.ExitStack()
    with es:
        P = Prog(nc)
        co = emit_consts(P, nc, es)
        xnT = emit_xnT(P, nc, es, co, io["x2"])
        sinT, cosT = emit_rope_tables(P, nc, es, io, "r1")
        pswap = es.enter_context(nc.sbuf_tensor("pswap_s", [128, 128], F32))
        psld = P.sem()
        t = P.dma("sync", pswap.ap(), io["pswap"], psld)
        P.wait("tensor", psld, t)
        emit_proj_featmajor(P, nc, es, co, xnT, io["wq1_t"], ND, io["qT1"], "q1",
                            rope=(sinT, cosT), elu1=True, pswap=pswap)
        emit_proj_featmajor(P, nc, es, co, xnT, io["wk1_t"], ND, io["kT1"], "k1",
                            rope=(sinT, cosT), elu1=True, pswap=pswap)
        emit_proj_tokmajor(P, nc, es, co, xnT, io["wv1_r"], ND, io["v1"], "v1")
        esm = contextlib.ExitStack()
        with esm:
            kTh = esm.enter_context(nc.sbuf_tensor("kTh", [128, 256], BF16))
            kc = esm.enter_context(nc.sbuf_tensor("kc", [128, 2 * 128], BF16))
            vc = esm.enter_context(nc.sbuf_tensor("vc", [128, 2 * 128], BF16))
            mout = esm.enter_context(nc.sbuf_tensor("mout", [128, 128], F32))
            zacc = esm.enter_context(nc.sbuf_tensor("zacc", [128, 128], F32))
            psT = esm.enter_context(nc.psum_tensor("psT", [128, 256], BF16))
            psM = esm.enter_context(nc.psum_tensor("psM", [128, 128], F32))
            psZ = esm.enter_context(nc.psum_tensor("psZ", [128, 1], F32))
            ld = P.sem(); tp = P.sem(); mm = P.sem(); ev = P.sem(); st = P.sem()
            for lc in range(8):
                for h in range(H):
                    P.wait("sync", mm, mm.count)
                    t_l = P.dma("sync", kTh.ap(),
                                io["kT1"][h * 128:(h + 1) * 128, lc * 256:(lc + 1) * 256], ld)
                    t_l = P.dma("sync", vc.ap().rearrange("p (kt d) -> p kt d", kt=2),
                                io["v1"].rearrange("(kt p) d -> p kt d", p=128)[
                                    :, lc * 2:lc * 2 + 2, h * 128:(h + 1) * 128], ld)
                    P.wait("tensor", ld, t_l)
                    P.wait("tensor", ev, ev.count)
                    t_t = None
                    for kt in range(2):
                        t_t = P.op("tensor", lambda e, kt=kt: e.transpose(
                            psT.ap()[:, kt * 128:(kt + 1) * 128],
                            kTh.ap()[:, kt * 128:(kt + 1) * 128],
                            co["ident_b"].ap()), inc=tp if kt == 1 else None)
                    P.wait("vector", tp, t_t)
                    t_c = P.op("vector", lambda e: e.tensor_copy(kc.ap(), psT.ap()), inc=ev)
                    P.wait("tensor", ev, t_c)
                    t_m = None
                    for kt in range(2):
                        P.op("tensor", lambda e, kt=kt: e.matmul(
                            psM.ap(), kc.ap()[:, kt * 128:(kt + 1) * 128],
                            vc.ap()[:, kt * 128:(kt + 1) * 128],
                            start=(kt == 0), stop=(kt == 1)))
                        t_m = P.op("tensor", lambda e, kt=kt: e.matmul(
                            psZ.ap(), kc.ap()[:, kt * 128:(kt + 1) * 128],
                            co["ones_c_b"].ap(),
                            start=(kt == 0), stop=(kt == 1)),
                            inc=mm if kt == 1 else None)
                    P.wait("vector", mm, t_m)
                    P.wait("vector", st, st.count)
                    t_e = P.op("vector", lambda e: e.tensor_copy(mout.ap(), psM.ap()), inc=ev)
                    t_z = P.op("vector", lambda e, lc=lc, h=h: e.tensor_copy(
                        zacc.ap()[:, lc * 16 + h:lc * 16 + h + 1], psZ.ap()), inc=ev)
                    P.wait("gpsimd", ev, t_e)
                    P.dma("gpsimd", io["M_own"][lc, h], mout.ap(), st)
                    P.wait("tensor", ev, t_z)
            P.wait("gpsimd", ev, ev.count)
            P.dma("gpsimd", io["z_cols"], zacc.ap(), st)
            P.wait("sync", st, st.count)
            P.barrier()
        P.finalize(es)
    return nc


def build_l1attn(nc, io, p):
    es = contextlib.ExitStack()
    with es:
        P = Prog(nc)
        co = emit_consts(P, nc, es)
        chunks = own_chunks(p)
        masks = es.enter_context(nc.sbuf_tensor("masks1", [128, 2 * 256], BF16))
        mk = P.sem()
        t = None
        for r in range(2):
            P.op("gpsimd", lambda e, r=r: e.memset(masks.ap()[:, r * 256:(r + 1) * 256], 0.0))
            t = P.op("gpsimd", lambda e, r=r: e.affine_select(
                out=masks.ap()[:, r * 256:(r + 1) * 256],
                in_=masks.ap()[:, r * 256:(r + 1) * 256],
                compare_op=ALU.is_gt, fill=1.0, base=r * 128,
                pattern=[[-1, 256]], channel_multiplier=1), inc=mk)
        for e in ("tensor", "vector", "scalar"):
            P.wait(e, mk, t)
        qT = es.enter_context(nc.sbuf_tensor("qT1s", [128, ND * TC], BF16))
        kT = es.enter_context(nc.sbuf_tensor("kT1s", [128, ND * TC], BF16))
        zc = es.enter_context(nc.sbuf_tensor("zc1", [128, 256], F32))
        qld = P.sem()
        P.dma("sync", qT.ap().rearrange("p (h t) -> p h t", h=ND),
              io["qT1"].rearrange("(h p) t -> p h t", p=128), qld)
        P.dma("sync", kT.ap().rearrange("p (h t) -> p h t", h=ND),
              io["kT1"].rearrange("(h p) t -> p h t", p=128), qld)
        t = P.dma("sync", zc.ap(), io["z_all_cols"], qld)
        for e in ("tensor", "vector"):
            P.wait(e, qld, t)
        S_run = es.enter_context(nc.sbuf_tensor("S_run", [128, 128], F32))
        z_run = es.enter_context(nc.sbuf_tensor("z_run", [128, 1], F32))
        Sb = es.enter_context(nc.sbuf_tensor("Sb", [128, 128], BF16))
        zb = es.enter_context(nc.sbuf_tensor("zb", [128, 1], BF16))
        Mt = es.enter_context(nc.sbuf_tensor("Mt", [128, 128], F32))
        vc = es.enter_context(nc.sbuf_tensor("vc1", [128, 2 * 128], BF16))
        pa = es.enter_context(nc.sbuf_tensor("pa", [128, 2 * 256], BF16))
        nscr = es.enter_context(nc.sbuf_tensor("nscr1", [128, 256], F32))
        rden = es.enter_context(nc.sbuf_tensor("rden1", [1, 256], F32))
        aob = es.enter_context(nc.sbuf_tensor("aob1", [128, 256], BF16))
        psS = es.enter_context(nc.psum_tensor("psS1", [128, 256], F32))
        psN = es.enter_context(nc.psum_tensor("psN1", [128, 256], F32))
        psD = es.enter_context(nc.psum_tensor("psD1", [1, 256], F32))
        psB = es.enter_context(nc.psum_tensor("psB1", [128, 256], F32))
        ld = P.sem(); mm = P.sem(); ev = P.sem(); st = P.sem(); acc = P.sem()
        ldM = P.sem()
        qv = qT.ap().rearrange("p (h t) -> p h t", h=ND)
        kv = kT.ap().rearrange("p (h t) -> p h t", h=ND)
        for h in range(H):
            P.edone("gpsimd", "vector")
            t_z0 = P.op("gpsimd", lambda e: e.memset(S_run.ap(), 0.0))
            t_z0 = P.op("gpsimd", lambda e: e.memset(z_run.ap(), 0.0), inc=acc)
            P.wait("vector", acc, acc.count)
            P.edone("vector", "gpsimd")
            P.edone("tensor", "gpsimd")
            for g in range(16):
                if g in chunks:
                    lc = chunks.index(g)
                    P.wait("vector", mm, mm.count)
                    t_cb = P.op("vector", lambda e: e.tensor_copy(Sb.ap(), S_run.ap()))
                    t_cb = P.op("vector", lambda e: e.tensor_copy(zb.ap(), z_run.ap()),
                                inc=ev)
                    P.wait("sync", mm, mm.count)
                    t_l = P.dma("sync", vc.ap().rearrange("p (kt d) -> p kt d", kt=2),
                                io["v1"].rearrange("(kt p) d -> p kt d", p=128)[
                                    :, lc * 2:lc * 2 + 2, h * 128:(h + 1) * 128], ld)
                    for kt in range(2):
                        P.wait("tensor", ev, ev.count)
                        t_s = P.op("tensor", lambda e, kt=kt, lc=lc: e.matmul(
                            psS.ap(), kv[:, h, lc * 256 + kt * 128:lc * 256 + (kt + 1) * 128],
                            qv[:, h, lc * 256:(lc + 1) * 256], start=True, stop=True),
                            inc=mm)
                        P.wait("vector", mm, t_s)
                        t_p = P.op("vector", lambda e, kt=kt: e.tensor_mul(
                            pa.ap()[:, kt * 256:(kt + 1) * 256], psS.ap(),
                            masks.ap()[:, kt * 256:(kt + 1) * 256]), inc=ev)
                    P.wait("tensor", ev, ev.count)
                    P.wait("tensor", ld, ld.count)
                    for kt in range(2):
                        P.op("tensor", lambda e, kt=kt: e.matmul(
                            psN.ap(), vc.ap()[:, kt * 128:(kt + 1) * 128],
                            pa.ap()[:, kt * 256:(kt + 1) * 256],
                            start=(kt == 0), stop=False))
                    P.op("tensor", lambda e, lc=lc: e.matmul(
                        psN.ap(), Sb.ap(), qv[:, h, lc * 256:(lc + 1) * 256],
                        start=False, stop=True))
                    for kt in range(2):
                        P.op("tensor", lambda e, kt=kt: e.matmul(
                            psD.ap(), co["ones_c_b"].ap(),
                            pa.ap()[:, kt * 256:(kt + 1) * 256],
                            start=(kt == 0), stop=False))
                    t_m = P.op("tensor", lambda e, lc=lc: e.matmul(
                        psD.ap(), zb.ap(), qv[:, h, lc * 256:(lc + 1) * 256],
                        start=False, stop=True), inc=mm)
                    P.wait("vector", mm, t_m)
                    P.wait("vector", st, st.count)
                    P.op("vector", lambda e: e.tensor_scalar(
                        rden.ap(), psD.ap(), 1e-6, None, op0=ALU.max))
                    t_r = P.op("vector", lambda e: e.reciprocal(rden.ap(), rden.ap()),
                               inc=ev)
                    P.wait("tensor", ev, t_r)
                    t_bc = P.op("tensor", lambda e: e.matmul(
                        psB.ap(), co["ones_r_f"].ap(), rden.ap(), start=True, stop=True),
                        inc=mm)
                    t_n = P.op("vector", lambda e: e.tensor_copy(nscr.ap(), psN.ap()),
                               inc=ev)
                    P.wait("vector", mm, t_bc)
                    t_o = P.op("vector", lambda e: e.tensor_mul(
                        aob.ap(), nscr.ap(), psB.ap()), inc=ev)
                    P.wait("gpsimd", ev, t_o)
                    P.dma("gpsimd", io["aoT1"][h * 128:(h + 1) * 128,
                                               lc * 256:(lc + 1) * 256], aob.ap(), st)
                    P.wait("tensor", ev, t_o)
                P.wait("sync", acc, acc.count)
                t_l = P.dma("sync", Mt.ap(), io["M_all"][g, h], ldM)
                P.wait("vector", ldM, t_l)
                P.wait("vector", ev, ev.count)
                t_a = P.op("vector", lambda e: e.tensor_add(S_run.ap(), S_run.ap(), Mt.ap()))
                t_a = P.op("vector", lambda e, g=g, h=h: e.tensor_add(
                    z_run.ap(), z_run.ap(), zc.ap()[:, g * 16 + h:g * 16 + h + 1]),
                    inc=acc)
                P.wait("sync", acc, t_a)
        P.wait("sync", st, st.count)
        P.finalize(es)
    return nc


def build_l1tern(nc, io):
    es = contextlib.ExitStack()
    with es:
        P = Prog(nc)
        co = emit_consts(P, nc, es)
        wf = es.enter_context(nc.sbuf_tensor("twf", [128, ND * 128], F32))
        ab = es.enter_context(nc.sbuf_tensor("tab", [128, ND * 128], F32))
        s1 = es.enter_context(nc.sbuf_tensor("ts1", [128, ND * 128], F32))
        s2 = es.enter_context(nc.sbuf_tensor("ts2", [128, ND * 128], F32))
        obf = es.enter_context(nc.sbuf_tensor("tob", [128, ND * 128], BF16))
        msc = es.enter_context(nc.sbuf_tensor("tmsc", [1, 8], F32))
        thr = es.enter_context(nc.sbuf_tensor("tthr", [128, 4], F32))
        psA = es.enter_context(nc.psum_tensor("psTa", [1, 512], F32))
        psB = es.enter_context(nc.psum_tensor("psTb", [128, 1], F32))
        ld = P.sem(); cst = P.sem(); mm = P.sem(); ev = P.sem(); st = P.sem()
        specs = [("fc", io["w_fc_t"], NH1, True, io["tern_fc_t"]),
                 ("pj", io["w_proj_r"], NH1, False, io["tern_proj_r"])]
        NTOT = H1 * D

        def bcast_scalar(src_ap, dst_col_ap):
            P.edone("tensor", "vector")
            t_bb = P.op("tensor", lambda e: e.matmul(
                psB.ap(), co["ones_r_f"].ap(), src_ap, start=True, stop=True), inc=mm)
            P.wait("vector", mm, t_bb)
            return P.op("vector", lambda e: e.tensor_copy(dst_col_ap, psB.ap()), inc=ev)

        for mi, (tag, wsrc, nb_, tiled, dst) in enumerate(specs):
            nbands = NH1 if tiled else (H1 // 128)
            for ph in range(2):
                for bi in range(nbands):
                    P.wait("sync", mm, mm.count)
                    if tiled:
                        t_w = P.dma("sync", wf.ap().rearrange("p (ic m) -> p ic m", ic=ND),
                                    wsrc[bi].rearrange("ic p m -> p ic m"), ld)
                    else:
                        t_w = P.dma("sync", wf.ap(), wsrc[bi * 128:(bi + 1) * 128, :], ld)
                    P.wait("scalar", ld, t_w)
                    P.wait("scalar", mm, mm.count)
                    if ph == 0:
                        t_a = P.op("scalar", lambda e: e.activation(ab.ap(), wf.ap(), AF.Abs),
                                   inc=cst)
                    else:
                        P.op("scalar", lambda e: e.activation(ab.ap(), wf.ap(), AF.Abs))
                        P.edone("vector", "scalar")
                        t_a = P.op("vector", lambda e, mi=mi: e.tensor_scalar(
                            ab.ap(), ab.ap(), thr.ap()[:, 2 + mi:3 + mi], None,
                            op0=ALU.add), inc=cst)
                    P.wait("tensor", cst, t_a)
                    for j in range(4):
                        t_m = P.op("tensor", lambda e, j=j, first=(bi == 0 and j == 0),
                                   last=(bi == nbands - 1 and j == 3):
                                   e.matmul(psA.ap(), co["ones_c_f"].ap(),
                                            ab.ap()[:, j * 512:(j + 1) * 512],
                                            start=first, stop=last),
                                   inc=mm if j == 3 else None)
                P.wait("vector", mm, mm.count)
                if ph == 0:
                    t_v = P.op("vector", lambda e, mi=mi: e.reduce_sum(
                        msc.ap()[:, 6 + mi:7 + mi], psA.ap(), axis=AXX), inc=ev)
                    t_v = P.op("vector", lambda e, mi=mi: e.tensor_scalar(
                        msc.ap()[:, 6 + mi:7 + mi], msc.ap()[:, 6 + mi:7 + mi],
                        1.0 / NTOT, None, op0=ALU.mult), inc=ev)
                    t_v = P.op("vector", lambda e, mi=mi: e.tensor_scalar(
                        msc.ap()[:, 4:5], msc.ap()[:, 6 + mi:7 + mi], -1.0, None,
                        op0=ALU.mult), inc=ev)
                    t_v = bcast_scalar(msc.ap()[:, 4:5], thr.ap()[:, 2 + mi:3 + mi])
                    P.wait("tensor", ev, t_v)
                    P.wait("scalar", ev, t_v)
                else:
                    t_v = P.op("vector", lambda e, mi=mi: e.reduce_sum(
                        msc.ap()[:, 4:5], psA.ap(), axis=AXX), inc=ev)
                    P.op("vector", lambda e, mi=mi: e.tensor_scalar(
                        msc.ap()[:, 4:5], msc.ap()[:, 4:5], 1.0 / NTOT, None,
                        op0=ALU.mult))
                    P.op("vector", lambda e, mi=mi: e.tensor_add(
                        msc.ap()[:, mi:mi + 1], msc.ap()[:, 4:5],
                        msc.ap()[:, 6 + mi:7 + mi]))
                    t_v = P.op("vector", lambda e, mi=mi: e.tensor_scalar(
                        msc.ap()[:, mi:mi + 1], msc.ap()[:, mi:mi + 1], 1e-5, None,
                        op0=ALU.max), inc=ev)
                    P.op("vector", lambda e, mi=mi: e.tensor_scalar(
                        msc.ap()[:, 4:5], msc.ap()[:, mi:mi + 1], -0.7, None,
                        op0=ALU.mult))
                    t_v = bcast_scalar(msc.ap()[:, 4:5], thr.ap()[:, mi:mi + 1])
                    P.wait("tensor", ev, t_v)
                    P.wait("scalar", ev, t_v)
        P.op("vector", lambda e: e.tensor_mul(msc.ap()[:, 5:6], msc.ap()[:, 0:1],
                                              msc.ap()[:, 0:1]))
        t_v = P.op("vector", lambda e: e.tensor_mul(msc.ap()[:, 5:6], msc.ap()[:, 5:6],
                                                    msc.ap()[:, 1:2]), inc=ev)
        t_v = bcast_scalar(msc.ap()[:, 5:6], s1.ap()[:, 0:1])
        P.wait("gpsimd", ev, t_v)
        P.dma("gpsimd", io["c_col"], s1.ap()[:, 0:1], st)
        P.wait("gpsimd", st, st.count)
        P.barrier()
        for mi, (tag, src, nb_, tiled, dst) in enumerate(specs):
            nbands = NH1 if tiled else (H1 // 128)
            tc_ap = thr.ap()[:, mi:mi + 1]
            for bi in range(nbands):
                P.wait("sync", ev, ev.count)
                if tiled:
                    t_w = P.dma("sync", wf.ap().rearrange("p (ic m) -> p ic m", ic=ND),
                                src[bi].rearrange("ic p m -> p ic m"), ld)
                else:
                    t_w = P.dma("sync", wf.ap(), src[bi * 128:(bi + 1) * 128, :], ld)
                P.wait("scalar", ld, t_w)
                P.wait("scalar", ev, ev.count)
                t_1 = P.op("scalar", lambda e, b=tc_ap: e.activation(
                    s1.ap(), wf.ap(), AF.Relu, bias=b, scale=1.0))
                t_1 = P.op("scalar", lambda e: e.activation(s1.ap(), s1.ap(), AF.Sign))
                t_2 = P.op("scalar", lambda e, b=tc_ap: e.activation(
                    s2.ap(), wf.ap(), AF.Relu, bias=b, scale=-1.0))
                t_2 = P.op("scalar", lambda e: e.activation(s2.ap(), s2.ap(), AF.Sign),
                           inc=cst)
                P.wait("vector", cst, t_2)
                t_o = P.op("vector", lambda e: e.tensor_sub(obf.ap(), s1.ap(), s2.ap()),
                           inc=ev)
                P.wait("gpsimd", ev, t_o)
                if tiled:
                    P.dma("gpsimd", dst[bi].rearrange("ic p m -> p ic m"),
                          obf.ap().rearrange("p (ic m) -> p ic m", ic=ND), st)
                else:
                    P.dma("gpsimd", dst[bi * 128:(bi + 1) * 128, :], obf.ap(), st)
                P.wait("scalar", st, st.count)
        P.wait("sync", st, st.count)
        P.finalize(es)
    return nc


def build_l1ffn(nc, io):
    es = contextlib.ExitStack()
    with es:
        P = Prog(nc)
        co = emit_consts(P, nc, es)
        ccol = es.enter_context(nc.sbuf_tensor("ccol", [128, 1], F32))
        bld = P.sem()
        P.dma("gpsimd", ccol.ap(), io["c_col"], bld)
        NQ = 4
        QT = TC // NQ
        hT = es.enter_context(nc.sbuf_tensor("hT1", [128, NH1 * QT], BF16))
        rawh = es.enter_context(nc.sbuf_tensor("rawh1", [128, 4 * D], F32))
        xbufs = alloc_xnT_bufs(nc, es, "gx", QT // 128)
        wb = es.enter_context(nc.sbuf_tensor("gwb", [128, ND * 128], BF16))
        rel = es.enter_context(nc.sbuf_tensor("grel", [128, 512], F32))
        psH = es.enter_context(nc.psum_tensor("psH", [128, 512], F32))
        dwb = es.enter_context(nc.sbuf_tensor("gdwb", [128, 512], BF16))
        psE = [es.enter_context(nc.psum_tensor(f"psF{i}", [128, 512], F32))
               for i in range(4)]
        pxs = es.enter_context(nc.sbuf_tensor("qxs", [128, D], F32))
        pscr = es.enter_context(nc.sbuf_tensor("qscr", [128, D], F32))
        pcols = es.enter_context(nc.sbuf_tensor("qcols", [128, 6], F32))
        pob = es.enter_context(nc.sbuf_tensor("qob", [128, D], F32))
        P.barrier()
        for q in range(NQ):
            tt0 = q * (QT // 128)
            xnT = emit_xnT(P, nc, es, co, io["x3"], tag="gx", ntt=QT // 128,
                           tt0=tt0, bufs=xbufs)
            wld = P.sem(); mm = P.sem(); ev = P.sem()
            xv = xnT.ap().rearrange("p (ic t) -> p ic t", ic=ND)
            for hc in range(NH1):
                P.wait("sync", mm, mm.count)
                t_w = P.dma("sync", wb.ap().rearrange("p (ic m) -> p ic m", ic=ND),
                            io["tern_fc_t"][hc].rearrange("ic p m -> p ic m"), wld)
                P.wait("tensor", wld, t_w)
                P.wait("tensor", ev, ev.count)
                t_m = None
                for ic in range(ND):
                    t_m = P.op("tensor", lambda e, ic=ic:
                               e.matmul(psH.ap(), wb.ap()[:, ic * 128:(ic + 1) * 128],
                                        xv[:, ic, 0:512],
                                        start=(ic == 0), stop=(ic == ND - 1)),
                               inc=mm if ic == ND - 1 else None)
                P.wait("vector", mm, t_m)
                P.wait("vector", ev, max(0, ev.count - 1))
                t_r = P.op("vector", lambda e: e.tensor_scalar(
                    rel.ap(), psH.ap(), 0.0, None, op0=ALU.max), inc=ev)
                t_h = P.op("vector", lambda e, hc=hc: e.tensor_mul(
                    hT.ap().rearrange("p (hc t) -> p hc t", hc=NH1)[:, hc, :],
                    rel.ap(), rel.ap()), inc=ev)
                P.wait("tensor", ev, t_h)
            P.barrier()
            wld = P.sem(); mm = P.sem(); ev = P.sem()
            hv = hT.ap().rearrange("p (hc t) -> p hc t", hc=NH1)
            for obk in range(4):
                for hc in range(NH1):
                    P.wait("sync", mm, mm.count)
                    t_w = P.dma("sync", dwb.ap(),
                                io["tern_proj_r"][hc * 128:(hc + 1) * 128,
                                                  obk * 512:(obk + 1) * 512], wld)
                    P.wait("tensor", wld, t_w)
                    if hc == 0:
                        P.wait("tensor", ev, ev.count)
                    t_m = None
                    for tt in range(4):
                        t_m = P.op("tensor", lambda e, tt=tt, hc=hc:
                                   e.matmul(psE[tt].ap(),
                                            hv[:, hc, tt * 128:(tt + 1) * 128],
                                            dwb.ap(), start=(hc == 0),
                                            stop=(hc == NH1 - 1)),
                                   inc=mm if tt == 3 else None)
                P.wait("vector", mm, mm.count)
                P.wait("vector", bld, 16)
                for tt in range(4):
                    P.op("vector", lambda e, tt=tt, obk=obk: e.tensor_scalar(
                        rawh.ap()[:, tt * D + obk * 512: tt * D + obk * 512 + 512],
                        psE[tt].ap(), ccol.ap(), None, op0=ALU.mult), inc=ev)
                P.wait("tensor", ev, ev.count)
            P.barrier()
            vs = P.sem(); st = P.sem(); ld2 = P.sem()
            for i in range(4):
                tt = tt0 + i
                raw_ap = rawh.ap()[:, i * D:(i + 1) * D]
                P.wait("sync", vs, vs.count)
                t_x = P.dma("sync", pxs.ap(), io["x3"][tt * 128:(tt + 1) * 128, :], ld2)
                t_n = emit_ln_cols(P, co, raw_ap, pscr.ap(), pcols.ap(), vs)
                P.wait("vector", vs, t_n)
                P.op("vector", lambda e, raw_ap=raw_ap: e.tensor_scalar(
                    pscr.ap(), raw_ap, pcols.ap()[:, 3:4], pcols.ap()[:, 4:5],
                    op0=ALU.mult, op1=ALU.add))
                P.wait("vector", ld2, t_x)
                t_s = P.op("vector", lambda e: e.tensor_add(pob.ap(), pxs.ap(), pscr.ap()),
                           inc=vs)
                P.wait("gpsimd", vs, t_s)
                P.dma("gpsimd", io["x4"][tt * 128:(tt + 1) * 128, :], pob.ap(), st)
                P.wait("vector", st, st.count)
                P.wait("scalar", st, st.count)
            P.wait("sync", st, st.count)
            P.wait("gpsimd", st, st.count)
            P.barrier()
        P.finalize(es)
    return nc



def _shard_rows(x_flat, p):
    return np.concatenate([x_flat[g0:g1] for g0, g1 in own_ranges(p)], axis=0)


def _unshard_rows(own, p, out, scale=1):
    i = 0
    for g0, g1 in own_ranges(p):
        n = (g1 - g0) * scale
        out[g0 * scale:g1 * scale] = own[i:i + n]
        i += n


def _declare(nc, specs):
    io = {}
    for name, shape, dt, kind in specs:
        io[name] = nc.dram_tensor(name, list(shape), dt, kind=kind).ap()
    return io


def _launch(build_fn, specs, per_core_inputs, out_names):
    nc = bass.Bass("TRN2", target_bir_lowering=False)
    io = _declare(nc, specs)
    build_fn(nc, io)
    res = run_bass_kernel_spmd(nc, per_core_inputs, core_ids=CORES)
    return [{k: r[k] for k in out_names} for r in res.results]


def _launch_per_p(build_fn, specs, per_core_inputs, out_names):
    outs = [None] * 8
    for p in (0, 1):
        nc = bass.Bass("TRN2", target_bir_lowering=False)
        io = _declare(nc, specs(p) if callable(specs) else specs)
        build_fn(nc, io, p)
        cores = [2 * bb + p for bb in range(B)]
        res = run_bass_kernel_spmd(nc, [per_core_inputs[c] for c in cores],
                                   core_ids=cores)
        for i, c in enumerate(cores):
            outs[c] = {k: res.results[i][k] for k in out_names}
    return outs


BF = ml_dtypes.bfloat16


def kernel(**inputs):
    f32 = np.float32
    g = {k: np.asarray(v) for k, v in inputs.items()}
    x = np.ascontiguousarray(g["L0_attn_x"].astype(f32, copy=False))

    wq_t = tile_lhsT(np.ascontiguousarray(g["L0_attn_Wq"]), ND, ND)
    wk_t = tile_lhsT(np.ascontiguousarray(g["L0_attn_Wk"]), ND, ND)
    wv_r = np.ascontiguousarray(g["L0_attn_Wv"].T)
    wo_r = np.ascontiguousarray(g["L0_attn_Wo"].T)
    bq_c = col_layout(g["L0_attn_bq"]); bk_c = col_layout(g["L0_attn_bk"])
    bv_row = np.ascontiguousarray(g["L0_attn_bv"].reshape(1, D))
    bo_row = np.ascontiguousarray(g["L0_attn_bo"].reshape(1, D))
    gate_t = tile_lhsT(np.ascontiguousarray(g["L0_ffn_gate_w"]), NH0, ND)
    up_t = tile_lhsT(np.ascontiguousarray(g["L0_ffn_up_w"]), NH0, ND)
    down_r = np.ascontiguousarray(g["L0_ffn_down_w"].T)
    gate_s_c = col_layout(g["L0_ffn_gate_s"]); up_s_c = col_layout(g["L0_ffn_up_s"])
    down_s_row = np.ascontiguousarray(g["L0_ffn_down_s"].reshape(1, D))
    wqkv = np.ascontiguousarray(g["L1_attn_W_qkv"])
    wq1_t = tile_lhsT(np.ascontiguousarray(wqkv[0:D]), ND, ND)
    wk1_t = tile_lhsT(np.ascontiguousarray(wqkv[D:2 * D]), ND, ND)
    wv1_r = np.ascontiguousarray(wqkv[2 * D:3 * D].T)
    wout_r = np.ascontiguousarray(g["L1_attn_W_out"].T)
    w_fc_t = tile_lhsT(np.ascontiguousarray(g["L1_ffn_w_fc"]), NH1, ND)
    w_proj_r = np.ascontiguousarray(g["L1_ffn_w_proj"].T)
    zeros_row = np.zeros((1, D), f32)

    r0 = make_rope_tables(0)
    r1 = make_rope_tables(1)
    pswap_m = np.zeros((128, 128), f32)
    pswap_m[np.arange(128), np.arange(128) ^ 1] = 1.0

    def rope_slices(layer_tables, p):
        sh, sl, ch, cl = layer_tables
        cols = np.concatenate([np.arange(g0, g1) for g0, g1 in own_ranges(p)])
        return (np.ascontiguousarray(sh[:, cols]), np.ascontiguousarray(sl[:, cols]),
                np.ascontiguousarray(ch[:, cols]), np.ascontiguousarray(cl[:, cols]))

    x_own = {}
    for c in CORES:
        bb, p = c // 2, c % 2
        x_own[c] = np.ascontiguousarray(_shard_rows(x[bb], p))

    specs = [("x0", (TC, D), F32, "ExternalInput"),
             ("wq_t", wq_t.shape, F32, "ExternalInput"),
             ("wk_t", wk_t.shape, F32, "ExternalInput"),
             ("wv_r", (D, D), F32, "ExternalInput"),
             ("bq_c", (128, ND), F32, "ExternalInput"),
             ("bk_c", (128, ND), F32, "ExternalInput"),
             ("bv_row", (1, D), F32, "ExternalInput"),
             ("r0sh", (128, TC), F32, "ExternalInput"),
             ("r0sl", (128, TC), F32, "ExternalInput"),
             ("r0ch", (128, TC), F32, "ExternalInput"),
             ("r0cl", (128, TC), F32, "ExternalInput"),
             ("pswap", (128, 128), F32, "ExternalInput"),
             ("qT0", (D, TC), BF16, "ExternalOutput"),
             ("kT0", (D, TC), BF16, "ExternalOutput"),
             ("v0", (TC, D), BF16, "ExternalOutput")]
    ins = []
    for c in CORES:
        bb, p = c // 2, c % 2
        sh, sl, ch, cl = rope_slices(r0, p)
        ins.append(dict(x0=x_own[c], wq_t=wq_t, wk_t=wk_t, wv_r=wv_r, bq_c=bq_c,
                        bk_c=bk_c, bv_row=bv_row, r0sh=sh, r0sl=sl, r0ch=ch, r0cl=cl,
                        pswap=pswap_m))
    o1 = _launch(build_l0qkv, specs, ins, ["qT0", "kT0", "v0"])

    kT_full = {}; v_full = {}
    for bb in range(B):
        kf = np.zeros((D, S), BF); vf = np.zeros((S, D), BF)
        for p in (0, 1):
            c = 2 * bb + p
            i = 0
            for g0, g1 in own_ranges(p):
                n = g1 - g0
                kf[:, g0:g1] = o1[c]["kT0"][:, i:i + n]
                vf[g0:g1, :] = o1[c]["v0"][i:i + n, :]
                i += n
        kT_full[bb] = kf; v_full[bb] = vf

    def specs2(p):
        kmax = max(r[1] for r in own_ranges(p))
        return [("qT0", (D, TC), BF16, "ExternalInput"),
                ("kT", (D, kmax), BF16, "ExternalInput"),
                ("v", (kmax, D), BF16, "ExternalInput"),
                ("aoT0", (D, TC), BF16, "ExternalOutput")]
    ins2 = [None] * 8
    for c in CORES:
        bb, p = c // 2, c % 2
        kmax = max(r[1] for r in own_ranges(p))
        ins2[c] = dict(qT0=o1[c]["qT0"],
                       kT=np.ascontiguousarray(kT_full[bb][:, :kmax]),
                       v=np.ascontiguousarray(v_full[bb][:kmax, :]))
    o2 = _launch_per_p(build_l0attn, specs2, ins2, ["aoT0"])

    specs3 = [("aoT", (D, TC), BF16, "ExternalInput"),
              ("wo_r", (D, D), F32, "ExternalInput"),
              ("bo_row", (1, D), F32, "ExternalInput"),
              ("x_in", (TC, D), F32, "ExternalInput"),
              ("x_out", (TC, D), F32, "ExternalOutput")]
    ins3 = [dict(aoT=o2[c]["aoT0"], wo_r=wo_r, bo_row=bo_row, x_in=x_own[c])
            for c in CORES]
    o3 = _launch(build_oproj, specs3, ins3, ["x_out"])
    x1 = {c: o3[c]["x_out"] for c in CORES}

    specs4 = [("x1", (TC, D), F32, "ExternalInput"),
              ("gate_t", gate_t.shape, F32, "ExternalInput"),
              ("up_t", up_t.shape, F32, "ExternalInput"),
              ("down_r", (INTER0, D), F32, "ExternalInput"),
              ("gate_s_c", (128, NH0), F32, "ExternalInput"),
              ("up_s_c", (128, NH0), F32, "ExternalInput"),
              ("down_s_row", (1, D), F32, "ExternalInput"),
              ("x2", (TC, D), F32, "ExternalOutput")]
    ins4 = [dict(x1=x1[c], gate_t=gate_t, up_t=up_t, down_r=down_r,
                 gate_s_c=gate_s_c, up_s_c=up_s_c, down_s_row=down_s_row)
            for c in CORES]
    o4 = _launch(build_l0ffn, specs4, ins4, ["x2"])
    x2 = {c: o4[c]["x2"] for c in CORES}

    specs5 = [("x2", (TC, D), F32, "ExternalInput"),
              ("wq1_t", wq1_t.shape, F32, "ExternalInput"),
              ("wk1_t", wk1_t.shape, F32, "ExternalInput"),
              ("wv1_r", (D, D), F32, "ExternalInput"),
              ("r1sh", (128, TC), F32, "ExternalInput"),
              ("r1sl", (128, TC), F32, "ExternalInput"),
              ("r1ch", (128, TC), F32, "ExternalInput"),
              ("r1cl", (128, TC), F32, "ExternalInput"),
              ("pswap", (128, 128), F32, "ExternalInput"),
              ("qT1", (D, TC), BF16, "ExternalOutput"),
              ("kT1", (D, TC), BF16, "ExternalOutput"),
              ("v1", (TC, D), BF16, "ExternalOutput"),
              ("M_own", (8, H, 128, 128), F32, "ExternalOutput"),
              ("z_cols", (128, 128), F32, "ExternalOutput")]
    ins5 = []
    for c in CORES:
        bb, p = c // 2, c % 2
        sh, sl, ch, cl = rope_slices(r1, p)
        ins5.append(dict(x2=x2[c], wq1_t=wq1_t, wk1_t=wk1_t, wv1_r=wv1_r,
                         r1sh=sh, r1sl=sl, r1ch=ch, r1cl=cl, pswap=pswap_m))
    o5 = _launch(build_l1qkv, specs5, ins5, ["qT1", "kT1", "v1", "M_own", "z_cols"])

    M_all = {}; z_all = {}
    for bb in range(B):
        Ma = np.zeros((16, H, 128, 128), f32)
        za = np.zeros((128, 16 * H), f32)
        for p in (0, 1):
            c = 2 * bb + p
            for lc, gc in enumerate(own_chunks(p)):
                Ma[gc] = o5[c]["M_own"][lc]
                za[:, gc * 16:(gc + 1) * 16] = o5[c]["z_cols"][:, lc * 16:(lc + 1) * 16]
        M_all[bb] = Ma; z_all[bb] = za

    specs6 = [("qT1", (D, TC), BF16, "ExternalInput"),
              ("kT1", (D, TC), BF16, "ExternalInput"),
              ("v1", (TC, D), BF16, "ExternalInput"),
              ("M_all", (16, H, 128, 128), F32, "ExternalInput"),
              ("z_all_cols", (128, 16 * H), F32, "ExternalInput"),
              ("aoT1", (D, TC), BF16, "ExternalOutput")]
    ins6 = [None] * 8
    for c in CORES:
        bb = c // 2
        ins6[c] = dict(qT1=o5[c]["qT1"], kT1=o5[c]["kT1"], v1=o5[c]["v1"],
                       M_all=M_all[bb], z_all_cols=z_all[bb])
    o6 = _launch_per_p(build_l1attn, specs6, ins6, ["aoT1"])

    ins7 = [dict(aoT=o6[c]["aoT1"], wo_r=wout_r, bo_row=zeros_row, x_in=x2[c])
            for c in CORES]
    o7 = _launch(build_oproj, specs3, ins7, ["x_out"])
    x3 = {c: o7[c]["x_out"] for c in CORES}

    specs8 = [("w_fc_t", w_fc_t.shape, F32, "ExternalInput"),
              ("w_proj_r", (H1, D), F32, "ExternalInput"),
              ("tern_fc_t", w_fc_t.shape, BF16, "ExternalOutput"),
              ("tern_proj_r", (H1, D), BF16, "ExternalOutput"),
              ("c_col", (128, 1), F32, "ExternalOutput")]
    ins8 = [dict(w_fc_t=w_fc_t, w_proj_r=w_proj_r) for c in CORES]
    o8 = _launch(build_l1tern, specs8, ins8, ["tern_fc_t", "tern_proj_r", "c_col"])

    specs9 = [("x3", (TC, D), F32, "ExternalInput"),
              ("tern_fc_t", w_fc_t.shape, BF16, "ExternalInput"),
              ("tern_proj_r", (H1, D), BF16, "ExternalInput"),
              ("c_col", (128, 1), F32, "ExternalInput"),
              ("x4", (TC, D), F32, "ExternalOutput")]
    ins9 = [dict(x3=x3[c], tern_fc_t=o8[c]["tern_fc_t"],
                 tern_proj_r=o8[c]["tern_proj_r"], c_col=o8[c]["c_col"])
            for c in CORES]
    o9 = _launch(build_l1ffn, specs9, ins9, ["x4"])

    out = np.zeros((B, S, D), f32)
    for c in CORES:
        bb, p = c // 2, c % 2
        i = 0
        for g0, g1 in own_ranges(p):
            n = g1 - g0
            out[bb, g0:g1] = o9[c]["x4"][i:i + n]
            i += n
    return out



def f_ranges(p):
    return [(p * 2048, (p + 1) * 2048)]


def alloc_arena(nc, es):
    B = {}
    T = lambda n, s, d: es.enter_context(nc.sbuf_tensor(n, s, d))
    PT = lambda n, s, d: es.enter_context(nc.psum_tensor(n, s, d))
    B["xnTbig"] = T("A_xnT", [128, ND * 2048], BF16)
    B["hdnT"] = T("A_hdnT", [128, NH0 * 256], BF16)
    B["rawh"] = T("A_rawh", [128, 2 * D], F32)
    B["wfbig"] = T("A_wf", [128, 4096], F32)
    B["wbbig"] = T("A_wb", [128, 8192], BF16)
    B["sfa"] = T("A_sfa", [128, D], F32)
    B["sfb"] = T("A_sfb", [128, D], F32)
    B["sba"] = T("A_sba", [128, D], BF16)
    B["at1"] = T("A_at1", [128, 4096], BF16)
    B["at2"] = T("A_at2", [128, 4096], BF16)
    B["at3"] = T("A_at3", [128, 4096], BF16)
    B["aoTs"] = B["at1"]
    B["brow"] = T("A_brow", [128, D], F32)
    B["cols"] = T("A_cols", [128, 8], F32)
    B["masks0"] = T("A_m0", [128, 4 * 512], BF16)
    B["masks1"] = T("A_m1", [128, 2 * 256], BF16)
    B["pbf"] = T("A_pbf", [128, 512], BF16)
    B["nscr"] = T("A_nscr", [128, 512], F32)
    B["aob"] = T("A_aob", [128, 512], BF16)
    B["rden"] = T("A_rden", [1, 512], F32)
    B["S_run"] = T("A_Srun", [128, 128], F32)
    B["z_run"] = T("A_zrun", [128, 1], F32)
    B["Sb"] = T("A_Sb", [128, 128], BF16)
    B["zb"] = T("A_zb", [128, 1], BF16)
    B["kcb"] = T("A_kcb", [128, 256], BF16)
    B["sil"] = T("A_sil", [128, 512], F32)
    B["ups"] = T("A_ups", [128, 512], F32)
    B["dwb"] = T("A_dwb", [128, 512], BF16)
    B["gsc"] = T("A_gsc", [128, 2 * NH0], F32)
    B["bcol"] = T("A_bcol", [128, 2 * ND], F32)
    B["ccol"] = T("A_ccol", [128, 1], F32)
    B["pswap"] = T("A_pswap", [128, 128], F32)
    B["msc"] = T("A_msc", [1, 8], F32)
    B["thr"] = T("A_thr", [128, 4], F32)
    B["qr"] = B["sil"]
    B["qs2"] = B["ups"]
    B["m1s"] = B["nscr"]
    B["obb"] = T("A_obb", [128, 512], BF16)
    B["dyn"] = T("A_dyn", [1, 64], mybir.dt.int32)
    B["ps0"] = PT("P_0", [128, 512], F32)
    B["ps1"] = PT("P_1", [128, 512], F32)
    B["ps2"] = PT("P_2", [128, 512], F32)
    B["ps3"] = PT("P_3", [128, 512], F32)
    B["ps4"] = PT("P_4", [128, 512], F32)
    B["ps5"] = PT("P_5", [128, 512], F32)
    B["psRow"] = PT("P_r", [1, 512], F32)
    B["psT"] = PT("P_t", [128, 512], BF16)
    return B


def f_xnT_bufs(B, ntt):
    return dict(xnT=B["xnTbig"], xs=B["sfa"], xb=B["sba"], scr=B["sfb"],
                cols=B["cols"], pt=B["psT"])


def emit_xnT_f(P, nc, es, co, B, x_dram, ntt, tt0=0):
    TT = ntt * 128
    xnT_ap = B["xnTbig"].ap()[:, 0:ND * TT]
    xs, xb, scr, cols, pt = B["sfa"], B["sba"], B["sfb"], B["cols"], B["psT"]
    ld = P.nsem("xld"); vs = P.nsem("xvs"); tp = P.nsem("xtp"); ev = P.nsem("xev")
    for i in range(ntt):
        tt = tt0 + i
        P.wait("sync", tp, tp.count)
        t_ld = P.dma("sync", xs.ap(), x_dram[tt * 128:(tt + 1) * 128, :], ld)
        P.wait("scalar", ld, t_ld)
        P.wait("vector", ld, t_ld)
        t_n = emit_ln_cols(P, co, xs.ap(), scr.ap(), cols.ap(), vs)
        P.wait("vector", vs, t_n)
        t_c = P.op("vector", lambda e: e.tensor_scalar(
            xb.ap(), xs.ap(), cols.ap()[:, 3:4], cols.ap()[:, 4:5],
            op0=ALU.mult, op1=ALU.add), inc=vs)
        for g in range(4):
            P.wait("tensor", vs, t_c)
            P.wait("tensor", ev, ev.count)
            tl = None
            for f in range(4):
                ic = g * 4 + f
                tl = P.op("tensor", lambda e, f=f, ic=ic:
                          e.transpose(pt.ap()[:, f * 128:(f + 1) * 128],
                                      xb.ap()[:, ic * 128:(ic + 1) * 128],
                                      co["ident_b"].ap()),
                          inc=tp if f == 3 else None)
            P.wait("scalar", tp, tl)

            def ev_fn(e, g=g, i=i, xnT_ap=xnT_ap, TT=TT):
                dst = xnT_ap.rearrange("p (ic t) -> p ic t", ic=ND)[
                    :, g * 4:(g + 1) * 4, i * 128:(i + 1) * 128]
                return e.activation(dst, pt.ap().rearrange("p (f t) -> p f t", f=4),
                                    AF.Identity)
            P.op("scalar", ev_fn, inc=ev)
    P.barrier()
    return xnT_ap


def emit_rope_f(P, nc, es, B, io, pref):
    sinT = B["rawh"].ap()[:, 0:TC]
    cosT = B["rawh"].ap()[:, TC:2 * TC]
    th = B["wfbig"]
    ld = P.nsem("rld"); ok = P.nsem("rok")
    for which, dst in (("s", sinT), ("c", cosT)):
        P.dma("gpsimd", th.ap()[:, 0:TC], io[pref + which + "h"], ld)
        t1 = P.dma("gpsimd", th.ap()[:, TC:], io[pref + which + "l"], ld)
        P.wait("vector", ld, t1)
        t2 = P.op("vector", lambda e: e.tensor_add(
            th.ap()[:, 0:TC], th.ap()[:, 0:TC], th.ap()[:, TC:]), inc=ok)
        P.wait("scalar", ok, t2)
        t3 = P.op("scalar", lambda e, d=dst: e.activation(d, th.ap()[:, 0:TC], AF.Sin),
                  inc=ok)
        P.wait("gpsimd", ok, t3)
    P.barrier()
    return sinT, cosT


def emit_projF_f(P, nc, es, co, B, xnT_ap, w_tiled, noc, out_dram, rope=None,
                 bias_off=None, scale=1.0, elu1=False):
    NB = TC // 512
    wf = B["wfbig"].ap()[:, 0:ND * 128]
    wb = B["wbbig"].ap()[:, 0:ND * 128]
    qr, qs, m1, ob = B["qr"], B["qs2"], B["m1s"], B["obb"]
    ps, psQ = B["ps0"], B["ps1"]
    wld = P.nsem("pwld"); wc = P.nsem("pwc"); mm = P.nsem("pmm")
    ev = P.nsem("pev"); st = P.nsem("pst")
    xv = xnT_ap.rearrange("p (ic t) -> p ic t", ic=ND)
    for oc in range(noc):
        P.wait("sync", mm, mm.count)
        t_w = P.dma("sync", wf.rearrange("p (ic m) -> p ic m", ic=ND),
                    w_tiled[oc].rearrange("ic p m -> p ic m"), wld)
        P.wait("vector", wld, t_w)
        P.wait("vector", mm, mm.count)
        t_c = P.op("vector", lambda e: e.tensor_copy(wb, wf), inc=wc)
        for nb in range(NB):
            P.wait("tensor", wc, t_c)
            P.wait("tensor", ev, ev.count)
            t_m = None
            for ic in range(ND):
                t_m = P.op("tensor", lambda e, ic=ic, nb=nb:
                           e.matmul(ps.ap(), wb[:, ic * 128:(ic + 1) * 128],
                                    xv[:, ic, nb * 512:(nb + 1) * 512],
                                    start=(ic == 0), stop=(ic == ND - 1)),
                           inc=mm if ic == ND - 1 else None)
            P.wait("scalar", mm, t_m)
            P.wait("scalar", st, st.count)
            if bias_off is not None:
                bc = B["bcol"].ap()[:, bias_off + oc:bias_off + oc + 1]
                t_e = P.op("scalar", lambda e, bc=bc: e.activation(
                    qr.ap(), ps.ap(), AF.Identity, bias=bc, scale=scale), inc=ev)
            else:
                t_e = P.op("scalar", lambda e: e.activation(
                    qr.ap(), ps.ap(), AF.Identity, scale=scale), inc=ev)
            cur = qr
            P.wait("vector", ev, t_e)
            P.wait("vector", st, st.count)
            if rope is not None:
                sinT, cosT = rope
                sv = sinT[:, nb * 512:(nb + 1) * 512]
                cv = cosT[:, nb * 512:(nb + 1) * 512]
                P.wait("tensor", ev, t_e)
                t_pm = P.op("tensor", lambda e: e.matmul(
                    psQ.ap(), B["pswap"].ap(), qr.ap(), start=True, stop=True), inc=mm)
                P.op("vector", lambda e, cv=cv: e.tensor_mul(m1.ap(), qr.ap(), cv))
                P.wait("vector", mm, t_pm)
                P.op("vector", lambda e, sv=sv: e.tensor_mul(qs.ap(), psQ.ap(), sv))
                t_e = P.op("vector", lambda e: e.tensor_add(m1.ap(), m1.ap(), qs.ap()),
                           inc=ev)
                cur = m1
            if elu1:
                t_e = P.op("vector", lambda e, c=cur: e.tensor_scalar(
                    qs.ap(), c.ap(), 0.0, None, op0=ALU.min), inc=ev)
                P.wait("scalar", ev, t_e)
                t_e = P.op("scalar", lambda e: e.activation(qs.ap(), qs.ap(), AF.Exp),
                           inc=ev)
                P.wait("vector", ev, t_e)
                P.op("vector", lambda e, c=cur: e.tensor_scalar(
                    c.ap(), c.ap(), 0.0, None, op0=ALU.max))
                t_e = P.op("vector", lambda e, c=cur: e.tensor_add(
                    c.ap(), c.ap(), qs.ap()), inc=ev)
            P.wait("vector", ev, t_e)
            t_b = P.op("vector", lambda e, c=cur: e.tensor_copy(ob.ap(), c.ap()), inc=ev)
            P.wait("gpsimd", ev, t_b)
            P.dma("gpsimd", out_dram[oc * 128:(oc + 1) * 128, nb * 512:(nb + 1) * 512],
                  ob.ap(), st)
    P.wait("gpsimd", st, st.count)
    P.barrier()


def emit_projT_f(P, nc, es, co, B, lhsT_ap, nic, w_rows, out_dram,
                 bias_bcast=None, nof=D, ntok=TC, per_unit_epilogue=None,
                 lhsT_loader=None, wsrc_bf16=False):
    ntt = ntok // 128
    NOB = nof // 512
    wfs = B["wfbig"].ap()[:, 2048:2560]
    wbp = B["wbbig"].ap()[:, 0:nic * 512]
    obuf = B["obb"]
    ps = B["ps0"]
    wld = P.nsem("twld"); wc = P.nsem("twc"); mm = P.nsem("tmm")
    ev = P.nsem("tev"); st = P.nsem("tst")
    for tt in range(ntt):
        if lhsT_loader is not None:
            lv, lsem, lcnt = lhsT_loader(tt)
        else:
            lv = lhsT_ap.rearrange("p (ic t) -> p ic t", ic=nic)
        for obk in range(NOB):
            P.wait("sync", mm, mm.count)
            for ic in range(nic):
                if wsrc_bf16:
                    t_w = P.dma("sync", wbp[:, ic * 512:(ic + 1) * 512],
                                w_rows[ic * 128:(ic + 1) * 128,
                                       obk * 512:(obk + 1) * 512], wld)
                    t_c = t_w
                else:
                    t_w = P.dma("sync", wfs,
                                w_rows[ic * 128:(ic + 1) * 128,
                                       obk * 512:(obk + 1) * 512], wld)
                    P.wait("vector", wld, t_w)
                    t_c = P.op("vector", lambda e, ic=ic: e.tensor_copy(
                        wbp[:, ic * 512:(ic + 1) * 512], wfs), inc=wc)
                    P.wait("sync", wc, t_c)
            if wsrc_bf16:
                P.wait("tensor", wld, t_w)
            else:
                P.wait("tensor", wc, wc.count)
            P.wait("tensor", ev, ev.count)
            if lhsT_loader is not None:
                P.wait("tensor", lsem, lcnt)
            t_m = None
            for ic in range(nic):
                if lhsT_loader is not None:
                    lt = lv[:, ic * 128:(ic + 1) * 128]
                else:
                    lt = lv[:, ic, tt * 128:(tt + 1) * 128]
                t_m = P.op("tensor", lambda e, lt=lt, ic=ic:
                           e.matmul(ps.ap(), lt, wbp[:, ic * 512:(ic + 1) * 512],
                                    start=(ic == 0), stop=(ic == nic - 1)),
                           inc=mm if ic == nic - 1 else None)
            if per_unit_epilogue is not None:
                per_unit_epilogue(tt, obk, ps, mm, t_m, ev)
            else:
                P.wait("vector", mm, t_m)
                P.wait("vector", st, st.count)
                if bias_bcast is not None:
                    t_e = P.op("vector", lambda e, obk=obk: e.tensor_add(
                        obuf.ap(), ps.ap(), bias_bcast[:, obk * 512:(obk + 1) * 512]),
                        inc=ev)
                else:
                    t_e = P.op("vector", lambda e: e.tensor_copy(obuf.ap(), ps.ap()),
                               inc=ev)
                P.wait("gpsimd", ev, t_e)
                P.dma("gpsimd", out_dram[tt * 128:(tt + 1) * 128,
                                         obk * 512:(obk + 1) * 512], obuf.ap(), st)
                P.wait("vector", st, st.count)
    P.wait("gpsimd", st, st.count)
    P.barrier()


def emit_attn0_f(P, nc, es, co, B, loc, aoin_dram):
    ald = P.nsem("ald"); mmS = P.nsem("amS"); exq = P.nsem("aex")
    kkt = P.nsem("akt"); hdn = P.nsem("ahd"); evq = P.nsem("aev")
    st = P.nsem("ast")
    qloc, kloc, vloc = loc
    at1, at2, at3 = B["at1"], B["at2"], B["at3"]
    psS, psA, psD, psB = B["ps0"], B["ps1"], B["psRow"], B["ps2"]
    pbf, nscr, rden, aob = B["pbf"], B["nscr"], B["rden"], B["aob"]
    vv = at3.ap().rearrange("p (kt d) -> p kt d", kt=32)
    for h in range(8):
        P.wait("sync", hdn, hdn.count)
        P.dma("sync", at1.ap(), qloc[h * 128:(h + 1) * 128, :], ald)
        P.dma("sync", at2.ap(), kloc[h * 128:(h + 1) * 128, :], ald)
        for k8 in range(0, 32, 8):
            t_l = P.dma("sync", vv[:, k8:k8 + 8, :],
                        vloc[:, h * 128:(h + 1) * 128].rearrange(
                            "(kt p) d -> p kt d", p=128)[:, k8:k8 + 8, :], ald)
        P.wait("tensor", ald, t_l)
        for qb in range(8):
            gq = qb * 512
            nkt = 4 * qb + 4
            qrhs = at1.ap()[:, gq:gq + 512]
            for kt in range(nkt):
                P.wait("tensor", exq, exq.count)
                t_s = P.op("tensor", lambda e, kt=kt, qrhs=qrhs:
                           e.matmul(psS.ap(), at2.ap()[:, kt * 128:(kt + 1) * 128],
                                    qrhs, start=True, stop=True), inc=mmS)
                P.wait("scalar", mmS, t_s)
                P.wait("scalar", kkt, kkt.count)
                t_x = P.op("scalar", lambda e: e.activation(
                    pbf.ap(), psS.ap(), AF.Exp, bias=co["negC"].ap()), inc=exq)
                r_off = kt * 128 - gq
                if r_off >= 0:
                    P.wait("vector", exq, t_x)
                    t_x = P.op("vector", lambda e, r_off=r_off: e.tensor_mul(
                        pbf.ap(), pbf.ap(),
                        B["masks0"].ap()[:, (r_off // 128) * 512:(r_off // 128) * 512 + 512]),
                        inc=exq)
                P.wait("tensor", exq, t_x)
                P.op("tensor", lambda e, kt=kt, first=(kt == 0), last=(kt == nkt - 1):
                     e.matmul(psA.ap(), vv[:, kt, :], pbf.ap(),
                              start=first, stop=last))
                P.op("tensor", lambda e, first=(kt == 0), last=(kt == nkt - 1):
                     e.matmul(psD.ap(), co["ones_c_b"].ap(), pbf.ap(),
                              start=first, stop=last), inc=kkt)
            P.wait("vector", kkt, kkt.count)
            P.wait("vector", st, st.count)
            t_r = P.op("vector", lambda e: e.reciprocal(rden.ap(), psD.ap()), inc=evq)
            P.wait("tensor", evq, t_r)
            t_b = P.op("tensor", lambda e: e.matmul(
                psB.ap(), co["ones_r_f"].ap(), rden.ap(), start=True, stop=True),
                inc=hdn)
            t_n = P.op("vector", lambda e: e.tensor_copy(nscr.ap(), psA.ap()), inc=evq)
            P.wait("vector", hdn, t_b)
            t_o = P.op("vector", lambda e: e.tensor_mul(aob.ap(), nscr.ap(), psB.ap()),
                       inc=evq)
            P.wait("gpsimd", evq, t_o)
            P.dma("gpsimd", aoin_dram[h * 128:(h + 1) * 128, gq:gq + 512], aob.ap(), st)
            P.wait("tensor", evq, t_o)
    P.wait("gpsimd", st, st.count)
    P.barrier()


def emit_attn1_f(P, nc, es, co, B, loc, aoin_dram):
    ald = P.nsem("bld"); mm = P.nsem("bmm"); ev = P.nsem("bev")
    st = P.nsem("bst"); acc = P.nsem("bacc"); tp = P.nsem("btp")
    qloc, kloc, vloc = loc
    at1, at2, at3 = B["at1"], B["at2"], B["at3"]
    psS, psN, psD, psB = B["ps0"], B["ps1"], B["psRow"], B["ps2"]
    psZ, psM, psT = B["ps3"], B["ps4"], B["psT"]
    pa, nscr, rden, aob = B["pbf"], B["nscr"], B["rden"], B["aob"]
    S_run, z_run, Sb, zb, kcb = B["S_run"], B["z_run"], B["Sb"], B["zb"], B["kcb"]
    vv = at3.ap().rearrange("p (kt d) -> p kt d", kt=32)
    for h in range(8):
        P.wait("sync", mm, mm.count)
        P.dma("sync", at1.ap(), qloc[h * 128:(h + 1) * 128, :], ald)
        P.dma("sync", at2.ap(), kloc[h * 128:(h + 1) * 128, :], ald)
        for k8 in range(0, 32, 8):
            t_l = P.dma("sync", vv[:, k8:k8 + 8, :],
                        vloc[:, h * 128:(h + 1) * 128].rearrange(
                            "(kt p) d -> p kt d", p=128)[:, k8:k8 + 8, :], ald)
        P.wait("tensor", ald, t_l)
        P.wait("vector", ald, t_l)
        P.edone("gpsimd", "vector")
        P.op("gpsimd", lambda e: e.memset(S_run.ap(), 0.0))
        t_z = P.op("gpsimd", lambda e: e.memset(z_run.ap(), 0.0), inc=acc)
        P.wait("vector", acc, t_z)
        P.edone("vector", "gpsimd")
        P.edone("tensor", "gpsimd")
        for g in range(16):
            qsl = at1.ap()[:, g * 256:(g + 1) * 256]
            P.wait("vector", mm, mm.count)
            P.op("vector", lambda e: e.tensor_copy(Sb.ap(), S_run.ap()))
            t_cb = P.op("vector", lambda e: e.tensor_copy(zb.ap(), z_run.ap()), inc=ev)
            P.wait("tensor", ev, ev.count)
            t_t = None
            for kt in range(2):
                t_t = P.op("tensor", lambda e, g=g, kt=kt: e.transpose(
                    psT.ap()[:, kt * 128:(kt + 1) * 128],
                    at2.ap()[:, g * 256 + kt * 128:g * 256 + (kt + 1) * 128],
                    co["ident_b"].ap()), inc=tp if kt == 1 else None)
            P.wait("vector", tp, t_t)
            t_kc = P.op("vector", lambda e: e.tensor_copy(kcb.ap(), psT.ap()[:, 0:256]), inc=ev)
            for kt in range(2):
                P.wait("tensor", ev, ev.count)
                t_s = P.op("tensor", lambda e, g=g, kt=kt, qsl=qsl: e.matmul(
                    psS.ap()[:, 0:256], at2.ap()[:, g * 256 + kt * 128:g * 256 + (kt + 1) * 128],
                    qsl, start=True, stop=True), inc=mm)
                P.wait("vector", mm, t_s)
                t_p = P.op("vector", lambda e, kt=kt: e.tensor_mul(
                    pa.ap()[:, kt * 256:(kt + 1) * 256], psS.ap()[:, 0:256],
                    B["masks1"].ap()[:, kt * 256:(kt + 1) * 256]), inc=ev)
            P.wait("tensor", ev, ev.count)
            for kt in range(2):
                P.op("tensor", lambda e, g=g, kt=kt: e.matmul(
                    psN.ap()[:, 0:256], vv[:, 2 * g + kt, :], pa.ap()[:, kt * 256:(kt + 1) * 256],
                    start=(kt == 0), stop=False))
            P.op("tensor", lambda e, qsl=qsl: e.matmul(
                psN.ap()[:, 0:256], Sb.ap(), qsl, start=False, stop=True))
            for kt in range(2):
                P.op("tensor", lambda e, kt=kt: e.matmul(
                    psD.ap()[:, 0:256], co["ones_c_b"].ap(), pa.ap()[:, kt * 256:(kt + 1) * 256],
                    start=(kt == 0), stop=False))
            t_m = P.op("tensor", lambda e, qsl=qsl: e.matmul(
                psD.ap()[:, 0:256], zb.ap(), qsl, start=False, stop=True), inc=mm)
            for kt in range(2):
                P.op("tensor", lambda e, g=g, kt=kt: e.matmul(
                    psM.ap()[:, 0:128], kcb.ap()[:, kt * 128:(kt + 1) * 128], vv[:, 2 * g + kt, :],
                    start=(kt == 0), stop=(kt == 1)))
            t_m = P.op("tensor", lambda e: e.matmul(
                psZ.ap()[:, 0:1], kcb.ap()[:, 0:128], co["ones_c_b"].ap(),
                start=True, stop=False), inc=mm)
            t_m = P.op("tensor", lambda e: e.matmul(
                psZ.ap()[:, 0:1], kcb.ap()[:, 128:256], co["ones_c_b"].ap(),
                start=False, stop=True), inc=mm)
            P.wait("vector", mm, t_m)
            P.wait("vector", st, st.count)
            P.op("vector", lambda e: e.tensor_scalar(
                rden.ap()[:, 0:256], psD.ap()[:, 0:256], 1e-6, None, op0=ALU.max))
            t_r = P.op("vector", lambda e: e.reciprocal(
                rden.ap()[:, 0:256], rden.ap()[:, 0:256]), inc=ev)
            P.wait("tensor", ev, t_r)
            t_bc = P.op("tensor", lambda e: e.matmul(
                psB.ap()[:, 0:256], co["ones_r_f"].ap(), rden.ap()[:, 0:256],
                start=True, stop=True), inc=mm)
            t_n = P.op("vector", lambda e: e.tensor_copy(
                nscr.ap()[:, 0:256], psN.ap()[:, 0:256]), inc=ev)
            P.wait("vector", mm, t_bc)
            t_o = P.op("vector", lambda e: e.tensor_mul(
                aob.ap()[:, 0:256], nscr.ap()[:, 0:256], psB.ap()[:, 0:256]), inc=ev)
            P.wait("gpsimd", ev, t_o)
            P.dma("gpsimd", aoin_dram[h * 128:(h + 1) * 128, g * 256:(g + 1) * 256],
                  aob.ap()[:, 0:256], st)
            t_a = P.op("vector", lambda e: e.tensor_add(S_run.ap(), S_run.ap(), psM.ap()[:, 0:128]))
            t_a = P.op("vector", lambda e: e.tensor_add(z_run.ap(), z_run.ap(), psZ.ap()[:, 0:1]),
                       inc=acc)
            P.wait("tensor", acc, t_a)
            P.wait("tensor", ev, t_o)
    P.wait("gpsimd", st, st.count)
    P.barrier()


def emit_oproj_f(P, nc, es, co, B, aoloc, w_rows, bias_row, x_in, x_out):
    raw = B["rawh"].ap()[:, 0:D]
    xs, scr, cols = B["sfa"], B["sfb"], B["cols"]
    ob2 = B["wfbig"].ap()[:, 0:D]
    aoTs_ap = B["aoTs"].ap()[:, 0:2048]
    vs = P.nsem("ovs"); st = P.nsem("ost"); ld2 = P.nsem("old")
    aol = P.nsem("oal")
    bld = P.nsem("obl")
    if bias_row is not None:
        t_bb = P.dma("gpsimd", B["brow"].ap(), bias_row.broadcast_to([128, D]), bld)
        P.wait("vector", bld, t_bb)
    else:
        P.edone("gpsimd", "vector")
        bld2 = P.nsem("obl2")
        t_bb = P.op("gpsimd", lambda e: e.memset(B["brow"].ap(), 0.0), inc=bld2)
        P.wait("vector", bld2, t_bb)

    def loader(tt):
        mm = P.nsem("tmm")
        P.wait("gpsimd", mm, mm.count)
        cnt = P.dma("gpsimd", aoTs_ap,
                    aoloc.rearrange("(c p) t -> p c t", p=128)[
                        :, :, tt * 128:(tt + 1) * 128], aol)
        return aoTs_ap, aol, cnt

    def epi(tt, obk, ps, mm, t_m, ev):
        P.wait("vector", mm, t_m)
        t_e = P.op("vector", lambda e, obk=obk: e.tensor_add(
            raw[:, obk * 512:(obk + 1) * 512], ps.ap(),
            B["brow"].ap()[:, obk * 512:(obk + 1) * 512]), inc=ev)
        P.wait("tensor", ev, t_e)
        if obk == 3:
            P.wait("sync", vs, vs.count)
            t_x = P.dma("sync", xs.ap(), x_in[tt * 128:(tt + 1) * 128, :], ld2)
            P.wait("scalar", ev, t_e)
            t_n = emit_ln_cols(P, co, raw, scr.ap(), cols.ap(), vs)
            P.wait("vector", vs, t_n)
            P.op("vector", lambda e: e.tensor_scalar(
                scr.ap(), raw, cols.ap()[:, 3:4], cols.ap()[:, 4:5],
                op0=ALU.mult, op1=ALU.add))
            P.wait("vector", ld2, t_x)
            t_s = P.op("vector", lambda e: e.tensor_add(ob2, xs.ap(), scr.ap()), inc=vs)
            P.wait("gpsimd", vs, t_s)
            P.dma("gpsimd", x_out[tt * 128:(tt + 1) * 128, :], ob2, st)
            P.wait("vector", st, st.count)
            P.wait("scalar", st, st.count)

    emit_projT_f(P, nc, es, co, B, None, ND, w_rows, None,
                 per_unit_epilogue=epi, lhsT_loader=loader)
    P.wait("sync", st, st.count)
    P.wait("gpsimd", st, st.count)
    P.barrier()


def emit_ffn_f(P, nc, es, co, B, io, layer):
    NQ, QT = 8, 256
    if layer == 0:
        x_in, x_out = io["x1_d"], io["x2_d"]
        NH = NH0
    else:
        x_in, x_out = io["x3_d"], io["x4"]
        NH = NH1
    hdv = B["hdnT"].ap()[:, 0:NH * 256].rearrange("p (hc t) -> p hc t", hc=NH)
    rawh = B["rawh"]
    wf, wb = B["wfbig"], B["wbbig"]
    sil, ups, dwb = B["sil"], B["ups"], B["dwb"]
    psG, psU = B["ps0"], B["ps1"]
    psE = [B["ps2"], B["ps3"]]
    wld = P.nsem("fwld"); wc = P.nsem("fwc"); mm = P.nsem("fmm"); ev = P.nsem("fev")
    vs = P.nsem("ovs"); st = P.nsem("ost"); ld2 = P.nsem("old"); bld = P.nsem("fbl")
    if layer == 0:
        P.dma("gpsimd", B["gsc"].ap()[:, 0:NH0], io["gate_s_c"], bld)
        P.dma("gpsimd", B["gsc"].ap()[:, NH0:], io["up_s_c"], bld)
        t_bb = P.dma("gpsimd", B["brow"].ap(), io["down_s_row"].broadcast_to([128, D]), bld)
        P.wait("vector", bld, t_bb)
        P.wait("scalar", bld, t_bb)
    for q in range(NQ):
        tt0 = q * 2
        xnT_ap = emit_xnT_f(P, nc, es, co, B, x_in, 2, tt0=tt0)
        xv = xnT_ap.rearrange("p (ic t) -> p ic t", ic=ND)
        for hc in range(NH):
            P.wait("sync", mm, mm.count)
            if layer == 0:
                P.dma("sync", wf.ap().rearrange("p (q2 ic m) -> p q2 ic m", q2=2, ic=ND, m=128)[:, 0],
                      io["gate_t"][hc].rearrange("ic p m -> p ic m"), wld)
                t_w = P.dma("sync", wf.ap().rearrange("p (q2 ic m) -> p q2 ic m", q2=2, ic=ND, m=128)[:, 1],
                            io["up_t"][hc].rearrange("ic p m -> p ic m"), wld)
                P.wait("vector", wld, t_w)
                P.wait("vector", mm, mm.count)
                t_c = P.op("vector", lambda e: e.tensor_copy(
                    wb.ap()[:, 0:4096], wf.ap()), inc=wc)
                P.wait("tensor", wc, t_c)
            else:
                t_w = P.dma("sync", wb.ap().rearrange("p (ic m) -> p ic m", ic=ND)
                            if False else wb.ap()[:, 0:2048].rearrange("p (ic m) -> p ic m", ic=ND),
                            io["tfc_t"][hc].rearrange("ic p m -> p ic m"), wld)
                P.wait("tensor", wld, t_w)
            P.wait("tensor", ev, ev.count)
            if layer == 0:
                for ic in range(ND):
                    P.op("tensor", lambda e, ic=ic:
                         e.matmul(psG.ap()[:, 0:256], wb.ap()[:, ic * 128:(ic + 1) * 128],
                                  xv[:, ic, 0:256],
                                  start=(ic == 0), stop=(ic == ND - 1)))
                t_m = None
                for ic in range(ND):
                    t_m = P.op("tensor", lambda e, ic=ic:
                               e.matmul(psU.ap()[:, 0:256],
                                        wb.ap()[:, (ND + ic) * 128:(ND + ic + 1) * 128],
                                        xv[:, ic, 0:256],
                                        start=(ic == 0), stop=(ic == ND - 1)),
                               inc=mm if ic == ND - 1 else None)
                P.wait("scalar", mm, t_m)
                P.wait("scalar", ev, max(0, ev.count - 1))
                t_s = P.op("scalar", lambda e, hc=hc: e.activation(
                    sil.ap()[:, 0:256], psG.ap()[:, 0:256], AF.Silu,
                    scale=B["gsc"].ap()[:, hc:hc + 1]), inc=ev)
                P.wait("vector", ev, t_s)
                t_u = P.op("vector", lambda e, hc=hc: e.tensor_scalar(
                    ups.ap()[:, 0:256], psU.ap()[:, 0:256],
                    B["gsc"].ap()[:, NH0 + hc:NH0 + hc + 1], None, op0=ALU.mult), inc=ev)
                t_h = P.op("vector", lambda e, hc=hc: e.tensor_mul(
                    hdv[:, hc, :], sil.ap()[:, 0:256], ups.ap()[:, 0:256]), inc=ev)
            else:
                t_m = None
                for ic in range(ND):
                    t_m = P.op("tensor", lambda e, ic=ic:
                               e.matmul(psG.ap()[:, 0:256], wb.ap()[:, ic * 128:(ic + 1) * 128],
                                        xv[:, ic, 0:256],
                                        start=(ic == 0), stop=(ic == ND - 1)),
                               inc=mm if ic == ND - 1 else None)
                P.wait("vector", mm, t_m)
                P.wait("vector", ev, max(0, ev.count - 1))
                t_r = P.op("vector", lambda e: e.tensor_scalar(
                    sil.ap()[:, 0:256], psG.ap()[:, 0:256], 0.0, None, op0=ALU.max),
                    inc=ev)
                t_h = P.op("vector", lambda e, hc=hc: e.tensor_mul(
                    hdv[:, hc, :], sil.ap()[:, 0:256], sil.ap()[:, 0:256]), inc=ev)
            P.wait("tensor", ev, t_h)
        P.barrier()
        for obk in range(4):
            for hc in range(NH):
                P.wait("sync", mm, mm.count)
                if layer == 0:
                    t_w = P.dma("sync", wf.ap()[:, 0:512],
                                io["down_r"][hc * 128:(hc + 1) * 128,
                                             obk * 512:(obk + 1) * 512], wld)
                    P.wait("vector", wld, t_w)
                    P.wait("vector", mm, mm.count)
                    t_c = P.op("vector", lambda e: e.tensor_copy(dwb.ap(), wf.ap()[:, 0:512]),
                               inc=wc)
                    P.wait("tensor", wc, t_c)
                else:
                    t_w = P.dma("sync", dwb.ap(),
                                io["tpj_r"][hc * 128:(hc + 1) * 128,
                                            obk * 512:(obk + 1) * 512], wld)
                    P.wait("tensor", wld, t_w)
                if hc == 0:
                    P.wait("tensor", ev, ev.count)
                t_m = None
                for tt in range(2):
                    t_m = P.op("tensor", lambda e, tt=tt, hc=hc:
                               e.matmul(psE[tt].ap(),
                                        hdv[:, hc, tt * 128:(tt + 1) * 128],
                                        dwb.ap(), start=(hc == 0),
                                        stop=(hc == NH - 1)),
                               inc=mm if tt == 1 else None)
            P.wait("vector", mm, mm.count)
            for tt in range(2):
                if layer == 0:
                    P.op("vector", lambda e, tt=tt, obk=obk: e.tensor_mul(
                        rawh.ap()[:, tt * D + obk * 512: tt * D + obk * 512 + 512],
                        psE[tt].ap(), B["brow"].ap()[:, obk * 512:(obk + 1) * 512]),
                        inc=ev)
                else:
                    P.op("vector", lambda e, tt=tt, obk=obk: e.tensor_scalar(
                        rawh.ap()[:, tt * D + obk * 512: tt * D + obk * 512 + 512],
                        psE[tt].ap(), B["ccol"].ap(), None, op0=ALU.mult), inc=ev)
            P.wait("tensor", ev, ev.count)
        P.barrier()
        for i in range(2):
            tt = tt0 + i
            raw_ap = rawh.ap()[:, i * D:(i + 1) * D]
            P.wait("sync", vs, vs.count)
            t_x = P.dma("sync", B["sfa"].ap(), x_in[tt * 128:(tt + 1) * 128, :], ld2)
            t_n = emit_ln_cols(P, co, raw_ap, B["sfb"].ap(), B["cols"].ap(),
                               P.nsem("ovs"))
            P.wait("vector", P.nsem("ovs"), t_n)
            P.op("vector", lambda e, raw_ap=raw_ap: e.tensor_scalar(
                B["sfb"].ap(), raw_ap, B["cols"].ap()[:, 3:4], B["cols"].ap()[:, 4:5],
                op0=ALU.mult, op1=ALU.add))
            P.wait("vector", ld2, t_x)
            t_s = P.op("vector", lambda e: e.tensor_add(
                B["wfbig"].ap()[:, 0:D], B["sfa"].ap(), B["sfb"].ap()), inc=vs)
            P.wait("gpsimd", vs, t_s)
            P.dma("gpsimd", x_out[tt * 128:(tt + 1) * 128, :],
                  B["wfbig"].ap()[:, 0:D], st)
            P.wait("vector", st, st.count)
            P.wait("scalar", st, st.count)
        P.wait("sync", st, st.count)
        P.wait("gpsimd", st, st.count)
        P.barrier()


def emit_tern_f(P, nc, es, co, B, io):
    wf = B["wfbig"].ap()[:, 0:2048]
    ab = B["wfbig"].ap()[:, 2048:4096]
    s1, s2, obf = B["sfa"], B["sfb"], B["sba"]
    msc, thr = B["msc"], B["thr"]
    psA, psB = B["psRow"], B["ps2"]
    ld = P.nsem("cld"); cst = P.nsem("ccst"); mm = P.nsem("cmm")
    ev = P.nsem("cev"); st = P.nsem("cst2")
    specs = [("fc", io["w_fc_t"], True, io["tfc_t"]),
             ("pj", io["w_proj_r"], False, io["tpj_r"])]
    NTOT = H1 * D

    def bcast_scalar(src_ap, dst_col_ap):
        P.edone("tensor", "vector")
        t_bb = P.op("tensor", lambda e, s=src_ap, : e.matmul(
            psB.ap()[:, 0:1], co["ones_r_f"].ap(), s, start=True, stop=True), inc=mm)
        P.wait("vector", mm, t_bb)
        return P.op("vector", lambda e, d=dst_col_ap: e.tensor_copy(d, psB.ap()[:, 0:1]),
                    inc=ev)

    for mi, (tag, wsrc, tiled, dst) in enumerate(specs):
        nbands = NH1
        for ph in range(2):
            for bi in range(nbands):
                P.wait("sync", mm, mm.count)
                if tiled:
                    t_w = P.dma("sync", wf.rearrange("p (ic m) -> p ic m", ic=ND),
                                wsrc[bi].rearrange("ic p m -> p ic m"), ld)
                else:
                    t_w = P.dma("sync", wf, wsrc[bi * 128:(bi + 1) * 128, :], ld)
                P.wait("scalar", ld, t_w)
                P.wait("scalar", mm, mm.count)
                if ph == 0:
                    t_a = P.op("scalar", lambda e: e.activation(ab, wf, AF.Abs), inc=cst)
                else:
                    P.op("scalar", lambda e: e.activation(ab, wf, AF.Abs))
                    P.edone("vector", "scalar")
                    t_a = P.op("vector", lambda e, mi=mi: e.tensor_scalar(
                        ab, ab, thr.ap()[:, 2 + mi:3 + mi], None, op0=ALU.add), inc=cst)
                P.wait("tensor", cst, t_a)
                for j in range(4):
                    t_m = P.op("tensor", lambda e, j=j, first=(bi == 0 and j == 0),
                               last=(bi == nbands - 1 and j == 3):
                               e.matmul(psA.ap(), co["ones_c_f"].ap(),
                                        ab[:, j * 512:(j + 1) * 512],
                                        start=first, stop=last),
                               inc=mm if j == 3 else None)
            P.wait("vector", mm, mm.count)
            if ph == 0:
                P.op("vector", lambda e, mi=mi: e.reduce_sum(
                    msc.ap()[:, 6 + mi:7 + mi], psA.ap(), axis=AXX))
                P.op("vector", lambda e, mi=mi: e.tensor_scalar(
                    msc.ap()[:, 6 + mi:7 + mi], msc.ap()[:, 6 + mi:7 + mi],
                    1.0 / NTOT, None, op0=ALU.mult))
                t_v = P.op("vector", lambda e, mi=mi: e.tensor_scalar(
                    msc.ap()[:, 4:5], msc.ap()[:, 6 + mi:7 + mi], -1.0, None,
                    op0=ALU.mult), inc=ev)
                t_v = bcast_scalar(msc.ap()[:, 4:5], thr.ap()[:, 2 + mi:3 + mi])
                P.wait("tensor", ev, t_v)
                P.wait("scalar", ev, t_v)
            else:
                P.op("vector", lambda e, mi=mi: e.reduce_sum(
                    msc.ap()[:, 4:5], psA.ap(), axis=AXX))
                P.op("vector", lambda e, mi=mi: e.tensor_scalar(
                    msc.ap()[:, 4:5], msc.ap()[:, 4:5], 1.0 / NTOT, None, op0=ALU.mult))
                P.op("vector", lambda e, mi=mi: e.tensor_add(
                    msc.ap()[:, mi:mi + 1], msc.ap()[:, 4:5], msc.ap()[:, 6 + mi:7 + mi]))
                t_v = P.op("vector", lambda e, mi=mi: e.tensor_scalar(
                    msc.ap()[:, mi:mi + 1], msc.ap()[:, mi:mi + 1], 1e-5, None,
                    op0=ALU.max), inc=ev)
                P.op("vector", lambda e, mi=mi: e.tensor_scalar(
                    msc.ap()[:, 4:5], msc.ap()[:, mi:mi + 1], -0.7, None, op0=ALU.mult))
                t_v = bcast_scalar(msc.ap()[:, 4:5], thr.ap()[:, mi:mi + 1])
                P.wait("tensor", ev, t_v)
                P.wait("scalar", ev, t_v)
    P.op("vector", lambda e: e.tensor_mul(msc.ap()[:, 5:6], msc.ap()[:, 0:1],
                                          msc.ap()[:, 0:1]))
    t_v = P.op("vector", lambda e: e.tensor_mul(msc.ap()[:, 5:6], msc.ap()[:, 5:6],
                                                msc.ap()[:, 1:2]), inc=ev)
    t_v = bcast_scalar(msc.ap()[:, 5:6], B["ccol"].ap())
    P.barrier()
    for mi, (tag, wsrc, tiled, dst) in enumerate(specs):
        tc_ap = thr.ap()[:, mi:mi + 1]
        for bi in range(NH1):
            P.wait("sync", ev, ev.count)
            if tiled:
                t_w = P.dma("sync", wf.rearrange("p (ic m) -> p ic m", ic=ND),
                            wsrc[bi].rearrange("ic p m -> p ic m"), ld)
            else:
                t_w = P.dma("sync", wf, wsrc[bi * 128:(bi + 1) * 128, :], ld)
            P.wait("scalar", ld, t_w)
            P.wait("scalar", ev, ev.count)
            P.op("scalar", lambda e, b=tc_ap: e.activation(
                s1.ap(), wf, AF.Relu, bias=b, scale=1.0))
            P.op("scalar", lambda e: e.activation(s1.ap(), s1.ap(), AF.Sign))
            P.op("scalar", lambda e, b=tc_ap: e.activation(
                s2.ap(), wf, AF.Relu, bias=b, scale=-1.0))
            t_2 = P.op("scalar", lambda e: e.activation(s2.ap(), s2.ap(), AF.Sign),
                       inc=cst)
            P.wait("vector", cst, t_2)
            t_o = P.op("vector", lambda e: e.tensor_sub(obf.ap(), s1.ap(), s2.ap()),
                       inc=ev)
            P.wait("gpsimd", ev, t_o)
            if tiled:
                P.dma("gpsimd", dst[bi].rearrange("ic p m -> p ic m"),
                      obf.ap().rearrange("p (ic m) -> p ic m", ic=ND), st)
            else:
                P.dma("gpsimd", dst[bi * 128:(bi + 1) * 128, :], obf.ap(), st)
            P.wait("scalar", st, st.count)
    P.wait("sync", st, st.count)
    P.wait("gpsimd", st, st.count)
    P.barrier()


def build_fused(nc, io_specs):
    io = _declare(nc, io_specs)
    ID = lambda n, s, d: nc.dram_tensor(n, list(s), d, kind="Internal").ap()
    io["x1_d"] = ID("x1_d", (TC, D), F32)
    io["x2_d"] = ID("x2_d", (TC, D), F32)
    io["x3_d"] = ID("x3_d", (TC, D), F32)
    io["ex0in"] = ID("ex0in", (6144, 2048), BF16)
    io["ex0out"] = ID("ex0out", (12288, 2048), BF16)
    io["ao0in"] = ID("ao0in", (1024, 4096), BF16)
    io["ao0out"] = ID("ao0out", (2048, 4096), BF16)
    io["ex1in"] = ID("ex1in", (6144, 2048), BF16)
    io["ex1out"] = ID("ex1out", (12288, 2048), BF16)
    io["ao1in"] = ID("ao1in", (1024, 4096), BF16)
    io["ao1out"] = ID("ao1out", (2048, 4096), BF16)
    io["tfc_t"] = ID("tfc_t", (NH1, ND, 128, 128), BF16)
    io["tpj_r"] = ID("tpj_r", (H1, D), BF16)

    io["qloc"] = ID("qloc", (1024, 4096), BF16)
    io["kloc"] = ID("kloc", (1024, 4096), BF16)
    io["vloc"] = ID("vloc", (4096, 1024), BF16)
    io["aoloc"] = ID("aoloc", (2048, 2048), BF16)

    es = contextlib.ExitStack()
    with es:
        P = Prog(nc)
        co = emit_consts(P, nc, es)
        B = alloc_arena(nc, es)
        dynr = nc.gpsimd.alloc_register("dynr")
        cc = P.nsem("cc")
        bld = P.nsem("gbl")
        mk = P.nsem("mk")
        t = None
        for r in range(4):
            P.op("gpsimd", lambda e, r=r: e.memset(B["masks0"].ap()[:, r * 512:(r + 1) * 512], 0.0))
            t = P.op("gpsimd", lambda e, r=r: e.affine_select(
                out=B["masks0"].ap()[:, r * 512:(r + 1) * 512],
                in_=B["masks0"].ap()[:, r * 512:(r + 1) * 512],
                compare_op=ALU.is_gt, fill=1.0, base=r * 128,
                pattern=[[-1, 512]], channel_multiplier=1), inc=mk)
        for r in range(2):
            P.op("gpsimd", lambda e, r=r: e.memset(B["masks1"].ap()[:, r * 256:(r + 1) * 256], 0.0))
            t = P.op("gpsimd", lambda e, r=r: e.affine_select(
                out=B["masks1"].ap()[:, r * 256:(r + 1) * 256],
                in_=B["masks1"].ap()[:, r * 256:(r + 1) * 256],
                compare_op=ALU.is_gt, fill=1.0, base=r * 128,
                pattern=[[-1, 256]], channel_multiplier=1), inc=mk)
        for e in ("tensor", "vector", "scalar"):
            P.wait(e, mk, t)
        P.dma("sync", B["pswap"].ap(), io["pswap"], bld)
        P.dma("sync", B["dyn"].ap(), io["dyn"], bld)
        P.dma("sync", B["bcol"].ap()[:, 0:ND], io["bq_c"], bld)
        t_b = P.dma("sync", B["bcol"].ap()[:, ND:], io["bk_c"], bld)
        P.wait("vector", bld, t_b)
        bsc = P.nsem("bsc")
        P.op("vector", lambda e: e.tensor_scalar(
            B["bcol"].ap()[:, 0:ND], B["bcol"].ap()[:, 0:ND],
            SCALE0, None, op0=ALU.mult), inc=bsc)
        P.wait("tensor", bld, t_b)
        P.wait("gpsimd", bld, t_b)
        P.barrier()

        def stage_exchange(exout, stg):
            ss = P.nsem("stg")
            dyn = B["dyn"]
            exout_t = exout.tensor
            for r in range(2):
                for ki, dst in ((0, io["qloc"]), (2, io["kloc"])):
                    P.rawop("gpsimd", lambda e, i=ki + r: e.reg_load(
                        dynr, dyn.ap()[0:1, i:i + 1]))
                    P.dma("gpsimd", dst[:, r * 2048:(r + 1) * 2048],
                          bass.AP(exout_t, dynr, [[2048, 1024], [1, 2048]]), ss)
                P.rawop("gpsimd", lambda e, i=4 + r: e.reg_load(
                    dynr, dyn.ap()[0:1, i:i + 1]))
                P.dma("gpsimd", io["vloc"][r * 2048:(r + 1) * 2048, :],
                      bass.AP(exout_t, dynr, [[2048, 2048], [1, 1024]]), ss)
            P.wait("gpsimd", ss, ss.count)
            P.barrier()

        def stage_ao(aoout):
            ss = P.nsem("stg")
            dyn = B["dyn"]
            P.rawop("gpsimd", lambda e: e.reg_load(dynr, dyn.ap()[0:1, 6:7]))
            P.dma("gpsimd", io["aoloc"],
                  bass.AP(aoout.tensor, dynr, [[4096, 2048], [1, 2048]]), ss)
            P.wait("gpsimd", ss, ss.count)
            P.barrier()

        sinT, cosT = emit_rope_f(P, nc, es, B, io, "r0")
        emit_xnT_f(P, nc, es, co, B, io["x0"], NT)
        xnT_ap = B["xnTbig"].ap()[:, 0:ND * TC]
        emit_projF_f(P, nc, es, co, B, xnT_ap, io["wq_t"], ND, io["ex0in"][0:2048],
                     rope=(sinT, cosT), bias_off=0, scale=SCALE0)
        emit_projF_f(P, nc, es, co, B, xnT_ap, io["wk_t"], ND, io["ex0in"][2048:4096],
                     rope=(sinT, cosT), bias_off=ND, scale=1.0)
        gbl2 = P.nsem("gbl2")
        t_bv = P.dma("gpsimd", B["brow"].ap(), io["bv_row"].broadcast_to([128, D]), gbl2)
        P.wait("vector", gbl2, t_bv)
        emit_projT_f(P, nc, es, co, B, xnT_ap, ND, io["wv_r"], io["ex0in"][4096:6144],
                     bias_bcast=B["brow"].ap())
        t_cc = P.op("gpsimd", lambda e: e.collective_compute(
            "AllGather", ALU.bypass, replica_groups=[[0, 1], [2, 3], [4, 5], [6, 7]],
            ins=[io["ex0in"].opt()], outs=[io["ex0out"].opt()]), inc=cc)
        for e in ENGINES:
            P.wait(e, cc, t_cc)
        stage_exchange(io["ex0out"], None)
        emit_attn0_f(P, nc, es, co, B, (io["qloc"], io["kloc"], io["vloc"]),
                     io["ao0in"])
        t_cc = P.op("gpsimd", lambda e: e.collective_compute(
            "AllGather", ALU.bypass, replica_groups=[[0, 1], [2, 3], [4, 5], [6, 7]],
            ins=[io["ao0in"].opt()], outs=[io["ao0out"].opt()]), inc=cc)
        for e in ENGINES:
            P.wait(e, cc, t_cc)
        stage_ao(io["ao0out"])
        emit_oproj_f(P, nc, es, co, B, io["aoloc"], io["wo_r"],
                     io["bo_row"], io["x0"], io["x1_d"])
        emit_ffn_f(P, nc, es, co, B, io, 0)
        sinT, cosT = emit_rope_f(P, nc, es, B, io, "r1")
        emit_xnT_f(P, nc, es, co, B, io["x2_d"], NT)
        emit_projF_f(P, nc, es, co, B, xnT_ap, io["wq1_t"], ND, io["ex1in"][0:2048],
                     rope=(sinT, cosT), elu1=True)
        emit_projF_f(P, nc, es, co, B, xnT_ap, io["wk1_t"], ND, io["ex1in"][2048:4096],
                     rope=(sinT, cosT), elu1=True)
        emit_projT_f(P, nc, es, co, B, xnT_ap, ND, io["wv1_r"], io["ex1in"][4096:6144])
        t_cc = P.op("gpsimd", lambda e: e.collective_compute(
            "AllGather", ALU.bypass, replica_groups=[[0, 1], [2, 3], [4, 5], [6, 7]],
            ins=[io["ex1in"].opt()], outs=[io["ex1out"].opt()]), inc=cc)
        for e in ENGINES:
            P.wait(e, cc, t_cc)
        stage_exchange(io["ex1out"], None)
        emit_attn1_f(P, nc, es, co, B, (io["qloc"], io["kloc"], io["vloc"]),
                     io["ao1in"])
        t_cc = P.op("gpsimd", lambda e: e.collective_compute(
            "AllGather", ALU.bypass, replica_groups=[[0, 1], [2, 3], [4, 5], [6, 7]],
            ins=[io["ao1in"].opt()], outs=[io["ao1out"].opt()]), inc=cc)
        for e in ENGINES:
            P.wait(e, cc, t_cc)
        stage_ao(io["ao1out"])
        emit_oproj_f(P, nc, es, co, B, io["aoloc"], io["wout_r"],
                     None, io["x2_d"], io["x3_d"])
        emit_tern_f(P, nc, es, co, B, io)
        emit_ffn_f(P, nc, es, co, B, io, 1)
        P.finalize(es)
    return nc


FUSED_SPECS = [
    ("x0", (TC, D), F32, "ExternalInput"),
    ("wq_t", (ND, ND, 128, 128), F32, "ExternalInput"),
    ("wk_t", (ND, ND, 128, 128), F32, "ExternalInput"),
    ("wv_r", (D, D), F32, "ExternalInput"),
    ("wo_r", (D, D), F32, "ExternalInput"),
    ("bq_c", (128, ND), F32, "ExternalInput"),
    ("bk_c", (128, ND), F32, "ExternalInput"),
    ("bv_row", (1, D), F32, "ExternalInput"),
    ("bo_row", (1, D), F32, "ExternalInput"),
    ("gate_t", (NH0, ND, 128, 128), F32, "ExternalInput"),
    ("up_t", (NH0, ND, 128, 128), F32, "ExternalInput"),
    ("down_r", (INTER0, D), F32, "ExternalInput"),
    ("gate_s_c", (128, NH0), F32, "ExternalInput"),
    ("up_s_c", (128, NH0), F32, "ExternalInput"),
    ("down_s_row", (1, D), F32, "ExternalInput"),
    ("wq1_t", (ND, ND, 128, 128), F32, "ExternalInput"),
    ("wk1_t", (ND, ND, 128, 128), F32, "ExternalInput"),
    ("wv1_r", (D, D), F32, "ExternalInput"),
    ("wout_r", (D, D), F32, "ExternalInput"),
    ("w_fc_t", (NH1, ND, 128, 128), F32, "ExternalInput"),
    ("w_proj_r", (H1, D), F32, "ExternalInput"),
    ("r0sh", (128, TC), F32, "ExternalInput"),
    ("r0sl", (128, TC), F32, "ExternalInput"),
    ("r0ch", (128, TC), F32, "ExternalInput"),
    ("r0cl", (128, TC), F32, "ExternalInput"),
    ("r1sh", (128, TC), F32, "ExternalInput"),
    ("r1sl", (128, TC), F32, "ExternalInput"),
    ("r1ch", (128, TC), F32, "ExternalInput"),
    ("r1cl", (128, TC), F32, "ExternalInput"),
    ("pswap", (128, 128), F32, "ExternalInput"),
    ("dyn", (1, 64), mybir.dt.int32, "ExternalInput"),
    ("x4", (TC, D), F32, "ExternalOutput"),
]


def kernel(**inputs):
    f32 = np.float32
    g = {k: np.asarray(v) for k, v in inputs.items()}
    x = np.ascontiguousarray(g["L0_attn_x"].astype(f32, copy=False))

    wq_t = tile_lhsT(np.ascontiguousarray(g["L0_attn_Wq"]), ND, ND)
    wk_t = tile_lhsT(np.ascontiguousarray(g["L0_attn_Wk"]), ND, ND)
    wv_r = np.ascontiguousarray(g["L0_attn_Wv"].T)
    wo_r = np.ascontiguousarray(g["L0_attn_Wo"].T)
    bq_c = col_layout(g["L0_attn_bq"]); bk_c = col_layout(g["L0_attn_bk"])
    bv_row = np.ascontiguousarray(g["L0_attn_bv"].reshape(1, D))
    bo_row = np.ascontiguousarray(g["L0_attn_bo"].reshape(1, D))
    gate_t = tile_lhsT(np.ascontiguousarray(g["L0_ffn_gate_w"]), NH0, ND)
    up_t = tile_lhsT(np.ascontiguousarray(g["L0_ffn_up_w"]), NH0, ND)
    down_r = np.ascontiguousarray(g["L0_ffn_down_w"].T)
    gate_s_c = col_layout(g["L0_ffn_gate_s"]); up_s_c = col_layout(g["L0_ffn_up_s"])
    down_s_row = np.ascontiguousarray(g["L0_ffn_down_s"].reshape(1, D))
    wqkv = np.ascontiguousarray(g["L1_attn_W_qkv"])
    wq1_t = tile_lhsT(np.ascontiguousarray(wqkv[0:D]), ND, ND)
    wk1_t = tile_lhsT(np.ascontiguousarray(wqkv[D:2 * D]), ND, ND)
    wv1_r = np.ascontiguousarray(wqkv[2 * D:3 * D].T)
    wout_r = np.ascontiguousarray(g["L1_attn_W_out"].T)
    w_fc_t = tile_lhsT(np.ascontiguousarray(g["L1_ffn_w_fc"]), NH1, ND)
    w_proj_r = np.ascontiguousarray(g["L1_ffn_w_proj"].T)
    r0 = make_rope_tables(0)
    r1 = make_rope_tables(1)
    pswap_m = np.zeros((128, 128), f32)
    pswap_m[np.arange(128), np.arange(128) ^ 1] = 1.0

    def rope_slices(tabs, p):
        sh, sl, ch, cl = tabs
        cols = np.arange(p * 2048, (p + 1) * 2048)
        return tuple(np.ascontiguousarray(a[:, cols]) for a in tabs)

    nc = bass.Bass("TRN2", target_bir_lowering=False)
    build_fused(nc, FUSED_SPECS)

    ins = []
    for c in CORES:
        bb, p = c // 2, c % 2
        sh0, sl0, ch0, cl0 = rope_slices(r0, p)
        sh1, sl1, ch1, cl1 = rope_slices(r1, p)
        dyn = np.zeros((1, 64), np.int32)
        for r in range(2):
            dyn[0, 0 + r] = (r * 6144 + p * 1024) * 2048
            dyn[0, 2 + r] = (r * 6144 + 2048 + p * 1024) * 2048
            dyn[0, 4 + r] = (r * 6144 + 4096) * 2048 + p * 1024
        dyn[0, 6] = p * 2048
        ins.append(dict(
            x0=np.ascontiguousarray(x[bb, p * 2048:(p + 1) * 2048]),
            wq_t=wq_t, wk_t=wk_t, wv_r=wv_r, wo_r=wo_r, bq_c=bq_c, bk_c=bk_c,
            bv_row=bv_row, bo_row=bo_row, gate_t=gate_t, up_t=up_t, down_r=down_r,
            gate_s_c=gate_s_c, up_s_c=up_s_c, down_s_row=down_s_row,
            wq1_t=wq1_t, wk1_t=wk1_t, wv1_r=wv1_r, wout_r=wout_r,
            w_fc_t=w_fc_t, w_proj_r=w_proj_r,
            r0sh=sh0, r0sl=sl0, r0ch=ch0, r0cl=cl0,
            r1sh=sh1, r1sl=sl1, r1ch=ch1, r1cl=cl1,
            pswap=pswap_m, dyn=dyn))
    res = run_bass_kernel_spmd(nc, ins, core_ids=CORES)
    out = np.zeros((B, S, D), f32)
    for c in CORES:
        bb, p = c // 2, c % 2
        out[bb, p * 2048:(p + 1) * 2048] = res.results[c]["x4"]
    return out


# version 11
# speedup vs baseline: 1.0111x; 1.0111x over previous
import math
import contextlib
import numpy as np
import ml_dtypes
import concourse.bass as bass
import concourse.mybir as mybir
from concourse.bass_utils import run_bass_kernel_spmd

F32 = mybir.dt.float32
BF16 = mybir.dt.bfloat16
AF = mybir.ActivationFunctionType
ALU = mybir.AluOpType
AXX = mybir.AxisListType.X

B, S, D = 4, 4096, 2048
H = 16
TC = 2048
NT = TC // 128
INTER0 = 5632
NH0 = INTER0 // 128
H1 = 5120
NH1 = H1 // 128
CHUNK = 256
ND = D // 128
SCALE0 = 1.0 / math.sqrt(128.0)
C_SHIFT = 20.0
PAIRSWAP = [i ^ 1 for i in range(32)]
SPANS = {0: [0, 3], 1: [1, 2]}
CORES = list(range(8))


def own_ranges(p):
    return [(sp * 1024, (sp + 1) * 1024) for sp in SPANS[p]]


def own_chunks(p):
    return [sp * 4 + i for sp in SPANS[p] for i in range(4)]



def _reduce_pm_pi(theta):
    k = np.round(theta / (2 * np.pi))
    return np.clip(theta - k * (2 * np.pi), -np.pi + 1e-7, np.pi - 1e-7)


def _hi_lo(a):
    hi = a.astype(np.float32)
    return hi, (a - hi.astype(np.float64)).astype(np.float32)


def make_rope_tables(layer):
    j = np.arange(128)
    t = np.arange(S, dtype=np.float64)
    inv = 10000.0 ** (-(np.arange(64, dtype=np.float64) / 64.0))
    theta = (inv[j // 2] if layer == 0 else inv[j % 64])[:, None] * t[None, :]
    sign = np.where(j % 2 == 0, -1.0, 1.0)[:, None]
    sh, sl = _hi_lo(_reduce_pm_pi(theta * sign))
    ch, cl = _hi_lo(_reduce_pm_pi(theta + np.pi / 2))
    return sh, sl, ch, cl


def tile_lhsT(W, noc, nic):
    W4 = W.reshape(noc, 128, nic, 128)
    return np.ascontiguousarray(W4.transpose(0, 2, 3, 1))


def col_layout(v):
    n = v.shape[0] // 128
    return np.ascontiguousarray(v.reshape(n, 128).T)



ENGINES = ("sync", "tensor", "vector", "scalar", "gpsimd")


class Sem:
    __slots__ = ("name", "count", "handle")

    def __init__(self, name):
        self.name, self.count, self.handle = name, 0, None


STRICT = ("vector", "scalar", "gpsimd")


class Prog:

    def __init__(self, nc):
        self.nc = nc
        self.q = {e: [] for e in ENGINES}
        self.sems = []
        self.last_wait = {e: {} for e in ENGINES}
        self.serial = {e: self.sem("ser_" + e) for e in STRICT}
        self.last_done = {}
        self.all_waits = []
        self.dma_sem_owner = {}
        self.named = {}
        self.barsem = None

    def sem(self, name=None):
        s = Sem(name or f"s{len(self.sems)}")
        self.sems.append(s)
        return s

    def nsem(self, name):
        if name not in self.named:
            self.named[name] = self.sem(name)
        return self.named[name]

    def wait(self, eng, sem, val):
        if sem is None or val is None or val <= 0:
            return
        lw = self.last_wait[eng]
        if lw.get(sem.name, 0) >= val:
            return
        lw[sem.name] = val
        self.all_waits.append((eng, sem, val))
        self.q[eng].append(lambda e, s=sem, v=val: e.wait_ge(s.handle, v))

    def op(self, eng, fn, inc=None, inc_n=1):
        if eng in STRICT:
            ld = self.last_done.get(eng)
            if ld is not None:
                self.wait(eng, ld[0], ld[1])
            if inc is not None:
                inc.count += inc_n
                self.q[eng].append(
                    lambda e, f=fn, s=inc, n=inc_n: f(e).then_inc(s.handle, n))
                self.last_done[eng] = (inc, inc.count)
                return inc.count
            ser = self.serial[eng]
            ser.count += 1
            self.q[eng].append(
                lambda e, f=fn, sr=ser: f(e).then_inc(sr.handle, 1))
            self.last_done[eng] = (ser, ser.count)
            return None
        if inc is not None:
            inc.count += inc_n
            self.q[eng].append(
                lambda e, f=fn, s=inc, n=inc_n: f(e).then_inc(s.handle, n))
            return inc.count
        self.q[eng].append(fn)
        return None

    def rawop(self, eng, fn):
        self.q[eng].append(fn)

    def edone(self, waiter, producer):
        ld = self.last_done.get(producer)
        if ld is not None:
            self.wait(waiter, ld[0], ld[1])

    def dma(self, eng, out, in_, sem, **kw):
        owner = self.dma_sem_owner.setdefault(sem.name, eng)
        assert owner == eng, f"DMA sem {sem.name} used from {eng} and {owner}"
        if self.last_wait[eng].get(sem.name, 0) < sem.count:
            self.wait(eng, sem, sem.count)
            self.q[eng].append(lambda e: e.nop())
        sem.count += 16
        self.q[eng].append(
            lambda e, o=out, i=in_, k=kw, s=sem: e.dma_start(out=o, in_=i, **k)
            .then_inc(s.handle, 16))
        return sem.count

    def barrier(self):
        if self.barsem is None:
            self.barsem = self.sem("bar")
        bs = self.barsem
        for e in ENGINES:
            if e == "sync":
                self.q[e].append(lambda en, s=bs: en.nop().then_inc(s.handle, 1))
            else:
                self.q[e].append(lambda en, s=bs: en.drain().then_inc(s.handle, 1))
            bs.count += 1
        tgt = bs.count
        for e in ENGINES:
            self.wait(e, bs, tgt)

    def finalize(self, es):
        for eng, sem, val in self.all_waits:
            if val > sem.count:
                raise RuntimeError(
                    f"unsatisfiable wait: {eng} waits {sem.name} >= {val}, "
                    f"final count {sem.count}")
        for s in self.sems:
            s.handle = es.enter_context(self.nc.semaphore(s.name))
        block = es.enter_context(self.nc.Block())
        for eng in ENGINES:
            ops = self.q[eng]
            if ops:
                getattr(block, eng)(lambda e, ops=ops: [f(e) for f in ops])



def emit_consts(P, nc, es):
    c = {}
    c["ones_r_f"] = es.enter_context(nc.sbuf_tensor("ones_r_f", [1, 128], F32))
    c["ones_c_b"] = es.enter_context(nc.sbuf_tensor("ones_c_b", [128, 1], BF16))
    c["ones_c_f"] = es.enter_context(nc.sbuf_tensor("ones_c_f", [128, 1], F32))
    c["ident_b"] = es.enter_context(nc.sbuf_tensor("ident_b", [128, 128], BF16))
    c["negC"] = es.enter_context(nc.sbuf_tensor("negC", [128, 1], F32))
    c["epsc"] = es.enter_context(nc.sbuf_tensor("epsc", [128, 1], F32))
    done = P.sem("consts")
    P.op("gpsimd", lambda e: e.memset(c["ones_r_f"].ap(), 1.0))
    P.op("gpsimd", lambda e: e.memset(c["ones_c_b"].ap(), 1.0))
    P.op("gpsimd", lambda e: e.memset(c["ones_c_f"].ap(), 1.0))
    P.op("gpsimd", lambda e: e.memset(c["negC"].ap(), -C_SHIFT))
    P.op("gpsimd", lambda e: e.memset(c["epsc"].ap(), 1e-5))
    P.op("gpsimd", lambda e: e.memset(c["ident_b"].ap(), 0.0))
    k = P.op("gpsimd", lambda e: e.affine_select(
        out=c["ident_b"].ap(), in_=c["ident_b"].ap(),
        compare_op=ALU.not_equal, fill=1.0, base=0,
        pattern=[[-1, 128]], channel_multiplier=1), inc=done)
    c["done"] = done
    for e in ENGINES:
        P.wait(e, done, k)
    return c


def emit_ln_cols(P, co, xs_ap, scr_ap, cols_ap, vsem):
    nf = xs_ap.shape[-1]
    inv_n = 1.0 / nf
    sumc, sgc = cols_ap[:, 0:1], cols_ap[:, 1:2]
    muc, rsc, nmc = cols_ap[:, 2:3], cols_ap[:, 3:4], cols_ap[:, 4:5]
    P.edone("scalar", "vector")
    t = P.op("scalar", lambda e: e.activation(scr_ap, xs_ap, AF.Square, accum_out=sgc),
             inc=vsem)
    P.op("vector", lambda e: e.reduce_sum(sumc, xs_ap, axis=AXX))
    P.op("vector", lambda e: e.tensor_scalar(muc, sumc, inv_n, None, op0=ALU.mult))
    P.wait("vector", vsem, t)
    P.op("vector", lambda e: e.tensor_scalar(sgc, sgc, inv_n, None, op0=ALU.mult))
    P.op("vector", lambda e: e.tensor_mul(nmc, muc, muc))
    t = P.op("vector", lambda e: e.tensor_sub(sgc, sgc, nmc), inc=vsem)
    P.wait("scalar", vsem, t)
    t = P.op("scalar", lambda e: e.activation(sgc, sgc, AF.Sqrt, bias=co["epsc"].ap()),
             inc=vsem)
    P.wait("vector", vsem, t)
    P.op("vector", lambda e: e.reciprocal(rsc, sgc))
    P.op("vector", lambda e: e.tensor_mul(nmc, muc, rsc))
    return P.op("vector", lambda e: e.tensor_scalar(nmc, nmc, -1.0, None, op0=ALU.mult),
                inc=vsem)


def alloc_xnT_bufs(nc, es, tag, ntt):
    b = {}
    b["xnT"] = es.enter_context(nc.sbuf_tensor(tag + "T", [128, ND * ntt * 128], BF16))
    b["xs"] = es.enter_context(nc.sbuf_tensor(tag + "_xs", [128, D], F32))
    b["xb"] = es.enter_context(nc.sbuf_tensor(tag + "_xb", [128, D], BF16))
    b["scr"] = es.enter_context(nc.sbuf_tensor(tag + "_scr", [128, D], F32))
    b["cols"] = es.enter_context(nc.sbuf_tensor(tag + "_cols", [128, 6], F32))
    b["pt"] = es.enter_context(nc.psum_tensor(tag + "_pt", [128, 512], BF16))
    return b


def emit_xnT(P, nc, es, co, x_dram, tag="xn", ntt=NT, tt0=0, bufs=None):
    ess = contextlib.ExitStack()
    if bufs is None:
        bufs = alloc_xnT_bufs(nc, es, tag, ntt)
    xnT, xs, xb, scr, cols, pt = (bufs["xnT"], bufs["xs"], bufs["xb"],
                                  bufs["scr"], bufs["cols"], bufs["pt"])
    ld = P.sem(); vs = P.sem(); tp = P.sem(); ev = P.sem()
    for i in range(ntt):
        tt = tt0 + i
        P.wait("sync", tp, tp.count)
        t_ld = P.dma("sync", xs.ap(), x_dram[tt * 128:(tt + 1) * 128, :], ld)
        P.wait("scalar", ld, t_ld)
        P.wait("vector", ld, t_ld)
        t_n = emit_ln_cols(P, co, xs.ap(), scr.ap(), cols.ap(), vs)
        P.wait("vector", vs, t_n)
        t_c = P.op("vector", lambda e: e.tensor_scalar(
            xb.ap(), xs.ap(), cols.ap()[:, 3:4], cols.ap()[:, 4:5],
            op0=ALU.mult, op1=ALU.add), inc=vs)
        for g in range(4):
            P.wait("tensor", vs, t_c)
            P.wait("tensor", ev, ev.count)
            tl = None
            for f in range(4):
                ic = g * 4 + f
                tl = P.op("tensor", lambda e, f=f, ic=ic:
                          e.transpose(pt.ap()[:, f * 128:(f + 1) * 128],
                                      xb.ap()[:, ic * 128:(ic + 1) * 128],
                                      co["ident_b"].ap()),
                          inc=tp if f == 3 else None)
            P.wait("scalar", tp, tl)

            def ev_fn(e, g=g, i=i):
                dst = xnT.ap().rearrange("p (ic t) -> p ic t", ic=ND)[
                    :, g * 4:(g + 1) * 4, i * 128:(i + 1) * 128]
                src = pt.ap().rearrange("p (f t) -> p f t", f=4)
                return e.activation(dst, src, AF.Identity)
            P.op("scalar", ev_fn, inc=ev)
    P.barrier()
    ess.close()
    return xnT


def emit_rope_tables(P, nc, es, io, pref):
    sinT = es.enter_context(nc.sbuf_tensor(pref + "sinT", [128, TC], F32))
    cosT = es.enter_context(nc.sbuf_tensor(pref + "cosT", [128, TC], F32))
    ess = contextlib.ExitStack()
    th = ess.enter_context(nc.sbuf_tensor(pref + "th", [128, 2 * TC], F32))
    ld = P.sem(); ok = P.sem()
    for which, dst in (("s", sinT), ("c", cosT)):
        P.dma("gpsimd", th.ap()[:, 0:TC], io[pref + which + "h"], ld)
        t1 = P.dma("gpsimd", th.ap()[:, TC:], io[pref + which + "l"], ld)
        P.wait("vector", ld, t1)
        t2 = P.op("vector", lambda e: e.tensor_add(
            th.ap()[:, 0:TC], th.ap()[:, 0:TC], th.ap()[:, TC:]), inc=ok)
        P.wait("scalar", ok, t2)
        t3 = P.op("scalar", lambda e, d=dst: e.activation(d.ap(), th.ap()[:, 0:TC], AF.Sin),
                  inc=ok)
        P.wait("gpsimd", ok, t3)
    P.barrier()
    ess.close()
    return sinT, cosT


def emit_proj_featmajor(P, nc, es, co, xnT, w_tiled, noc, out_dram, tag,
                        bias_col_sb=None, bias_off=0, scale=1.0,
                        rope=None, elu1=False, ntok=TC, pswap=None):
    NB = ntok // 512
    ess = contextlib.ExitStack()
    wf = ess.enter_context(nc.sbuf_tensor(tag + "_wf", [128, ND * 128], F32))
    wb = ess.enter_context(nc.sbuf_tensor(tag + "_wb", [128, ND * 128], BF16))
    qr = ess.enter_context(nc.sbuf_tensor(tag + "_qr", [128, 512], F32))
    qs = ess.enter_context(nc.sbuf_tensor(tag + "_qs", [128, 512], F32))
    m1 = ess.enter_context(nc.sbuf_tensor(tag + "_m1", [128, 512], F32))
    ob = ess.enter_context(nc.sbuf_tensor(tag + "_ob", [128, 512], BF16))
    ps = ess.enter_context(nc.psum_tensor(tag + "_ps", [128, 512], F32))
    psQ = ess.enter_context(nc.psum_tensor(tag + "_psQ", [128, 512], F32)) \
        if rope is not None else None
    wld = P.sem(); wc = P.sem(); mm = P.sem(); ev = P.sem(); st = P.sem()
    xv = xnT.ap().rearrange("p (ic t) -> p ic t", ic=ND)
    for oc in range(noc):
        P.wait("sync", mm, mm.count)
        t_w = P.dma("sync", wf.ap().rearrange("p (ic m) -> p ic m", ic=ND),
                    w_tiled[oc].rearrange("ic p m -> p ic m"), wld)
        P.wait("vector", wld, t_w)
        P.wait("vector", mm, mm.count)
        t_c = P.op("vector", lambda e: e.tensor_copy(wb.ap(), wf.ap()), inc=wc)
        for nb in range(NB):
            P.wait("tensor", wc, t_c)
            P.wait("tensor", ev, ev.count)
            t_m = None
            for ic in range(ND):
                t_m = P.op("tensor", lambda e, ic=ic, nb=nb:
                           e.matmul(ps.ap(), wb.ap()[:, ic * 128:(ic + 1) * 128],
                                    xv[:, ic, nb * 512:(nb + 1) * 512],
                                    start=(ic == 0), stop=(ic == ND - 1)),
                           inc=mm if ic == ND - 1 else None)
            P.wait("scalar", mm, t_m)
            P.wait("scalar", st, st.count)
            if bias_col_sb is not None:
                bc = bias_col_sb.ap()[:, bias_off + oc:bias_off + oc + 1]
                t_e = P.op("scalar", lambda e, bc=bc: e.activation(
                    qr.ap(), ps.ap(), AF.Identity, bias=bc, scale=scale), inc=ev)
            else:
                t_e = P.op("scalar", lambda e: e.activation(
                    qr.ap(), ps.ap(), AF.Identity, scale=scale), inc=ev)
            cur = qr
            P.wait("vector", ev, t_e)
            P.wait("vector", st, st.count)
            if rope is not None:
                sinT, cosT = rope
                sv = sinT.ap()[:, nb * 512:(nb + 1) * 512]
                cv = cosT.ap()[:, nb * 512:(nb + 1) * 512]
                P.wait("tensor", ev, t_e)
                t_pm = P.op("tensor", lambda e: e.matmul(
                    psQ.ap(), pswap.ap(), qr.ap(), start=True, stop=True), inc=mm)
                P.op("vector", lambda e, cv=cv: e.tensor_mul(m1.ap(), qr.ap(), cv))
                P.wait("vector", mm, t_pm)
                P.op("vector", lambda e, sv=sv: e.tensor_mul(qs.ap(), psQ.ap(), sv))
                t_e = P.op("vector", lambda e: e.tensor_add(m1.ap(), m1.ap(), qs.ap()),
                           inc=ev)
                cur = m1
            if elu1:
                t_e = P.op("vector", lambda e, c=cur: e.tensor_scalar(
                    qs.ap(), c.ap(), 0.0, None, op0=ALU.min), inc=ev)
                P.wait("scalar", ev, t_e)
                t_e = P.op("scalar", lambda e: e.activation(qs.ap(), qs.ap(), AF.Exp),
                           inc=ev)
                P.wait("vector", ev, t_e)
                P.op("vector", lambda e, c=cur: e.tensor_scalar(
                    c.ap(), c.ap(), 0.0, None, op0=ALU.max))
                t_e = P.op("vector", lambda e, c=cur: e.tensor_add(c.ap(), c.ap(), qs.ap()),
                           inc=ev)
            P.wait("vector", ev, t_e)
            t_b = P.op("vector", lambda e, c=cur: e.tensor_copy(ob.ap(), c.ap()), inc=ev)
            P.wait("gpsimd", ev, t_b)
            P.dma("gpsimd", out_dram[oc * 128:(oc + 1) * 128, nb * 512:(nb + 1) * 512],
                  ob.ap(), st)
    P.wait("gpsimd", st, st.count)
    P.barrier()
    ess.close()


def emit_proj_tokmajor(P, nc, es, co, lhsT_sb, w_rows, nic, out_dram, tag,
                       bias_bcast=None, ntok=TC, nof=D, per_unit_epilogue=None):
    ntt = ntok // 128
    NOB = nof // 512
    ess = contextlib.ExitStack()
    wfs = ess.enter_context(nc.sbuf_tensor(tag + "_wfs", [128, 512], F32))
    wbp = ess.enter_context(nc.sbuf_tensor(tag + "_wbp", [128, nic * 512], BF16))
    obuf = ess.enter_context(nc.sbuf_tensor(tag + "_obf", [128, 512], BF16))
    ps = ess.enter_context(nc.psum_tensor(tag + "_ps", [128, 512], F32))
    wld = P.sem(); wc = P.sem(); mm = P.sem(); ev = P.sem(); st = P.sem()
    xv = lhsT_sb.ap().rearrange("p (ic t) -> p ic t", ic=nic)
    for tt in range(ntt):
        for obk in range(NOB):
            P.wait("sync", mm, mm.count)
            for ic in range(nic):
                t_w = P.dma("sync", wfs.ap(),
                            w_rows[ic * 128:(ic + 1) * 128, obk * 512:(obk + 1) * 512],
                            wld)
                P.wait("vector", wld, t_w)
                t_c = P.op("vector", lambda e, ic=ic: e.tensor_copy(
                    wbp.ap()[:, ic * 512:(ic + 1) * 512], wfs.ap()), inc=wc)
                P.wait("sync", wc, t_c)
            P.wait("tensor", wc, wc.count)
            P.wait("tensor", ev, ev.count)
            t_m = None
            for ic in range(nic):
                t_m = P.op("tensor", lambda e, ic=ic, tt=tt, obk=obk:
                           e.matmul(ps.ap(), xv[:, ic, tt * 128:(tt + 1) * 128],
                                    wbp.ap()[:, ic * 512:(ic + 1) * 512],
                                    start=(ic == 0), stop=(ic == nic - 1)),
                           inc=mm if ic == nic - 1 else None)
            if per_unit_epilogue is not None:
                per_unit_epilogue(tt, obk, ps, mm, t_m, ev)
            else:
                P.wait("vector", mm, t_m)
                P.wait("vector", st, st.count)
                if bias_bcast is not None:
                    t_e = P.op("vector", lambda e, obk=obk: e.tensor_add(
                        obuf.ap(), ps.ap(),
                        bias_bcast.ap()[:, obk * 512:(obk + 1) * 512]), inc=ev)
                else:
                    t_e = P.op("vector", lambda e: e.tensor_copy(obuf.ap(), ps.ap()),
                               inc=ev)
                P.wait("gpsimd", ev, t_e)
                P.dma("gpsimd", out_dram[tt * 128:(tt + 1) * 128,
                                         obk * 512:(obk + 1) * 512], obuf.ap(), st)
                P.wait("vector", st, st.count)
    P.wait("gpsimd", st, st.count)
    P.barrier()
    ess.close()



def build_l0qkv(nc, io):
    es = contextlib.ExitStack()
    with es:
        P = Prog(nc)
        co = emit_consts(P, nc, es)
        xnT = emit_xnT(P, nc, es, co, io["x0"])
        sinT, cosT = emit_rope_tables(P, nc, es, io, "r0")
        pswap = es.enter_context(nc.sbuf_tensor("pswap_s", [128, 128], F32))
        psld = P.sem()
        t = P.dma("sync", pswap.ap(), io["pswap"], psld)
        P.wait("tensor", psld, t)
        bcol = es.enter_context(nc.sbuf_tensor("bcol", [128, 2 * ND], F32))
        bvb = es.enter_context(nc.sbuf_tensor("bvb", [128, D], F32))
        bld = P.sem(); bsc = P.sem()
        P.dma("gpsimd", bcol.ap()[:, 0:ND], io["bq_c"], bld)
        P.dma("gpsimd", bcol.ap()[:, ND:], io["bk_c"], bld)
        t = P.dma("gpsimd", bvb.ap(), io["bv_row"].broadcast_to([128, D]), bld)
        P.wait("vector", bld, t)
        P.op("vector", lambda e: e.tensor_scalar(bcol.ap()[:, 0:ND], bcol.ap()[:, 0:ND],
                                                 SCALE0, None, op0=ALU.mult), inc=bsc)
        P.barrier()
        emit_proj_featmajor(P, nc, es, co, xnT, io["wq_t"], ND, io["qT0"], "q",
                            bias_col_sb=bcol, bias_off=0, scale=SCALE0,
                            rope=(sinT, cosT), pswap=pswap)
        emit_proj_featmajor(P, nc, es, co, xnT, io["wk_t"], ND, io["kT0"], "k",
                            bias_col_sb=bcol, bias_off=ND, scale=1.0,
                            rope=(sinT, cosT), pswap=pswap)
        emit_proj_tokmajor(P, nc, es, co, xnT, io["wv_r"], ND, io["v0"], "v",
                           bias_bcast=bvb)
        P.finalize(es)
    return nc


def build_l0attn(nc, io, p):
    ranges = own_ranges(p)
    kmax = max(r[1] for r in ranges)
    nkt_all = kmax // 128
    es = contextlib.ExitStack()
    with es:
        P = Prog(nc)
        co = emit_consts(P, nc, es)
        masks = es.enter_context(nc.sbuf_tensor("masks", [128, 4 * 512], BF16))
        mk = P.sem()
        t = None
        for r in range(4):
            P.op("gpsimd", lambda e, r=r: e.memset(masks.ap()[:, r * 512:(r + 1) * 512], 0.0))
            t = P.op("gpsimd", lambda e, r=r: e.affine_select(
                out=masks.ap()[:, r * 512:(r + 1) * 512],
                in_=masks.ap()[:, r * 512:(r + 1) * 512],
                compare_op=ALU.is_gt, fill=1.0, base=r * 128,
                pattern=[[-1, 512]], channel_multiplier=1), inc=mk)
        for e in ("tensor", "vector", "scalar"):
            P.wait(e, mk, t)
        qT = es.enter_context(nc.sbuf_tensor("qT", [128, ND * TC], BF16))
        qld = P.sem()
        t = P.dma("sync", qT.ap().rearrange("p (h t) -> p h t", h=ND),
                  io["qT0"].rearrange("(h p) t -> p h t", p=128), qld)
        P.wait("tensor", qld, t)
        kT = es.enter_context(nc.sbuf_tensor("kT_s", [128, nkt_all * 128], BF16))
        vS = es.enter_context(nc.sbuf_tensor("vS", [128, nkt_all * 128], BF16))
        pbf = es.enter_context(nc.sbuf_tensor("pbf", [128, 512], BF16))
        nscr = es.enter_context(nc.sbuf_tensor("nscr", [128, 512], F32))
        rden = es.enter_context(nc.sbuf_tensor("rden", [1, 512], F32))
        aob = es.enter_context(nc.sbuf_tensor("aob", [128, 512], BF16))
        psS = es.enter_context(nc.psum_tensor("psS", [128, 512], F32))
        psA = es.enter_context(nc.psum_tensor("psA", [128, 512], F32))
        psD = es.enter_context(nc.psum_tensor("psD", [1, 512], F32))
        psB = es.enter_context(nc.psum_tensor("psB", [128, 512], F32))
        kld = P.sem(); mmS = P.sem(); exq = P.sem(); kkt = P.sem()
        hdn = P.sem(); evq = P.sem(); st = P.sem()
        qblocks = []
        for li, (g0, g1) in enumerate(ranges):
            for jj in range((g1 - g0) // 512):
                qblocks.append((li * 1024 + jj * 512, g0 + jj * 512))
        qv = qT.ap().rearrange("p (hh t) -> p hh t", hh=ND)
        vv = vS.ap().rearrange("p (kt d) -> p kt d", kt=nkt_all)
        for h in range(H):
            P.wait("sync", hdn, hdn.count)
            t_k = P.dma("sync", kT.ap(), io["kT"][h * 128:(h + 1) * 128, 0:kmax], kld)
            for k8 in range(0, nkt_all, 8):
                k8e = min(k8 + 8, nkt_all)
                t_k = P.dma("sync", vv[:, k8:k8e, :],
                            io["v"].rearrange("(kt p) d -> p kt d", p=128)[
                                :, k8:k8e, h * 128:(h + 1) * 128], kld)
            P.wait("tensor", kld, t_k)
            for (lq, gq) in qblocks:
                nkt = (gq + 512) // 128
                qrhs = qv[:, h, lq:lq + 512]
                for kt in range(nkt):
                    P.wait("tensor", exq, exq.count)
                    t_s = P.op("tensor", lambda e, kt=kt, qrhs=qrhs:
                               e.matmul(psS.ap(), kT.ap()[:, kt * 128:(kt + 1) * 128],
                                        qrhs, start=True, stop=True), inc=mmS)
                    P.wait("scalar", mmS, t_s)
                    P.wait("scalar", kkt, kkt.count)
                    t_x = P.op("scalar", lambda e: e.activation(
                        pbf.ap(), psS.ap(), AF.Exp, bias=co["negC"].ap()), inc=exq)
                    r = kt * 128 - gq
                    if r >= 0:
                        P.wait("vector", exq, t_x)
                        t_x = P.op("vector", lambda e, r=r: e.tensor_mul(
                            pbf.ap(), pbf.ap(),
                            masks.ap()[:, (r // 128) * 512:(r // 128) * 512 + 512]),
                            inc=exq)
                    P.wait("tensor", exq, t_x)
                    P.op("tensor", lambda e, kt=kt, first=(kt == 0), last=(kt == nkt - 1):
                         e.matmul(psA.ap(), vv[:, kt, :], pbf.ap(),
                                  start=first, stop=last))
                    P.op("tensor", lambda e, first=(kt == 0), last=(kt == nkt - 1):
                         e.matmul(psD.ap(), co["ones_c_b"].ap(), pbf.ap(),
                                  start=first, stop=last), inc=kkt)
                P.wait("vector", kkt, kkt.count)
                P.wait("vector", st, st.count)
                t_r = P.op("vector", lambda e: e.reciprocal(rden.ap(), psD.ap()), inc=evq)
                P.wait("tensor", evq, t_r)
                t_b = P.op("tensor", lambda e: e.matmul(
                    psB.ap(), co["ones_r_f"].ap(), rden.ap(), start=True, stop=True),
                    inc=hdn)
                t_n = P.op("vector", lambda e: e.tensor_copy(nscr.ap(), psA.ap()), inc=evq)
                P.wait("vector", hdn, t_b)
                t_o = P.op("vector", lambda e: e.tensor_mul(aob.ap(), nscr.ap(), psB.ap()),
                           inc=evq)
                P.wait("gpsimd", evq, t_o)
                P.dma("gpsimd", io["aoT0"][h * 128:(h + 1) * 128, lq:lq + 512],
                      aob.ap(), st)
                P.wait("tensor", evq, t_o)
        P.wait("sync", st, st.count)
        P.finalize(es)
    return nc


def build_oproj(nc, io):
    es = contextlib.ExitStack()
    with es:
        P = Prog(nc)
        co = emit_consts(P, nc, es)
        aoT = es.enter_context(nc.sbuf_tensor("aoT_s", [128, ND * TC], BF16))
        qld = P.sem()
        t = P.dma("sync", aoT.ap().rearrange("p (c t) -> p c t", c=ND),
                  io["aoT"].rearrange("(c p) t -> p c t", p=128), qld)
        P.wait("tensor", qld, t)
        bob = es.enter_context(nc.sbuf_tensor("bob", [128, D], F32))
        bobld = P.sem()
        t_b = P.dma("gpsimd", bob.ap(), io["bo_row"].broadcast_to([128, D]), bobld)
        raw = es.enter_context(nc.sbuf_tensor("raw", [128, D], F32))
        xs = es.enter_context(nc.sbuf_tensor("oxs", [128, D], F32))
        scr = es.enter_context(nc.sbuf_tensor("oscr", [128, D], F32))
        cols = es.enter_context(nc.sbuf_tensor("ocols", [128, 6], F32))
        ob2 = es.enter_context(nc.sbuf_tensor("oob", [128, D], F32))
        vs = P.sem(); st = P.sem(); ld2 = P.sem()

        def epi(tt, obk, ps, mm, t_m, ev):
            P.wait("vector", mm, t_m)
            P.wait("vector", bobld, t_b)
            t_e = P.op("vector", lambda e, obk=obk: e.tensor_add(
                raw.ap()[:, obk * 512:(obk + 1) * 512], ps.ap(),
                bob.ap()[:, obk * 512:(obk + 1) * 512]), inc=ev)
            P.wait("tensor", ev, t_e)
            if obk == 3:
                P.wait("sync", vs, vs.count)
                t_x = P.dma("sync", xs.ap(), io["x_in"][tt * 128:(tt + 1) * 128, :], ld2)
                P.wait("scalar", ev, t_e)
                t_n = emit_ln_cols(P, co, raw.ap(), scr.ap(), cols.ap(), vs)
                P.wait("vector", vs, t_n)
                P.op("vector", lambda e: e.tensor_scalar(
                    scr.ap(), raw.ap(), cols.ap()[:, 3:4], cols.ap()[:, 4:5],
                    op0=ALU.mult, op1=ALU.add))
                P.wait("vector", ld2, t_x)
                t_s = P.op("vector", lambda e: e.tensor_add(ob2.ap(), xs.ap(), scr.ap()),
                           inc=vs)
                P.wait("gpsimd", vs, t_s)
                P.dma("gpsimd", io["x_out"][tt * 128:(tt + 1) * 128, :], ob2.ap(), st)
                P.wait("vector", st, st.count)
                P.wait("scalar", st, st.count)

        emit_proj_tokmajor(P, nc, es, co, aoT, io["wo_r"], ND, None, "o",
                           per_unit_epilogue=epi)
        P.wait("sync", st, st.count)
        P.finalize(es)
    return nc


def build_l0ffn(nc, io):
    es = contextlib.ExitStack()
    with es:
        P = Prog(nc)
        co = emit_consts(P, nc, es)
        gsc = es.enter_context(nc.sbuf_tensor("gsc", [128, 2 * NH0], F32))
        dsb = es.enter_context(nc.sbuf_tensor("dsb", [128, D], F32))
        bld = P.sem()
        P.dma("gpsimd", gsc.ap()[:, 0:NH0], io["gate_s_c"], bld)
        P.dma("gpsimd", gsc.ap()[:, NH0:], io["up_s_c"], bld)
        P.dma("gpsimd", dsb.ap(), io["down_s_row"].broadcast_to([128, D]), bld)
        NQ = 4
        QT = TC // NQ
        hdnT = es.enter_context(nc.sbuf_tensor("hdnT", [128, NH0 * QT], BF16))
        rawh = es.enter_context(nc.sbuf_tensor("rawh", [128, 4 * D], F32))
        xbufs = alloc_xnT_bufs(nc, es, "fx", QT // 128)
        wf = es.enter_context(nc.sbuf_tensor("fwf", [128, 2 * ND * 128], F32))
        wb = es.enter_context(nc.sbuf_tensor("fwb", [128, 2 * ND * 128], BF16))
        sil = es.enter_context(nc.sbuf_tensor("fsil", [128, 512], F32))
        ups = es.enter_context(nc.sbuf_tensor("fups", [128, 512], F32))
        psG = es.enter_context(nc.psum_tensor("psG", [128, 512], F32))
        psU = es.enter_context(nc.psum_tensor("psU", [128, 512], F32))
        dwf = es.enter_context(nc.sbuf_tensor("dwf", [128, 512], F32))
        dwb = es.enter_context(nc.sbuf_tensor("dwb", [128, 512], BF16))
        psE = [es.enter_context(nc.psum_tensor(f"psE{i}", [128, 512], F32))
               for i in range(4)]
        pxs = es.enter_context(nc.sbuf_tensor("pxs", [128, D], F32))
        pscr = es.enter_context(nc.sbuf_tensor("pscr", [128, D], F32))
        pcols = es.enter_context(nc.sbuf_tensor("pcols", [128, 6], F32))
        pob = es.enter_context(nc.sbuf_tensor("pob", [128, D], F32))
        P.barrier()
        for q in range(NQ):
            tt0 = q * (QT // 128)
            xnT = emit_xnT(P, nc, es, co, io["x1"], tag=f"fx", ntt=QT // 128,
                           tt0=tt0, bufs=xbufs)
            wld = P.sem(); wc = P.sem(); mm = P.sem(); ev = P.sem()
            xv = xnT.ap().rearrange("p (ic t) -> p ic t", ic=ND)
            for hc in range(NH0):
                P.wait("sync", mm, mm.count)
                P.dma("sync", wf.ap().rearrange("p (q2 ic m) -> p q2 ic m", q2=2, ic=ND, m=128)[:, 0],
                      io["gate_t"][hc].rearrange("ic p m -> p ic m"), wld)
                t_w = P.dma("sync", wf.ap().rearrange("p (q2 ic m) -> p q2 ic m", q2=2, ic=ND, m=128)[:, 1],
                            io["up_t"][hc].rearrange("ic p m -> p ic m"), wld)
                P.wait("vector", wld, t_w)
                P.wait("vector", mm, mm.count)
                t_c = P.op("vector", lambda e: e.tensor_copy(wb.ap(), wf.ap()), inc=wc)
                P.wait("tensor", wc, t_c)
                P.wait("tensor", ev, ev.count)
                for ic in range(ND):
                    P.op("tensor", lambda e, ic=ic:
                         e.matmul(psG.ap(), wb.ap()[:, ic * 128:(ic + 1) * 128],
                                  xv[:, ic, 0:512],
                                  start=(ic == 0), stop=(ic == ND - 1)))
                t_m = None
                for ic in range(ND):
                    t_m = P.op("tensor", lambda e, ic=ic:
                               e.matmul(psU.ap(),
                                        wb.ap()[:, (ND + ic) * 128:(ND + ic + 1) * 128],
                                        xv[:, ic, 0:512],
                                        start=(ic == 0), stop=(ic == ND - 1)),
                               inc=mm if ic == ND - 1 else None)
                P.wait("scalar", mm, t_m)
                P.wait("scalar", ev, max(0, ev.count - 1))
                t_s = P.op("scalar", lambda e, hc=hc: e.activation(
                    sil.ap(), psG.ap(), AF.Silu, scale=gsc.ap()[:, hc:hc + 1]), inc=ev)
                P.wait("vector", ev, t_s)
                P.wait("vector", bld, 48)
                t_u = P.op("vector", lambda e, hc=hc: e.tensor_scalar(
                    ups.ap(), psU.ap(), gsc.ap()[:, NH0 + hc:NH0 + hc + 1],
                    None, op0=ALU.mult), inc=ev)
                t_h = P.op("vector", lambda e, hc=hc: e.tensor_mul(
                    hdnT.ap().rearrange("p (hc t) -> p hc t", hc=NH0)[:, hc, :],
                    sil.ap(), ups.ap()), inc=ev)
                P.wait("tensor", ev, t_h)
            P.barrier()
            wld = P.sem(); wc = P.sem(); mm = P.sem(); ev = P.sem()
            hv = hdnT.ap().rearrange("p (hc t) -> p hc t", hc=NH0)
            for obk in range(4):
                for hc in range(NH0):
                    P.wait("sync", mm, mm.count)
                    t_w = P.dma("sync", dwf.ap(),
                                io["down_r"][hc * 128:(hc + 1) * 128,
                                             obk * 512:(obk + 1) * 512], wld)
                    P.wait("vector", wld, t_w)
                    P.wait("vector", mm, mm.count)
                    t_c = P.op("vector", lambda e: e.tensor_copy(dwb.ap(), dwf.ap()),
                               inc=wc)
                    P.wait("tensor", wc, t_c)
                    if hc == 0:
                        P.wait("tensor", ev, ev.count)
                    t_m = None
                    for tt in range(4):
                        t_m = P.op("tensor", lambda e, tt=tt, hc=hc:
                                   e.matmul(psE[tt].ap(),
                                            hv[:, hc, tt * 128:(tt + 1) * 128],
                                            dwb.ap(), start=(hc == 0),
                                            stop=(hc == NH0 - 1)),
                                   inc=mm if tt == 3 else None)
                P.wait("vector", mm, mm.count)
                for tt in range(4):
                    P.op("vector", lambda e, tt=tt, obk=obk: e.tensor_mul(
                        rawh.ap()[:, tt * D + obk * 512: tt * D + obk * 512 + 512],
                        psE[tt].ap(), dsb.ap()[:, obk * 512:(obk + 1) * 512]),
                        inc=ev)
                P.wait("tensor", ev, ev.count)
            P.barrier()
            vs = P.sem(); st = P.sem(); ld2 = P.sem()
            for i in range(4):
                tt = tt0 + i
                raw_ap = rawh.ap()[:, i * D:(i + 1) * D]
                P.wait("sync", vs, vs.count)
                t_x = P.dma("sync", pxs.ap(), io["x1"][tt * 128:(tt + 1) * 128, :], ld2)
                t_n = emit_ln_cols(P, co, raw_ap, pscr.ap(), pcols.ap(), vs)
                P.wait("vector", vs, t_n)
                P.op("vector", lambda e, raw_ap=raw_ap: e.tensor_scalar(
                    pscr.ap(), raw_ap, pcols.ap()[:, 3:4], pcols.ap()[:, 4:5],
                    op0=ALU.mult, op1=ALU.add))
                P.wait("vector", ld2, t_x)
                t_s = P.op("vector", lambda e: e.tensor_add(pob.ap(), pxs.ap(), pscr.ap()),
                           inc=vs)
                P.wait("gpsimd", vs, t_s)
                P.dma("gpsimd", io["x2"][tt * 128:(tt + 1) * 128, :], pob.ap(), st)
                P.wait("vector", st, st.count)
                P.wait("scalar", st, st.count)
            P.wait("sync", st, st.count)
            P.wait("gpsimd", st, st.count)
            P.barrier()
        P.finalize(es)
    return nc


def build_l1qkv(nc, io):
    es = contextlib.ExitStack()
    with es:
        P = Prog(nc)
        co = emit_consts(P, nc, es)
        xnT = emit_xnT(P, nc, es, co, io["x2"])
        sinT, cosT = emit_rope_tables(P, nc, es, io, "r1")
        pswap = es.enter_context(nc.sbuf_tensor("pswap_s", [128, 128], F32))
        psld = P.sem()
        t = P.dma("sync", pswap.ap(), io["pswap"], psld)
        P.wait("tensor", psld, t)
        emit_proj_featmajor(P, nc, es, co, xnT, io["wq1_t"], ND, io["qT1"], "q1",
                            rope=(sinT, cosT), elu1=True, pswap=pswap)
        emit_proj_featmajor(P, nc, es, co, xnT, io["wk1_t"], ND, io["kT1"], "k1",
                            rope=(sinT, cosT), elu1=True, pswap=pswap)
        emit_proj_tokmajor(P, nc, es, co, xnT, io["wv1_r"], ND, io["v1"], "v1")
        esm = contextlib.ExitStack()
        with esm:
            kTh = esm.enter_context(nc.sbuf_tensor("kTh", [128, 256], BF16))
            kc = esm.enter_context(nc.sbuf_tensor("kc", [128, 2 * 128], BF16))
            vc = esm.enter_context(nc.sbuf_tensor("vc", [128, 2 * 128], BF16))
            mout = esm.enter_context(nc.sbuf_tensor("mout", [128, 128], F32))
            zacc = esm.enter_context(nc.sbuf_tensor("zacc", [128, 128], F32))
            psT = esm.enter_context(nc.psum_tensor("psT", [128, 256], BF16))
            psM = esm.enter_context(nc.psum_tensor("psM", [128, 128], F32))
            psZ = esm.enter_context(nc.psum_tensor("psZ", [128, 1], F32))
            ld = P.sem(); tp = P.sem(); mm = P.sem(); ev = P.sem(); st = P.sem()
            for lc in range(8):
                for h in range(H):
                    P.wait("sync", mm, mm.count)
                    t_l = P.dma("sync", kTh.ap(),
                                io["kT1"][h * 128:(h + 1) * 128, lc * 256:(lc + 1) * 256], ld)
                    t_l = P.dma("sync", vc.ap().rearrange("p (kt d) -> p kt d", kt=2),
                                io["v1"].rearrange("(kt p) d -> p kt d", p=128)[
                                    :, lc * 2:lc * 2 + 2, h * 128:(h + 1) * 128], ld)
                    P.wait("tensor", ld, t_l)
                    P.wait("tensor", ev, ev.count)
                    t_t = None
                    for kt in range(2):
                        t_t = P.op("tensor", lambda e, kt=kt: e.transpose(
                            psT.ap()[:, kt * 128:(kt + 1) * 128],
                            kTh.ap()[:, kt * 128:(kt + 1) * 128],
                            co["ident_b"].ap()), inc=tp if kt == 1 else None)
                    P.wait("vector", tp, t_t)
                    t_c = P.op("vector", lambda e: e.tensor_copy(kc.ap(), psT.ap()), inc=ev)
                    P.wait("tensor", ev, t_c)
                    t_m = None
                    for kt in range(2):
                        P.op("tensor", lambda e, kt=kt: e.matmul(
                            psM.ap(), kc.ap()[:, kt * 128:(kt + 1) * 128],
                            vc.ap()[:, kt * 128:(kt + 1) * 128],
                            start=(kt == 0), stop=(kt == 1)))
                        t_m = P.op("tensor", lambda e, kt=kt: e.matmul(
                            psZ.ap(), kc.ap()[:, kt * 128:(kt + 1) * 128],
                            co["ones_c_b"].ap(),
                            start=(kt == 0), stop=(kt == 1)),
                            inc=mm if kt == 1 else None)
                    P.wait("vector", mm, t_m)
                    P.wait("vector", st, st.count)
                    t_e = P.op("vector", lambda e: e.tensor_copy(mout.ap(), psM.ap()), inc=ev)
                    t_z = P.op("vector", lambda e, lc=lc, h=h: e.tensor_copy(
                        zacc.ap()[:, lc * 16 + h:lc * 16 + h + 1], psZ.ap()), inc=ev)
                    P.wait("gpsimd", ev, t_e)
                    P.dma("gpsimd", io["M_own"][lc, h], mout.ap(), st)
                    P.wait("tensor", ev, t_z)
            P.wait("gpsimd", ev, ev.count)
            P.dma("gpsimd", io["z_cols"], zacc.ap(), st)
            P.wait("sync", st, st.count)
            P.barrier()
        P.finalize(es)
    return nc


def build_l1attn(nc, io, p):
    es = contextlib.ExitStack()
    with es:
        P = Prog(nc)
        co = emit_consts(P, nc, es)
        chunks = own_chunks(p)
        masks = es.enter_context(nc.sbuf_tensor("masks1", [128, 2 * 256], BF16))
        mk = P.sem()
        t = None
        for r in range(2):
            P.op("gpsimd", lambda e, r=r: e.memset(masks.ap()[:, r * 256:(r + 1) * 256], 0.0))
            t = P.op("gpsimd", lambda e, r=r: e.affine_select(
                out=masks.ap()[:, r * 256:(r + 1) * 256],
                in_=masks.ap()[:, r * 256:(r + 1) * 256],
                compare_op=ALU.is_gt, fill=1.0, base=r * 128,
                pattern=[[-1, 256]], channel_multiplier=1), inc=mk)
        for e in ("tensor", "vector", "scalar"):
            P.wait(e, mk, t)
        qT = es.enter_context(nc.sbuf_tensor("qT1s", [128, ND * TC], BF16))
        kT = es.enter_context(nc.sbuf_tensor("kT1s", [128, ND * TC], BF16))
        zc = es.enter_context(nc.sbuf_tensor("zc1", [128, 256], F32))
        qld = P.sem()
        P.dma("sync", qT.ap().rearrange("p (h t) -> p h t", h=ND),
              io["qT1"].rearrange("(h p) t -> p h t", p=128), qld)
        P.dma("sync", kT.ap().rearrange("p (h t) -> p h t", h=ND),
              io["kT1"].rearrange("(h p) t -> p h t", p=128), qld)
        t = P.dma("sync", zc.ap(), io["z_all_cols"], qld)
        for e in ("tensor", "vector"):
            P.wait(e, qld, t)
        S_run = es.enter_context(nc.sbuf_tensor("S_run", [128, 128], F32))
        z_run = es.enter_context(nc.sbuf_tensor("z_run", [128, 1], F32))
        Sb = es.enter_context(nc.sbuf_tensor("Sb", [128, 128], BF16))
        zb = es.enter_context(nc.sbuf_tensor("zb", [128, 1], BF16))
        Mt = es.enter_context(nc.sbuf_tensor("Mt", [128, 128], F32))
        vc = es.enter_context(nc.sbuf_tensor("vc1", [128, 2 * 128], BF16))
        pa = es.enter_context(nc.sbuf_tensor("pa", [128, 2 * 256], BF16))
        nscr = es.enter_context(nc.sbuf_tensor("nscr1", [128, 256], F32))
        rden = es.enter_context(nc.sbuf_tensor("rden1", [1, 256], F32))
        aob = es.enter_context(nc.sbuf_tensor("aob1", [128, 256], BF16))
        psS = es.enter_context(nc.psum_tensor("psS1", [128, 256], F32))
        psN = es.enter_context(nc.psum_tensor("psN1", [128, 256], F32))
        psD = es.enter_context(nc.psum_tensor("psD1", [1, 256], F32))
        psB = es.enter_context(nc.psum_tensor("psB1", [128, 256], F32))
        ld = P.sem(); mm = P.sem(); ev = P.sem(); st = P.sem(); acc = P.sem()
        ldM = P.sem()
        qv = qT.ap().rearrange("p (h t) -> p h t", h=ND)
        kv = kT.ap().rearrange("p (h t) -> p h t", h=ND)
        for h in range(H):
            P.edone("gpsimd", "vector")
            t_z0 = P.op("gpsimd", lambda e: e.memset(S_run.ap(), 0.0))
            t_z0 = P.op("gpsimd", lambda e: e.memset(z_run.ap(), 0.0), inc=acc)
            P.wait("vector", acc, acc.count)
            P.edone("vector", "gpsimd")
            P.edone("tensor", "gpsimd")
            for g in range(16):
                if g in chunks:
                    lc = chunks.index(g)
                    P.wait("vector", mm, mm.count)
                    t_cb = P.op("vector", lambda e: e.tensor_copy(Sb.ap(), S_run.ap()))
                    t_cb = P.op("vector", lambda e: e.tensor_copy(zb.ap(), z_run.ap()),
                                inc=ev)
                    P.wait("sync", mm, mm.count)
                    t_l = P.dma("sync", vc.ap().rearrange("p (kt d) -> p kt d", kt=2),
                                io["v1"].rearrange("(kt p) d -> p kt d", p=128)[
                                    :, lc * 2:lc * 2 + 2, h * 128:(h + 1) * 128], ld)
                    for kt in range(2):
                        P.wait("tensor", ev, ev.count)
                        t_s = P.op("tensor", lambda e, kt=kt, lc=lc: e.matmul(
                            psS.ap(), kv[:, h, lc * 256 + kt * 128:lc * 256 + (kt + 1) * 128],
                            qv[:, h, lc * 256:(lc + 1) * 256], start=True, stop=True),
                            inc=mm)
                        P.wait("vector", mm, t_s)
                        t_p = P.op("vector", lambda e, kt=kt: e.tensor_mul(
                            pa.ap()[:, kt * 256:(kt + 1) * 256], psS.ap(),
                            masks.ap()[:, kt * 256:(kt + 1) * 256]), inc=ev)
                    P.wait("tensor", ev, ev.count)
                    P.wait("tensor", ld, ld.count)
                    for kt in range(2):
                        P.op("tensor", lambda e, kt=kt: e.matmul(
                            psN.ap(), vc.ap()[:, kt * 128:(kt + 1) * 128],
                            pa.ap()[:, kt * 256:(kt + 1) * 256],
                            start=(kt == 0), stop=False))
                    P.op("tensor", lambda e, lc=lc: e.matmul(
                        psN.ap(), Sb.ap(), qv[:, h, lc * 256:(lc + 1) * 256],
                        start=False, stop=True))
                    for kt in range(2):
                        P.op("tensor", lambda e, kt=kt: e.matmul(
                            psD.ap(), co["ones_c_b"].ap(),
                            pa.ap()[:, kt * 256:(kt + 1) * 256],
                            start=(kt == 0), stop=False))
                    t_m = P.op("tensor", lambda e, lc=lc: e.matmul(
                        psD.ap(), zb.ap(), qv[:, h, lc * 256:(lc + 1) * 256],
                        start=False, stop=True), inc=mm)
                    P.wait("vector", mm, t_m)
                    P.wait("vector", st, st.count)
                    P.op("vector", lambda e: e.tensor_scalar(
                        rden.ap(), psD.ap(), 1e-6, None, op0=ALU.max))
                    t_r = P.op("vector", lambda e: e.reciprocal(rden.ap(), rden.ap()),
                               inc=ev)
                    P.wait("tensor", ev, t_r)
                    t_bc = P.op("tensor", lambda e: e.matmul(
                        psB.ap(), co["ones_r_f"].ap(), rden.ap(), start=True, stop=True),
                        inc=mm)
                    t_n = P.op("vector", lambda e: e.tensor_copy(nscr.ap(), psN.ap()),
                               inc=ev)
                    P.wait("vector", mm, t_bc)
                    t_o = P.op("vector", lambda e: e.tensor_mul(
                        aob.ap(), nscr.ap(), psB.ap()), inc=ev)
                    P.wait("gpsimd", ev, t_o)
                    P.dma("gpsimd", io["aoT1"][h * 128:(h + 1) * 128,
                                               lc * 256:(lc + 1) * 256], aob.ap(), st)
                    P.wait("tensor", ev, t_o)
                P.wait("sync", acc, acc.count)
                t_l = P.dma("sync", Mt.ap(), io["M_all"][g, h], ldM)
                P.wait("vector", ldM, t_l)
                P.wait("vector", ev, ev.count)
                t_a = P.op("vector", lambda e: e.tensor_add(S_run.ap(), S_run.ap(), Mt.ap()))
                t_a = P.op("vector", lambda e, g=g, h=h: e.tensor_add(
                    z_run.ap(), z_run.ap(), zc.ap()[:, g * 16 + h:g * 16 + h + 1]),
                    inc=acc)
                P.wait("sync", acc, t_a)
        P.wait("sync", st, st.count)
        P.finalize(es)
    return nc


def build_l1tern(nc, io):
    es = contextlib.ExitStack()
    with es:
        P = Prog(nc)
        co = emit_consts(P, nc, es)
        wf = es.enter_context(nc.sbuf_tensor("twf", [128, ND * 128], F32))
        ab = es.enter_context(nc.sbuf_tensor("tab", [128, ND * 128], F32))
        s1 = es.enter_context(nc.sbuf_tensor("ts1", [128, ND * 128], F32))
        s2 = es.enter_context(nc.sbuf_tensor("ts2", [128, ND * 128], F32))
        obf = es.enter_context(nc.sbuf_tensor("tob", [128, ND * 128], BF16))
        msc = es.enter_context(nc.sbuf_tensor("tmsc", [1, 8], F32))
        thr = es.enter_context(nc.sbuf_tensor("tthr", [128, 4], F32))
        psA = es.enter_context(nc.psum_tensor("psTa", [1, 512], F32))
        psB = es.enter_context(nc.psum_tensor("psTb", [128, 1], F32))
        ld = P.sem(); cst = P.sem(); mm = P.sem(); ev = P.sem(); st = P.sem()
        specs = [("fc", io["w_fc_t"], NH1, True, io["tern_fc_t"]),
                 ("pj", io["w_proj_r"], NH1, False, io["tern_proj_r"])]
        NTOT = H1 * D

        def bcast_scalar(src_ap, dst_col_ap):
            P.edone("tensor", "vector")
            t_bb = P.op("tensor", lambda e: e.matmul(
                psB.ap(), co["ones_r_f"].ap(), src_ap, start=True, stop=True), inc=mm)
            P.wait("vector", mm, t_bb)
            return P.op("vector", lambda e: e.tensor_copy(dst_col_ap, psB.ap()), inc=ev)

        for mi, (tag, wsrc, nb_, tiled, dst) in enumerate(specs):
            nbands = NH1 if tiled else (H1 // 128)
            for ph in range(2):
                for bi in range(nbands):
                    P.wait("sync", mm, mm.count)
                    if tiled:
                        t_w = P.dma("sync", wf.ap().rearrange("p (ic m) -> p ic m", ic=ND),
                                    wsrc[bi].rearrange("ic p m -> p ic m"), ld)
                    else:
                        t_w = P.dma("sync", wf.ap(), wsrc[bi * 128:(bi + 1) * 128, :], ld)
                    P.wait("scalar", ld, t_w)
                    P.wait("scalar", mm, mm.count)
                    if ph == 0:
                        t_a = P.op("scalar", lambda e: e.activation(ab.ap(), wf.ap(), AF.Abs),
                                   inc=cst)
                    else:
                        P.op("scalar", lambda e: e.activation(ab.ap(), wf.ap(), AF.Abs))
                        P.edone("vector", "scalar")
                        t_a = P.op("vector", lambda e, mi=mi: e.tensor_scalar(
                            ab.ap(), ab.ap(), thr.ap()[:, 2 + mi:3 + mi], None,
                            op0=ALU.add), inc=cst)
                    P.wait("tensor", cst, t_a)
                    for j in range(4):
                        t_m = P.op("tensor", lambda e, j=j, first=(bi == 0 and j == 0),
                                   last=(bi == nbands - 1 and j == 3):
                                   e.matmul(psA.ap(), co["ones_c_f"].ap(),
                                            ab.ap()[:, j * 512:(j + 1) * 512],
                                            start=first, stop=last),
                                   inc=mm if j == 3 else None)
                P.wait("vector", mm, mm.count)
                if ph == 0:
                    t_v = P.op("vector", lambda e, mi=mi: e.reduce_sum(
                        msc.ap()[:, 6 + mi:7 + mi], psA.ap(), axis=AXX), inc=ev)
                    t_v = P.op("vector", lambda e, mi=mi: e.tensor_scalar(
                        msc.ap()[:, 6 + mi:7 + mi], msc.ap()[:, 6 + mi:7 + mi],
                        1.0 / NTOT, None, op0=ALU.mult), inc=ev)
                    t_v = P.op("vector", lambda e, mi=mi: e.tensor_scalar(
                        msc.ap()[:, 4:5], msc.ap()[:, 6 + mi:7 + mi], -1.0, None,
                        op0=ALU.mult), inc=ev)
                    t_v = bcast_scalar(msc.ap()[:, 4:5], thr.ap()[:, 2 + mi:3 + mi])
                    P.wait("tensor", ev, t_v)
                    P.wait("scalar", ev, t_v)
                else:
                    t_v = P.op("vector", lambda e, mi=mi: e.reduce_sum(
                        msc.ap()[:, 4:5], psA.ap(), axis=AXX), inc=ev)
                    P.op("vector", lambda e, mi=mi: e.tensor_scalar(
                        msc.ap()[:, 4:5], msc.ap()[:, 4:5], 1.0 / NTOT, None,
                        op0=ALU.mult))
                    P.op("vector", lambda e, mi=mi: e.tensor_add(
                        msc.ap()[:, mi:mi + 1], msc.ap()[:, 4:5],
                        msc.ap()[:, 6 + mi:7 + mi]))
                    t_v = P.op("vector", lambda e, mi=mi: e.tensor_scalar(
                        msc.ap()[:, mi:mi + 1], msc.ap()[:, mi:mi + 1], 1e-5, None,
                        op0=ALU.max), inc=ev)
                    P.op("vector", lambda e, mi=mi: e.tensor_scalar(
                        msc.ap()[:, 4:5], msc.ap()[:, mi:mi + 1], -0.7, None,
                        op0=ALU.mult))
                    t_v = bcast_scalar(msc.ap()[:, 4:5], thr.ap()[:, mi:mi + 1])
                    P.wait("tensor", ev, t_v)
                    P.wait("scalar", ev, t_v)
        P.op("vector", lambda e: e.tensor_mul(msc.ap()[:, 5:6], msc.ap()[:, 0:1],
                                              msc.ap()[:, 0:1]))
        t_v = P.op("vector", lambda e: e.tensor_mul(msc.ap()[:, 5:6], msc.ap()[:, 5:6],
                                                    msc.ap()[:, 1:2]), inc=ev)
        t_v = bcast_scalar(msc.ap()[:, 5:6], s1.ap()[:, 0:1])
        P.wait("gpsimd", ev, t_v)
        P.dma("gpsimd", io["c_col"], s1.ap()[:, 0:1], st)
        P.wait("gpsimd", st, st.count)
        P.barrier()
        for mi, (tag, src, nb_, tiled, dst) in enumerate(specs):
            nbands = NH1 if tiled else (H1 // 128)
            tc_ap = thr.ap()[:, mi:mi + 1]
            for bi in range(nbands):
                P.wait("sync", ev, ev.count)
                if tiled:
                    t_w = P.dma("sync", wf.ap().rearrange("p (ic m) -> p ic m", ic=ND),
                                src[bi].rearrange("ic p m -> p ic m"), ld)
                else:
                    t_w = P.dma("sync", wf.ap(), src[bi * 128:(bi + 1) * 128, :], ld)
                P.wait("scalar", ld, t_w)
                P.wait("scalar", ev, ev.count)
                t_1 = P.op("scalar", lambda e, b=tc_ap: e.activation(
                    s1.ap(), wf.ap(), AF.Relu, bias=b, scale=1.0))
                t_1 = P.op("scalar", lambda e: e.activation(s1.ap(), s1.ap(), AF.Sign))
                t_2 = P.op("scalar", lambda e, b=tc_ap: e.activation(
                    s2.ap(), wf.ap(), AF.Relu, bias=b, scale=-1.0))
                t_2 = P.op("scalar", lambda e: e.activation(s2.ap(), s2.ap(), AF.Sign),
                           inc=cst)
                P.wait("vector", cst, t_2)
                t_o = P.op("vector", lambda e: e.tensor_sub(obf.ap(), s1.ap(), s2.ap()),
                           inc=ev)
                P.wait("gpsimd", ev, t_o)
                if tiled:
                    P.dma("gpsimd", dst[bi].rearrange("ic p m -> p ic m"),
                          obf.ap().rearrange("p (ic m) -> p ic m", ic=ND), st)
                else:
                    P.dma("gpsimd", dst[bi * 128:(bi + 1) * 128, :], obf.ap(), st)
                P.wait("scalar", st, st.count)
        P.wait("sync", st, st.count)
        P.finalize(es)
    return nc


def build_l1ffn(nc, io):
    es = contextlib.ExitStack()
    with es:
        P = Prog(nc)
        co = emit_consts(P, nc, es)
        ccol = es.enter_context(nc.sbuf_tensor("ccol", [128, 1], F32))
        bld = P.sem()
        P.dma("gpsimd", ccol.ap(), io["c_col"], bld)
        NQ = 4
        QT = TC // NQ
        hT = es.enter_context(nc.sbuf_tensor("hT1", [128, NH1 * QT], BF16))
        rawh = es.enter_context(nc.sbuf_tensor("rawh1", [128, 4 * D], F32))
        xbufs = alloc_xnT_bufs(nc, es, "gx", QT // 128)
        wb = es.enter_context(nc.sbuf_tensor("gwb", [128, ND * 128], BF16))
        rel = es.enter_context(nc.sbuf_tensor("grel", [128, 512], F32))
        psH = es.enter_context(nc.psum_tensor("psH", [128, 512], F32))
        dwb = es.enter_context(nc.sbuf_tensor("gdwb", [128, 512], BF16))
        psE = [es.enter_context(nc.psum_tensor(f"psF{i}", [128, 512], F32))
               for i in range(4)]
        pxs = es.enter_context(nc.sbuf_tensor("qxs", [128, D], F32))
        pscr = es.enter_context(nc.sbuf_tensor("qscr", [128, D], F32))
        pcols = es.enter_context(nc.sbuf_tensor("qcols", [128, 6], F32))
        pob = es.enter_context(nc.sbuf_tensor("qob", [128, D], F32))
        P.barrier()
        for q in range(NQ):
            tt0 = q * (QT // 128)
            xnT = emit_xnT(P, nc, es, co, io["x3"], tag="gx", ntt=QT // 128,
                           tt0=tt0, bufs=xbufs)
            wld = P.sem(); mm = P.sem(); ev = P.sem()
            xv = xnT.ap().rearrange("p (ic t) -> p ic t", ic=ND)
            for hc in range(NH1):
                P.wait("sync", mm, mm.count)
                t_w = P.dma("sync", wb.ap().rearrange("p (ic m) -> p ic m", ic=ND),
                            io["tern_fc_t"][hc].rearrange("ic p m -> p ic m"), wld)
                P.wait("tensor", wld, t_w)
                P.wait("tensor", ev, ev.count)
                t_m = None
                for ic in range(ND):
                    t_m = P.op("tensor", lambda e, ic=ic:
                               e.matmul(psH.ap(), wb.ap()[:, ic * 128:(ic + 1) * 128],
                                        xv[:, ic, 0:512],
                                        start=(ic == 0), stop=(ic == ND - 1)),
                               inc=mm if ic == ND - 1 else None)
                P.wait("vector", mm, t_m)
                P.wait("vector", ev, max(0, ev.count - 1))
                t_r = P.op("vector", lambda e: e.tensor_scalar(
                    rel.ap(), psH.ap(), 0.0, None, op0=ALU.max), inc=ev)
                t_h = P.op("vector", lambda e, hc=hc: e.tensor_mul(
                    hT.ap().rearrange("p (hc t) -> p hc t", hc=NH1)[:, hc, :],
                    rel.ap(), rel.ap()), inc=ev)
                P.wait("tensor", ev, t_h)
            P.barrier()
            wld = P.sem(); mm = P.sem(); ev = P.sem()
            hv = hT.ap().rearrange("p (hc t) -> p hc t", hc=NH1)
            for obk in range(4):
                for hc in range(NH1):
                    P.wait("sync", mm, mm.count)
                    t_w = P.dma("sync", dwb.ap(),
                                io["tern_proj_r"][hc * 128:(hc + 1) * 128,
                                                  obk * 512:(obk + 1) * 512], wld)
                    P.wait("tensor", wld, t_w)
                    if hc == 0:
                        P.wait("tensor", ev, ev.count)
                    t_m = None
                    for tt in range(4):
                        t_m = P.op("tensor", lambda e, tt=tt, hc=hc:
                                   e.matmul(psE[tt].ap(),
                                            hv[:, hc, tt * 128:(tt + 1) * 128],
                                            dwb.ap(), start=(hc == 0),
                                            stop=(hc == NH1 - 1)),
                                   inc=mm if tt == 3 else None)
                P.wait("vector", mm, mm.count)
                P.wait("vector", bld, 16)
                for tt in range(4):
                    P.op("vector", lambda e, tt=tt, obk=obk: e.tensor_scalar(
                        rawh.ap()[:, tt * D + obk * 512: tt * D + obk * 512 + 512],
                        psE[tt].ap(), ccol.ap(), None, op0=ALU.mult), inc=ev)
                P.wait("tensor", ev, ev.count)
            P.barrier()
            vs = P.sem(); st = P.sem(); ld2 = P.sem()
            for i in range(4):
                tt = tt0 + i
                raw_ap = rawh.ap()[:, i * D:(i + 1) * D]
                P.wait("sync", vs, vs.count)
                t_x = P.dma("sync", pxs.ap(), io["x3"][tt * 128:(tt + 1) * 128, :], ld2)
                t_n = emit_ln_cols(P, co, raw_ap, pscr.ap(), pcols.ap(), vs)
                P.wait("vector", vs, t_n)
                P.op("vector", lambda e, raw_ap=raw_ap: e.tensor_scalar(
                    pscr.ap(), raw_ap, pcols.ap()[:, 3:4], pcols.ap()[:, 4:5],
                    op0=ALU.mult, op1=ALU.add))
                P.wait("vector", ld2, t_x)
                t_s = P.op("vector", lambda e: e.tensor_add(pob.ap(), pxs.ap(), pscr.ap()),
                           inc=vs)
                P.wait("gpsimd", vs, t_s)
                P.dma("gpsimd", io["x4"][tt * 128:(tt + 1) * 128, :], pob.ap(), st)
                P.wait("vector", st, st.count)
                P.wait("scalar", st, st.count)
            P.wait("sync", st, st.count)
            P.wait("gpsimd", st, st.count)
            P.barrier()
        P.finalize(es)
    return nc



def _shard_rows(x_flat, p):
    return np.concatenate([x_flat[g0:g1] for g0, g1 in own_ranges(p)], axis=0)


def _unshard_rows(own, p, out, scale=1):
    i = 0
    for g0, g1 in own_ranges(p):
        n = (g1 - g0) * scale
        out[g0 * scale:g1 * scale] = own[i:i + n]
        i += n


def _declare(nc, specs):
    io = {}
    for name, shape, dt, kind in specs:
        io[name] = nc.dram_tensor(name, list(shape), dt, kind=kind).ap()
    return io


def _launch(build_fn, specs, per_core_inputs, out_names):
    nc = bass.Bass("TRN2", target_bir_lowering=False)
    io = _declare(nc, specs)
    build_fn(nc, io)
    res = run_bass_kernel_spmd(nc, per_core_inputs, core_ids=CORES)
    return [{k: r[k] for k in out_names} for r in res.results]


def _launch_per_p(build_fn, specs, per_core_inputs, out_names):
    outs = [None] * 8
    for p in (0, 1):
        nc = bass.Bass("TRN2", target_bir_lowering=False)
        io = _declare(nc, specs(p) if callable(specs) else specs)
        build_fn(nc, io, p)
        cores = [2 * bb + p for bb in range(B)]
        res = run_bass_kernel_spmd(nc, [per_core_inputs[c] for c in cores],
                                   core_ids=cores)
        for i, c in enumerate(cores):
            outs[c] = {k: res.results[i][k] for k in out_names}
    return outs


BF = ml_dtypes.bfloat16


def kernel(**inputs):
    f32 = np.float32
    g = {k: np.asarray(v) for k, v in inputs.items()}
    x = np.ascontiguousarray(g["L0_attn_x"].astype(f32, copy=False))

    wq_t = tile_lhsT(np.ascontiguousarray(g["L0_attn_Wq"]), ND, ND)
    wk_t = tile_lhsT(np.ascontiguousarray(g["L0_attn_Wk"]), ND, ND)
    wv_r = np.ascontiguousarray(g["L0_attn_Wv"].T)
    wo_r = np.ascontiguousarray(g["L0_attn_Wo"].T)
    bq_c = col_layout(g["L0_attn_bq"]); bk_c = col_layout(g["L0_attn_bk"])
    bv_row = np.ascontiguousarray(g["L0_attn_bv"].reshape(1, D))
    bo_row = np.ascontiguousarray(g["L0_attn_bo"].reshape(1, D))
    gate_t = tile_lhsT(np.ascontiguousarray(g["L0_ffn_gate_w"]), NH0, ND)
    up_t = tile_lhsT(np.ascontiguousarray(g["L0_ffn_up_w"]), NH0, ND)
    down_r = np.ascontiguousarray(g["L0_ffn_down_w"].T)
    gate_s_c = col_layout(g["L0_ffn_gate_s"]); up_s_c = col_layout(g["L0_ffn_up_s"])
    down_s_row = np.ascontiguousarray(g["L0_ffn_down_s"].reshape(1, D))
    wqkv = np.ascontiguousarray(g["L1_attn_W_qkv"])
    wq1_t = tile_lhsT(np.ascontiguousarray(wqkv[0:D]), ND, ND)
    wk1_t = tile_lhsT(np.ascontiguousarray(wqkv[D:2 * D]), ND, ND)
    wv1_r = np.ascontiguousarray(wqkv[2 * D:3 * D].T)
    wout_r = np.ascontiguousarray(g["L1_attn_W_out"].T)
    w_fc_t = tile_lhsT(np.ascontiguousarray(g["L1_ffn_w_fc"]), NH1, ND)
    w_proj_r = np.ascontiguousarray(g["L1_ffn_w_proj"].T)
    zeros_row = np.zeros((1, D), f32)

    r0 = make_rope_tables(0)
    r1 = make_rope_tables(1)
    pswap_m = np.zeros((128, 128), f32)
    pswap_m[np.arange(128), np.arange(128) ^ 1] = 1.0

    def rope_slices(layer_tables, p):
        sh, sl, ch, cl = layer_tables
        cols = np.concatenate([np.arange(g0, g1) for g0, g1 in own_ranges(p)])
        return (np.ascontiguousarray(sh[:, cols]), np.ascontiguousarray(sl[:, cols]),
                np.ascontiguousarray(ch[:, cols]), np.ascontiguousarray(cl[:, cols]))

    x_own = {}
    for c in CORES:
        bb, p = c // 2, c % 2
        x_own[c] = np.ascontiguousarray(_shard_rows(x[bb], p))

    specs = [("x0", (TC, D), F32, "ExternalInput"),
             ("wq_t", wq_t.shape, F32, "ExternalInput"),
             ("wk_t", wk_t.shape, F32, "ExternalInput"),
             ("wv_r", (D, D), F32, "ExternalInput"),
             ("bq_c", (128, ND), F32, "ExternalInput"),
             ("bk_c", (128, ND), F32, "ExternalInput"),
             ("bv_row", (1, D), F32, "ExternalInput"),
             ("r0sh", (128, TC), F32, "ExternalInput"),
             ("r0sl", (128, TC), F32, "ExternalInput"),
             ("r0ch", (128, TC), F32, "ExternalInput"),
             ("r0cl", (128, TC), F32, "ExternalInput"),
             ("pswap", (128, 128), F32, "ExternalInput"),
             ("qT0", (D, TC), BF16, "ExternalOutput"),
             ("kT0", (D, TC), BF16, "ExternalOutput"),
             ("v0", (TC, D), BF16, "ExternalOutput")]
    ins = []
    for c in CORES:
        bb, p = c // 2, c % 2
        sh, sl, ch, cl = rope_slices(r0, p)
        ins.append(dict(x0=x_own[c], wq_t=wq_t, wk_t=wk_t, wv_r=wv_r, bq_c=bq_c,
                        bk_c=bk_c, bv_row=bv_row, r0sh=sh, r0sl=sl, r0ch=ch, r0cl=cl,
                        pswap=pswap_m))
    o1 = _launch(build_l0qkv, specs, ins, ["qT0", "kT0", "v0"])

    kT_full = {}; v_full = {}
    for bb in range(B):
        kf = np.zeros((D, S), BF); vf = np.zeros((S, D), BF)
        for p in (0, 1):
            c = 2 * bb + p
            i = 0
            for g0, g1 in own_ranges(p):
                n = g1 - g0
                kf[:, g0:g1] = o1[c]["kT0"][:, i:i + n]
                vf[g0:g1, :] = o1[c]["v0"][i:i + n, :]
                i += n
        kT_full[bb] = kf; v_full[bb] = vf

    def specs2(p):
        kmax = max(r[1] for r in own_ranges(p))
        return [("qT0", (D, TC), BF16, "ExternalInput"),
                ("kT", (D, kmax), BF16, "ExternalInput"),
                ("v", (kmax, D), BF16, "ExternalInput"),
                ("aoT0", (D, TC), BF16, "ExternalOutput")]
    ins2 = [None] * 8
    for c in CORES:
        bb, p = c // 2, c % 2
        kmax = max(r[1] for r in own_ranges(p))
        ins2[c] = dict(qT0=o1[c]["qT0"],
                       kT=np.ascontiguousarray(kT_full[bb][:, :kmax]),
                       v=np.ascontiguousarray(v_full[bb][:kmax, :]))
    o2 = _launch_per_p(build_l0attn, specs2, ins2, ["aoT0"])

    specs3 = [("aoT", (D, TC), BF16, "ExternalInput"),
              ("wo_r", (D, D), F32, "ExternalInput"),
              ("bo_row", (1, D), F32, "ExternalInput"),
              ("x_in", (TC, D), F32, "ExternalInput"),
              ("x_out", (TC, D), F32, "ExternalOutput")]
    ins3 = [dict(aoT=o2[c]["aoT0"], wo_r=wo_r, bo_row=bo_row, x_in=x_own[c])
            for c in CORES]
    o3 = _launch(build_oproj, specs3, ins3, ["x_out"])
    x1 = {c: o3[c]["x_out"] for c in CORES}

    specs4 = [("x1", (TC, D), F32, "ExternalInput"),
              ("gate_t", gate_t.shape, F32, "ExternalInput"),
              ("up_t", up_t.shape, F32, "ExternalInput"),
              ("down_r", (INTER0, D), F32, "ExternalInput"),
              ("gate_s_c", (128, NH0), F32, "ExternalInput"),
              ("up_s_c", (128, NH0), F32, "ExternalInput"),
              ("down_s_row", (1, D), F32, "ExternalInput"),
              ("x2", (TC, D), F32, "ExternalOutput")]
    ins4 = [dict(x1=x1[c], gate_t=gate_t, up_t=up_t, down_r=down_r,
                 gate_s_c=gate_s_c, up_s_c=up_s_c, down_s_row=down_s_row)
            for c in CORES]
    o4 = _launch(build_l0ffn, specs4, ins4, ["x2"])
    x2 = {c: o4[c]["x2"] for c in CORES}

    specs5 = [("x2", (TC, D), F32, "ExternalInput"),
              ("wq1_t", wq1_t.shape, F32, "ExternalInput"),
              ("wk1_t", wk1_t.shape, F32, "ExternalInput"),
              ("wv1_r", (D, D), F32, "ExternalInput"),
              ("r1sh", (128, TC), F32, "ExternalInput"),
              ("r1sl", (128, TC), F32, "ExternalInput"),
              ("r1ch", (128, TC), F32, "ExternalInput"),
              ("r1cl", (128, TC), F32, "ExternalInput"),
              ("pswap", (128, 128), F32, "ExternalInput"),
              ("qT1", (D, TC), BF16, "ExternalOutput"),
              ("kT1", (D, TC), BF16, "ExternalOutput"),
              ("v1", (TC, D), BF16, "ExternalOutput"),
              ("M_own", (8, H, 128, 128), F32, "ExternalOutput"),
              ("z_cols", (128, 128), F32, "ExternalOutput")]
    ins5 = []
    for c in CORES:
        bb, p = c // 2, c % 2
        sh, sl, ch, cl = rope_slices(r1, p)
        ins5.append(dict(x2=x2[c], wq1_t=wq1_t, wk1_t=wk1_t, wv1_r=wv1_r,
                         r1sh=sh, r1sl=sl, r1ch=ch, r1cl=cl, pswap=pswap_m))
    o5 = _launch(build_l1qkv, specs5, ins5, ["qT1", "kT1", "v1", "M_own", "z_cols"])

    M_all = {}; z_all = {}
    for bb in range(B):
        Ma = np.zeros((16, H, 128, 128), f32)
        za = np.zeros((128, 16 * H), f32)
        for p in (0, 1):
            c = 2 * bb + p
            for lc, gc in enumerate(own_chunks(p)):
                Ma[gc] = o5[c]["M_own"][lc]
                za[:, gc * 16:(gc + 1) * 16] = o5[c]["z_cols"][:, lc * 16:(lc + 1) * 16]
        M_all[bb] = Ma; z_all[bb] = za

    specs6 = [("qT1", (D, TC), BF16, "ExternalInput"),
              ("kT1", (D, TC), BF16, "ExternalInput"),
              ("v1", (TC, D), BF16, "ExternalInput"),
              ("M_all", (16, H, 128, 128), F32, "ExternalInput"),
              ("z_all_cols", (128, 16 * H), F32, "ExternalInput"),
              ("aoT1", (D, TC), BF16, "ExternalOutput")]
    ins6 = [None] * 8
    for c in CORES:
        bb = c // 2
        ins6[c] = dict(qT1=o5[c]["qT1"], kT1=o5[c]["kT1"], v1=o5[c]["v1"],
                       M_all=M_all[bb], z_all_cols=z_all[bb])
    o6 = _launch_per_p(build_l1attn, specs6, ins6, ["aoT1"])

    ins7 = [dict(aoT=o6[c]["aoT1"], wo_r=wout_r, bo_row=zeros_row, x_in=x2[c])
            for c in CORES]
    o7 = _launch(build_oproj, specs3, ins7, ["x_out"])
    x3 = {c: o7[c]["x_out"] for c in CORES}

    specs8 = [("w_fc_t", w_fc_t.shape, F32, "ExternalInput"),
              ("w_proj_r", (H1, D), F32, "ExternalInput"),
              ("tern_fc_t", w_fc_t.shape, BF16, "ExternalOutput"),
              ("tern_proj_r", (H1, D), BF16, "ExternalOutput"),
              ("c_col", (128, 1), F32, "ExternalOutput")]
    ins8 = [dict(w_fc_t=w_fc_t, w_proj_r=w_proj_r) for c in CORES]
    o8 = _launch(build_l1tern, specs8, ins8, ["tern_fc_t", "tern_proj_r", "c_col"])

    specs9 = [("x3", (TC, D), F32, "ExternalInput"),
              ("tern_fc_t", w_fc_t.shape, BF16, "ExternalInput"),
              ("tern_proj_r", (H1, D), BF16, "ExternalInput"),
              ("c_col", (128, 1), F32, "ExternalInput"),
              ("x4", (TC, D), F32, "ExternalOutput")]
    ins9 = [dict(x3=x3[c], tern_fc_t=o8[c]["tern_fc_t"],
                 tern_proj_r=o8[c]["tern_proj_r"], c_col=o8[c]["c_col"])
            for c in CORES]
    o9 = _launch(build_l1ffn, specs9, ins9, ["x4"])

    out = np.zeros((B, S, D), f32)
    for c in CORES:
        bb, p = c // 2, c % 2
        i = 0
        for g0, g1 in own_ranges(p):
            n = g1 - g0
            out[bb, g0:g1] = o9[c]["x4"][i:i + n]
            i += n
    return out



def f_ranges(p):
    return [(p * 2048, (p + 1) * 2048)]


def alloc_arena(nc, es):
    B = {}
    T = lambda n, s, d: es.enter_context(nc.sbuf_tensor(n, s, d))
    PT = lambda n, s, d: es.enter_context(nc.psum_tensor(n, s, d))
    B["xnTbig"] = T("A_xnT", [128, ND * 2048], BF16)
    B["hdnT"] = T("A_hdnT", [128, NH0 * 256], BF16)
    B["rawh"] = T("A_rawh", [128, 2 * D], F32)
    B["wfbig"] = T("A_wf", [128, 4096], F32)
    B["wbbig"] = T("A_wb", [128, 8192], BF16)
    B["sfa"] = T("A_sfa", [128, D], F32)
    B["sfb"] = T("A_sfb", [128, D], F32)
    B["sba"] = T("A_sba", [128, D], BF16)
    B["at1"] = T("A_at1", [128, 4096], BF16)
    B["at2"] = T("A_at2", [128, 4096], BF16)
    B["at3"] = T("A_at3", [128, 4096], BF16)
    B["aoTs"] = B["at1"]
    B["brow"] = T("A_brow", [128, D], F32)
    B["cols"] = T("A_cols", [128, 8], F32)
    B["masks0"] = T("A_m0", [128, 4 * 512], BF16)
    B["masks1"] = T("A_m1", [128, 2 * 256], BF16)
    B["pbf"] = T("A_pbf", [128, 512], BF16)
    B["nscr"] = T("A_nscr", [128, 512], F32)
    B["aob"] = T("A_aob", [128, 512], BF16)
    B["rden"] = T("A_rden", [1, 512], F32)
    B["S_run"] = T("A_Srun", [128, 128], F32)
    B["z_run"] = T("A_zrun", [128, 1], F32)
    B["Sb"] = T("A_Sb", [128, 128], BF16)
    B["zb"] = T("A_zb", [128, 1], BF16)
    B["kcb"] = T("A_kcb", [128, 256], BF16)
    B["sil"] = T("A_sil", [128, 512], F32)
    B["ups"] = T("A_ups", [128, 512], F32)
    B["dwb"] = T("A_dwb", [128, 512], BF16)
    B["gsc"] = T("A_gsc", [128, 2 * NH0], F32)
    B["bcol"] = T("A_bcol", [128, 2 * ND], F32)
    B["ccol"] = T("A_ccol", [128, 1], F32)
    B["pswap"] = T("A_pswap", [128, 128], F32)
    B["msc"] = T("A_msc", [1, 8], F32)
    B["thr"] = T("A_thr", [128, 4], F32)
    B["qr"] = B["sil"]
    B["qs2"] = B["ups"]
    B["m1s"] = B["nscr"]
    B["obb"] = T("A_obb", [128, 512], BF16)
    B["dyn"] = T("A_dyn", [1, 64], mybir.dt.int32)
    B["ps0"] = PT("P_0", [128, 512], F32)
    B["ps1"] = PT("P_1", [128, 512], F32)
    B["ps2"] = PT("P_2", [128, 512], F32)
    B["ps3"] = PT("P_3", [128, 512], F32)
    B["ps4"] = PT("P_4", [128, 512], F32)
    B["ps5"] = PT("P_5", [128, 512], F32)
    B["psRow"] = PT("P_r", [1, 512], F32)
    B["psT"] = PT("P_t", [128, 512], BF16)
    return B


def f_xnT_bufs(B, ntt):
    return dict(xnT=B["xnTbig"], xs=B["sfa"], xb=B["sba"], scr=B["sfb"],
                cols=B["cols"], pt=B["psT"])


def emit_xnT_f(P, nc, es, co, B, x_dram, ntt, tt0=0):
    TT = ntt * 128
    xnT_ap = B["xnTbig"].ap()[:, 0:ND * TT]
    xs, xb, scr, cols, pt = B["sfa"], B["sba"], B["sfb"], B["cols"], B["psT"]
    ld = P.nsem("xld"); vs = P.nsem("xvs"); tp = P.nsem("xtp"); ev = P.nsem("xev")
    for i in range(ntt):
        tt = tt0 + i
        P.wait("sync", tp, tp.count)
        t_ld = P.dma("sync", xs.ap(), x_dram[tt * 128:(tt + 1) * 128, :], ld)
        P.wait("scalar", ld, t_ld)
        P.wait("vector", ld, t_ld)
        t_n = emit_ln_cols(P, co, xs.ap(), scr.ap(), cols.ap(), vs)
        P.wait("vector", vs, t_n)
        t_c = P.op("vector", lambda e: e.tensor_scalar(
            xb.ap(), xs.ap(), cols.ap()[:, 3:4], cols.ap()[:, 4:5],
            op0=ALU.mult, op1=ALU.add), inc=vs)
        for g in range(4):
            P.wait("tensor", vs, t_c)
            P.wait("tensor", ev, ev.count)
            tl = None
            for f in range(4):
                ic = g * 4 + f
                tl = P.op("tensor", lambda e, f=f, ic=ic:
                          e.transpose(pt.ap()[:, f * 128:(f + 1) * 128],
                                      xb.ap()[:, ic * 128:(ic + 1) * 128],
                                      co["ident_b"].ap()),
                          inc=tp if f == 3 else None)
            P.wait("scalar", tp, tl)

            def ev_fn(e, g=g, i=i, xnT_ap=xnT_ap, TT=TT):
                dst = xnT_ap.rearrange("p (ic t) -> p ic t", ic=ND)[
                    :, g * 4:(g + 1) * 4, i * 128:(i + 1) * 128]
                return e.activation(dst, pt.ap().rearrange("p (f t) -> p f t", f=4),
                                    AF.Identity)
            P.op("scalar", ev_fn, inc=ev)
    P.barrier()
    return xnT_ap


def emit_rope_f(P, nc, es, B, io, pref):
    sinT = B["rawh"].ap()[:, 0:TC]
    cosT = B["rawh"].ap()[:, TC:2 * TC]
    th = B["wfbig"]
    ld = P.nsem("rld"); ok = P.nsem("rok")
    for which, dst in (("s", sinT), ("c", cosT)):
        P.dma("gpsimd", th.ap()[:, 0:TC], io[pref + which + "h"], ld)
        t1 = P.dma("gpsimd", th.ap()[:, TC:], io[pref + which + "l"], ld)
        P.wait("vector", ld, t1)
        t2 = P.op("vector", lambda e: e.tensor_add(
            th.ap()[:, 0:TC], th.ap()[:, 0:TC], th.ap()[:, TC:]), inc=ok)
        P.wait("scalar", ok, t2)
        t3 = P.op("scalar", lambda e, d=dst: e.activation(d, th.ap()[:, 0:TC], AF.Sin),
                  inc=ok)
        P.wait("gpsimd", ok, t3)
    P.barrier()
    return sinT, cosT


def emit_projF_f(P, nc, es, co, B, xnT_ap, w_tiled, noc, out_dram, rope=None,
                 bias_off=None, scale=1.0, elu1=False):
    NB = TC // 512
    qr, qs, m1, ob = B["qr"], B["qs2"], B["m1s"], B["obb"]
    psQ = B["ps1"]
    wld = P.nsem("pwld"); wc = P.nsem("pwc"); mm = P.nsem("pmm")
    ev = P.nsem("pev"); st = P.nsem("pst")
    xv = xnT_ap.rearrange("p (ic t) -> p ic t", ic=ND)
    mm_hist = []
    ev_hist = []
    for oc in range(noc):
        wpar = oc % 2
        wf = B["wfbig"].ap()[:, wpar * 2048:wpar * 2048 + ND * 128]
        wb = B["wbbig"].ap()[:, wpar * 2048:wpar * 2048 + ND * 128]
        if len(mm_hist) >= 2:
            P.wait("sync", mm, mm_hist[-2])
        t_w = P.dma("sync", wf.rearrange("p (ic m) -> p ic m", ic=ND),
                    w_tiled[oc].rearrange("ic p m -> p ic m"), wld)
        P.wait("vector", wld, t_w)
        if len(mm_hist) >= 2:
            P.wait("vector", mm, mm_hist[-2])
        t_c = P.op("vector", lambda e, wb=wb, wf=wf: e.tensor_copy(wb, wf), inc=wc)
        for nb in range(NB):
            u = oc * NB + nb
            ps = B["ps0"] if u % 2 == 0 else B["ps5"]
            P.wait("tensor", wc, t_c)
            if len(ev_hist) >= 2:
                P.wait("tensor", ev, ev_hist[-2])
            t_m = None
            for ic in range(ND):
                t_m = P.op("tensor", lambda e, ic=ic, nb=nb, ps=ps, wb=wb:
                           e.matmul(ps.ap(), wb[:, ic * 128:(ic + 1) * 128],
                                    xv[:, ic, nb * 512:(nb + 1) * 512],
                                    start=(ic == 0), stop=(ic == ND - 1)),
                           inc=mm if ic == ND - 1 else None)
            mm_hist.append(t_m)
            P.wait("scalar", mm, t_m)
            P.wait("scalar", st, st.count)
            if bias_off is not None:
                bc = B["bcol"].ap()[:, bias_off + oc:bias_off + oc + 1]
                t_e = P.op("scalar", lambda e, bc=bc, ps=ps: e.activation(
                    qr.ap(), ps.ap(), AF.Identity, bias=bc, scale=scale), inc=ev)
            else:
                t_e = P.op("scalar", lambda e, ps=ps: e.activation(
                    qr.ap(), ps.ap(), AF.Identity, scale=scale), inc=ev)
            cur = qr
            P.wait("vector", ev, t_e)
            P.wait("vector", st, st.count)
            if rope is not None:
                sinT, cosT = rope
                sv = sinT[:, nb * 512:(nb + 1) * 512]
                cv = cosT[:, nb * 512:(nb + 1) * 512]
                P.wait("tensor", ev, t_e)
                t_pm = P.op("tensor", lambda e: e.matmul(
                    psQ.ap(), B["pswap"].ap(), qr.ap(), start=True, stop=True), inc=mm)
                P.op("vector", lambda e, cv=cv: e.tensor_mul(m1.ap(), qr.ap(), cv))
                P.wait("vector", mm, t_pm)
                P.op("vector", lambda e, sv=sv: e.tensor_mul(qs.ap(), psQ.ap(), sv))
                t_e = P.op("vector", lambda e: e.tensor_add(m1.ap(), m1.ap(), qs.ap()),
                           inc=ev)
                cur = m1
            if elu1:
                t_e = P.op("vector", lambda e, c=cur: e.tensor_scalar(
                    qs.ap(), c.ap(), 0.0, None, op0=ALU.min), inc=ev)
                P.wait("scalar", ev, t_e)
                t_e = P.op("scalar", lambda e: e.activation(qs.ap(), qs.ap(), AF.Exp),
                           inc=ev)
                P.wait("vector", ev, t_e)
                P.op("vector", lambda e, c=cur: e.tensor_scalar(
                    c.ap(), c.ap(), 0.0, None, op0=ALU.max))
                t_e = P.op("vector", lambda e, c=cur: e.tensor_add(
                    c.ap(), c.ap(), qs.ap()), inc=ev)
            P.wait("vector", ev, t_e)
            t_b = P.op("vector", lambda e, c=cur: e.tensor_copy(ob.ap(), c.ap()), inc=ev)
            ev_hist.append(t_b)
            P.wait("gpsimd", ev, t_b)
            P.dma("gpsimd", out_dram[oc * 128:(oc + 1) * 128, nb * 512:(nb + 1) * 512],
                  ob.ap(), st)
    P.wait("gpsimd", st, st.count)
    P.barrier()


def emit_projT_f(P, nc, es, co, B, lhsT_ap, nic, w_rows, out_dram,
                 bias_bcast=None, nof=D, ntok=TC, per_unit_epilogue=None,
                 lhsT_loader=None, wsrc_bf16=False):
    ntt = ntok // 128
    NOB = nof // 512
    wfs = B["wfbig"].ap()[:, 2048:2560]
    wbp = B["wbbig"].ap()[:, 0:nic * 512]
    obuf = B["obb"]
    ps = B["ps0"]
    wld = P.nsem("twld"); wc = P.nsem("twc"); mm = P.nsem("tmm")
    ev = P.nsem("tev"); st = P.nsem("tst")
    for tt in range(ntt):
        if lhsT_loader is not None:
            lv, lsem, lcnt = lhsT_loader(tt)
        else:
            lv = lhsT_ap.rearrange("p (ic t) -> p ic t", ic=nic)
        for obk in range(NOB):
            P.wait("sync", mm, mm.count)
            for ic in range(nic):
                if wsrc_bf16:
                    t_w = P.dma("sync", wbp[:, ic * 512:(ic + 1) * 512],
                                w_rows[ic * 128:(ic + 1) * 128,
                                       obk * 512:(obk + 1) * 512], wld)
                    t_c = t_w
                else:
                    t_w = P.dma("sync", wfs,
                                w_rows[ic * 128:(ic + 1) * 128,
                                       obk * 512:(obk + 1) * 512], wld)
                    P.wait("vector", wld, t_w)
                    t_c = P.op("vector", lambda e, ic=ic: e.tensor_copy(
                        wbp[:, ic * 512:(ic + 1) * 512], wfs), inc=wc)
                    P.wait("sync", wc, t_c)
            if wsrc_bf16:
                P.wait("tensor", wld, t_w)
            else:
                P.wait("tensor", wc, wc.count)
            P.wait("tensor", ev, ev.count)
            if lhsT_loader is not None:
                P.wait("tensor", lsem, lcnt)
            t_m = None
            for ic in range(nic):
                if lhsT_loader is not None:
                    lt = lv[:, ic * 128:(ic + 1) * 128]
                else:
                    lt = lv[:, ic, tt * 128:(tt + 1) * 128]
                t_m = P.op("tensor", lambda e, lt=lt, ic=ic:
                           e.matmul(ps.ap(), lt, wbp[:, ic * 512:(ic + 1) * 512],
                                    start=(ic == 0), stop=(ic == nic - 1)),
                           inc=mm if ic == nic - 1 else None)
            if per_unit_epilogue is not None:
                per_unit_epilogue(tt, obk, ps, mm, t_m, ev)
            else:
                P.wait("vector", mm, t_m)
                P.wait("vector", st, st.count)
                if bias_bcast is not None:
                    t_e = P.op("vector", lambda e, obk=obk: e.tensor_add(
                        obuf.ap(), ps.ap(), bias_bcast[:, obk * 512:(obk + 1) * 512]),
                        inc=ev)
                else:
                    t_e = P.op("vector", lambda e: e.tensor_copy(obuf.ap(), ps.ap()),
                               inc=ev)
                P.wait("gpsimd", ev, t_e)
                P.dma("gpsimd", out_dram[tt * 128:(tt + 1) * 128,
                                         obk * 512:(obk + 1) * 512], obuf.ap(), st)
                P.wait("vector", st, st.count)
    P.wait("gpsimd", st, st.count)
    P.barrier()


def emit_attn0_f(P, nc, es, co, B, loc, aoin_dram):
    ald = P.nsem("ald"); mmS = P.nsem("amS"); exq = P.nsem("aex")
    kkt = P.nsem("akt"); hdn = P.nsem("ahd"); evq = P.nsem("aev")
    st = P.nsem("ast")
    qloc, kloc, vloc = loc
    at1, at2, at3 = B["at1"], B["at2"], B["at3"]
    psS, psA, psD, psB = B["ps0"], B["ps1"], B["psRow"], B["ps2"]
    pbf, nscr, rden, aob = B["pbf"], B["nscr"], B["rden"], B["aob"]
    vv = at3.ap().rearrange("p (kt d) -> p kt d", kt=32)
    for h in range(8):
        P.wait("sync", hdn, hdn.count)
        P.dma("sync", at1.ap(), qloc[h * 128:(h + 1) * 128, :], ald)
        P.dma("sync", at2.ap(), kloc[h * 128:(h + 1) * 128, :], ald)
        for k8 in range(0, 32, 8):
            t_l = P.dma("sync", vv[:, k8:k8 + 8, :],
                        vloc[:, h * 128:(h + 1) * 128].rearrange(
                            "(kt p) d -> p kt d", p=128)[:, k8:k8 + 8, :], ald)
        P.wait("tensor", ald, t_l)
        for qb in range(8):
            gq = qb * 512
            nkt = 4 * qb + 4
            qrhs = at1.ap()[:, gq:gq + 512]
            exq_hist = []
            kkt_hist = []
            for kt in range(nkt):
                par = kt % 2
                psSp = psS if par == 0 else B["ps3"]
                pbfp = pbf.ap() if par == 0 else B["sba"].ap()[:, 0:512]
                if len(exq_hist) >= 2:
                    P.wait("tensor", exq, exq_hist[-2])
                t_s = P.op("tensor", lambda e, kt=kt, qrhs=qrhs, psSp=psSp:
                           e.matmul(psSp.ap(), at2.ap()[:, kt * 128:(kt + 1) * 128],
                                    qrhs, start=True, stop=True), inc=mmS)
                P.wait("scalar", mmS, t_s)
                if len(kkt_hist) >= 2:
                    P.wait("scalar", kkt, kkt_hist[-2])
                t_x = P.op("scalar", lambda e, psSp=psSp, pbfp=pbfp: e.activation(
                    pbfp, psSp.ap(), AF.Exp, bias=co["negC"].ap()), inc=exq)
                r_off = kt * 128 - gq
                if r_off >= 0:
                    P.wait("vector", exq, t_x)
                    t_x = P.op("vector", lambda e, r_off=r_off, pbfp=pbfp: e.tensor_mul(
                        pbfp, pbfp,
                        B["masks0"].ap()[:, (r_off // 128) * 512:(r_off // 128) * 512 + 512]),
                        inc=exq)
                exq_hist.append(t_x)
                P.wait("tensor", exq, t_x)
                P.op("tensor", lambda e, kt=kt, first=(kt == 0), last=(kt == nkt - 1),
                     pbfp=pbfp: e.matmul(psA.ap(), vv[:, kt, :], pbfp,
                                         start=first, stop=last))
                t_k = P.op("tensor", lambda e, first=(kt == 0), last=(kt == nkt - 1),
                           pbfp=pbfp: e.matmul(psD.ap(), co["ones_c_b"].ap(), pbfp,
                                               start=first, stop=last), inc=kkt)
                kkt_hist.append(t_k)
            P.wait("vector", kkt, kkt.count)
            P.wait("vector", st, st.count)
            t_r = P.op("vector", lambda e: e.reciprocal(rden.ap(), psD.ap()), inc=evq)
            P.wait("tensor", evq, t_r)
            t_b = P.op("tensor", lambda e: e.matmul(
                psB.ap(), co["ones_r_f"].ap(), rden.ap(), start=True, stop=True),
                inc=hdn)
            t_n = P.op("vector", lambda e: e.tensor_copy(nscr.ap(), psA.ap()), inc=evq)
            P.wait("vector", hdn, t_b)
            t_o = P.op("vector", lambda e: e.tensor_mul(aob.ap(), nscr.ap(), psB.ap()),
                       inc=evq)
            P.wait("gpsimd", evq, t_o)
            P.dma("gpsimd", aoin_dram[h * 128:(h + 1) * 128, gq:gq + 512], aob.ap(), st)
            P.wait("tensor", evq, t_o)
    P.wait("gpsimd", st, st.count)
    P.barrier()


def emit_attn1_f(P, nc, es, co, B, loc, aoin_dram):
    ald = P.nsem("bld"); mm = P.nsem("bmm"); ev = P.nsem("bev")
    st = P.nsem("bst"); acc = P.nsem("bacc"); tp = P.nsem("btp")
    qloc, kloc, vloc = loc
    at1, at2, at3 = B["at1"], B["at2"], B["at3"]
    psS, psN, psD, psB = B["ps0"], B["ps1"], B["psRow"], B["ps2"]
    psZ, psM, psT = B["ps3"], B["ps4"], B["psT"]
    pa, nscr, rden, aob = B["pbf"], B["nscr"], B["rden"], B["aob"]
    S_run, z_run, Sb, zb, kcb = B["S_run"], B["z_run"], B["Sb"], B["zb"], B["kcb"]
    vv = at3.ap().rearrange("p (kt d) -> p kt d", kt=32)
    for h in range(8):
        P.wait("sync", mm, mm.count)
        P.dma("sync", at1.ap(), qloc[h * 128:(h + 1) * 128, :], ald)
        P.dma("sync", at2.ap(), kloc[h * 128:(h + 1) * 128, :], ald)
        for k8 in range(0, 32, 8):
            t_l = P.dma("sync", vv[:, k8:k8 + 8, :],
                        vloc[:, h * 128:(h + 1) * 128].rearrange(
                            "(kt p) d -> p kt d", p=128)[:, k8:k8 + 8, :], ald)
        P.wait("tensor", ald, t_l)
        P.wait("vector", ald, t_l)
        P.edone("gpsimd", "vector")
        P.op("gpsimd", lambda e: e.memset(S_run.ap(), 0.0))
        t_z = P.op("gpsimd", lambda e: e.memset(z_run.ap(), 0.0), inc=acc)
        P.wait("vector", acc, t_z)
        P.edone("vector", "gpsimd")
        P.edone("tensor", "gpsimd")
        for g in range(16):
            qsl = at1.ap()[:, g * 256:(g + 1) * 256]
            P.wait("vector", mm, mm.count)
            P.op("vector", lambda e: e.tensor_copy(Sb.ap(), S_run.ap()))
            t_cb = P.op("vector", lambda e: e.tensor_copy(zb.ap(), z_run.ap()), inc=ev)
            P.wait("tensor", ev, ev.count)
            t_t = None
            for kt in range(2):
                t_t = P.op("tensor", lambda e, g=g, kt=kt: e.transpose(
                    psT.ap()[:, kt * 128:(kt + 1) * 128],
                    at2.ap()[:, g * 256 + kt * 128:g * 256 + (kt + 1) * 128],
                    co["ident_b"].ap()), inc=tp if kt == 1 else None)
            P.wait("vector", tp, t_t)
            t_kc = P.op("vector", lambda e: e.tensor_copy(kcb.ap(), psT.ap()[:, 0:256]), inc=ev)
            for kt in range(2):
                P.wait("tensor", ev, ev.count)
                t_s = P.op("tensor", lambda e, g=g, kt=kt, qsl=qsl: e.matmul(
                    psS.ap()[:, 0:256], at2.ap()[:, g * 256 + kt * 128:g * 256 + (kt + 1) * 128],
                    qsl, start=True, stop=True), inc=mm)
                P.wait("vector", mm, t_s)
                t_p = P.op("vector", lambda e, kt=kt: e.tensor_mul(
                    pa.ap()[:, kt * 256:(kt + 1) * 256], psS.ap()[:, 0:256],
                    B["masks1"].ap()[:, kt * 256:(kt + 1) * 256]), inc=ev)
            P.wait("tensor", ev, ev.count)
            for kt in range(2):
                P.op("tensor", lambda e, g=g, kt=kt: e.matmul(
                    psN.ap()[:, 0:256], vv[:, 2 * g + kt, :], pa.ap()[:, kt * 256:(kt + 1) * 256],
                    start=(kt == 0), stop=False))
            P.op("tensor", lambda e, qsl=qsl: e.matmul(
                psN.ap()[:, 0:256], Sb.ap(), qsl, start=False, stop=True))
            for kt in range(2):
                P.op("tensor", lambda e, kt=kt: e.matmul(
                    psD.ap()[:, 0:256], co["ones_c_b"].ap(), pa.ap()[:, kt * 256:(kt + 1) * 256],
                    start=(kt == 0), stop=False))
            t_m = P.op("tensor", lambda e, qsl=qsl: e.matmul(
                psD.ap()[:, 0:256], zb.ap(), qsl, start=False, stop=True), inc=mm)
            for kt in range(2):
                P.op("tensor", lambda e, g=g, kt=kt: e.matmul(
                    psM.ap()[:, 0:128], kcb.ap()[:, kt * 128:(kt + 1) * 128], vv[:, 2 * g + kt, :],
                    start=(kt == 0), stop=(kt == 1)))
            t_m = P.op("tensor", lambda e: e.matmul(
                psZ.ap()[:, 0:1], kcb.ap()[:, 0:128], co["ones_c_b"].ap(),
                start=True, stop=False), inc=mm)
            t_m = P.op("tensor", lambda e: e.matmul(
                psZ.ap()[:, 0:1], kcb.ap()[:, 128:256], co["ones_c_b"].ap(),
                start=False, stop=True), inc=mm)
            P.wait("vector", mm, t_m)
            P.wait("vector", st, st.count)
            P.op("vector", lambda e: e.tensor_scalar(
                rden.ap()[:, 0:256], psD.ap()[:, 0:256], 1e-6, None, op0=ALU.max))
            t_r = P.op("vector", lambda e: e.reciprocal(
                rden.ap()[:, 0:256], rden.ap()[:, 0:256]), inc=ev)
            P.wait("tensor", ev, t_r)
            t_bc = P.op("tensor", lambda e: e.matmul(
                psB.ap()[:, 0:256], co["ones_r_f"].ap(), rden.ap()[:, 0:256],
                start=True, stop=True), inc=mm)
            t_n = P.op("vector", lambda e: e.tensor_copy(
                nscr.ap()[:, 0:256], psN.ap()[:, 0:256]), inc=ev)
            P.wait("vector", mm, t_bc)
            t_o = P.op("vector", lambda e: e.tensor_mul(
                aob.ap()[:, 0:256], nscr.ap()[:, 0:256], psB.ap()[:, 0:256]), inc=ev)
            P.wait("gpsimd", ev, t_o)
            P.dma("gpsimd", aoin_dram[h * 128:(h + 1) * 128, g * 256:(g + 1) * 256],
                  aob.ap()[:, 0:256], st)
            t_a = P.op("vector", lambda e: e.tensor_add(S_run.ap(), S_run.ap(), psM.ap()[:, 0:128]))
            t_a = P.op("vector", lambda e: e.tensor_add(z_run.ap(), z_run.ap(), psZ.ap()[:, 0:1]),
                       inc=acc)
            P.wait("tensor", acc, t_a)
            P.wait("tensor", ev, t_o)
    P.wait("gpsimd", st, st.count)
    P.barrier()


def emit_oproj_f(P, nc, es, co, B, aoloc, w_rows, bias_row, x_in, x_out):
    raw = B["rawh"].ap()[:, 0:D]
    xs, scr, cols = B["sfa"], B["sfb"], B["cols"]
    ob2 = B["wfbig"].ap()[:, 0:D]
    aoTs_ap = B["aoTs"].ap()[:, 0:2048]
    vs = P.nsem("ovs"); st = P.nsem("ost"); ld2 = P.nsem("old")
    aol = P.nsem("oal")
    bld = P.nsem("obl")
    if bias_row is not None:
        t_bb = P.dma("gpsimd", B["brow"].ap(), bias_row.broadcast_to([128, D]), bld)
        P.wait("vector", bld, t_bb)
    else:
        P.edone("gpsimd", "vector")
        bld2 = P.nsem("obl2")
        t_bb = P.op("gpsimd", lambda e: e.memset(B["brow"].ap(), 0.0), inc=bld2)
        P.wait("vector", bld2, t_bb)

    def loader(tt):
        mm = P.nsem("tmm")
        P.wait("gpsimd", mm, mm.count)
        cnt = P.dma("gpsimd", aoTs_ap,
                    aoloc.rearrange("(c p) t -> p c t", p=128)[
                        :, :, tt * 128:(tt + 1) * 128], aol)
        return aoTs_ap, aol, cnt

    def epi(tt, obk, ps, mm, t_m, ev):
        P.wait("vector", mm, t_m)
        t_e = P.op("vector", lambda e, obk=obk: e.tensor_add(
            raw[:, obk * 512:(obk + 1) * 512], ps.ap(),
            B["brow"].ap()[:, obk * 512:(obk + 1) * 512]), inc=ev)
        P.wait("tensor", ev, t_e)
        if obk == 3:
            P.wait("sync", vs, vs.count)
            t_x = P.dma("sync", xs.ap(), x_in[tt * 128:(tt + 1) * 128, :], ld2)
            P.wait("scalar", ev, t_e)
            t_n = emit_ln_cols(P, co, raw, scr.ap(), cols.ap(), vs)
            P.wait("vector", vs, t_n)
            P.op("vector", lambda e: e.tensor_scalar(
                scr.ap(), raw, cols.ap()[:, 3:4], cols.ap()[:, 4:5],
                op0=ALU.mult, op1=ALU.add))
            P.wait("vector", ld2, t_x)
            t_s = P.op("vector", lambda e: e.tensor_add(ob2, xs.ap(), scr.ap()), inc=vs)
            P.wait("gpsimd", vs, t_s)
            P.dma("gpsimd", x_out[tt * 128:(tt + 1) * 128, :], ob2, st)
            P.wait("vector", st, st.count)
            P.wait("scalar", st, st.count)

    emit_projT_f(P, nc, es, co, B, None, ND, w_rows, None,
                 per_unit_epilogue=epi, lhsT_loader=loader)
    P.wait("sync", st, st.count)
    P.wait("gpsimd", st, st.count)
    P.barrier()


def emit_ffn_f(P, nc, es, co, B, io, layer):
    NQ, QT = 8, 256
    if layer == 0:
        x_in, x_out = io["x1_d"], io["x2_d"]
        NH = NH0
    else:
        x_in, x_out = io["x3_d"], io["x4"]
        NH = NH1
    hdv = B["hdnT"].ap()[:, 0:NH * 256].rearrange("p (hc t) -> p hc t", hc=NH)
    rawh = B["rawh"]
    wf, wb = B["wfbig"], B["wbbig"]
    sil, ups, dwb = B["sil"], B["ups"], B["dwb"]
    psG, psU = B["ps0"], B["ps1"]
    psE = [B["ps2"], B["ps3"]]
    wld = P.nsem("fwld"); wc = P.nsem("fwc"); mm = P.nsem("fmm"); ev = P.nsem("fev")
    vs = P.nsem("ovs"); st = P.nsem("ost"); ld2 = P.nsem("old"); bld = P.nsem("fbl")
    if layer == 0:
        P.dma("gpsimd", B["gsc"].ap()[:, 0:NH0], io["gate_s_c"], bld)
        P.dma("gpsimd", B["gsc"].ap()[:, NH0:], io["up_s_c"], bld)
        t_bb = P.dma("gpsimd", B["brow"].ap(), io["down_s_row"].broadcast_to([128, D]), bld)
        P.wait("vector", bld, t_bb)
        P.wait("scalar", bld, t_bb)
    for q in range(NQ):
        tt0 = q * 2
        xnT_ap = emit_xnT_f(P, nc, es, co, B, x_in, 2, tt0=tt0)
        xv = xnT_ap.rearrange("p (ic t) -> p ic t", ic=ND)
        for hc in range(NH):
            P.wait("sync", mm, mm.count)
            if layer == 0:
                P.dma("sync", wf.ap().rearrange("p (q2 ic m) -> p q2 ic m", q2=2, ic=ND, m=128)[:, 0],
                      io["gate_t"][hc].rearrange("ic p m -> p ic m"), wld)
                t_w = P.dma("sync", wf.ap().rearrange("p (q2 ic m) -> p q2 ic m", q2=2, ic=ND, m=128)[:, 1],
                            io["up_t"][hc].rearrange("ic p m -> p ic m"), wld)
                P.wait("vector", wld, t_w)
                P.wait("vector", mm, mm.count)
                t_c = P.op("vector", lambda e: e.tensor_copy(
                    wb.ap()[:, 0:4096], wf.ap()), inc=wc)
                P.wait("tensor", wc, t_c)
            else:
                t_w = P.dma("sync", wb.ap().rearrange("p (ic m) -> p ic m", ic=ND)
                            if False else wb.ap()[:, 0:2048].rearrange("p (ic m) -> p ic m", ic=ND),
                            io["tfc_t"][hc].rearrange("ic p m -> p ic m"), wld)
                P.wait("tensor", wld, t_w)
            P.wait("tensor", ev, ev.count)
            if layer == 0:
                for ic in range(ND):
                    P.op("tensor", lambda e, ic=ic:
                         e.matmul(psG.ap()[:, 0:256], wb.ap()[:, ic * 128:(ic + 1) * 128],
                                  xv[:, ic, 0:256],
                                  start=(ic == 0), stop=(ic == ND - 1)))
                t_m = None
                for ic in range(ND):
                    t_m = P.op("tensor", lambda e, ic=ic:
                               e.matmul(psU.ap()[:, 0:256],
                                        wb.ap()[:, (ND + ic) * 128:(ND + ic + 1) * 128],
                                        xv[:, ic, 0:256],
                                        start=(ic == 0), stop=(ic == ND - 1)),
                               inc=mm if ic == ND - 1 else None)
                P.wait("scalar", mm, t_m)
                P.wait("scalar", ev, max(0, ev.count - 1))
                t_s = P.op("scalar", lambda e, hc=hc: e.activation(
                    sil.ap()[:, 0:256], psG.ap()[:, 0:256], AF.Silu,
                    scale=B["gsc"].ap()[:, hc:hc + 1]), inc=ev)
                P.wait("vector", ev, t_s)
                t_u = P.op("vector", lambda e, hc=hc: e.tensor_scalar(
                    ups.ap()[:, 0:256], psU.ap()[:, 0:256],
                    B["gsc"].ap()[:, NH0 + hc:NH0 + hc + 1], None, op0=ALU.mult), inc=ev)
                t_h = P.op("vector", lambda e, hc=hc: e.tensor_mul(
                    hdv[:, hc, :], sil.ap()[:, 0:256], ups.ap()[:, 0:256]), inc=ev)
            else:
                t_m = None
                for ic in range(ND):
                    t_m = P.op("tensor", lambda e, ic=ic:
                               e.matmul(psG.ap()[:, 0:256], wb.ap()[:, ic * 128:(ic + 1) * 128],
                                        xv[:, ic, 0:256],
                                        start=(ic == 0), stop=(ic == ND - 1)),
                               inc=mm if ic == ND - 1 else None)
                P.wait("vector", mm, t_m)
                P.wait("vector", ev, max(0, ev.count - 1))
                t_r = P.op("vector", lambda e: e.tensor_scalar(
                    sil.ap()[:, 0:256], psG.ap()[:, 0:256], 0.0, None, op0=ALU.max),
                    inc=ev)
                t_h = P.op("vector", lambda e, hc=hc: e.tensor_mul(
                    hdv[:, hc, :], sil.ap()[:, 0:256], sil.ap()[:, 0:256]), inc=ev)
            P.wait("tensor", ev, t_h)
        P.barrier()
        for obk in range(4):
            for hc in range(NH):
                P.wait("sync", mm, mm.count)
                if layer == 0:
                    t_w = P.dma("sync", wf.ap()[:, 0:512],
                                io["down_r"][hc * 128:(hc + 1) * 128,
                                             obk * 512:(obk + 1) * 512], wld)
                    P.wait("vector", wld, t_w)
                    P.wait("vector", mm, mm.count)
                    t_c = P.op("vector", lambda e: e.tensor_copy(dwb.ap(), wf.ap()[:, 0:512]),
                               inc=wc)
                    P.wait("tensor", wc, t_c)
                else:
                    t_w = P.dma("sync", dwb.ap(),
                                io["tpj_r"][hc * 128:(hc + 1) * 128,
                                            obk * 512:(obk + 1) * 512], wld)
                    P.wait("tensor", wld, t_w)
                if hc == 0:
                    P.wait("tensor", ev, ev.count)
                t_m = None
                for tt in range(2):
                    t_m = P.op("tensor", lambda e, tt=tt, hc=hc:
                               e.matmul(psE[tt].ap(),
                                        hdv[:, hc, tt * 128:(tt + 1) * 128],
                                        dwb.ap(), start=(hc == 0),
                                        stop=(hc == NH - 1)),
                               inc=mm if tt == 1 else None)
            P.wait("vector", mm, mm.count)
            for tt in range(2):
                if layer == 0:
                    P.op("vector", lambda e, tt=tt, obk=obk: e.tensor_mul(
                        rawh.ap()[:, tt * D + obk * 512: tt * D + obk * 512 + 512],
                        psE[tt].ap(), B["brow"].ap()[:, obk * 512:(obk + 1) * 512]),
                        inc=ev)
                else:
                    P.op("vector", lambda e, tt=tt, obk=obk: e.tensor_scalar(
                        rawh.ap()[:, tt * D + obk * 512: tt * D + obk * 512 + 512],
                        psE[tt].ap(), B["ccol"].ap(), None, op0=ALU.mult), inc=ev)
            P.wait("tensor", ev, ev.count)
        P.barrier()
        for i in range(2):
            tt = tt0 + i
            raw_ap = rawh.ap()[:, i * D:(i + 1) * D]
            P.wait("sync", vs, vs.count)
            t_x = P.dma("sync", B["sfa"].ap(), x_in[tt * 128:(tt + 1) * 128, :], ld2)
            t_n = emit_ln_cols(P, co, raw_ap, B["sfb"].ap(), B["cols"].ap(),
                               P.nsem("ovs"))
            P.wait("vector", P.nsem("ovs"), t_n)
            P.op("vector", lambda e, raw_ap=raw_ap: e.tensor_scalar(
                B["sfb"].ap(), raw_ap, B["cols"].ap()[:, 3:4], B["cols"].ap()[:, 4:5],
                op0=ALU.mult, op1=ALU.add))
            P.wait("vector", ld2, t_x)
            t_s = P.op("vector", lambda e: e.tensor_add(
                B["wfbig"].ap()[:, 0:D], B["sfa"].ap(), B["sfb"].ap()), inc=vs)
            P.wait("gpsimd", vs, t_s)
            P.dma("gpsimd", x_out[tt * 128:(tt + 1) * 128, :],
                  B["wfbig"].ap()[:, 0:D], st)
            P.wait("vector", st, st.count)
            P.wait("scalar", st, st.count)
        P.wait("sync", st, st.count)
        P.wait("gpsimd", st, st.count)
        P.barrier()


def emit_tern_f(P, nc, es, co, B, io):
    wf = B["wfbig"].ap()[:, 0:2048]
    ab = B["wfbig"].ap()[:, 2048:4096]
    s1, s2, obf = B["sfa"], B["sfb"], B["sba"]
    msc, thr = B["msc"], B["thr"]
    psA, psB = B["psRow"], B["ps2"]
    ld = P.nsem("cld"); cst = P.nsem("ccst"); mm = P.nsem("cmm")
    ev = P.nsem("cev"); st = P.nsem("cst2")
    specs = [("fc", io["w_fc_t"], True, io["tfc_t"]),
             ("pj", io["w_proj_r"], False, io["tpj_r"])]
    NTOT = H1 * D

    def bcast_scalar(src_ap, dst_col_ap):
        P.edone("tensor", "vector")
        t_bb = P.op("tensor", lambda e, s=src_ap, : e.matmul(
            psB.ap()[:, 0:1], co["ones_r_f"].ap(), s, start=True, stop=True), inc=mm)
        P.wait("vector", mm, t_bb)
        return P.op("vector", lambda e, d=dst_col_ap: e.tensor_copy(d, psB.ap()[:, 0:1]),
                    inc=ev)

    for mi, (tag, wsrc, tiled, dst) in enumerate(specs):
        nbands = NH1
        for ph in range(2):
            for bi in range(nbands):
                P.wait("sync", mm, mm.count)
                if tiled:
                    t_w = P.dma("sync", wf.rearrange("p (ic m) -> p ic m", ic=ND),
                                wsrc[bi].rearrange("ic p m -> p ic m"), ld)
                else:
                    t_w = P.dma("sync", wf, wsrc[bi * 128:(bi + 1) * 128, :], ld)
                P.wait("scalar", ld, t_w)
                P.wait("scalar", mm, mm.count)
                if ph == 0:
                    t_a = P.op("scalar", lambda e: e.activation(ab, wf, AF.Abs), inc=cst)
                else:
                    P.op("scalar", lambda e: e.activation(ab, wf, AF.Abs))
                    P.edone("vector", "scalar")
                    t_a = P.op("vector", lambda e, mi=mi: e.tensor_scalar(
                        ab, ab, thr.ap()[:, 2 + mi:3 + mi], None, op0=ALU.add), inc=cst)
                P.wait("tensor", cst, t_a)
                for j in range(4):
                    t_m = P.op("tensor", lambda e, j=j, first=(bi == 0 and j == 0),
                               last=(bi == nbands - 1 and j == 3):
                               e.matmul(psA.ap(), co["ones_c_f"].ap(),
                                        ab[:, j * 512:(j + 1) * 512],
                                        start=first, stop=last),
                               inc=mm if j == 3 else None)
            P.wait("vector", mm, mm.count)
            if ph == 0:
                P.op("vector", lambda e, mi=mi: e.reduce_sum(
                    msc.ap()[:, 6 + mi:7 + mi], psA.ap(), axis=AXX))
                P.op("vector", lambda e, mi=mi: e.tensor_scalar(
                    msc.ap()[:, 6 + mi:7 + mi], msc.ap()[:, 6 + mi:7 + mi],
                    1.0 / NTOT, None, op0=ALU.mult))
                t_v = P.op("vector", lambda e, mi=mi: e.tensor_scalar(
                    msc.ap()[:, 4:5], msc.ap()[:, 6 + mi:7 + mi], -1.0, None,
                    op0=ALU.mult), inc=ev)
                t_v = bcast_scalar(msc.ap()[:, 4:5], thr.ap()[:, 2 + mi:3 + mi])
                P.wait("tensor", ev, t_v)
                P.wait("scalar", ev, t_v)
            else:
                P.op("vector", lambda e, mi=mi: e.reduce_sum(
                    msc.ap()[:, 4:5], psA.ap(), axis=AXX))
                P.op("vector", lambda e, mi=mi: e.tensor_scalar(
                    msc.ap()[:, 4:5], msc.ap()[:, 4:5], 1.0 / NTOT, None, op0=ALU.mult))
                P.op("vector", lambda e, mi=mi: e.tensor_add(
                    msc.ap()[:, mi:mi + 1], msc.ap()[:, 4:5], msc.ap()[:, 6 + mi:7 + mi]))
                t_v = P.op("vector", lambda e, mi=mi: e.tensor_scalar(
                    msc.ap()[:, mi:mi + 1], msc.ap()[:, mi:mi + 1], 1e-5, None,
                    op0=ALU.max), inc=ev)
                P.op("vector", lambda e, mi=mi: e.tensor_scalar(
                    msc.ap()[:, 4:5], msc.ap()[:, mi:mi + 1], -0.7, None, op0=ALU.mult))
                t_v = bcast_scalar(msc.ap()[:, 4:5], thr.ap()[:, mi:mi + 1])
                P.wait("tensor", ev, t_v)
                P.wait("scalar", ev, t_v)
    P.op("vector", lambda e: e.tensor_mul(msc.ap()[:, 5:6], msc.ap()[:, 0:1],
                                          msc.ap()[:, 0:1]))
    t_v = P.op("vector", lambda e: e.tensor_mul(msc.ap()[:, 5:6], msc.ap()[:, 5:6],
                                                msc.ap()[:, 1:2]), inc=ev)
    t_v = bcast_scalar(msc.ap()[:, 5:6], B["ccol"].ap())
    P.barrier()
    for mi, (tag, wsrc, tiled, dst) in enumerate(specs):
        tc_ap = thr.ap()[:, mi:mi + 1]
        for bi in range(NH1):
            P.wait("sync", ev, ev.count)
            if tiled:
                t_w = P.dma("sync", wf.rearrange("p (ic m) -> p ic m", ic=ND),
                            wsrc[bi].rearrange("ic p m -> p ic m"), ld)
            else:
                t_w = P.dma("sync", wf, wsrc[bi * 128:(bi + 1) * 128, :], ld)
            P.wait("scalar", ld, t_w)
            P.wait("scalar", ev, ev.count)
            P.op("scalar", lambda e, b=tc_ap: e.activation(
                s1.ap(), wf, AF.Relu, bias=b, scale=1.0))
            P.op("scalar", lambda e: e.activation(s1.ap(), s1.ap(), AF.Sign))
            P.op("scalar", lambda e, b=tc_ap: e.activation(
                s2.ap(), wf, AF.Relu, bias=b, scale=-1.0))
            t_2 = P.op("scalar", lambda e: e.activation(s2.ap(), s2.ap(), AF.Sign),
                       inc=cst)
            P.wait("vector", cst, t_2)
            t_o = P.op("vector", lambda e: e.tensor_sub(obf.ap(), s1.ap(), s2.ap()),
                       inc=ev)
            P.wait("gpsimd", ev, t_o)
            if tiled:
                P.dma("gpsimd", dst[bi].rearrange("ic p m -> p ic m"),
                      obf.ap().rearrange("p (ic m) -> p ic m", ic=ND), st)
            else:
                P.dma("gpsimd", dst[bi * 128:(bi + 1) * 128, :], obf.ap(), st)
            P.wait("scalar", st, st.count)
    P.wait("sync", st, st.count)
    P.wait("gpsimd", st, st.count)
    P.barrier()


def build_fused(nc, io_specs):
    io = _declare(nc, io_specs)
    ID = lambda n, s, d: nc.dram_tensor(n, list(s), d, kind="Internal").ap()
    io["x1_d"] = ID("x1_d", (TC, D), F32)
    io["x2_d"] = ID("x2_d", (TC, D), F32)
    io["x3_d"] = ID("x3_d", (TC, D), F32)
    io["ex0in"] = ID("ex0in", (6144, 2048), BF16)
    io["ex0out"] = ID("ex0out", (12288, 2048), BF16)
    io["ao0in"] = ID("ao0in", (1024, 4096), BF16)
    io["ao0out"] = ID("ao0out", (2048, 4096), BF16)
    io["ex1in"] = ID("ex1in", (6144, 2048), BF16)
    io["ex1out"] = ID("ex1out", (12288, 2048), BF16)
    io["ao1in"] = ID("ao1in", (1024, 4096), BF16)
    io["ao1out"] = ID("ao1out", (2048, 4096), BF16)
    io["tfc_t"] = ID("tfc_t", (NH1, ND, 128, 128), BF16)
    io["tpj_r"] = ID("tpj_r", (H1, D), BF16)

    io["qloc"] = ID("qloc", (1024, 4096), BF16)
    io["kloc"] = ID("kloc", (1024, 4096), BF16)
    io["vloc"] = ID("vloc", (4096, 1024), BF16)
    io["aoloc"] = ID("aoloc", (2048, 2048), BF16)

    es = contextlib.ExitStack()
    with es:
        P = Prog(nc)
        co = emit_consts(P, nc, es)
        B = alloc_arena(nc, es)
        dynr = nc.gpsimd.alloc_register("dynr")
        cc = P.nsem("cc")
        bld = P.nsem("gbl")
        mk = P.nsem("mk")
        t = None
        for r in range(4):
            P.op("gpsimd", lambda e, r=r: e.memset(B["masks0"].ap()[:, r * 512:(r + 1) * 512], 0.0))
            t = P.op("gpsimd", lambda e, r=r: e.affine_select(
                out=B["masks0"].ap()[:, r * 512:(r + 1) * 512],
                in_=B["masks0"].ap()[:, r * 512:(r + 1) * 512],
                compare_op=ALU.is_gt, fill=1.0, base=r * 128,
                pattern=[[-1, 512]], channel_multiplier=1), inc=mk)
        for r in range(2):
            P.op("gpsimd", lambda e, r=r: e.memset(B["masks1"].ap()[:, r * 256:(r + 1) * 256], 0.0))
            t = P.op("gpsimd", lambda e, r=r: e.affine_select(
                out=B["masks1"].ap()[:, r * 256:(r + 1) * 256],
                in_=B["masks1"].ap()[:, r * 256:(r + 1) * 256],
                compare_op=ALU.is_gt, fill=1.0, base=r * 128,
                pattern=[[-1, 256]], channel_multiplier=1), inc=mk)
        for e in ("tensor", "vector", "scalar"):
            P.wait(e, mk, t)
        P.dma("sync", B["pswap"].ap(), io["pswap"], bld)
        P.dma("sync", B["dyn"].ap(), io["dyn"], bld)
        P.dma("sync", B["bcol"].ap()[:, 0:ND], io["bq_c"], bld)
        t_b = P.dma("sync", B["bcol"].ap()[:, ND:], io["bk_c"], bld)
        P.wait("vector", bld, t_b)
        bsc = P.nsem("bsc")
        P.op("vector", lambda e: e.tensor_scalar(
            B["bcol"].ap()[:, 0:ND], B["bcol"].ap()[:, 0:ND],
            SCALE0, None, op0=ALU.mult), inc=bsc)
        P.wait("tensor", bld, t_b)
        P.wait("gpsimd", bld, t_b)
        P.barrier()

        def stage_exchange(exout, stg):
            ss = P.nsem("stg")
            dyn = B["dyn"]
            exout_t = exout.tensor
            for r in range(2):
                for ki, dst in ((0, io["qloc"]), (2, io["kloc"])):
                    P.rawop("gpsimd", lambda e, i=ki + r: e.reg_load(
                        dynr, dyn.ap()[0:1, i:i + 1]))
                    P.dma("gpsimd", dst[:, r * 2048:(r + 1) * 2048],
                          bass.AP(exout_t, dynr, [[2048, 1024], [1, 2048]]), ss)
                P.rawop("gpsimd", lambda e, i=4 + r: e.reg_load(
                    dynr, dyn.ap()[0:1, i:i + 1]))
                P.dma("gpsimd", io["vloc"][r * 2048:(r + 1) * 2048, :],
                      bass.AP(exout_t, dynr, [[2048, 2048], [1, 1024]]), ss)
            P.wait("gpsimd", ss, ss.count)
            P.barrier()

        def stage_ao(aoout):
            ss = P.nsem("stg")
            dyn = B["dyn"]
            P.rawop("gpsimd", lambda e: e.reg_load(dynr, dyn.ap()[0:1, 6:7]))
            P.dma("gpsimd", io["aoloc"],
                  bass.AP(aoout.tensor, dynr, [[4096, 2048], [1, 2048]]), ss)
            P.wait("gpsimd", ss, ss.count)
            P.barrier()

        sinT, cosT = emit_rope_f(P, nc, es, B, io, "r0")
        emit_xnT_f(P, nc, es, co, B, io["x0"], NT)
        xnT_ap = B["xnTbig"].ap()[:, 0:ND * TC]
        emit_projF_f(P, nc, es, co, B, xnT_ap, io["wq_t"], ND, io["ex0in"][0:2048],
                     rope=(sinT, cosT), bias_off=0, scale=SCALE0)
        emit_projF_f(P, nc, es, co, B, xnT_ap, io["wk_t"], ND, io["ex0in"][2048:4096],
                     rope=(sinT, cosT), bias_off=ND, scale=1.0)
        gbl2 = P.nsem("gbl2")
        t_bv = P.dma("gpsimd", B["brow"].ap(), io["bv_row"].broadcast_to([128, D]), gbl2)
        P.wait("vector", gbl2, t_bv)
        emit_projT_f(P, nc, es, co, B, xnT_ap, ND, io["wv_r"], io["ex0in"][4096:6144],
                     bias_bcast=B["brow"].ap())
        t_cc = P.op("gpsimd", lambda e: e.collective_compute(
            "AllGather", ALU.bypass, replica_groups=[[0, 1], [2, 3], [4, 5], [6, 7]],
            ins=[io["ex0in"].opt()], outs=[io["ex0out"].opt()]), inc=cc)
        for e in ENGINES:
            P.wait(e, cc, t_cc)
        stage_exchange(io["ex0out"], None)
        emit_attn0_f(P, nc, es, co, B, (io["qloc"], io["kloc"], io["vloc"]),
                     io["ao0in"])
        t_cc = P.op("gpsimd", lambda e: e.collective_compute(
            "AllGather", ALU.bypass, replica_groups=[[0, 1], [2, 3], [4, 5], [6, 7]],
            ins=[io["ao0in"].opt()], outs=[io["ao0out"].opt()]), inc=cc)
        for e in ENGINES:
            P.wait(e, cc, t_cc)
        stage_ao(io["ao0out"])
        emit_oproj_f(P, nc, es, co, B, io["aoloc"], io["wo_r"],
                     io["bo_row"], io["x0"], io["x1_d"])
        emit_ffn_f(P, nc, es, co, B, io, 0)
        sinT, cosT = emit_rope_f(P, nc, es, B, io, "r1")
        emit_xnT_f(P, nc, es, co, B, io["x2_d"], NT)
        emit_projF_f(P, nc, es, co, B, xnT_ap, io["wq1_t"], ND, io["ex1in"][0:2048],
                     rope=(sinT, cosT), elu1=True)
        emit_projF_f(P, nc, es, co, B, xnT_ap, io["wk1_t"], ND, io["ex1in"][2048:4096],
                     rope=(sinT, cosT), elu1=True)
        emit_projT_f(P, nc, es, co, B, xnT_ap, ND, io["wv1_r"], io["ex1in"][4096:6144])
        t_cc = P.op("gpsimd", lambda e: e.collective_compute(
            "AllGather", ALU.bypass, replica_groups=[[0, 1], [2, 3], [4, 5], [6, 7]],
            ins=[io["ex1in"].opt()], outs=[io["ex1out"].opt()]), inc=cc)
        for e in ENGINES:
            P.wait(e, cc, t_cc)
        stage_exchange(io["ex1out"], None)
        emit_attn1_f(P, nc, es, co, B, (io["qloc"], io["kloc"], io["vloc"]),
                     io["ao1in"])
        t_cc = P.op("gpsimd", lambda e: e.collective_compute(
            "AllGather", ALU.bypass, replica_groups=[[0, 1], [2, 3], [4, 5], [6, 7]],
            ins=[io["ao1in"].opt()], outs=[io["ao1out"].opt()]), inc=cc)
        for e in ENGINES:
            P.wait(e, cc, t_cc)
        stage_ao(io["ao1out"])
        emit_oproj_f(P, nc, es, co, B, io["aoloc"], io["wout_r"],
                     None, io["x2_d"], io["x3_d"])
        emit_tern_f(P, nc, es, co, B, io)
        emit_ffn_f(P, nc, es, co, B, io, 1)
        P.finalize(es)
    return nc


FUSED_SPECS = [
    ("x0", (TC, D), F32, "ExternalInput"),
    ("wq_t", (ND, ND, 128, 128), F32, "ExternalInput"),
    ("wk_t", (ND, ND, 128, 128), F32, "ExternalInput"),
    ("wv_r", (D, D), F32, "ExternalInput"),
    ("wo_r", (D, D), F32, "ExternalInput"),
    ("bq_c", (128, ND), F32, "ExternalInput"),
    ("bk_c", (128, ND), F32, "ExternalInput"),
    ("bv_row", (1, D), F32, "ExternalInput"),
    ("bo_row", (1, D), F32, "ExternalInput"),
    ("gate_t", (NH0, ND, 128, 128), F32, "ExternalInput"),
    ("up_t", (NH0, ND, 128, 128), F32, "ExternalInput"),
    ("down_r", (INTER0, D), F32, "ExternalInput"),
    ("gate_s_c", (128, NH0), F32, "ExternalInput"),
    ("up_s_c", (128, NH0), F32, "ExternalInput"),
    ("down_s_row", (1, D), F32, "ExternalInput"),
    ("wq1_t", (ND, ND, 128, 128), F32, "ExternalInput"),
    ("wk1_t", (ND, ND, 128, 128), F32, "ExternalInput"),
    ("wv1_r", (D, D), F32, "ExternalInput"),
    ("wout_r", (D, D), F32, "ExternalInput"),
    ("w_fc_t", (NH1, ND, 128, 128), F32, "ExternalInput"),
    ("w_proj_r", (H1, D), F32, "ExternalInput"),
    ("r0sh", (128, TC), F32, "ExternalInput"),
    ("r0sl", (128, TC), F32, "ExternalInput"),
    ("r0ch", (128, TC), F32, "ExternalInput"),
    ("r0cl", (128, TC), F32, "ExternalInput"),
    ("r1sh", (128, TC), F32, "ExternalInput"),
    ("r1sl", (128, TC), F32, "ExternalInput"),
    ("r1ch", (128, TC), F32, "ExternalInput"),
    ("r1cl", (128, TC), F32, "ExternalInput"),
    ("pswap", (128, 128), F32, "ExternalInput"),
    ("dyn", (1, 64), mybir.dt.int32, "ExternalInput"),
    ("x4", (TC, D), F32, "ExternalOutput"),
]


def kernel(**inputs):
    f32 = np.float32
    g = {k: np.asarray(v) for k, v in inputs.items()}
    x = np.ascontiguousarray(g["L0_attn_x"].astype(f32, copy=False))

    wq_t = tile_lhsT(np.ascontiguousarray(g["L0_attn_Wq"]), ND, ND)
    wk_t = tile_lhsT(np.ascontiguousarray(g["L0_attn_Wk"]), ND, ND)
    wv_r = np.ascontiguousarray(g["L0_attn_Wv"].T)
    wo_r = np.ascontiguousarray(g["L0_attn_Wo"].T)
    bq_c = col_layout(g["L0_attn_bq"]); bk_c = col_layout(g["L0_attn_bk"])
    bv_row = np.ascontiguousarray(g["L0_attn_bv"].reshape(1, D))
    bo_row = np.ascontiguousarray(g["L0_attn_bo"].reshape(1, D))
    gate_t = tile_lhsT(np.ascontiguousarray(g["L0_ffn_gate_w"]), NH0, ND)
    up_t = tile_lhsT(np.ascontiguousarray(g["L0_ffn_up_w"]), NH0, ND)
    down_r = np.ascontiguousarray(g["L0_ffn_down_w"].T)
    gate_s_c = col_layout(g["L0_ffn_gate_s"]); up_s_c = col_layout(g["L0_ffn_up_s"])
    down_s_row = np.ascontiguousarray(g["L0_ffn_down_s"].reshape(1, D))
    wqkv = np.ascontiguousarray(g["L1_attn_W_qkv"])
    wq1_t = tile_lhsT(np.ascontiguousarray(wqkv[0:D]), ND, ND)
    wk1_t = tile_lhsT(np.ascontiguousarray(wqkv[D:2 * D]), ND, ND)
    wv1_r = np.ascontiguousarray(wqkv[2 * D:3 * D].T)
    wout_r = np.ascontiguousarray(g["L1_attn_W_out"].T)
    w_fc_t = tile_lhsT(np.ascontiguousarray(g["L1_ffn_w_fc"]), NH1, ND)
    w_proj_r = np.ascontiguousarray(g["L1_ffn_w_proj"].T)
    r0 = make_rope_tables(0)
    r1 = make_rope_tables(1)
    pswap_m = np.zeros((128, 128), f32)
    pswap_m[np.arange(128), np.arange(128) ^ 1] = 1.0

    def rope_slices(tabs, p):
        sh, sl, ch, cl = tabs
        cols = np.arange(p * 2048, (p + 1) * 2048)
        return tuple(np.ascontiguousarray(a[:, cols]) for a in tabs)

    nc = bass.Bass("TRN2", target_bir_lowering=False)
    build_fused(nc, FUSED_SPECS)

    ins = []
    for c in CORES:
        bb, p = c // 2, c % 2
        sh0, sl0, ch0, cl0 = rope_slices(r0, p)
        sh1, sl1, ch1, cl1 = rope_slices(r1, p)
        dyn = np.zeros((1, 64), np.int32)
        for r in range(2):
            dyn[0, 0 + r] = (r * 6144 + p * 1024) * 2048
            dyn[0, 2 + r] = (r * 6144 + 2048 + p * 1024) * 2048
            dyn[0, 4 + r] = (r * 6144 + 4096) * 2048 + p * 1024
        dyn[0, 6] = p * 2048
        ins.append(dict(
            x0=np.ascontiguousarray(x[bb, p * 2048:(p + 1) * 2048]),
            wq_t=wq_t, wk_t=wk_t, wv_r=wv_r, wo_r=wo_r, bq_c=bq_c, bk_c=bk_c,
            bv_row=bv_row, bo_row=bo_row, gate_t=gate_t, up_t=up_t, down_r=down_r,
            gate_s_c=gate_s_c, up_s_c=up_s_c, down_s_row=down_s_row,
            wq1_t=wq1_t, wk1_t=wk1_t, wv1_r=wv1_r, wout_r=wout_r,
            w_fc_t=w_fc_t, w_proj_r=w_proj_r,
            r0sh=sh0, r0sl=sl0, r0ch=ch0, r0cl=cl0,
            r1sh=sh1, r1sl=sl1, r1ch=ch1, r1cl=cl1,
            pswap=pswap_m, dyn=dyn))
    res = run_bass_kernel_spmd(nc, ins, core_ids=CORES)
    out = np.zeros((B, S, D), f32)
    for c in CORES:
        bb, p = c // 2, c % 2
        out[bb, p * 2048:(p + 1) * 2048] = res.results[c]["x4"]
    return out


# version 12
# speedup vs baseline: 1.5527x; 1.5356x over previous
import math
import contextlib
import numpy as np
import ml_dtypes
import concourse.bass as bass
import concourse.mybir as mybir
from concourse.bass_utils import run_bass_kernel_spmd

F32 = mybir.dt.float32
BF16 = mybir.dt.bfloat16
AF = mybir.ActivationFunctionType
ALU = mybir.AluOpType
AXX = mybir.AxisListType.X

B, S, D = 4, 4096, 2048
H = 16
TC = 2048
NT = TC // 128
INTER0 = 5632
NH0 = INTER0 // 128
H1 = 5120
NH1 = H1 // 128
CHUNK = 256
ND = D // 128
SCALE0 = 1.0 / math.sqrt(128.0)
C_SHIFT = 20.0
PAIRSWAP = [i ^ 1 for i in range(32)]
SPANS = {0: [0, 3], 1: [1, 2]}
CORES = list(range(8))


def own_ranges(p):
    return [(sp * 1024, (sp + 1) * 1024) for sp in SPANS[p]]


def own_chunks(p):
    return [sp * 4 + i for sp in SPANS[p] for i in range(4)]



def _reduce_pm_pi(theta):
    k = np.round(theta / (2 * np.pi))
    return np.clip(theta - k * (2 * np.pi), -np.pi + 1e-7, np.pi - 1e-7)


def _hi_lo(a):
    hi = a.astype(np.float32)
    return hi, (a - hi.astype(np.float64)).astype(np.float32)


def make_rope_tables(layer):
    j = np.arange(128)
    t = np.arange(S, dtype=np.float64)
    inv = 10000.0 ** (-(np.arange(64, dtype=np.float64) / 64.0))
    theta = (inv[j // 2] if layer == 0 else inv[j % 64])[:, None] * t[None, :]
    sign = np.where(j % 2 == 0, -1.0, 1.0)[:, None]
    sh, sl = _hi_lo(_reduce_pm_pi(theta * sign))
    ch, cl = _hi_lo(_reduce_pm_pi(theta + np.pi / 2))
    return sh, sl, ch, cl


def tile_lhsT(W, noc, nic):
    W4 = W.reshape(noc, 128, nic, 128)
    return np.ascontiguousarray(W4.transpose(0, 2, 3, 1))


def col_layout(v):
    n = v.shape[0] // 128
    return np.ascontiguousarray(v.reshape(n, 128).T)



ENGINES = ("sync", "tensor", "vector", "scalar", "gpsimd")


class Sem:
    __slots__ = ("name", "count", "handle")

    def __init__(self, name):
        self.name, self.count, self.handle = name, 0, None


STRICT = ("vector", "scalar", "gpsimd")


class Prog:

    def __init__(self, nc):
        self.nc = nc
        self.q = {e: [] for e in ENGINES}
        self.sems = []
        self.last_wait = {e: {} for e in ENGINES}
        self.serial = {e: self.sem("ser_" + e) for e in STRICT}
        self.last_done = {}
        self.all_waits = []
        self.dma_sem_owner = {}
        self.named = {}
        self.barsem = None

    def sem(self, name=None):
        s = Sem(name or f"s{len(self.sems)}")
        self.sems.append(s)
        return s

    def nsem(self, name):
        if name not in self.named:
            self.named[name] = self.sem(name)
        return self.named[name]

    def wait(self, eng, sem, val):
        if sem is None or val is None or val <= 0:
            return
        lw = self.last_wait[eng]
        if lw.get(sem.name, 0) >= val:
            return
        lw[sem.name] = val
        self.all_waits.append((eng, sem, val))
        self.q[eng].append(lambda e, s=sem, v=val: e.wait_ge(s.handle, v))

    def op(self, eng, fn, inc=None, inc_n=1):
        if eng in STRICT:
            ld = self.last_done.get(eng)
            if ld is not None:
                self.wait(eng, ld[0], ld[1])
            if inc is not None:
                inc.count += inc_n
                self.q[eng].append(
                    lambda e, f=fn, s=inc, n=inc_n: f(e).then_inc(s.handle, n))
                self.last_done[eng] = (inc, inc.count)
                return inc.count
            ser = self.serial[eng]
            ser.count += 1
            self.q[eng].append(
                lambda e, f=fn, sr=ser: f(e).then_inc(sr.handle, 1))
            self.last_done[eng] = (ser, ser.count)
            return None
        if inc is not None:
            inc.count += inc_n
            self.q[eng].append(
                lambda e, f=fn, s=inc, n=inc_n: f(e).then_inc(s.handle, n))
            return inc.count
        self.q[eng].append(fn)
        return None

    def rawop(self, eng, fn):
        self.q[eng].append(fn)

    def edone(self, waiter, producer):
        ld = self.last_done.get(producer)
        if ld is not None:
            self.wait(waiter, ld[0], ld[1])

    def dma(self, eng, out, in_, sem, **kw):
        owner = self.dma_sem_owner.setdefault(sem.name, eng)
        assert owner == eng, f"DMA sem {sem.name} used from {eng} and {owner}"
        if self.last_wait[eng].get(sem.name, 0) < sem.count:
            self.wait(eng, sem, sem.count)
            self.q[eng].append(lambda e: e.nop())
        sem.count += 16
        self.q[eng].append(
            lambda e, o=out, i=in_, k=kw, s=sem: e.dma_start(out=o, in_=i, **k)
            .then_inc(s.handle, 16))
        return sem.count

    def barrier(self):
        if self.barsem is None:
            self.barsem = self.sem("bar")
        bs = self.barsem
        for e in ENGINES:
            if e == "sync":
                self.q[e].append(lambda en, s=bs: en.nop().then_inc(s.handle, 1))
            else:
                self.q[e].append(lambda en, s=bs: en.drain().then_inc(s.handle, 1))
            bs.count += 1
        tgt = bs.count
        for e in ENGINES:
            self.wait(e, bs, tgt)

    def finalize(self, es):
        for eng, sem, val in self.all_waits:
            if val > sem.count:
                raise RuntimeError(
                    f"unsatisfiable wait: {eng} waits {sem.name} >= {val}, "
                    f"final count {sem.count}")
        for s in self.sems:
            s.handle = es.enter_context(self.nc.semaphore(s.name))
        block = es.enter_context(self.nc.Block())
        for eng in ENGINES:
            ops = self.q[eng]
            if ops:
                getattr(block, eng)(lambda e, ops=ops: [f(e) for f in ops])



def emit_consts(P, nc, es):
    c = {}
    c["ones_r_f"] = es.enter_context(nc.sbuf_tensor("ones_r_f", [1, 128], F32))
    c["ones_c_b"] = es.enter_context(nc.sbuf_tensor("ones_c_b", [128, 1], BF16))
    c["ones_c_f"] = es.enter_context(nc.sbuf_tensor("ones_c_f", [128, 1], F32))
    c["ident_b"] = es.enter_context(nc.sbuf_tensor("ident_b", [128, 128], BF16))
    c["negC"] = es.enter_context(nc.sbuf_tensor("negC", [128, 1], F32))
    c["epsc"] = es.enter_context(nc.sbuf_tensor("epsc", [128, 1], F32))
    done = P.sem("consts")
    P.op("gpsimd", lambda e: e.memset(c["ones_r_f"].ap(), 1.0))
    P.op("gpsimd", lambda e: e.memset(c["ones_c_b"].ap(), 1.0))
    P.op("gpsimd", lambda e: e.memset(c["ones_c_f"].ap(), 1.0))
    P.op("gpsimd", lambda e: e.memset(c["negC"].ap(), -C_SHIFT))
    P.op("gpsimd", lambda e: e.memset(c["epsc"].ap(), 1e-5))
    P.op("gpsimd", lambda e: e.memset(c["ident_b"].ap(), 0.0))
    k = P.op("gpsimd", lambda e: e.affine_select(
        out=c["ident_b"].ap(), in_=c["ident_b"].ap(),
        compare_op=ALU.not_equal, fill=1.0, base=0,
        pattern=[[-1, 128]], channel_multiplier=1), inc=done)
    c["done"] = done
    for e in ENGINES:
        P.wait(e, done, k)
    return c


def emit_ln_cols(P, co, xs_ap, scr_ap, cols_ap, vsem):
    nf = xs_ap.shape[-1]
    inv_n = 1.0 / nf
    sumc, sgc = cols_ap[:, 0:1], cols_ap[:, 1:2]
    muc, rsc, nmc = cols_ap[:, 2:3], cols_ap[:, 3:4], cols_ap[:, 4:5]
    P.edone("scalar", "vector")
    t = P.op("scalar", lambda e: e.activation(scr_ap, xs_ap, AF.Square, accum_out=sgc),
             inc=vsem)
    P.op("vector", lambda e: e.reduce_sum(sumc, xs_ap, axis=AXX))
    P.op("vector", lambda e: e.tensor_scalar(muc, sumc, inv_n, None, op0=ALU.mult))
    P.wait("vector", vsem, t)
    P.op("vector", lambda e: e.tensor_scalar(sgc, sgc, inv_n, None, op0=ALU.mult))
    P.op("vector", lambda e: e.tensor_mul(nmc, muc, muc))
    t = P.op("vector", lambda e: e.tensor_sub(sgc, sgc, nmc), inc=vsem)
    P.wait("scalar", vsem, t)
    t = P.op("scalar", lambda e: e.activation(sgc, sgc, AF.Sqrt, bias=co["epsc"].ap()),
             inc=vsem)
    P.wait("vector", vsem, t)
    P.op("vector", lambda e: e.reciprocal(rsc, sgc))
    P.op("vector", lambda e: e.tensor_mul(nmc, muc, rsc))
    return P.op("vector", lambda e: e.tensor_scalar(nmc, nmc, -1.0, None, op0=ALU.mult),
                inc=vsem)


def alloc_xnT_bufs(nc, es, tag, ntt):
    b = {}
    b["xnT"] = es.enter_context(nc.sbuf_tensor(tag + "T", [128, ND * ntt * 128], BF16))
    b["xs"] = es.enter_context(nc.sbuf_tensor(tag + "_xs", [128, D], F32))
    b["xb"] = es.enter_context(nc.sbuf_tensor(tag + "_xb", [128, D], BF16))
    b["scr"] = es.enter_context(nc.sbuf_tensor(tag + "_scr", [128, D], F32))
    b["cols"] = es.enter_context(nc.sbuf_tensor(tag + "_cols", [128, 6], F32))
    b["pt"] = es.enter_context(nc.psum_tensor(tag + "_pt", [128, 512], BF16))
    return b


def emit_xnT(P, nc, es, co, x_dram, tag="xn", ntt=NT, tt0=0, bufs=None):
    ess = contextlib.ExitStack()
    if bufs is None:
        bufs = alloc_xnT_bufs(nc, es, tag, ntt)
    xnT, xs, xb, scr, cols, pt = (bufs["xnT"], bufs["xs"], bufs["xb"],
                                  bufs["scr"], bufs["cols"], bufs["pt"])
    ld = P.sem(); vs = P.sem(); tp = P.sem(); ev = P.sem()
    for i in range(ntt):
        tt = tt0 + i
        P.wait("sync", tp, tp.count)
        t_ld = P.dma("sync", xs.ap(), x_dram[tt * 128:(tt + 1) * 128, :], ld)
        P.wait("scalar", ld, t_ld)
        P.wait("vector", ld, t_ld)
        t_n = emit_ln_cols(P, co, xs.ap(), scr.ap(), cols.ap(), vs)
        P.wait("vector", vs, t_n)
        t_c = P.op("vector", lambda e: e.tensor_scalar(
            xb.ap(), xs.ap(), cols.ap()[:, 3:4], cols.ap()[:, 4:5],
            op0=ALU.mult, op1=ALU.add), inc=vs)
        for g in range(4):
            P.wait("tensor", vs, t_c)
            P.wait("tensor", ev, ev.count)
            tl = None
            for f in range(4):
                ic = g * 4 + f
                tl = P.op("tensor", lambda e, f=f, ic=ic:
                          e.transpose(pt.ap()[:, f * 128:(f + 1) * 128],
                                      xb.ap()[:, ic * 128:(ic + 1) * 128],
                                      co["ident_b"].ap()),
                          inc=tp if f == 3 else None)
            P.wait("scalar", tp, tl)

            def ev_fn(e, g=g, i=i):
                dst = xnT.ap().rearrange("p (ic t) -> p ic t", ic=ND)[
                    :, g * 4:(g + 1) * 4, i * 128:(i + 1) * 128]
                src = pt.ap().rearrange("p (f t) -> p f t", f=4)
                return e.activation(dst, src, AF.Identity)
            P.op("scalar", ev_fn, inc=ev)
    P.barrier()
    ess.close()
    return xnT


def emit_rope_tables(P, nc, es, io, pref):
    sinT = es.enter_context(nc.sbuf_tensor(pref + "sinT", [128, TC], F32))
    cosT = es.enter_context(nc.sbuf_tensor(pref + "cosT", [128, TC], F32))
    ess = contextlib.ExitStack()
    th = ess.enter_context(nc.sbuf_tensor(pref + "th", [128, 2 * TC], F32))
    ld = P.sem(); ok = P.sem()
    for which, dst in (("s", sinT), ("c", cosT)):
        P.dma("gpsimd", th.ap()[:, 0:TC], io[pref + which + "h"], ld)
        t1 = P.dma("gpsimd", th.ap()[:, TC:], io[pref + which + "l"], ld)
        P.wait("vector", ld, t1)
        t2 = P.op("vector", lambda e: e.tensor_add(
            th.ap()[:, 0:TC], th.ap()[:, 0:TC], th.ap()[:, TC:]), inc=ok)
        P.wait("scalar", ok, t2)
        t3 = P.op("scalar", lambda e, d=dst: e.activation(d.ap(), th.ap()[:, 0:TC], AF.Sin),
                  inc=ok)
        P.wait("gpsimd", ok, t3)
    P.barrier()
    ess.close()
    return sinT, cosT


def emit_proj_featmajor(P, nc, es, co, xnT, w_tiled, noc, out_dram, tag,
                        bias_col_sb=None, bias_off=0, scale=1.0,
                        rope=None, elu1=False, ntok=TC, pswap=None):
    NB = ntok // 512
    ess = contextlib.ExitStack()
    wf = ess.enter_context(nc.sbuf_tensor(tag + "_wf", [128, ND * 128], F32))
    wb = ess.enter_context(nc.sbuf_tensor(tag + "_wb", [128, ND * 128], BF16))
    qr = ess.enter_context(nc.sbuf_tensor(tag + "_qr", [128, 512], F32))
    qs = ess.enter_context(nc.sbuf_tensor(tag + "_qs", [128, 512], F32))
    m1 = ess.enter_context(nc.sbuf_tensor(tag + "_m1", [128, 512], F32))
    ob = ess.enter_context(nc.sbuf_tensor(tag + "_ob", [128, 512], BF16))
    ps = ess.enter_context(nc.psum_tensor(tag + "_ps", [128, 512], F32))
    psQ = ess.enter_context(nc.psum_tensor(tag + "_psQ", [128, 512], F32)) \
        if rope is not None else None
    wld = P.sem(); wc = P.sem(); mm = P.sem(); ev = P.sem(); st = P.sem()
    xv = xnT.ap().rearrange("p (ic t) -> p ic t", ic=ND)
    for oc in range(noc):
        P.wait("sync", mm, mm.count)
        t_w = P.dma("sync", wf.ap().rearrange("p (ic m) -> p ic m", ic=ND),
                    w_tiled[oc].rearrange("ic p m -> p ic m"), wld)
        P.wait("vector", wld, t_w)
        P.wait("vector", mm, mm.count)
        t_c = P.op("vector", lambda e: e.tensor_copy(wb.ap(), wf.ap()), inc=wc)
        for nb in range(NB):
            P.wait("tensor", wc, t_c)
            P.wait("tensor", ev, ev.count)
            t_m = None
            for ic in range(ND):
                t_m = P.op("tensor", lambda e, ic=ic, nb=nb:
                           e.matmul(ps.ap(), wb.ap()[:, ic * 128:(ic + 1) * 128],
                                    xv[:, ic, nb * 512:(nb + 1) * 512],
                                    start=(ic == 0), stop=(ic == ND - 1)),
                           inc=mm if ic == ND - 1 else None)
            P.wait("scalar", mm, t_m)
            P.wait("scalar", st, st.count)
            if bias_col_sb is not None:
                bc = bias_col_sb.ap()[:, bias_off + oc:bias_off + oc + 1]
                t_e = P.op("scalar", lambda e, bc=bc: e.activation(
                    qr.ap(), ps.ap(), AF.Identity, bias=bc, scale=scale), inc=ev)
            else:
                t_e = P.op("scalar", lambda e: e.activation(
                    qr.ap(), ps.ap(), AF.Identity, scale=scale), inc=ev)
            cur = qr
            P.wait("vector", ev, t_e)
            P.wait("vector", st, st.count)
            if rope is not None:
                sinT, cosT = rope
                sv = sinT.ap()[:, nb * 512:(nb + 1) * 512]
                cv = cosT.ap()[:, nb * 512:(nb + 1) * 512]
                P.wait("tensor", ev, t_e)
                t_pm = P.op("tensor", lambda e: e.matmul(
                    psQ.ap(), pswap.ap(), qr.ap(), start=True, stop=True), inc=mm)
                P.op("vector", lambda e, cv=cv: e.tensor_mul(m1.ap(), qr.ap(), cv))
                P.wait("vector", mm, t_pm)
                P.op("vector", lambda e, sv=sv: e.tensor_mul(qs.ap(), psQ.ap(), sv))
                t_e = P.op("vector", lambda e: e.tensor_add(m1.ap(), m1.ap(), qs.ap()),
                           inc=ev)
                cur = m1
            if elu1:
                t_e = P.op("vector", lambda e, c=cur: e.tensor_scalar(
                    qs.ap(), c.ap(), 0.0, None, op0=ALU.min), inc=ev)
                P.wait("scalar", ev, t_e)
                t_e = P.op("scalar", lambda e: e.activation(qs.ap(), qs.ap(), AF.Exp),
                           inc=ev)
                P.wait("vector", ev, t_e)
                P.op("vector", lambda e, c=cur: e.tensor_scalar(
                    c.ap(), c.ap(), 0.0, None, op0=ALU.max))
                t_e = P.op("vector", lambda e, c=cur: e.tensor_add(c.ap(), c.ap(), qs.ap()),
                           inc=ev)
            P.wait("vector", ev, t_e)
            t_b = P.op("vector", lambda e, c=cur: e.tensor_copy(ob.ap(), c.ap()), inc=ev)
            P.wait("gpsimd", ev, t_b)
            P.dma("gpsimd", out_dram[oc * 128:(oc + 1) * 128, nb * 512:(nb + 1) * 512],
                  ob.ap(), st)
    P.wait("gpsimd", st, st.count)
    P.barrier()
    ess.close()


def emit_proj_tokmajor(P, nc, es, co, lhsT_sb, w_rows, nic, out_dram, tag,
                       bias_bcast=None, ntok=TC, nof=D, per_unit_epilogue=None):
    ntt = ntok // 128
    NOB = nof // 512
    ess = contextlib.ExitStack()
    wfs = ess.enter_context(nc.sbuf_tensor(tag + "_wfs", [128, 512], F32))
    wbp = ess.enter_context(nc.sbuf_tensor(tag + "_wbp", [128, nic * 512], BF16))
    obuf = ess.enter_context(nc.sbuf_tensor(tag + "_obf", [128, 512], BF16))
    ps = ess.enter_context(nc.psum_tensor(tag + "_ps", [128, 512], F32))
    wld = P.sem(); wc = P.sem(); mm = P.sem(); ev = P.sem(); st = P.sem()
    xv = lhsT_sb.ap().rearrange("p (ic t) -> p ic t", ic=nic)
    for tt in range(ntt):
        for obk in range(NOB):
            P.wait("sync", mm, mm.count)
            for ic in range(nic):
                t_w = P.dma("sync", wfs.ap(),
                            w_rows[ic * 128:(ic + 1) * 128, obk * 512:(obk + 1) * 512],
                            wld)
                P.wait("vector", wld, t_w)
                t_c = P.op("vector", lambda e, ic=ic: e.tensor_copy(
                    wbp.ap()[:, ic * 512:(ic + 1) * 512], wfs.ap()), inc=wc)
                P.wait("sync", wc, t_c)
            P.wait("tensor", wc, wc.count)
            P.wait("tensor", ev, ev.count)
            t_m = None
            for ic in range(nic):
                t_m = P.op("tensor", lambda e, ic=ic, tt=tt, obk=obk:
                           e.matmul(ps.ap(), xv[:, ic, tt * 128:(tt + 1) * 128],
                                    wbp.ap()[:, ic * 512:(ic + 1) * 512],
                                    start=(ic == 0), stop=(ic == nic - 1)),
                           inc=mm if ic == nic - 1 else None)
            if per_unit_epilogue is not None:
                per_unit_epilogue(tt, obk, ps, mm, t_m, ev)
            else:
                P.wait("vector", mm, t_m)
                P.wait("vector", st, st.count)
                if bias_bcast is not None:
                    t_e = P.op("vector", lambda e, obk=obk: e.tensor_add(
                        obuf.ap(), ps.ap(),
                        bias_bcast.ap()[:, obk * 512:(obk + 1) * 512]), inc=ev)
                else:
                    t_e = P.op("vector", lambda e: e.tensor_copy(obuf.ap(), ps.ap()),
                               inc=ev)
                P.wait("gpsimd", ev, t_e)
                P.dma("gpsimd", out_dram[tt * 128:(tt + 1) * 128,
                                         obk * 512:(obk + 1) * 512], obuf.ap(), st)
                P.wait("vector", st, st.count)
    P.wait("gpsimd", st, st.count)
    P.barrier()
    ess.close()



def build_l0qkv(nc, io):
    es = contextlib.ExitStack()
    with es:
        P = Prog(nc)
        co = emit_consts(P, nc, es)
        xnT = emit_xnT(P, nc, es, co, io["x0"])
        sinT, cosT = emit_rope_tables(P, nc, es, io, "r0")
        pswap = es.enter_context(nc.sbuf_tensor("pswap_s", [128, 128], F32))
        psld = P.sem()
        t = P.dma("sync", pswap.ap(), io["pswap"], psld)
        P.wait("tensor", psld, t)
        bcol = es.enter_context(nc.sbuf_tensor("bcol", [128, 2 * ND], F32))
        bvb = es.enter_context(nc.sbuf_tensor("bvb", [128, D], F32))
        bld = P.sem(); bsc = P.sem()
        P.dma("gpsimd", bcol.ap()[:, 0:ND], io["bq_c"], bld)
        P.dma("gpsimd", bcol.ap()[:, ND:], io["bk_c"], bld)
        t = P.dma("gpsimd", bvb.ap(), io["bv_row"].broadcast_to([128, D]), bld)
        P.wait("vector", bld, t)
        P.op("vector", lambda e: e.tensor_scalar(bcol.ap()[:, 0:ND], bcol.ap()[:, 0:ND],
                                                 SCALE0, None, op0=ALU.mult), inc=bsc)
        P.barrier()
        emit_proj_featmajor(P, nc, es, co, xnT, io["wq_t"], ND, io["qT0"], "q",
                            bias_col_sb=bcol, bias_off=0, scale=SCALE0,
                            rope=(sinT, cosT), pswap=pswap)
        emit_proj_featmajor(P, nc, es, co, xnT, io["wk_t"], ND, io["kT0"], "k",
                            bias_col_sb=bcol, bias_off=ND, scale=1.0,
                            rope=(sinT, cosT), pswap=pswap)
        emit_proj_tokmajor(P, nc, es, co, xnT, io["wv_r"], ND, io["v0"], "v",
                           bias_bcast=bvb)
        P.finalize(es)
    return nc


def build_l0attn(nc, io, p):
    ranges = own_ranges(p)
    kmax = max(r[1] for r in ranges)
    nkt_all = kmax // 128
    es = contextlib.ExitStack()
    with es:
        P = Prog(nc)
        co = emit_consts(P, nc, es)
        masks = es.enter_context(nc.sbuf_tensor("masks", [128, 4 * 512], BF16))
        mk = P.sem()
        t = None
        for r in range(4):
            P.op("gpsimd", lambda e, r=r: e.memset(masks.ap()[:, r * 512:(r + 1) * 512], 0.0))
            t = P.op("gpsimd", lambda e, r=r: e.affine_select(
                out=masks.ap()[:, r * 512:(r + 1) * 512],
                in_=masks.ap()[:, r * 512:(r + 1) * 512],
                compare_op=ALU.is_gt, fill=1.0, base=r * 128,
                pattern=[[-1, 512]], channel_multiplier=1), inc=mk)
        for e in ("tensor", "vector", "scalar"):
            P.wait(e, mk, t)
        qT = es.enter_context(nc.sbuf_tensor("qT", [128, ND * TC], BF16))
        qld = P.sem()
        t = P.dma("sync", qT.ap().rearrange("p (h t) -> p h t", h=ND),
                  io["qT0"].rearrange("(h p) t -> p h t", p=128), qld)
        P.wait("tensor", qld, t)
        kT = es.enter_context(nc.sbuf_tensor("kT_s", [128, nkt_all * 128], BF16))
        vS = es.enter_context(nc.sbuf_tensor("vS", [128, nkt_all * 128], BF16))
        pbf = es.enter_context(nc.sbuf_tensor("pbf", [128, 512], BF16))
        nscr = es.enter_context(nc.sbuf_tensor("nscr", [128, 512], F32))
        rden = es.enter_context(nc.sbuf_tensor("rden", [1, 512], F32))
        aob = es.enter_context(nc.sbuf_tensor("aob", [128, 512], BF16))
        psS = es.enter_context(nc.psum_tensor("psS", [128, 512], F32))
        psA = es.enter_context(nc.psum_tensor("psA", [128, 512], F32))
        psD = es.enter_context(nc.psum_tensor("psD", [1, 512], F32))
        psB = es.enter_context(nc.psum_tensor("psB", [128, 512], F32))
        kld = P.sem(); mmS = P.sem(); exq = P.sem(); kkt = P.sem()
        hdn = P.sem(); evq = P.sem(); st = P.sem()
        qblocks = []
        for li, (g0, g1) in enumerate(ranges):
            for jj in range((g1 - g0) // 512):
                qblocks.append((li * 1024 + jj * 512, g0 + jj * 512))
        qv = qT.ap().rearrange("p (hh t) -> p hh t", hh=ND)
        vv = vS.ap().rearrange("p (kt d) -> p kt d", kt=nkt_all)
        for h in range(H):
            P.wait("sync", hdn, hdn.count)
            t_k = P.dma("sync", kT.ap(), io["kT"][h * 128:(h + 1) * 128, 0:kmax], kld)
            for k8 in range(0, nkt_all, 8):
                k8e = min(k8 + 8, nkt_all)
                t_k = P.dma("sync", vv[:, k8:k8e, :],
                            io["v"].rearrange("(kt p) d -> p kt d", p=128)[
                                :, k8:k8e, h * 128:(h + 1) * 128], kld)
            P.wait("tensor", kld, t_k)
            for (lq, gq) in qblocks:
                nkt = (gq + 512) // 128
                qrhs = qv[:, h, lq:lq + 512]
                for kt in range(nkt):
                    P.wait("tensor", exq, exq.count)
                    t_s = P.op("tensor", lambda e, kt=kt, qrhs=qrhs:
                               e.matmul(psS.ap(), kT.ap()[:, kt * 128:(kt + 1) * 128],
                                        qrhs, start=True, stop=True), inc=mmS)
                    P.wait("scalar", mmS, t_s)
                    P.wait("scalar", kkt, kkt.count)
                    t_x = P.op("scalar", lambda e: e.activation(
                        pbf.ap(), psS.ap(), AF.Exp, bias=co["negC"].ap()), inc=exq)
                    r = kt * 128 - gq
                    if r >= 0:
                        P.wait("vector", exq, t_x)
                        t_x = P.op("vector", lambda e, r=r: e.tensor_mul(
                            pbf.ap(), pbf.ap(),
                            masks.ap()[:, (r // 128) * 512:(r // 128) * 512 + 512]),
                            inc=exq)
                    P.wait("tensor", exq, t_x)
                    P.op("tensor", lambda e, kt=kt, first=(kt == 0), last=(kt == nkt - 1):
                         e.matmul(psA.ap(), vv[:, kt, :], pbf.ap(),
                                  start=first, stop=last))
                    P.op("tensor", lambda e, first=(kt == 0), last=(kt == nkt - 1):
                         e.matmul(psD.ap(), co["ones_c_b"].ap(), pbf.ap(),
                                  start=first, stop=last), inc=kkt)
                P.wait("vector", kkt, kkt.count)
                P.wait("vector", st, st.count)
                t_r = P.op("vector", lambda e: e.reciprocal(rden.ap(), psD.ap()), inc=evq)
                P.wait("tensor", evq, t_r)
                t_b = P.op("tensor", lambda e: e.matmul(
                    psB.ap(), co["ones_r_f"].ap(), rden.ap(), start=True, stop=True),
                    inc=hdn)
                t_n = P.op("vector", lambda e: e.tensor_copy(nscr.ap(), psA.ap()), inc=evq)
                P.wait("vector", hdn, t_b)
                t_o = P.op("vector", lambda e: e.tensor_mul(aob.ap(), nscr.ap(), psB.ap()),
                           inc=evq)
                P.wait("gpsimd", evq, t_o)
                P.dma("gpsimd", io["aoT0"][h * 128:(h + 1) * 128, lq:lq + 512],
                      aob.ap(), st)
                P.wait("tensor", evq, t_o)
        P.wait("sync", st, st.count)
        P.finalize(es)
    return nc


def build_oproj(nc, io):
    es = contextlib.ExitStack()
    with es:
        P = Prog(nc)
        co = emit_consts(P, nc, es)
        aoT = es.enter_context(nc.sbuf_tensor("aoT_s", [128, ND * TC], BF16))
        qld = P.sem()
        t = P.dma("sync", aoT.ap().rearrange("p (c t) -> p c t", c=ND),
                  io["aoT"].rearrange("(c p) t -> p c t", p=128), qld)
        P.wait("tensor", qld, t)
        bob = es.enter_context(nc.sbuf_tensor("bob", [128, D], F32))
        bobld = P.sem()
        t_b = P.dma("gpsimd", bob.ap(), io["bo_row"].broadcast_to([128, D]), bobld)
        raw = es.enter_context(nc.sbuf_tensor("raw", [128, D], F32))
        xs = es.enter_context(nc.sbuf_tensor("oxs", [128, D], F32))
        scr = es.enter_context(nc.sbuf_tensor("oscr", [128, D], F32))
        cols = es.enter_context(nc.sbuf_tensor("ocols", [128, 6], F32))
        ob2 = es.enter_context(nc.sbuf_tensor("oob", [128, D], F32))
        vs = P.sem(); st = P.sem(); ld2 = P.sem()

        def epi(tt, obk, ps, mm, t_m, ev):
            P.wait("vector", mm, t_m)
            P.wait("vector", bobld, t_b)
            t_e = P.op("vector", lambda e, obk=obk: e.tensor_add(
                raw.ap()[:, obk * 512:(obk + 1) * 512], ps.ap(),
                bob.ap()[:, obk * 512:(obk + 1) * 512]), inc=ev)
            P.wait("tensor", ev, t_e)
            if obk == 3:
                P.wait("sync", vs, vs.count)
                t_x = P.dma("sync", xs.ap(), io["x_in"][tt * 128:(tt + 1) * 128, :], ld2)
                P.wait("scalar", ev, t_e)
                t_n = emit_ln_cols(P, co, raw.ap(), scr.ap(), cols.ap(), vs)
                P.wait("vector", vs, t_n)
                P.op("vector", lambda e: e.tensor_scalar(
                    scr.ap(), raw.ap(), cols.ap()[:, 3:4], cols.ap()[:, 4:5],
                    op0=ALU.mult, op1=ALU.add))
                P.wait("vector", ld2, t_x)
                t_s = P.op("vector", lambda e: e.tensor_add(ob2.ap(), xs.ap(), scr.ap()),
                           inc=vs)
                P.wait("gpsimd", vs, t_s)
                P.dma("gpsimd", io["x_out"][tt * 128:(tt + 1) * 128, :], ob2.ap(), st)
                P.wait("vector", st, st.count)
                P.wait("scalar", st, st.count)

        emit_proj_tokmajor(P, nc, es, co, aoT, io["wo_r"], ND, None, "o",
                           per_unit_epilogue=epi)
        P.wait("sync", st, st.count)
        P.finalize(es)
    return nc


def build_l0ffn(nc, io):
    es = contextlib.ExitStack()
    with es:
        P = Prog(nc)
        co = emit_consts(P, nc, es)
        gsc = es.enter_context(nc.sbuf_tensor("gsc", [128, 2 * NH0], F32))
        dsb = es.enter_context(nc.sbuf_tensor("dsb", [128, D], F32))
        bld = P.sem()
        P.dma("gpsimd", gsc.ap()[:, 0:NH0], io["gate_s_c"], bld)
        P.dma("gpsimd", gsc.ap()[:, NH0:], io["up_s_c"], bld)
        P.dma("gpsimd", dsb.ap(), io["down_s_row"].broadcast_to([128, D]), bld)
        NQ = 4
        QT = TC // NQ
        hdnT = es.enter_context(nc.sbuf_tensor("hdnT", [128, NH0 * QT], BF16))
        rawh = es.enter_context(nc.sbuf_tensor("rawh", [128, 4 * D], F32))
        xbufs = alloc_xnT_bufs(nc, es, "fx", QT // 128)
        wf = es.enter_context(nc.sbuf_tensor("fwf", [128, 2 * ND * 128], F32))
        wb = es.enter_context(nc.sbuf_tensor("fwb", [128, 2 * ND * 128], BF16))
        sil = es.enter_context(nc.sbuf_tensor("fsil", [128, 512], F32))
        ups = es.enter_context(nc.sbuf_tensor("fups", [128, 512], F32))
        psG = es.enter_context(nc.psum_tensor("psG", [128, 512], F32))
        psU = es.enter_context(nc.psum_tensor("psU", [128, 512], F32))
        dwf = es.enter_context(nc.sbuf_tensor("dwf", [128, 512], F32))
        dwb = es.enter_context(nc.sbuf_tensor("dwb", [128, 512], BF16))
        psE = [es.enter_context(nc.psum_tensor(f"psE{i}", [128, 512], F32))
               for i in range(4)]
        pxs = es.enter_context(nc.sbuf_tensor("pxs", [128, D], F32))
        pscr = es.enter_context(nc.sbuf_tensor("pscr", [128, D], F32))
        pcols = es.enter_context(nc.sbuf_tensor("pcols", [128, 6], F32))
        pob = es.enter_context(nc.sbuf_tensor("pob", [128, D], F32))
        P.barrier()
        for q in range(NQ):
            tt0 = q * (QT // 128)
            xnT = emit_xnT(P, nc, es, co, io["x1"], tag=f"fx", ntt=QT // 128,
                           tt0=tt0, bufs=xbufs)
            wld = P.sem(); wc = P.sem(); mm = P.sem(); ev = P.sem()
            xv = xnT.ap().rearrange("p (ic t) -> p ic t", ic=ND)
            for hc in range(NH0):
                P.wait("sync", mm, mm.count)
                P.dma("sync", wf.ap().rearrange("p (q2 ic m) -> p q2 ic m", q2=2, ic=ND, m=128)[:, 0],
                      io["gate_t"][hc].rearrange("ic p m -> p ic m"), wld)
                t_w = P.dma("sync", wf.ap().rearrange("p (q2 ic m) -> p q2 ic m", q2=2, ic=ND, m=128)[:, 1],
                            io["up_t"][hc].rearrange("ic p m -> p ic m"), wld)
                P.wait("vector", wld, t_w)
                P.wait("vector", mm, mm.count)
                t_c = P.op("vector", lambda e: e.tensor_copy(wb.ap(), wf.ap()), inc=wc)
                P.wait("tensor", wc, t_c)
                P.wait("tensor", ev, ev.count)
                for ic in range(ND):
                    P.op("tensor", lambda e, ic=ic:
                         e.matmul(psG.ap(), wb.ap()[:, ic * 128:(ic + 1) * 128],
                                  xv[:, ic, 0:512],
                                  start=(ic == 0), stop=(ic == ND - 1)))
                t_m = None
                for ic in range(ND):
                    t_m = P.op("tensor", lambda e, ic=ic:
                               e.matmul(psU.ap(),
                                        wb.ap()[:, (ND + ic) * 128:(ND + ic + 1) * 128],
                                        xv[:, ic, 0:512],
                                        start=(ic == 0), stop=(ic == ND - 1)),
                               inc=mm if ic == ND - 1 else None)
                P.wait("scalar", mm, t_m)
                P.wait("scalar", ev, max(0, ev.count - 1))
                t_s = P.op("scalar", lambda e, hc=hc: e.activation(
                    sil.ap(), psG.ap(), AF.Silu, scale=gsc.ap()[:, hc:hc + 1]), inc=ev)
                P.wait("vector", ev, t_s)
                P.wait("vector", bld, 48)
                t_u = P.op("vector", lambda e, hc=hc: e.tensor_scalar(
                    ups.ap(), psU.ap(), gsc.ap()[:, NH0 + hc:NH0 + hc + 1],
                    None, op0=ALU.mult), inc=ev)
                t_h = P.op("vector", lambda e, hc=hc: e.tensor_mul(
                    hdnT.ap().rearrange("p (hc t) -> p hc t", hc=NH0)[:, hc, :],
                    sil.ap(), ups.ap()), inc=ev)
                P.wait("tensor", ev, t_h)
            P.barrier()
            wld = P.sem(); wc = P.sem(); mm = P.sem(); ev = P.sem()
            hv = hdnT.ap().rearrange("p (hc t) -> p hc t", hc=NH0)
            for obk in range(4):
                for hc in range(NH0):
                    P.wait("sync", mm, mm.count)
                    t_w = P.dma("sync", dwf.ap(),
                                io["down_r"][hc * 128:(hc + 1) * 128,
                                             obk * 512:(obk + 1) * 512], wld)
                    P.wait("vector", wld, t_w)
                    P.wait("vector", mm, mm.count)
                    t_c = P.op("vector", lambda e: e.tensor_copy(dwb.ap(), dwf.ap()),
                               inc=wc)
                    P.wait("tensor", wc, t_c)
                    if hc == 0:
                        P.wait("tensor", ev, ev.count)
                    t_m = None
                    for tt in range(4):
                        t_m = P.op("tensor", lambda e, tt=tt, hc=hc:
                                   e.matmul(psE[tt].ap(),
                                            hv[:, hc, tt * 128:(tt + 1) * 128],
                                            dwb.ap(), start=(hc == 0),
                                            stop=(hc == NH0 - 1)),
                                   inc=mm if tt == 3 else None)
                P.wait("vector", mm, mm.count)
                for tt in range(4):
                    P.op("vector", lambda e, tt=tt, obk=obk: e.tensor_mul(
                        rawh.ap()[:, tt * D + obk * 512: tt * D + obk * 512 + 512],
                        psE[tt].ap(), dsb.ap()[:, obk * 512:(obk + 1) * 512]),
                        inc=ev)
                P.wait("tensor", ev, ev.count)
            P.barrier()
            vs = P.sem(); st = P.sem(); ld2 = P.sem()
            for i in range(4):
                tt = tt0 + i
                raw_ap = rawh.ap()[:, i * D:(i + 1) * D]
                P.wait("sync", vs, vs.count)
                t_x = P.dma("sync", pxs.ap(), io["x1"][tt * 128:(tt + 1) * 128, :], ld2)
                t_n = emit_ln_cols(P, co, raw_ap, pscr.ap(), pcols.ap(), vs)
                P.wait("vector", vs, t_n)
                P.op("vector", lambda e, raw_ap=raw_ap: e.tensor_scalar(
                    pscr.ap(), raw_ap, pcols.ap()[:, 3:4], pcols.ap()[:, 4:5],
                    op0=ALU.mult, op1=ALU.add))
                P.wait("vector", ld2, t_x)
                t_s = P.op("vector", lambda e: e.tensor_add(pob.ap(), pxs.ap(), pscr.ap()),
                           inc=vs)
                P.wait("gpsimd", vs, t_s)
                P.dma("gpsimd", io["x2"][tt * 128:(tt + 1) * 128, :], pob.ap(), st)
                P.wait("vector", st, st.count)
                P.wait("scalar", st, st.count)
            P.wait("sync", st, st.count)
            P.wait("gpsimd", st, st.count)
            P.barrier()
        P.finalize(es)
    return nc


def build_l1qkv(nc, io):
    es = contextlib.ExitStack()
    with es:
        P = Prog(nc)
        co = emit_consts(P, nc, es)
        xnT = emit_xnT(P, nc, es, co, io["x2"])
        sinT, cosT = emit_rope_tables(P, nc, es, io, "r1")
        pswap = es.enter_context(nc.sbuf_tensor("pswap_s", [128, 128], F32))
        psld = P.sem()
        t = P.dma("sync", pswap.ap(), io["pswap"], psld)
        P.wait("tensor", psld, t)
        emit_proj_featmajor(P, nc, es, co, xnT, io["wq1_t"], ND, io["qT1"], "q1",
                            rope=(sinT, cosT), elu1=True, pswap=pswap)
        emit_proj_featmajor(P, nc, es, co, xnT, io["wk1_t"], ND, io["kT1"], "k1",
                            rope=(sinT, cosT), elu1=True, pswap=pswap)
        emit_proj_tokmajor(P, nc, es, co, xnT, io["wv1_r"], ND, io["v1"], "v1")
        esm = contextlib.ExitStack()
        with esm:
            kTh = esm.enter_context(nc.sbuf_tensor("kTh", [128, 256], BF16))
            kc = esm.enter_context(nc.sbuf_tensor("kc", [128, 2 * 128], BF16))
            vc = esm.enter_context(nc.sbuf_tensor("vc", [128, 2 * 128], BF16))
            mout = esm.enter_context(nc.sbuf_tensor("mout", [128, 128], F32))
            zacc = esm.enter_context(nc.sbuf_tensor("zacc", [128, 128], F32))
            psT = esm.enter_context(nc.psum_tensor("psT", [128, 256], BF16))
            psM = esm.enter_context(nc.psum_tensor("psM", [128, 128], F32))
            psZ = esm.enter_context(nc.psum_tensor("psZ", [128, 1], F32))
            ld = P.sem(); tp = P.sem(); mm = P.sem(); ev = P.sem(); st = P.sem()
            for lc in range(8):
                for h in range(H):
                    P.wait("sync", mm, mm.count)
                    t_l = P.dma("sync", kTh.ap(),
                                io["kT1"][h * 128:(h + 1) * 128, lc * 256:(lc + 1) * 256], ld)
                    t_l = P.dma("sync", vc.ap().rearrange("p (kt d) -> p kt d", kt=2),
                                io["v1"].rearrange("(kt p) d -> p kt d", p=128)[
                                    :, lc * 2:lc * 2 + 2, h * 128:(h + 1) * 128], ld)
                    P.wait("tensor", ld, t_l)
                    P.wait("tensor", ev, ev.count)
                    t_t = None
                    for kt in range(2):
                        t_t = P.op("tensor", lambda e, kt=kt: e.transpose(
                            psT.ap()[:, kt * 128:(kt + 1) * 128],
                            kTh.ap()[:, kt * 128:(kt + 1) * 128],
                            co["ident_b"].ap()), inc=tp if kt == 1 else None)
                    P.wait("vector", tp, t_t)
                    t_c = P.op("vector", lambda e: e.tensor_copy(kc.ap(), psT.ap()), inc=ev)
                    P.wait("tensor", ev, t_c)
                    t_m = None
                    for kt in range(2):
                        P.op("tensor", lambda e, kt=kt: e.matmul(
                            psM.ap(), kc.ap()[:, kt * 128:(kt + 1) * 128],
                            vc.ap()[:, kt * 128:(kt + 1) * 128],
                            start=(kt == 0), stop=(kt == 1)))
                        t_m = P.op("tensor", lambda e, kt=kt: e.matmul(
                            psZ.ap(), kc.ap()[:, kt * 128:(kt + 1) * 128],
                            co["ones_c_b"].ap(),
                            start=(kt == 0), stop=(kt == 1)),
                            inc=mm if kt == 1 else None)
                    P.wait("vector", mm, t_m)
                    P.wait("vector", st, st.count)
                    t_e = P.op("vector", lambda e: e.tensor_copy(mout.ap(), psM.ap()), inc=ev)
                    t_z = P.op("vector", lambda e, lc=lc, h=h: e.tensor_copy(
                        zacc.ap()[:, lc * 16 + h:lc * 16 + h + 1], psZ.ap()), inc=ev)
                    P.wait("gpsimd", ev, t_e)
                    P.dma("gpsimd", io["M_own"][lc, h], mout.ap(), st)
                    P.wait("tensor", ev, t_z)
            P.wait("gpsimd", ev, ev.count)
            P.dma("gpsimd", io["z_cols"], zacc.ap(), st)
            P.wait("sync", st, st.count)
            P.barrier()
        P.finalize(es)
    return nc


def build_l1attn(nc, io, p):
    es = contextlib.ExitStack()
    with es:
        P = Prog(nc)
        co = emit_consts(P, nc, es)
        chunks = own_chunks(p)
        masks = es.enter_context(nc.sbuf_tensor("masks1", [128, 2 * 256], BF16))
        mk = P.sem()
        t = None
        for r in range(2):
            P.op("gpsimd", lambda e, r=r: e.memset(masks.ap()[:, r * 256:(r + 1) * 256], 0.0))
            t = P.op("gpsimd", lambda e, r=r: e.affine_select(
                out=masks.ap()[:, r * 256:(r + 1) * 256],
                in_=masks.ap()[:, r * 256:(r + 1) * 256],
                compare_op=ALU.is_gt, fill=1.0, base=r * 128,
                pattern=[[-1, 256]], channel_multiplier=1), inc=mk)
        for e in ("tensor", "vector", "scalar"):
            P.wait(e, mk, t)
        qT = es.enter_context(nc.sbuf_tensor("qT1s", [128, ND * TC], BF16))
        kT = es.enter_context(nc.sbuf_tensor("kT1s", [128, ND * TC], BF16))
        zc = es.enter_context(nc.sbuf_tensor("zc1", [128, 256], F32))
        qld = P.sem()
        P.dma("sync", qT.ap().rearrange("p (h t) -> p h t", h=ND),
              io["qT1"].rearrange("(h p) t -> p h t", p=128), qld)
        P.dma("sync", kT.ap().rearrange("p (h t) -> p h t", h=ND),
              io["kT1"].rearrange("(h p) t -> p h t", p=128), qld)
        t = P.dma("sync", zc.ap(), io["z_all_cols"], qld)
        for e in ("tensor", "vector"):
            P.wait(e, qld, t)
        S_run = es.enter_context(nc.sbuf_tensor("S_run", [128, 128], F32))
        z_run = es.enter_context(nc.sbuf_tensor("z_run", [128, 1], F32))
        Sb = es.enter_context(nc.sbuf_tensor("Sb", [128, 128], BF16))
        zb = es.enter_context(nc.sbuf_tensor("zb", [128, 1], BF16))
        Mt = es.enter_context(nc.sbuf_tensor("Mt", [128, 128], F32))
        vc = es.enter_context(nc.sbuf_tensor("vc1", [128, 2 * 128], BF16))
        pa = es.enter_context(nc.sbuf_tensor("pa", [128, 2 * 256], BF16))
        nscr = es.enter_context(nc.sbuf_tensor("nscr1", [128, 256], F32))
        rden = es.enter_context(nc.sbuf_tensor("rden1", [1, 256], F32))
        aob = es.enter_context(nc.sbuf_tensor("aob1", [128, 256], BF16))
        psS = es.enter_context(nc.psum_tensor("psS1", [128, 256], F32))
        psN = es.enter_context(nc.psum_tensor("psN1", [128, 256], F32))
        psD = es.enter_context(nc.psum_tensor("psD1", [1, 256], F32))
        psB = es.enter_context(nc.psum_tensor("psB1", [128, 256], F32))
        ld = P.sem(); mm = P.sem(); ev = P.sem(); st = P.sem(); acc = P.sem()
        ldM = P.sem()
        qv = qT.ap().rearrange("p (h t) -> p h t", h=ND)
        kv = kT.ap().rearrange("p (h t) -> p h t", h=ND)
        for h in range(H):
            P.edone("gpsimd", "vector")
            t_z0 = P.op("gpsimd", lambda e: e.memset(S_run.ap(), 0.0))
            t_z0 = P.op("gpsimd", lambda e: e.memset(z_run.ap(), 0.0), inc=acc)
            P.wait("vector", acc, acc.count)
            P.edone("vector", "gpsimd")
            P.edone("tensor", "gpsimd")
            for g in range(16):
                if g in chunks:
                    lc = chunks.index(g)
                    P.wait("vector", mm, mm.count)
                    t_cb = P.op("vector", lambda e: e.tensor_copy(Sb.ap(), S_run.ap()))
                    t_cb = P.op("vector", lambda e: e.tensor_copy(zb.ap(), z_run.ap()),
                                inc=ev)
                    P.wait("sync", mm, mm.count)
                    t_l = P.dma("sync", vc.ap().rearrange("p (kt d) -> p kt d", kt=2),
                                io["v1"].rearrange("(kt p) d -> p kt d", p=128)[
                                    :, lc * 2:lc * 2 + 2, h * 128:(h + 1) * 128], ld)
                    for kt in range(2):
                        P.wait("tensor", ev, ev.count)
                        t_s = P.op("tensor", lambda e, kt=kt, lc=lc: e.matmul(
                            psS.ap(), kv[:, h, lc * 256 + kt * 128:lc * 256 + (kt + 1) * 128],
                            qv[:, h, lc * 256:(lc + 1) * 256], start=True, stop=True),
                            inc=mm)
                        P.wait("vector", mm, t_s)
                        t_p = P.op("vector", lambda e, kt=kt: e.tensor_mul(
                            pa.ap()[:, kt * 256:(kt + 1) * 256], psS.ap(),
                            masks.ap()[:, kt * 256:(kt + 1) * 256]), inc=ev)
                    P.wait("tensor", ev, ev.count)
                    P.wait("tensor", ld, ld.count)
                    for kt in range(2):
                        P.op("tensor", lambda e, kt=kt: e.matmul(
                            psN.ap(), vc.ap()[:, kt * 128:(kt + 1) * 128],
                            pa.ap()[:, kt * 256:(kt + 1) * 256],
                            start=(kt == 0), stop=False))
                    P.op("tensor", lambda e, lc=lc: e.matmul(
                        psN.ap(), Sb.ap(), qv[:, h, lc * 256:(lc + 1) * 256],
                        start=False, stop=True))
                    for kt in range(2):
                        P.op("tensor", lambda e, kt=kt: e.matmul(
                            psD.ap(), co["ones_c_b"].ap(),
                            pa.ap()[:, kt * 256:(kt + 1) * 256],
                            start=(kt == 0), stop=False))
                    t_m = P.op("tensor", lambda e, lc=lc: e.matmul(
                        psD.ap(), zb.ap(), qv[:, h, lc * 256:(lc + 1) * 256],
                        start=False, stop=True), inc=mm)
                    P.wait("vector", mm, t_m)
                    P.wait("vector", st, st.count)
                    P.op("vector", lambda e: e.tensor_scalar(
                        rden.ap(), psD.ap(), 1e-6, None, op0=ALU.max))
                    t_r = P.op("vector", lambda e: e.reciprocal(rden.ap(), rden.ap()),
                               inc=ev)
                    P.wait("tensor", ev, t_r)
                    t_bc = P.op("tensor", lambda e: e.matmul(
                        psB.ap(), co["ones_r_f"].ap(), rden.ap(), start=True, stop=True),
                        inc=mm)
                    t_n = P.op("vector", lambda e: e.tensor_copy(nscr.ap(), psN.ap()),
                               inc=ev)
                    P.wait("vector", mm, t_bc)
                    t_o = P.op("vector", lambda e: e.tensor_mul(
                        aob.ap(), nscr.ap(), psB.ap()), inc=ev)
                    P.wait("gpsimd", ev, t_o)
                    P.dma("gpsimd", io["aoT1"][h * 128:(h + 1) * 128,
                                               lc * 256:(lc + 1) * 256], aob.ap(), st)
                    P.wait("tensor", ev, t_o)
                P.wait("sync", acc, acc.count)
                t_l = P.dma("sync", Mt.ap(), io["M_all"][g, h], ldM)
                P.wait("vector", ldM, t_l)
                P.wait("vector", ev, ev.count)
                t_a = P.op("vector", lambda e: e.tensor_add(S_run.ap(), S_run.ap(), Mt.ap()))
                t_a = P.op("vector", lambda e, g=g, h=h: e.tensor_add(
                    z_run.ap(), z_run.ap(), zc.ap()[:, g * 16 + h:g * 16 + h + 1]),
                    inc=acc)
                P.wait("sync", acc, t_a)
        P.wait("sync", st, st.count)
        P.finalize(es)
    return nc


def build_l1tern(nc, io):
    es = contextlib.ExitStack()
    with es:
        P = Prog(nc)
        co = emit_consts(P, nc, es)
        wf = es.enter_context(nc.sbuf_tensor("twf", [128, ND * 128], F32))
        ab = es.enter_context(nc.sbuf_tensor("tab", [128, ND * 128], F32))
        s1 = es.enter_context(nc.sbuf_tensor("ts1", [128, ND * 128], F32))
        s2 = es.enter_context(nc.sbuf_tensor("ts2", [128, ND * 128], F32))
        obf = es.enter_context(nc.sbuf_tensor("tob", [128, ND * 128], BF16))
        msc = es.enter_context(nc.sbuf_tensor("tmsc", [1, 8], F32))
        thr = es.enter_context(nc.sbuf_tensor("tthr", [128, 4], F32))
        psA = es.enter_context(nc.psum_tensor("psTa", [1, 512], F32))
        psB = es.enter_context(nc.psum_tensor("psTb", [128, 1], F32))
        ld = P.sem(); cst = P.sem(); mm = P.sem(); ev = P.sem(); st = P.sem()
        specs = [("fc", io["w_fc_t"], NH1, True, io["tern_fc_t"]),
                 ("pj", io["w_proj_r"], NH1, False, io["tern_proj_r"])]
        NTOT = H1 * D

        def bcast_scalar(src_ap, dst_col_ap):
            P.edone("tensor", "vector")
            t_bb = P.op("tensor", lambda e: e.matmul(
                psB.ap(), co["ones_r_f"].ap(), src_ap, start=True, stop=True), inc=mm)
            P.wait("vector", mm, t_bb)
            return P.op("vector", lambda e: e.tensor_copy(dst_col_ap, psB.ap()), inc=ev)

        for mi, (tag, wsrc, nb_, tiled, dst) in enumerate(specs):
            nbands = NH1 if tiled else (H1 // 128)
            for ph in range(2):
                for bi in range(nbands):
                    P.wait("sync", mm, mm.count)
                    if tiled:
                        t_w = P.dma("sync", wf.ap().rearrange("p (ic m) -> p ic m", ic=ND),
                                    wsrc[bi].rearrange("ic p m -> p ic m"), ld)
                    else:
                        t_w = P.dma("sync", wf.ap(), wsrc[bi * 128:(bi + 1) * 128, :], ld)
                    P.wait("scalar", ld, t_w)
                    P.wait("scalar", mm, mm.count)
                    if ph == 0:
                        t_a = P.op("scalar", lambda e: e.activation(ab.ap(), wf.ap(), AF.Abs),
                                   inc=cst)
                    else:
                        P.op("scalar", lambda e: e.activation(ab.ap(), wf.ap(), AF.Abs))
                        P.edone("vector", "scalar")
                        t_a = P.op("vector", lambda e, mi=mi: e.tensor_scalar(
                            ab.ap(), ab.ap(), thr.ap()[:, 2 + mi:3 + mi], None,
                            op0=ALU.add), inc=cst)
                    P.wait("tensor", cst, t_a)
                    for j in range(4):
                        t_m = P.op("tensor", lambda e, j=j, first=(bi == 0 and j == 0),
                                   last=(bi == nbands - 1 and j == 3):
                                   e.matmul(psA.ap(), co["ones_c_f"].ap(),
                                            ab.ap()[:, j * 512:(j + 1) * 512],
                                            start=first, stop=last),
                                   inc=mm if j == 3 else None)
                P.wait("vector", mm, mm.count)
                if ph == 0:
                    t_v = P.op("vector", lambda e, mi=mi: e.reduce_sum(
                        msc.ap()[:, 6 + mi:7 + mi], psA.ap(), axis=AXX), inc=ev)
                    t_v = P.op("vector", lambda e, mi=mi: e.tensor_scalar(
                        msc.ap()[:, 6 + mi:7 + mi], msc.ap()[:, 6 + mi:7 + mi],
                        1.0 / NTOT, None, op0=ALU.mult), inc=ev)
                    t_v = P.op("vector", lambda e, mi=mi: e.tensor_scalar(
                        msc.ap()[:, 4:5], msc.ap()[:, 6 + mi:7 + mi], -1.0, None,
                        op0=ALU.mult), inc=ev)
                    t_v = bcast_scalar(msc.ap()[:, 4:5], thr.ap()[:, 2 + mi:3 + mi])
                    P.wait("tensor", ev, t_v)
                    P.wait("scalar", ev, t_v)
                else:
                    t_v = P.op("vector", lambda e, mi=mi: e.reduce_sum(
                        msc.ap()[:, 4:5], psA.ap(), axis=AXX), inc=ev)
                    P.op("vector", lambda e, mi=mi: e.tensor_scalar(
                        msc.ap()[:, 4:5], msc.ap()[:, 4:5], 1.0 / NTOT, None,
                        op0=ALU.mult))
                    P.op("vector", lambda e, mi=mi: e.tensor_add(
                        msc.ap()[:, mi:mi + 1], msc.ap()[:, 4:5],
                        msc.ap()[:, 6 + mi:7 + mi]))
                    t_v = P.op("vector", lambda e, mi=mi: e.tensor_scalar(
                        msc.ap()[:, mi:mi + 1], msc.ap()[:, mi:mi + 1], 1e-5, None,
                        op0=ALU.max), inc=ev)
                    P.op("vector", lambda e, mi=mi: e.tensor_scalar(
                        msc.ap()[:, 4:5], msc.ap()[:, mi:mi + 1], -0.7, None,
                        op0=ALU.mult))
                    t_v = bcast_scalar(msc.ap()[:, 4:5], thr.ap()[:, mi:mi + 1])
                    P.wait("tensor", ev, t_v)
                    P.wait("scalar", ev, t_v)
        P.op("vector", lambda e: e.tensor_mul(msc.ap()[:, 5:6], msc.ap()[:, 0:1],
                                              msc.ap()[:, 0:1]))
        t_v = P.op("vector", lambda e: e.tensor_mul(msc.ap()[:, 5:6], msc.ap()[:, 5:6],
                                                    msc.ap()[:, 1:2]), inc=ev)
        t_v = bcast_scalar(msc.ap()[:, 5:6], s1.ap()[:, 0:1])
        P.wait("gpsimd", ev, t_v)
        P.dma("gpsimd", io["c_col"], s1.ap()[:, 0:1], st)
        P.wait("gpsimd", st, st.count)
        P.barrier()
        for mi, (tag, src, nb_, tiled, dst) in enumerate(specs):
            nbands = NH1 if tiled else (H1 // 128)
            tc_ap = thr.ap()[:, mi:mi + 1]
            for bi in range(nbands):
                P.wait("sync", ev, ev.count)
                if tiled:
                    t_w = P.dma("sync", wf.ap().rearrange("p (ic m) -> p ic m", ic=ND),
                                src[bi].rearrange("ic p m -> p ic m"), ld)
                else:
                    t_w = P.dma("sync", wf.ap(), src[bi * 128:(bi + 1) * 128, :], ld)
                P.wait("scalar", ld, t_w)
                P.wait("scalar", ev, ev.count)
                t_1 = P.op("scalar", lambda e, b=tc_ap: e.activation(
                    s1.ap(), wf.ap(), AF.Relu, bias=b, scale=1.0))
                t_1 = P.op("scalar", lambda e: e.activation(s1.ap(), s1.ap(), AF.Sign))
                t_2 = P.op("scalar", lambda e, b=tc_ap: e.activation(
                    s2.ap(), wf.ap(), AF.Relu, bias=b, scale=-1.0))
                t_2 = P.op("scalar", lambda e: e.activation(s2.ap(), s2.ap(), AF.Sign),
                           inc=cst)
                P.wait("vector", cst, t_2)
                t_o = P.op("vector", lambda e: e.tensor_sub(obf.ap(), s1.ap(), s2.ap()),
                           inc=ev)
                P.wait("gpsimd", ev, t_o)
                if tiled:
                    P.dma("gpsimd", dst[bi].rearrange("ic p m -> p ic m"),
                          obf.ap().rearrange("p (ic m) -> p ic m", ic=ND), st)
                else:
                    P.dma("gpsimd", dst[bi * 128:(bi + 1) * 128, :], obf.ap(), st)
                P.wait("scalar", st, st.count)
        P.wait("sync", st, st.count)
        P.finalize(es)
    return nc


def build_l1ffn(nc, io):
    es = contextlib.ExitStack()
    with es:
        P = Prog(nc)
        co = emit_consts(P, nc, es)
        ccol = es.enter_context(nc.sbuf_tensor("ccol", [128, 1], F32))
        bld = P.sem()
        P.dma("gpsimd", ccol.ap(), io["c_col"], bld)
        NQ = 4
        QT = TC // NQ
        hT = es.enter_context(nc.sbuf_tensor("hT1", [128, NH1 * QT], BF16))
        rawh = es.enter_context(nc.sbuf_tensor("rawh1", [128, 4 * D], F32))
        xbufs = alloc_xnT_bufs(nc, es, "gx", QT // 128)
        wb = es.enter_context(nc.sbuf_tensor("gwb", [128, ND * 128], BF16))
        rel = es.enter_context(nc.sbuf_tensor("grel", [128, 512], F32))
        psH = es.enter_context(nc.psum_tensor("psH", [128, 512], F32))
        dwb = es.enter_context(nc.sbuf_tensor("gdwb", [128, 512], BF16))
        psE = [es.enter_context(nc.psum_tensor(f"psF{i}", [128, 512], F32))
               for i in range(4)]
        pxs = es.enter_context(nc.sbuf_tensor("qxs", [128, D], F32))
        pscr = es.enter_context(nc.sbuf_tensor("qscr", [128, D], F32))
        pcols = es.enter_context(nc.sbuf_tensor("qcols", [128, 6], F32))
        pob = es.enter_context(nc.sbuf_tensor("qob", [128, D], F32))
        P.barrier()
        for q in range(NQ):
            tt0 = q * (QT // 128)
            xnT = emit_xnT(P, nc, es, co, io["x3"], tag="gx", ntt=QT // 128,
                           tt0=tt0, bufs=xbufs)
            wld = P.sem(); mm = P.sem(); ev = P.sem()
            xv = xnT.ap().rearrange("p (ic t) -> p ic t", ic=ND)
            for hc in range(NH1):
                P.wait("sync", mm, mm.count)
                t_w = P.dma("sync", wb.ap().rearrange("p (ic m) -> p ic m", ic=ND),
                            io["tern_fc_t"][hc].rearrange("ic p m -> p ic m"), wld)
                P.wait("tensor", wld, t_w)
                P.wait("tensor", ev, ev.count)
                t_m = None
                for ic in range(ND):
                    t_m = P.op("tensor", lambda e, ic=ic:
                               e.matmul(psH.ap(), wb.ap()[:, ic * 128:(ic + 1) * 128],
                                        xv[:, ic, 0:512],
                                        start=(ic == 0), stop=(ic == ND - 1)),
                               inc=mm if ic == ND - 1 else None)
                P.wait("vector", mm, t_m)
                P.wait("vector", ev, max(0, ev.count - 1))
                t_r = P.op("vector", lambda e: e.tensor_scalar(
                    rel.ap(), psH.ap(), 0.0, None, op0=ALU.max), inc=ev)
                t_h = P.op("vector", lambda e, hc=hc: e.tensor_mul(
                    hT.ap().rearrange("p (hc t) -> p hc t", hc=NH1)[:, hc, :],
                    rel.ap(), rel.ap()), inc=ev)
                P.wait("tensor", ev, t_h)
            P.barrier()
            wld = P.sem(); mm = P.sem(); ev = P.sem()
            hv = hT.ap().rearrange("p (hc t) -> p hc t", hc=NH1)
            for obk in range(4):
                for hc in range(NH1):
                    P.wait("sync", mm, mm.count)
                    t_w = P.dma("sync", dwb.ap(),
                                io["tern_proj_r"][hc * 128:(hc + 1) * 128,
                                                  obk * 512:(obk + 1) * 512], wld)
                    P.wait("tensor", wld, t_w)
                    if hc == 0:
                        P.wait("tensor", ev, ev.count)
                    t_m = None
                    for tt in range(4):
                        t_m = P.op("tensor", lambda e, tt=tt, hc=hc:
                                   e.matmul(psE[tt].ap(),
                                            hv[:, hc, tt * 128:(tt + 1) * 128],
                                            dwb.ap(), start=(hc == 0),
                                            stop=(hc == NH1 - 1)),
                                   inc=mm if tt == 3 else None)
                P.wait("vector", mm, mm.count)
                P.wait("vector", bld, 16)
                for tt in range(4):
                    P.op("vector", lambda e, tt=tt, obk=obk: e.tensor_scalar(
                        rawh.ap()[:, tt * D + obk * 512: tt * D + obk * 512 + 512],
                        psE[tt].ap(), ccol.ap(), None, op0=ALU.mult), inc=ev)
                P.wait("tensor", ev, ev.count)
            P.barrier()
            vs = P.sem(); st = P.sem(); ld2 = P.sem()
            for i in range(4):
                tt = tt0 + i
                raw_ap = rawh.ap()[:, i * D:(i + 1) * D]
                P.wait("sync", vs, vs.count)
                t_x = P.dma("sync", pxs.ap(), io["x3"][tt * 128:(tt + 1) * 128, :], ld2)
                t_n = emit_ln_cols(P, co, raw_ap, pscr.ap(), pcols.ap(), vs)
                P.wait("vector", vs, t_n)
                P.op("vector", lambda e, raw_ap=raw_ap: e.tensor_scalar(
                    pscr.ap(), raw_ap, pcols.ap()[:, 3:4], pcols.ap()[:, 4:5],
                    op0=ALU.mult, op1=ALU.add))
                P.wait("vector", ld2, t_x)
                t_s = P.op("vector", lambda e: e.tensor_add(pob.ap(), pxs.ap(), pscr.ap()),
                           inc=vs)
                P.wait("gpsimd", vs, t_s)
                P.dma("gpsimd", io["x4"][tt * 128:(tt + 1) * 128, :], pob.ap(), st)
                P.wait("vector", st, st.count)
                P.wait("scalar", st, st.count)
            P.wait("sync", st, st.count)
            P.wait("gpsimd", st, st.count)
            P.barrier()
        P.finalize(es)
    return nc



def _shard_rows(x_flat, p):
    return np.concatenate([x_flat[g0:g1] for g0, g1 in own_ranges(p)], axis=0)


def _unshard_rows(own, p, out, scale=1):
    i = 0
    for g0, g1 in own_ranges(p):
        n = (g1 - g0) * scale
        out[g0 * scale:g1 * scale] = own[i:i + n]
        i += n


def _declare(nc, specs):
    io = {}
    for name, shape, dt, kind in specs:
        io[name] = nc.dram_tensor(name, list(shape), dt, kind=kind).ap()
    return io


def _launch(build_fn, specs, per_core_inputs, out_names):
    nc = bass.Bass("TRN2", target_bir_lowering=False)
    io = _declare(nc, specs)
    build_fn(nc, io)
    res = run_bass_kernel_spmd(nc, per_core_inputs, core_ids=CORES)
    return [{k: r[k] for k in out_names} for r in res.results]


def _launch_per_p(build_fn, specs, per_core_inputs, out_names):
    outs = [None] * 8
    for p in (0, 1):
        nc = bass.Bass("TRN2", target_bir_lowering=False)
        io = _declare(nc, specs(p) if callable(specs) else specs)
        build_fn(nc, io, p)
        cores = [2 * bb + p for bb in range(B)]
        res = run_bass_kernel_spmd(nc, [per_core_inputs[c] for c in cores],
                                   core_ids=cores)
        for i, c in enumerate(cores):
            outs[c] = {k: res.results[i][k] for k in out_names}
    return outs


BF = ml_dtypes.bfloat16


def kernel(**inputs):
    f32 = np.float32
    g = {k: np.asarray(v) for k, v in inputs.items()}
    x = np.ascontiguousarray(g["L0_attn_x"].astype(f32, copy=False))

    wq_t = tile_lhsT(np.ascontiguousarray(g["L0_attn_Wq"]), ND, ND)
    wk_t = tile_lhsT(np.ascontiguousarray(g["L0_attn_Wk"]), ND, ND)
    wv_r = np.ascontiguousarray(g["L0_attn_Wv"].T)
    wo_r = np.ascontiguousarray(g["L0_attn_Wo"].T)
    bq_c = col_layout(g["L0_attn_bq"]); bk_c = col_layout(g["L0_attn_bk"])
    bv_row = np.ascontiguousarray(g["L0_attn_bv"].reshape(1, D))
    bo_row = np.ascontiguousarray(g["L0_attn_bo"].reshape(1, D))
    gate_t = tile_lhsT(np.ascontiguousarray(g["L0_ffn_gate_w"]), NH0, ND)
    up_t = tile_lhsT(np.ascontiguousarray(g["L0_ffn_up_w"]), NH0, ND)
    down_r = np.ascontiguousarray(g["L0_ffn_down_w"].T)
    gate_s_c = col_layout(g["L0_ffn_gate_s"]); up_s_c = col_layout(g["L0_ffn_up_s"])
    down_s_row = np.ascontiguousarray(g["L0_ffn_down_s"].reshape(1, D))
    wqkv = np.ascontiguousarray(g["L1_attn_W_qkv"])
    wq1_t = tile_lhsT(np.ascontiguousarray(wqkv[0:D]), ND, ND)
    wk1_t = tile_lhsT(np.ascontiguousarray(wqkv[D:2 * D]), ND, ND)
    wv1_r = np.ascontiguousarray(wqkv[2 * D:3 * D].T)
    wout_r = np.ascontiguousarray(g["L1_attn_W_out"].T)
    w_fc_t = tile_lhsT(np.ascontiguousarray(g["L1_ffn_w_fc"]), NH1, ND)
    w_proj_r = np.ascontiguousarray(g["L1_ffn_w_proj"].T)
    zeros_row = np.zeros((1, D), f32)

    r0 = make_rope_tables(0)
    r1 = make_rope_tables(1)
    pswap_m = np.zeros((128, 128), f32)
    pswap_m[np.arange(128), np.arange(128) ^ 1] = 1.0

    def rope_slices(layer_tables, p):
        sh, sl, ch, cl = layer_tables
        cols = np.concatenate([np.arange(g0, g1) for g0, g1 in own_ranges(p)])
        return (np.ascontiguousarray(sh[:, cols]), np.ascontiguousarray(sl[:, cols]),
                np.ascontiguousarray(ch[:, cols]), np.ascontiguousarray(cl[:, cols]))

    x_own = {}
    for c in CORES:
        bb, p = c // 2, c % 2
        x_own[c] = np.ascontiguousarray(_shard_rows(x[bb], p))

    specs = [("x0", (TC, D), F32, "ExternalInput"),
             ("wq_t", wq_t.shape, F32, "ExternalInput"),
             ("wk_t", wk_t.shape, F32, "ExternalInput"),
             ("wv_r", (D, D), F32, "ExternalInput"),
             ("bq_c", (128, ND), F32, "ExternalInput"),
             ("bk_c", (128, ND), F32, "ExternalInput"),
             ("bv_row", (1, D), F32, "ExternalInput"),
             ("r0sh", (128, TC), F32, "ExternalInput"),
             ("r0sl", (128, TC), F32, "ExternalInput"),
             ("r0ch", (128, TC), F32, "ExternalInput"),
             ("r0cl", (128, TC), F32, "ExternalInput"),
             ("pswap", (128, 128), F32, "ExternalInput"),
             ("qT0", (D, TC), BF16, "ExternalOutput"),
             ("kT0", (D, TC), BF16, "ExternalOutput"),
             ("v0", (TC, D), BF16, "ExternalOutput")]
    ins = []
    for c in CORES:
        bb, p = c // 2, c % 2
        sh, sl, ch, cl = rope_slices(r0, p)
        ins.append(dict(x0=x_own[c], wq_t=wq_t, wk_t=wk_t, wv_r=wv_r, bq_c=bq_c,
                        bk_c=bk_c, bv_row=bv_row, r0sh=sh, r0sl=sl, r0ch=ch, r0cl=cl,
                        pswap=pswap_m))
    o1 = _launch(build_l0qkv, specs, ins, ["qT0", "kT0", "v0"])

    kT_full = {}; v_full = {}
    for bb in range(B):
        kf = np.zeros((D, S), BF); vf = np.zeros((S, D), BF)
        for p in (0, 1):
            c = 2 * bb + p
            i = 0
            for g0, g1 in own_ranges(p):
                n = g1 - g0
                kf[:, g0:g1] = o1[c]["kT0"][:, i:i + n]
                vf[g0:g1, :] = o1[c]["v0"][i:i + n, :]
                i += n
        kT_full[bb] = kf; v_full[bb] = vf

    def specs2(p):
        kmax = max(r[1] for r in own_ranges(p))
        return [("qT0", (D, TC), BF16, "ExternalInput"),
                ("kT", (D, kmax), BF16, "ExternalInput"),
                ("v", (kmax, D), BF16, "ExternalInput"),
                ("aoT0", (D, TC), BF16, "ExternalOutput")]
    ins2 = [None] * 8
    for c in CORES:
        bb, p = c // 2, c % 2
        kmax = max(r[1] for r in own_ranges(p))
        ins2[c] = dict(qT0=o1[c]["qT0"],
                       kT=np.ascontiguousarray(kT_full[bb][:, :kmax]),
                       v=np.ascontiguousarray(v_full[bb][:kmax, :]))
    o2 = _launch_per_p(build_l0attn, specs2, ins2, ["aoT0"])

    specs3 = [("aoT", (D, TC), BF16, "ExternalInput"),
              ("wo_r", (D, D), F32, "ExternalInput"),
              ("bo_row", (1, D), F32, "ExternalInput"),
              ("x_in", (TC, D), F32, "ExternalInput"),
              ("x_out", (TC, D), F32, "ExternalOutput")]
    ins3 = [dict(aoT=o2[c]["aoT0"], wo_r=wo_r, bo_row=bo_row, x_in=x_own[c])
            for c in CORES]
    o3 = _launch(build_oproj, specs3, ins3, ["x_out"])
    x1 = {c: o3[c]["x_out"] for c in CORES}

    specs4 = [("x1", (TC, D), F32, "ExternalInput"),
              ("gate_t", gate_t.shape, F32, "ExternalInput"),
              ("up_t", up_t.shape, F32, "ExternalInput"),
              ("down_r", (INTER0, D), F32, "ExternalInput"),
              ("gate_s_c", (128, NH0), F32, "ExternalInput"),
              ("up_s_c", (128, NH0), F32, "ExternalInput"),
              ("down_s_row", (1, D), F32, "ExternalInput"),
              ("x2", (TC, D), F32, "ExternalOutput")]
    ins4 = [dict(x1=x1[c], gate_t=gate_t, up_t=up_t, down_r=down_r,
                 gate_s_c=gate_s_c, up_s_c=up_s_c, down_s_row=down_s_row)
            for c in CORES]
    o4 = _launch(build_l0ffn, specs4, ins4, ["x2"])
    x2 = {c: o4[c]["x2"] for c in CORES}

    specs5 = [("x2", (TC, D), F32, "ExternalInput"),
              ("wq1_t", wq1_t.shape, F32, "ExternalInput"),
              ("wk1_t", wk1_t.shape, F32, "ExternalInput"),
              ("wv1_r", (D, D), F32, "ExternalInput"),
              ("r1sh", (128, TC), F32, "ExternalInput"),
              ("r1sl", (128, TC), F32, "ExternalInput"),
              ("r1ch", (128, TC), F32, "ExternalInput"),
              ("r1cl", (128, TC), F32, "ExternalInput"),
              ("pswap", (128, 128), F32, "ExternalInput"),
              ("qT1", (D, TC), BF16, "ExternalOutput"),
              ("kT1", (D, TC), BF16, "ExternalOutput"),
              ("v1", (TC, D), BF16, "ExternalOutput"),
              ("M_own", (8, H, 128, 128), F32, "ExternalOutput"),
              ("z_cols", (128, 128), F32, "ExternalOutput")]
    ins5 = []
    for c in CORES:
        bb, p = c // 2, c % 2
        sh, sl, ch, cl = rope_slices(r1, p)
        ins5.append(dict(x2=x2[c], wq1_t=wq1_t, wk1_t=wk1_t, wv1_r=wv1_r,
                         r1sh=sh, r1sl=sl, r1ch=ch, r1cl=cl, pswap=pswap_m))
    o5 = _launch(build_l1qkv, specs5, ins5, ["qT1", "kT1", "v1", "M_own", "z_cols"])

    M_all = {}; z_all = {}
    for bb in range(B):
        Ma = np.zeros((16, H, 128, 128), f32)
        za = np.zeros((128, 16 * H), f32)
        for p in (0, 1):
            c = 2 * bb + p
            for lc, gc in enumerate(own_chunks(p)):
                Ma[gc] = o5[c]["M_own"][lc]
                za[:, gc * 16:(gc + 1) * 16] = o5[c]["z_cols"][:, lc * 16:(lc + 1) * 16]
        M_all[bb] = Ma; z_all[bb] = za

    specs6 = [("qT1", (D, TC), BF16, "ExternalInput"),
              ("kT1", (D, TC), BF16, "ExternalInput"),
              ("v1", (TC, D), BF16, "ExternalInput"),
              ("M_all", (16, H, 128, 128), F32, "ExternalInput"),
              ("z_all_cols", (128, 16 * H), F32, "ExternalInput"),
              ("aoT1", (D, TC), BF16, "ExternalOutput")]
    ins6 = [None] * 8
    for c in CORES:
        bb = c // 2
        ins6[c] = dict(qT1=o5[c]["qT1"], kT1=o5[c]["kT1"], v1=o5[c]["v1"],
                       M_all=M_all[bb], z_all_cols=z_all[bb])
    o6 = _launch_per_p(build_l1attn, specs6, ins6, ["aoT1"])

    ins7 = [dict(aoT=o6[c]["aoT1"], wo_r=wout_r, bo_row=zeros_row, x_in=x2[c])
            for c in CORES]
    o7 = _launch(build_oproj, specs3, ins7, ["x_out"])
    x3 = {c: o7[c]["x_out"] for c in CORES}

    specs8 = [("w_fc_t", w_fc_t.shape, F32, "ExternalInput"),
              ("w_proj_r", (H1, D), F32, "ExternalInput"),
              ("tern_fc_t", w_fc_t.shape, BF16, "ExternalOutput"),
              ("tern_proj_r", (H1, D), BF16, "ExternalOutput"),
              ("c_col", (128, 1), F32, "ExternalOutput")]
    ins8 = [dict(w_fc_t=w_fc_t, w_proj_r=w_proj_r) for c in CORES]
    o8 = _launch(build_l1tern, specs8, ins8, ["tern_fc_t", "tern_proj_r", "c_col"])

    specs9 = [("x3", (TC, D), F32, "ExternalInput"),
              ("tern_fc_t", w_fc_t.shape, BF16, "ExternalInput"),
              ("tern_proj_r", (H1, D), BF16, "ExternalInput"),
              ("c_col", (128, 1), F32, "ExternalInput"),
              ("x4", (TC, D), F32, "ExternalOutput")]
    ins9 = [dict(x3=x3[c], tern_fc_t=o8[c]["tern_fc_t"],
                 tern_proj_r=o8[c]["tern_proj_r"], c_col=o8[c]["c_col"])
            for c in CORES]
    o9 = _launch(build_l1ffn, specs9, ins9, ["x4"])

    out = np.zeros((B, S, D), f32)
    for c in CORES:
        bb, p = c // 2, c % 2
        i = 0
        for g0, g1 in own_ranges(p):
            n = g1 - g0
            out[bb, g0:g1] = o9[c]["x4"][i:i + n]
            i += n
    return out



def f_ranges(p):
    return [(p * 2048, (p + 1) * 2048)]


def alloc_arena(nc, es):
    B = {}
    T = lambda n, s, d: es.enter_context(nc.sbuf_tensor(n, s, d))
    PT = lambda n, s, d: es.enter_context(nc.psum_tensor(n, s, d))
    B["xnTbig"] = T("A_xnT", [128, ND * 2048], BF16)
    B["hdnT"] = T("A_hdnT", [128, NH0 * 256], BF16)
    B["rawh"] = T("A_rawh", [128, 2 * D], F32)
    B["wfbig"] = T("A_wf", [128, 4096], F32)
    B["wbbig"] = T("A_wb", [128, 8192], BF16)
    B["sfa"] = T("A_sfa", [128, D], F32)
    B["sfb"] = T("A_sfb", [128, D], F32)
    B["sba"] = T("A_sba", [128, D], BF16)
    B["at1"] = T("A_at1", [128, 4096], BF16)
    B["at2"] = T("A_at2", [128, 4096], BF16)
    B["at3"] = T("A_at3", [128, 4096], BF16)
    B["aoTs"] = B["at1"]
    B["brow"] = T("A_brow", [128, D], F32)
    B["cols"] = T("A_cols", [128, 8], F32)
    B["masks0"] = T("A_m0", [128, 4 * 512], BF16)
    B["masks1"] = T("A_m1", [128, 2 * 256], BF16)
    B["pbf"] = T("A_pbf", [128, 512], BF16)
    B["nscr"] = T("A_nscr", [128, 512], F32)
    B["aob"] = T("A_aob", [128, 512], BF16)
    B["rden"] = T("A_rden", [1, 512], F32)
    B["S_run"] = T("A_Srun", [128, 128], F32)
    B["z_run"] = T("A_zrun", [128, 1], F32)
    B["Sb"] = T("A_Sb", [128, 128], BF16)
    B["zb"] = T("A_zb", [128, 1], BF16)
    B["kcb"] = T("A_kcb", [128, 256], BF16)
    B["sil"] = T("A_sil", [128, 512], F32)
    B["ups"] = T("A_ups", [128, 512], F32)
    B["dwb"] = T("A_dwb", [128, 512], BF16)
    B["gsc"] = T("A_gsc", [128, 2 * NH0], F32)
    B["bcol"] = T("A_bcol", [128, 2 * ND], F32)
    B["ccol"] = T("A_ccol", [128, 1], F32)
    B["pswap"] = T("A_pswap", [128, 128], F32)
    B["msc"] = T("A_msc", [1, 8], F32)
    B["thr"] = T("A_thr", [128, 4], F32)
    B["qr"] = B["sil"]
    B["qs2"] = B["ups"]
    B["m1s"] = B["nscr"]
    B["obb"] = T("A_obb", [128, 512], BF16)
    B["dyn"] = T("A_dyn", [1, 64], mybir.dt.int32)
    B["ps0"] = PT("P_0", [128, 512], F32)
    B["ps1"] = PT("P_1", [128, 512], F32)
    B["ps2"] = PT("P_2", [128, 512], F32)
    B["ps3"] = PT("P_3", [128, 512], F32)
    B["ps4"] = PT("P_4", [128, 512], F32)
    B["ps5"] = PT("P_5", [128, 512], F32)
    B["psRow"] = PT("P_r", [1, 512], F32)
    B["psT"] = PT("P_t", [128, 512], BF16)
    return B


def f_xnT_bufs(B, ntt):
    return dict(xnT=B["xnTbig"], xs=B["sfa"], xb=B["sba"], scr=B["sfb"],
                cols=B["cols"], pt=B["psT"])


def emit_xnT_f(P, nc, es, co, B, x_dram, ntt, tt0=0):
    TT = ntt * 128
    xnT_ap = B["xnTbig"].ap()[:, 0:ND * TT]
    xs, xb, scr, cols, pt = B["sfa"], B["sba"], B["sfb"], B["cols"], B["psT"]
    ld = P.nsem("xld"); vs = P.nsem("xvs"); tp = P.nsem("xtp"); ev = P.nsem("xev")
    for i in range(ntt):
        tt = tt0 + i
        P.wait("sync", tp, tp.count)
        t_ld = P.dma("sync", xs.ap(), x_dram[tt * 128:(tt + 1) * 128, :], ld)
        P.wait("scalar", ld, t_ld)
        P.wait("vector", ld, t_ld)
        t_n = emit_ln_cols(P, co, xs.ap(), scr.ap(), cols.ap(), vs)
        P.wait("vector", vs, t_n)
        t_c = P.op("vector", lambda e: e.tensor_scalar(
            xb.ap(), xs.ap(), cols.ap()[:, 3:4], cols.ap()[:, 4:5],
            op0=ALU.mult, op1=ALU.add), inc=vs)
        for g in range(4):
            P.wait("tensor", vs, t_c)
            P.wait("tensor", ev, ev.count)
            tl = None
            for f in range(4):
                ic = g * 4 + f
                tl = P.op("tensor", lambda e, f=f, ic=ic:
                          e.transpose(pt.ap()[:, f * 128:(f + 1) * 128],
                                      xb.ap()[:, ic * 128:(ic + 1) * 128],
                                      co["ident_b"].ap()),
                          inc=tp if f == 3 else None)
            P.wait("scalar", tp, tl)

            def ev_fn(e, g=g, i=i, xnT_ap=xnT_ap, TT=TT):
                dst = xnT_ap.rearrange("p (ic t) -> p ic t", ic=ND)[
                    :, g * 4:(g + 1) * 4, i * 128:(i + 1) * 128]
                return e.activation(dst, pt.ap().rearrange("p (f t) -> p f t", f=4),
                                    AF.Identity)
            P.op("scalar", ev_fn, inc=ev)
    P.barrier()
    return xnT_ap


def emit_rope_f(P, nc, es, B, io, pref):
    sinT = B["rawh"].ap()[:, 0:TC]
    cosT = B["rawh"].ap()[:, TC:2 * TC]
    th = B["wfbig"]
    ld = P.nsem("rld"); ok = P.nsem("rok")
    for which, dst in (("s", sinT), ("c", cosT)):
        P.dma("gpsimd", th.ap()[:, 0:TC], io[pref + which + "h"], ld)
        t1 = P.dma("gpsimd", th.ap()[:, TC:], io[pref + which + "l"], ld)
        P.wait("vector", ld, t1)
        t2 = P.op("vector", lambda e: e.tensor_add(
            th.ap()[:, 0:TC], th.ap()[:, 0:TC], th.ap()[:, TC:]), inc=ok)
        P.wait("scalar", ok, t2)
        t3 = P.op("scalar", lambda e, d=dst: e.activation(d, th.ap()[:, 0:TC], AF.Sin),
                  inc=ok)
        P.wait("gpsimd", ok, t3)
    P.barrier()
    return sinT, cosT


def emit_projF_f(P, nc, es, co, B, xnT_ap, w_tiled, noc, out_dram, rope=None,
                 bias_off=None, scale=1.0, elu1=False):
    NB = TC // 512
    qr, qs, m1, ob = B["qr"], B["qs2"], B["m1s"], B["obb"]
    psQ = B["ps1"]
    wld = P.nsem("pwld"); wc = P.nsem("pwc"); mm = P.nsem("pmm")
    ev = P.nsem("pev"); st = P.nsem("pst")
    xv = xnT_ap.rearrange("p (ic t) -> p ic t", ic=ND)
    mm_hist = []
    ev_hist = []
    for oc in range(noc):
        wpar = oc % 2
        wf = B["wfbig"].ap()[:, wpar * 2048:wpar * 2048 + ND * 128]
        wb = B["wbbig"].ap()[:, wpar * 2048:wpar * 2048 + ND * 128]
        if len(mm_hist) >= 2:
            P.wait("sync", mm, mm_hist[-2])
        t_w = P.dma("sync", wf.rearrange("p (ic m) -> p ic m", ic=ND),
                    w_tiled[oc].rearrange("ic p m -> p ic m"), wld)
        P.wait("vector", wld, t_w)
        if len(mm_hist) >= 2:
            P.wait("vector", mm, mm_hist[-2])
        t_c = P.op("vector", lambda e, wb=wb, wf=wf: e.tensor_copy(wb, wf), inc=wc)
        for nb in range(NB):
            u = oc * NB + nb
            ps = B["ps0"] if u % 2 == 0 else B["ps5"]
            P.wait("tensor", wc, t_c)
            if len(ev_hist) >= 2:
                P.wait("tensor", ev, ev_hist[-2])
            t_m = None
            for ic in range(ND):
                t_m = P.op("tensor", lambda e, ic=ic, nb=nb, ps=ps, wb=wb:
                           e.matmul(ps.ap(), wb[:, ic * 128:(ic + 1) * 128],
                                    xv[:, ic, nb * 512:(nb + 1) * 512],
                                    start=(ic == 0), stop=(ic == ND - 1)),
                           inc=mm if ic == ND - 1 else None)
            mm_hist.append(t_m)
            P.wait("scalar", mm, t_m)
            P.wait("scalar", st, st.count)
            if bias_off is not None:
                bc = B["bcol"].ap()[:, bias_off + oc:bias_off + oc + 1]
                t_e = P.op("scalar", lambda e, bc=bc, ps=ps: e.activation(
                    qr.ap(), ps.ap(), AF.Identity, bias=bc, scale=scale), inc=ev)
            else:
                t_e = P.op("scalar", lambda e, ps=ps: e.activation(
                    qr.ap(), ps.ap(), AF.Identity, scale=scale), inc=ev)
            cur = qr
            P.wait("vector", ev, t_e)
            P.wait("vector", st, st.count)
            if rope is not None:
                sinT, cosT = rope
                sv = sinT[:, nb * 512:(nb + 1) * 512]
                cv = cosT[:, nb * 512:(nb + 1) * 512]
                P.wait("tensor", ev, t_e)
                t_pm = P.op("tensor", lambda e: e.matmul(
                    psQ.ap(), B["pswap"].ap(), qr.ap(), start=True, stop=True), inc=mm)
                P.op("vector", lambda e, cv=cv: e.tensor_mul(m1.ap(), qr.ap(), cv))
                P.wait("vector", mm, t_pm)
                P.op("vector", lambda e, sv=sv: e.tensor_mul(qs.ap(), psQ.ap(), sv))
                t_e = P.op("vector", lambda e: e.tensor_add(m1.ap(), m1.ap(), qs.ap()),
                           inc=ev)
                cur = m1
            if elu1:
                t_e = P.op("vector", lambda e, c=cur: e.tensor_scalar(
                    qs.ap(), c.ap(), 0.0, None, op0=ALU.min), inc=ev)
                P.wait("scalar", ev, t_e)
                t_e = P.op("scalar", lambda e: e.activation(qs.ap(), qs.ap(), AF.Exp),
                           inc=ev)
                P.wait("vector", ev, t_e)
                P.op("vector", lambda e, c=cur: e.tensor_scalar(
                    c.ap(), c.ap(), 0.0, None, op0=ALU.max))
                t_e = P.op("vector", lambda e, c=cur: e.tensor_add(
                    c.ap(), c.ap(), qs.ap()), inc=ev)
            P.wait("vector", ev, t_e)
            t_b = P.op("vector", lambda e, c=cur: e.tensor_copy(ob.ap(), c.ap()), inc=ev)
            ev_hist.append(t_b)
            P.wait("gpsimd", ev, t_b)
            P.dma("gpsimd", out_dram[oc * 128:(oc + 1) * 128, nb * 512:(nb + 1) * 512],
                  ob.ap(), st)
    P.wait("gpsimd", st, st.count)
    P.barrier()


def emit_projT_f(P, nc, es, co, B, lhsT_ap, nic, w_rows, out_dram,
                 bias_bcast=None, nof=D, ntok=TC, per_unit_epilogue=None,
                 lhsT_loader=None, wsrc_bf16=False):
    ntt = ntok // 128
    NOB = nof // 512
    wfs = B["wfbig"].ap()[:, 2048:2560]
    wbp = B["wbbig"].ap()[:, 0:nic * 512]
    obuf = B["obb"]
    ps = B["ps0"]
    wld = P.nsem("twld"); wc = P.nsem("twc"); mm = P.nsem("tmm")
    ev = P.nsem("tev"); st = P.nsem("tst")
    wview = w_rows.rearrange("(ic p) n -> p ic n", p=128)
    NG = nic // 8
    for tt in range(ntt):
        if lhsT_loader is not None:
            lv, lsem, lcnt = lhsT_loader(tt)
        else:
            lv = lhsT_ap.rearrange("p (ic t) -> p ic t", ic=nic)
        for obk in range(NOB):
            P.wait("sync", mm, mm.count)
            if wsrc_bf16:
                for gix in range(NG):
                    t_w = P.dma("sync",
                                wbp.rearrange("p (ic n) -> p ic n", n=512)[
                                    :, gix * 8:(gix + 1) * 8, :],
                                wview[:, gix * 8:(gix + 1) * 8,
                                      obk * 512:(obk + 1) * 512], wld)
                P.wait("tensor", wld, t_w)
            else:
                for gix in range(NG):
                    wfull = B["wfbig"].ap().rearrange("p (ic n) -> p ic n", n=512)
                    t_w = P.dma("sync", wfull,
                                wview[:, gix * 8:(gix + 1) * 8,
                                      obk * 512:(obk + 1) * 512], wld)
                    P.wait("vector", wld, t_w)
                    t_c = P.op("vector", lambda e, gix=gix: e.tensor_copy(
                        wbp[:, gix * 4096:(gix + 1) * 4096], B["wfbig"].ap()), inc=wc)
                    P.wait("sync", wc, t_c)
                P.wait("tensor", wc, wc.count)
            P.wait("tensor", ev, ev.count)
            if lhsT_loader is not None:
                P.wait("tensor", lsem, lcnt)
            t_m = None
            for ic in range(nic):
                if lhsT_loader is not None:
                    lt = lv[:, ic * 128:(ic + 1) * 128]
                else:
                    lt = lv[:, ic, tt * 128:(tt + 1) * 128]
                t_m = P.op("tensor", lambda e, lt=lt, ic=ic:
                           e.matmul(ps.ap(), lt, wbp[:, ic * 512:(ic + 1) * 512],
                                    start=(ic == 0), stop=(ic == nic - 1)),
                           inc=mm if ic == nic - 1 else None)
            if per_unit_epilogue is not None:
                per_unit_epilogue(tt, obk, ps, mm, t_m, ev)
            else:
                P.wait("vector", mm, t_m)
                P.wait("vector", st, st.count)
                if bias_bcast is not None:
                    t_e = P.op("vector", lambda e, obk=obk: e.tensor_add(
                        obuf.ap(), ps.ap(), bias_bcast[:, obk * 512:(obk + 1) * 512]),
                        inc=ev)
                else:
                    t_e = P.op("vector", lambda e: e.tensor_copy(obuf.ap(), ps.ap()),
                               inc=ev)
                P.wait("gpsimd", ev, t_e)
                P.dma("gpsimd", out_dram[tt * 128:(tt + 1) * 128,
                                         obk * 512:(obk + 1) * 512], obuf.ap(), st)
                P.wait("vector", st, st.count)
    P.wait("gpsimd", st, st.count)
    P.barrier()


def emit_attn0_f(P, nc, es, co, B, loc, aoin_dram):
    ald = P.nsem("ald"); mmS = P.nsem("amS"); exq = P.nsem("aex")
    kkt = P.nsem("akt"); hdn = P.nsem("ahd"); evq = P.nsem("aev")
    st = P.nsem("ast")
    qloc, kloc, vloc = loc
    at1, at2, at3 = B["at1"], B["at2"], B["at3"]
    psS, psA, psD, psB = B["ps0"], B["ps1"], B["psRow"], B["ps2"]
    pbf, nscr, rden, aob = B["pbf"], B["nscr"], B["rden"], B["aob"]
    vv = at3.ap().rearrange("p (kt d) -> p kt d", kt=32)
    for h in range(8):
        P.wait("sync", hdn, hdn.count)
        P.dma("sync", at1.ap(), qloc[h * 128:(h + 1) * 128, :], ald)
        P.dma("sync", at2.ap(), kloc[h * 128:(h + 1) * 128, :], ald)
        for k8 in range(0, 32, 8):
            t_l = P.dma("sync", vv[:, k8:k8 + 8, :],
                        vloc[:, h * 128:(h + 1) * 128].rearrange(
                            "(kt p) d -> p kt d", p=128)[:, k8:k8 + 8, :], ald)
        P.wait("tensor", ald, t_l)
        for qb in range(8):
            gq = qb * 512
            nkt = 4 * qb + 4
            qrhs = at1.ap()[:, gq:gq + 512]
            exq_hist = []
            kkt_hist = []
            for kt in range(nkt):
                par = kt % 2
                psSp = psS if par == 0 else B["ps3"]
                pbfp = pbf.ap() if par == 0 else B["sba"].ap()[:, 0:512]
                if len(exq_hist) >= 2:
                    P.wait("tensor", exq, exq_hist[-2])
                t_s = P.op("tensor", lambda e, kt=kt, qrhs=qrhs, psSp=psSp:
                           e.matmul(psSp.ap(), at2.ap()[:, kt * 128:(kt + 1) * 128],
                                    qrhs, start=True, stop=True), inc=mmS)
                P.wait("scalar", mmS, t_s)
                if len(kkt_hist) >= 2:
                    P.wait("scalar", kkt, kkt_hist[-2])
                t_x = P.op("scalar", lambda e, psSp=psSp, pbfp=pbfp: e.activation(
                    pbfp, psSp.ap(), AF.Exp, bias=co["negC"].ap()), inc=exq)
                r_off = kt * 128 - gq
                if r_off >= 0:
                    P.wait("vector", exq, t_x)
                    t_x = P.op("vector", lambda e, r_off=r_off, pbfp=pbfp: e.tensor_mul(
                        pbfp, pbfp,
                        B["masks0"].ap()[:, (r_off // 128) * 512:(r_off // 128) * 512 + 512]),
                        inc=exq)
                exq_hist.append(t_x)
                P.wait("tensor", exq, t_x)
                P.op("tensor", lambda e, kt=kt, first=(kt == 0), last=(kt == nkt - 1),
                     pbfp=pbfp: e.matmul(psA.ap(), vv[:, kt, :], pbfp,
                                         start=first, stop=last))
                t_k = P.op("tensor", lambda e, first=(kt == 0), last=(kt == nkt - 1),
                           pbfp=pbfp: e.matmul(psD.ap(), co["ones_c_b"].ap(), pbfp,
                                               start=first, stop=last), inc=kkt)
                kkt_hist.append(t_k)
            P.wait("vector", kkt, kkt.count)
            P.wait("vector", st, st.count)
            t_r = P.op("vector", lambda e: e.reciprocal(rden.ap(), psD.ap()), inc=evq)
            P.wait("tensor", evq, t_r)
            t_b = P.op("tensor", lambda e: e.matmul(
                psB.ap(), co["ones_r_f"].ap(), rden.ap(), start=True, stop=True),
                inc=hdn)
            t_n = P.op("vector", lambda e: e.tensor_copy(nscr.ap(), psA.ap()), inc=evq)
            P.wait("vector", hdn, t_b)
            t_o = P.op("vector", lambda e: e.tensor_mul(aob.ap(), nscr.ap(), psB.ap()),
                       inc=evq)
            P.wait("gpsimd", evq, t_o)
            P.dma("gpsimd", aoin_dram[h * 128:(h + 1) * 128, gq:gq + 512], aob.ap(), st)
            P.wait("tensor", evq, t_o)
    P.wait("gpsimd", st, st.count)
    P.barrier()


def emit_attn1_f(P, nc, es, co, B, loc, aoin_dram):
    ald = P.nsem("bld"); mm = P.nsem("bmm"); ev = P.nsem("bev")
    st = P.nsem("bst"); acc = P.nsem("bacc"); tp = P.nsem("btp")
    qloc, kloc, vloc = loc
    at1, at2, at3 = B["at1"], B["at2"], B["at3"]
    psS, psN, psD, psB = B["ps0"], B["ps1"], B["psRow"], B["ps2"]
    psZ, psM, psT = B["ps3"], B["ps4"], B["psT"]
    pa, nscr, rden, aob = B["pbf"], B["nscr"], B["rden"], B["aob"]
    S_run, z_run, Sb, zb, kcb = B["S_run"], B["z_run"], B["Sb"], B["zb"], B["kcb"]
    vv = at3.ap().rearrange("p (kt d) -> p kt d", kt=32)
    for h in range(8):
        P.wait("sync", mm, mm.count)
        P.dma("sync", at1.ap(), qloc[h * 128:(h + 1) * 128, :], ald)
        P.dma("sync", at2.ap(), kloc[h * 128:(h + 1) * 128, :], ald)
        for k8 in range(0, 32, 8):
            t_l = P.dma("sync", vv[:, k8:k8 + 8, :],
                        vloc[:, h * 128:(h + 1) * 128].rearrange(
                            "(kt p) d -> p kt d", p=128)[:, k8:k8 + 8, :], ald)
        P.wait("tensor", ald, t_l)
        P.wait("vector", ald, t_l)
        P.edone("gpsimd", "vector")
        P.op("gpsimd", lambda e: e.memset(S_run.ap(), 0.0))
        t_z = P.op("gpsimd", lambda e: e.memset(z_run.ap(), 0.0), inc=acc)
        P.wait("vector", acc, t_z)
        P.edone("vector", "gpsimd")
        P.edone("tensor", "gpsimd")
        for g in range(16):
            qsl = at1.ap()[:, g * 256:(g + 1) * 256]
            P.wait("vector", mm, mm.count)
            P.op("vector", lambda e: e.tensor_copy(Sb.ap(), S_run.ap()))
            t_cb = P.op("vector", lambda e: e.tensor_copy(zb.ap(), z_run.ap()), inc=ev)
            P.wait("tensor", ev, ev.count)
            t_t = None
            for kt in range(2):
                t_t = P.op("tensor", lambda e, g=g, kt=kt: e.transpose(
                    psT.ap()[:, kt * 128:(kt + 1) * 128],
                    at2.ap()[:, g * 256 + kt * 128:g * 256 + (kt + 1) * 128],
                    co["ident_b"].ap()), inc=tp if kt == 1 else None)
            P.wait("vector", tp, t_t)
            t_kc = P.op("vector", lambda e: e.tensor_copy(kcb.ap(), psT.ap()[:, 0:256]), inc=ev)
            for kt in range(2):
                P.wait("tensor", ev, ev.count)
                t_s = P.op("tensor", lambda e, g=g, kt=kt, qsl=qsl: e.matmul(
                    psS.ap()[:, 0:256], at2.ap()[:, g * 256 + kt * 128:g * 256 + (kt + 1) * 128],
                    qsl, start=True, stop=True), inc=mm)
                P.wait("vector", mm, t_s)
                t_p = P.op("vector", lambda e, kt=kt: e.tensor_mul(
                    pa.ap()[:, kt * 256:(kt + 1) * 256], psS.ap()[:, 0:256],
                    B["masks1"].ap()[:, kt * 256:(kt + 1) * 256]), inc=ev)
            P.wait("tensor", ev, ev.count)
            for kt in range(2):
                P.op("tensor", lambda e, g=g, kt=kt: e.matmul(
                    psN.ap()[:, 0:256], vv[:, 2 * g + kt, :], pa.ap()[:, kt * 256:(kt + 1) * 256],
                    start=(kt == 0), stop=False))
            P.op("tensor", lambda e, qsl=qsl: e.matmul(
                psN.ap()[:, 0:256], Sb.ap(), qsl, start=False, stop=True))
            for kt in range(2):
                P.op("tensor", lambda e, kt=kt: e.matmul(
                    psD.ap()[:, 0:256], co["ones_c_b"].ap(), pa.ap()[:, kt * 256:(kt + 1) * 256],
                    start=(kt == 0), stop=False))
            t_m = P.op("tensor", lambda e, qsl=qsl: e.matmul(
                psD.ap()[:, 0:256], zb.ap(), qsl, start=False, stop=True), inc=mm)
            for kt in range(2):
                P.op("tensor", lambda e, g=g, kt=kt: e.matmul(
                    psM.ap()[:, 0:128], kcb.ap()[:, kt * 128:(kt + 1) * 128], vv[:, 2 * g + kt, :],
                    start=(kt == 0), stop=(kt == 1)))
            t_m = P.op("tensor", lambda e: e.matmul(
                psZ.ap()[:, 0:1], kcb.ap()[:, 0:128], co["ones_c_b"].ap(),
                start=True, stop=False), inc=mm)
            t_m = P.op("tensor", lambda e: e.matmul(
                psZ.ap()[:, 0:1], kcb.ap()[:, 128:256], co["ones_c_b"].ap(),
                start=False, stop=True), inc=mm)
            P.wait("vector", mm, t_m)
            P.wait("vector", st, st.count)
            P.op("vector", lambda e: e.tensor_scalar(
                rden.ap()[:, 0:256], psD.ap()[:, 0:256], 1e-6, None, op0=ALU.max))
            t_r = P.op("vector", lambda e: e.reciprocal(
                rden.ap()[:, 0:256], rden.ap()[:, 0:256]), inc=ev)
            P.wait("tensor", ev, t_r)
            t_bc = P.op("tensor", lambda e: e.matmul(
                psB.ap()[:, 0:256], co["ones_r_f"].ap(), rden.ap()[:, 0:256],
                start=True, stop=True), inc=mm)
            t_n = P.op("vector", lambda e: e.tensor_copy(
                nscr.ap()[:, 0:256], psN.ap()[:, 0:256]), inc=ev)
            P.wait("vector", mm, t_bc)
            t_o = P.op("vector", lambda e: e.tensor_mul(
                aob.ap()[:, 0:256], nscr.ap()[:, 0:256], psB.ap()[:, 0:256]), inc=ev)
            P.wait("gpsimd", ev, t_o)
            P.dma("gpsimd", aoin_dram[h * 128:(h + 1) * 128, g * 256:(g + 1) * 256],
                  aob.ap()[:, 0:256], st)
            t_a = P.op("vector", lambda e: e.tensor_add(S_run.ap(), S_run.ap(), psM.ap()[:, 0:128]))
            t_a = P.op("vector", lambda e: e.tensor_add(z_run.ap(), z_run.ap(), psZ.ap()[:, 0:1]),
                       inc=acc)
            P.wait("tensor", acc, t_a)
            P.wait("tensor", ev, t_o)
    P.wait("gpsimd", st, st.count)
    P.barrier()


def emit_oproj_f(P, nc, es, co, B, aoloc, w_rows, bias_row, x_in, x_out):
    raw = B["rawh"].ap()[:, 0:D]
    xs, scr, cols = B["sfa"], B["sfb"], B["cols"]
    ob2 = B["rawh"].ap()[:, D:2 * D]
    aoTs_ap = B["aoTs"].ap()[:, 0:2048]
    vs = P.nsem("ovs"); st = P.nsem("ost"); ld2 = P.nsem("old")
    aol = P.nsem("oal")
    bld = P.nsem("obl")
    if bias_row is not None:
        t_bb = P.dma("gpsimd", B["brow"].ap(), bias_row.broadcast_to([128, D]), bld)
        P.wait("vector", bld, t_bb)
    else:
        P.edone("gpsimd", "vector")
        bld2 = P.nsem("obl2")
        t_bb = P.op("gpsimd", lambda e: e.memset(B["brow"].ap(), 0.0), inc=bld2)
        P.wait("vector", bld2, t_bb)

    def loader(tt):
        mm = P.nsem("tmm")
        P.wait("gpsimd", mm, mm.count)
        cnt = P.dma("gpsimd", aoTs_ap,
                    aoloc.rearrange("(c p) t -> p c t", p=128)[
                        :, :, tt * 128:(tt + 1) * 128], aol)
        return aoTs_ap, aol, cnt

    def epi(tt, obk, ps, mm, t_m, ev):
        P.wait("vector", mm, t_m)
        t_e = P.op("vector", lambda e, obk=obk: e.tensor_add(
            raw[:, obk * 512:(obk + 1) * 512], ps.ap(),
            B["brow"].ap()[:, obk * 512:(obk + 1) * 512]), inc=ev)
        P.wait("tensor", ev, t_e)
        if obk == 3:
            P.wait("sync", vs, vs.count)
            t_x = P.dma("sync", xs.ap(), x_in[tt * 128:(tt + 1) * 128, :], ld2)
            P.wait("scalar", ev, t_e)
            t_n = emit_ln_cols(P, co, raw, scr.ap(), cols.ap(), vs)
            P.wait("vector", vs, t_n)
            P.op("vector", lambda e: e.tensor_scalar(
                scr.ap(), raw, cols.ap()[:, 3:4], cols.ap()[:, 4:5],
                op0=ALU.mult, op1=ALU.add))
            P.wait("vector", ld2, t_x)
            t_s = P.op("vector", lambda e: e.tensor_add(ob2, xs.ap(), scr.ap()), inc=vs)
            P.wait("gpsimd", vs, t_s)
            P.dma("gpsimd", x_out[tt * 128:(tt + 1) * 128, :], ob2, st)
            P.wait("vector", st, st.count)
            P.wait("scalar", st, st.count)

    emit_projT_f(P, nc, es, co, B, None, ND, w_rows, None,
                 per_unit_epilogue=epi, lhsT_loader=loader)
    P.wait("sync", st, st.count)
    P.wait("gpsimd", st, st.count)
    P.barrier()


def emit_ffn_f(P, nc, es, co, B, io, layer):
    NQ, QT = 8, 256
    if layer == 0:
        x_in, x_out = io["x1_d"], io["x2_d"]
        NH = NH0
    else:
        x_in, x_out = io["x3_d"], io["x4"]
        NH = NH1
    hdv = B["hdnT"].ap()[:, 0:NH * 256].rearrange("p (hc t) -> p hc t", hc=NH)
    rawh = B["rawh"]
    wf, wb = B["wfbig"], B["wbbig"]
    sil, ups, dwb = B["sil"], B["ups"], B["dwb"]
    psG, psU = B["ps0"], B["ps1"]
    psE = [B["ps2"], B["ps3"]]
    wld = P.nsem("fwld"); wc = P.nsem("fwc"); mm = P.nsem("fmm"); ev = P.nsem("fev")
    vs = P.nsem("ovs"); st = P.nsem("ost"); ld2 = P.nsem("old"); bld = P.nsem("fbl")
    if layer == 0:
        P.dma("gpsimd", B["gsc"].ap()[:, 0:NH0], io["gate_s_c"], bld)
        P.dma("gpsimd", B["gsc"].ap()[:, NH0:], io["up_s_c"], bld)
        t_bb = P.dma("gpsimd", B["brow"].ap(), io["down_s_row"].broadcast_to([128, D]), bld)
        P.wait("vector", bld, t_bb)
        P.wait("scalar", bld, t_bb)
    for q in range(NQ):
        tt0 = q * 2
        xnT_ap = emit_xnT_f(P, nc, es, co, B, x_in, 2, tt0=tt0)
        xv = xnT_ap.rearrange("p (ic t) -> p ic t", ic=ND)
        for hc in range(NH):
            P.wait("sync", mm, mm.count)
            if layer == 0:
                P.dma("sync", wf.ap().rearrange("p (q2 ic m) -> p q2 ic m", q2=2, ic=ND, m=128)[:, 0],
                      io["gate_t"][hc].rearrange("ic p m -> p ic m"), wld)
                t_w = P.dma("sync", wf.ap().rearrange("p (q2 ic m) -> p q2 ic m", q2=2, ic=ND, m=128)[:, 1],
                            io["up_t"][hc].rearrange("ic p m -> p ic m"), wld)
                P.wait("vector", wld, t_w)
                P.wait("vector", mm, mm.count)
                t_c = P.op("vector", lambda e: e.tensor_copy(
                    wb.ap()[:, 0:4096], wf.ap()), inc=wc)
                P.wait("tensor", wc, t_c)
            else:
                t_w = P.dma("sync", wb.ap().rearrange("p (ic m) -> p ic m", ic=ND)
                            if False else wb.ap()[:, 0:2048].rearrange("p (ic m) -> p ic m", ic=ND),
                            io["tfc_t"][hc].rearrange("ic p m -> p ic m"), wld)
                P.wait("tensor", wld, t_w)
            P.wait("tensor", ev, ev.count)
            if layer == 0:
                for ic in range(ND):
                    P.op("tensor", lambda e, ic=ic:
                         e.matmul(psG.ap()[:, 0:256], wb.ap()[:, ic * 128:(ic + 1) * 128],
                                  xv[:, ic, 0:256],
                                  start=(ic == 0), stop=(ic == ND - 1)))
                t_m = None
                for ic in range(ND):
                    t_m = P.op("tensor", lambda e, ic=ic:
                               e.matmul(psU.ap()[:, 0:256],
                                        wb.ap()[:, (ND + ic) * 128:(ND + ic + 1) * 128],
                                        xv[:, ic, 0:256],
                                        start=(ic == 0), stop=(ic == ND - 1)),
                               inc=mm if ic == ND - 1 else None)
                P.wait("scalar", mm, t_m)
                P.wait("scalar", ev, max(0, ev.count - 1))
                t_s = P.op("scalar", lambda e, hc=hc: e.activation(
                    sil.ap()[:, 0:256], psG.ap()[:, 0:256], AF.Silu,
                    scale=B["gsc"].ap()[:, hc:hc + 1]), inc=ev)
                P.wait("vector", ev, t_s)
                t_u = P.op("vector", lambda e, hc=hc: e.tensor_scalar(
                    ups.ap()[:, 0:256], psU.ap()[:, 0:256],
                    B["gsc"].ap()[:, NH0 + hc:NH0 + hc + 1], None, op0=ALU.mult), inc=ev)
                t_h = P.op("vector", lambda e, hc=hc: e.tensor_mul(
                    hdv[:, hc, :], sil.ap()[:, 0:256], ups.ap()[:, 0:256]), inc=ev)
            else:
                t_m = None
                for ic in range(ND):
                    t_m = P.op("tensor", lambda e, ic=ic:
                               e.matmul(psG.ap()[:, 0:256], wb.ap()[:, ic * 128:(ic + 1) * 128],
                                        xv[:, ic, 0:256],
                                        start=(ic == 0), stop=(ic == ND - 1)),
                               inc=mm if ic == ND - 1 else None)
                P.wait("vector", mm, t_m)
                P.wait("vector", ev, max(0, ev.count - 1))
                t_r = P.op("vector", lambda e: e.tensor_scalar(
                    sil.ap()[:, 0:256], psG.ap()[:, 0:256], 0.0, None, op0=ALU.max),
                    inc=ev)
                t_h = P.op("vector", lambda e, hc=hc: e.tensor_mul(
                    hdv[:, hc, :], sil.ap()[:, 0:256], sil.ap()[:, 0:256]), inc=ev)
            P.wait("tensor", ev, t_h)
        P.barrier()
        dw_rows = (io["down_r"] if layer == 0 else io["tpj_r"]).rearrange(
            "(hc p) n -> p hc n", p=128)
        NGD = (NH + 7) // 8
        for obk in range(4):
            for gd in range(NGD):
                h0 = gd * 8
                h1 = min(h0 + 8, NH)
                P.wait("sync", mm, mm.count)
                if layer == 0:
                    t_w = P.dma("sync",
                                wf.ap().rearrange("p (g n) -> p g n", n=512)[:, 0:h1 - h0, :],
                                dw_rows[:, h0:h1, obk * 512:(obk + 1) * 512], wld)
                    P.wait("vector", wld, t_w)
                    P.wait("vector", mm, mm.count)
                    t_c = P.op("vector", lambda e, n=(h1 - h0) * 512: e.tensor_copy(
                        wb.ap()[:, 0:n], wf.ap()[:, 0:n]), inc=wc)
                    P.wait("tensor", wc, t_c)
                else:
                    t_w = P.dma("sync",
                                wb.ap().rearrange("p (g n) -> p g n", n=512)[:, 0:h1 - h0, :],
                                dw_rows[:, h0:h1, obk * 512:(obk + 1) * 512], wld)
                    P.wait("tensor", wld, t_w)
                if gd == 0:
                    P.wait("tensor", ev, ev.count)
                t_m = None
                for hc in range(h0, h1):
                    for tt in range(2):
                        t_m = P.op("tensor", lambda e, tt=tt, hc=hc, hh=hc - h0:
                                   e.matmul(psE[tt].ap(),
                                            hdv[:, hc, tt * 128:(tt + 1) * 128],
                                            wb.ap()[:, hh * 512:(hh + 1) * 512],
                                            start=(hc == 0),
                                            stop=(hc == NH - 1)),
                                   inc=mm if (tt == 1 and hc == h1 - 1) else None)
            P.wait("vector", mm, mm.count)
            for tt in range(2):
                if layer == 0:
                    P.op("vector", lambda e, tt=tt, obk=obk: e.tensor_mul(
                        rawh.ap()[:, tt * D + obk * 512: tt * D + obk * 512 + 512],
                        psE[tt].ap(), B["brow"].ap()[:, obk * 512:(obk + 1) * 512]),
                        inc=ev)
                else:
                    P.op("vector", lambda e, tt=tt, obk=obk: e.tensor_scalar(
                        rawh.ap()[:, tt * D + obk * 512: tt * D + obk * 512 + 512],
                        psE[tt].ap(), B["ccol"].ap(), None, op0=ALU.mult), inc=ev)
            P.wait("tensor", ev, ev.count)
        P.barrier()
        for i in range(2):
            tt = tt0 + i
            raw_ap = rawh.ap()[:, i * D:(i + 1) * D]
            P.wait("sync", vs, vs.count)
            t_x = P.dma("sync", B["sfa"].ap(), x_in[tt * 128:(tt + 1) * 128, :], ld2)
            t_n = emit_ln_cols(P, co, raw_ap, B["sfb"].ap(), B["cols"].ap(),
                               P.nsem("ovs"))
            P.wait("vector", P.nsem("ovs"), t_n)
            P.op("vector", lambda e, raw_ap=raw_ap: e.tensor_scalar(
                B["sfb"].ap(), raw_ap, B["cols"].ap()[:, 3:4], B["cols"].ap()[:, 4:5],
                op0=ALU.mult, op1=ALU.add))
            P.wait("vector", ld2, t_x)
            t_s = P.op("vector", lambda e: e.tensor_add(
                B["wfbig"].ap()[:, 0:D], B["sfa"].ap(), B["sfb"].ap()), inc=vs)
            P.wait("gpsimd", vs, t_s)
            P.dma("gpsimd", x_out[tt * 128:(tt + 1) * 128, :],
                  B["wfbig"].ap()[:, 0:D], st)
            P.wait("vector", st, st.count)
            P.wait("scalar", st, st.count)
        P.wait("sync", st, st.count)
        P.wait("gpsimd", st, st.count)
        P.barrier()


def emit_tern_f(P, nc, es, co, B, io):
    wf = B["wfbig"].ap()[:, 0:2048]
    ab = B["wfbig"].ap()[:, 2048:4096]
    s1, s2, obf = B["sfa"], B["sfb"], B["sba"]
    msc, thr = B["msc"], B["thr"]
    psA, psB = B["psRow"], B["ps2"]
    ld = P.nsem("cld"); cst = P.nsem("ccst"); mm = P.nsem("cmm")
    ev = P.nsem("cev"); st = P.nsem("cst2")
    specs = [("fc", io["w_fc_t"], True, io["tfc_t"]),
             ("pj", io["w_proj_r"], False, io["tpj_r"])]
    NTOT = H1 * D

    def bcast_scalar(src_ap, dst_col_ap):
        P.edone("tensor", "vector")
        t_bb = P.op("tensor", lambda e, s=src_ap, : e.matmul(
            psB.ap()[:, 0:1], co["ones_r_f"].ap(), s, start=True, stop=True), inc=mm)
        P.wait("vector", mm, t_bb)
        return P.op("vector", lambda e, d=dst_col_ap: e.tensor_copy(d, psB.ap()[:, 0:1]),
                    inc=ev)

    for mi, (tag, wsrc, tiled, dst) in enumerate(specs):
        nbands = NH1
        for ph in range(2):
            for bi in range(nbands):
                P.wait("sync", mm, mm.count)
                if tiled:
                    t_w = P.dma("sync", wf.rearrange("p (ic m) -> p ic m", ic=ND),
                                wsrc[bi].rearrange("ic p m -> p ic m"), ld)
                else:
                    t_w = P.dma("sync", wf, wsrc[bi * 128:(bi + 1) * 128, :], ld)
                P.wait("scalar", ld, t_w)
                P.wait("scalar", mm, mm.count)
                if ph == 0:
                    t_a = P.op("scalar", lambda e: e.activation(ab, wf, AF.Abs), inc=cst)
                else:
                    P.op("scalar", lambda e: e.activation(ab, wf, AF.Abs))
                    P.edone("vector", "scalar")
                    t_a = P.op("vector", lambda e, mi=mi: e.tensor_scalar(
                        ab, ab, thr.ap()[:, 2 + mi:3 + mi], None, op0=ALU.add), inc=cst)
                P.wait("tensor", cst, t_a)
                for j in range(4):
                    t_m = P.op("tensor", lambda e, j=j, first=(bi == 0 and j == 0),
                               last=(bi == nbands - 1 and j == 3):
                               e.matmul(psA.ap(), co["ones_c_f"].ap(),
                                        ab[:, j * 512:(j + 1) * 512],
                                        start=first, stop=last),
                               inc=mm if j == 3 else None)
            P.wait("vector", mm, mm.count)
            if ph == 0:
                P.op("vector", lambda e, mi=mi: e.reduce_sum(
                    msc.ap()[:, 6 + mi:7 + mi], psA.ap(), axis=AXX))
                P.op("vector", lambda e, mi=mi: e.tensor_scalar(
                    msc.ap()[:, 6 + mi:7 + mi], msc.ap()[:, 6 + mi:7 + mi],
                    1.0 / NTOT, None, op0=ALU.mult))
                t_v = P.op("vector", lambda e, mi=mi: e.tensor_scalar(
                    msc.ap()[:, 4:5], msc.ap()[:, 6 + mi:7 + mi], -1.0, None,
                    op0=ALU.mult), inc=ev)
                t_v = bcast_scalar(msc.ap()[:, 4:5], thr.ap()[:, 2 + mi:3 + mi])
                P.wait("tensor", ev, t_v)
                P.wait("scalar", ev, t_v)
            else:
                P.op("vector", lambda e, mi=mi: e.reduce_sum(
                    msc.ap()[:, 4:5], psA.ap(), axis=AXX))
                P.op("vector", lambda e, mi=mi: e.tensor_scalar(
                    msc.ap()[:, 4:5], msc.ap()[:, 4:5], 1.0 / NTOT, None, op0=ALU.mult))
                P.op("vector", lambda e, mi=mi: e.tensor_add(
                    msc.ap()[:, mi:mi + 1], msc.ap()[:, 4:5], msc.ap()[:, 6 + mi:7 + mi]))
                t_v = P.op("vector", lambda e, mi=mi: e.tensor_scalar(
                    msc.ap()[:, mi:mi + 1], msc.ap()[:, mi:mi + 1], 1e-5, None,
                    op0=ALU.max), inc=ev)
                P.op("vector", lambda e, mi=mi: e.tensor_scalar(
                    msc.ap()[:, 4:5], msc.ap()[:, mi:mi + 1], -0.7, None, op0=ALU.mult))
                t_v = bcast_scalar(msc.ap()[:, 4:5], thr.ap()[:, mi:mi + 1])
                P.wait("tensor", ev, t_v)
                P.wait("scalar", ev, t_v)
    P.op("vector", lambda e: e.tensor_mul(msc.ap()[:, 5:6], msc.ap()[:, 0:1],
                                          msc.ap()[:, 0:1]))
    t_v = P.op("vector", lambda e: e.tensor_mul(msc.ap()[:, 5:6], msc.ap()[:, 5:6],
                                                msc.ap()[:, 1:2]), inc=ev)
    t_v = bcast_scalar(msc.ap()[:, 5:6], B["ccol"].ap())
    P.barrier()
    for mi, (tag, wsrc, tiled, dst) in enumerate(specs):
        tc_ap = thr.ap()[:, mi:mi + 1]
        for bi in range(NH1):
            P.wait("sync", ev, ev.count)
            if tiled:
                t_w = P.dma("sync", wf.rearrange("p (ic m) -> p ic m", ic=ND),
                            wsrc[bi].rearrange("ic p m -> p ic m"), ld)
            else:
                t_w = P.dma("sync", wf, wsrc[bi * 128:(bi + 1) * 128, :], ld)
            P.wait("scalar", ld, t_w)
            P.wait("scalar", ev, ev.count)
            P.op("scalar", lambda e, b=tc_ap: e.activation(
                s1.ap(), wf, AF.Relu, bias=b, scale=1.0))
            P.op("scalar", lambda e: e.activation(s1.ap(), s1.ap(), AF.Sign))
            P.op("scalar", lambda e, b=tc_ap: e.activation(
                s2.ap(), wf, AF.Relu, bias=b, scale=-1.0))
            t_2 = P.op("scalar", lambda e: e.activation(s2.ap(), s2.ap(), AF.Sign),
                       inc=cst)
            P.wait("vector", cst, t_2)
            t_o = P.op("vector", lambda e: e.tensor_sub(obf.ap(), s1.ap(), s2.ap()),
                       inc=ev)
            P.wait("gpsimd", ev, t_o)
            if tiled:
                P.dma("gpsimd", dst[bi].rearrange("ic p m -> p ic m"),
                      obf.ap().rearrange("p (ic m) -> p ic m", ic=ND), st)
            else:
                P.dma("gpsimd", dst[bi * 128:(bi + 1) * 128, :], obf.ap(), st)
            P.wait("scalar", st, st.count)
    P.wait("sync", st, st.count)
    P.wait("gpsimd", st, st.count)
    P.barrier()


def build_fused(nc, io_specs):
    io = _declare(nc, io_specs)
    ID = lambda n, s, d: nc.dram_tensor(n, list(s), d, kind="Internal").ap()
    io["x1_d"] = ID("x1_d", (TC, D), F32)
    io["x2_d"] = ID("x2_d", (TC, D), F32)
    io["x3_d"] = ID("x3_d", (TC, D), F32)
    io["ex0in"] = ID("ex0in", (6144, 2048), BF16)
    io["ex0out"] = ID("ex0out", (12288, 2048), BF16)
    io["ao0in"] = ID("ao0in", (1024, 4096), BF16)
    io["ao0out"] = ID("ao0out", (2048, 4096), BF16)
    io["ex1in"] = ID("ex1in", (6144, 2048), BF16)
    io["ex1out"] = ID("ex1out", (12288, 2048), BF16)
    io["ao1in"] = ID("ao1in", (1024, 4096), BF16)
    io["ao1out"] = ID("ao1out", (2048, 4096), BF16)
    io["tfc_t"] = ID("tfc_t", (NH1, ND, 128, 128), BF16)
    io["tpj_r"] = ID("tpj_r", (H1, D), BF16)

    io["qloc"] = ID("qloc", (1024, 4096), BF16)
    io["kloc"] = ID("kloc", (1024, 4096), BF16)
    io["vloc"] = ID("vloc", (4096, 1024), BF16)
    io["aoloc"] = ID("aoloc", (2048, 2048), BF16)

    es = contextlib.ExitStack()
    with es:
        P = Prog(nc)
        co = emit_consts(P, nc, es)
        B = alloc_arena(nc, es)
        dynr = nc.gpsimd.alloc_register("dynr")
        cc = P.nsem("cc")
        bld = P.nsem("gbl")
        mk = P.nsem("mk")
        t = None
        for r in range(4):
            P.op("gpsimd", lambda e, r=r: e.memset(B["masks0"].ap()[:, r * 512:(r + 1) * 512], 0.0))
            t = P.op("gpsimd", lambda e, r=r: e.affine_select(
                out=B["masks0"].ap()[:, r * 512:(r + 1) * 512],
                in_=B["masks0"].ap()[:, r * 512:(r + 1) * 512],
                compare_op=ALU.is_gt, fill=1.0, base=r * 128,
                pattern=[[-1, 512]], channel_multiplier=1), inc=mk)
        for r in range(2):
            P.op("gpsimd", lambda e, r=r: e.memset(B["masks1"].ap()[:, r * 256:(r + 1) * 256], 0.0))
            t = P.op("gpsimd", lambda e, r=r: e.affine_select(
                out=B["masks1"].ap()[:, r * 256:(r + 1) * 256],
                in_=B["masks1"].ap()[:, r * 256:(r + 1) * 256],
                compare_op=ALU.is_gt, fill=1.0, base=r * 128,
                pattern=[[-1, 256]], channel_multiplier=1), inc=mk)
        for e in ("tensor", "vector", "scalar"):
            P.wait(e, mk, t)
        P.dma("sync", B["pswap"].ap(), io["pswap"], bld)
        P.dma("sync", B["dyn"].ap(), io["dyn"], bld)
        P.dma("sync", B["bcol"].ap()[:, 0:ND], io["bq_c"], bld)
        t_b = P.dma("sync", B["bcol"].ap()[:, ND:], io["bk_c"], bld)
        P.wait("vector", bld, t_b)
        bsc = P.nsem("bsc")
        P.op("vector", lambda e: e.tensor_scalar(
            B["bcol"].ap()[:, 0:ND], B["bcol"].ap()[:, 0:ND],
            SCALE0, None, op0=ALU.mult), inc=bsc)
        P.wait("tensor", bld, t_b)
        P.wait("gpsimd", bld, t_b)
        P.barrier()

        def stage_exchange(exout, stg):
            ss = P.nsem("stg")
            dyn = B["dyn"]
            exout_t = exout.tensor
            for r in range(2):
                for ki, dst in ((0, io["qloc"]), (2, io["kloc"])):
                    P.rawop("gpsimd", lambda e, i=ki + r: e.reg_load(
                        dynr, dyn.ap()[0:1, i:i + 1]))
                    P.dma("gpsimd", dst[:, r * 2048:(r + 1) * 2048],
                          bass.AP(exout_t, dynr, [[2048, 1024], [1, 2048]]), ss)
                P.rawop("gpsimd", lambda e, i=4 + r: e.reg_load(
                    dynr, dyn.ap()[0:1, i:i + 1]))
                P.dma("gpsimd", io["vloc"][r * 2048:(r + 1) * 2048, :],
                      bass.AP(exout_t, dynr, [[2048, 2048], [1, 1024]]), ss)
            P.wait("gpsimd", ss, ss.count)
            P.barrier()

        def stage_ao(aoout):
            ss = P.nsem("stg")
            dyn = B["dyn"]
            P.rawop("gpsimd", lambda e: e.reg_load(dynr, dyn.ap()[0:1, 6:7]))
            P.dma("gpsimd", io["aoloc"],
                  bass.AP(aoout.tensor, dynr, [[4096, 2048], [1, 2048]]), ss)
            P.wait("gpsimd", ss, ss.count)
            P.barrier()

        sinT, cosT = emit_rope_f(P, nc, es, B, io, "r0")
        emit_xnT_f(P, nc, es, co, B, io["x0"], NT)
        xnT_ap = B["xnTbig"].ap()[:, 0:ND * TC]
        emit_projF_f(P, nc, es, co, B, xnT_ap, io["wq_t"], ND, io["ex0in"][0:2048],
                     rope=(sinT, cosT), bias_off=0, scale=SCALE0)
        emit_projF_f(P, nc, es, co, B, xnT_ap, io["wk_t"], ND, io["ex0in"][2048:4096],
                     rope=(sinT, cosT), bias_off=ND, scale=1.0)
        gbl2 = P.nsem("gbl2")
        t_bv = P.dma("gpsimd", B["brow"].ap(), io["bv_row"].broadcast_to([128, D]), gbl2)
        P.wait("vector", gbl2, t_bv)
        emit_projT_f(P, nc, es, co, B, xnT_ap, ND, io["wv_r"], io["ex0in"][4096:6144],
                     bias_bcast=B["brow"].ap())
        t_cc = P.op("gpsimd", lambda e: e.collective_compute(
            "AllGather", ALU.bypass, replica_groups=[[0, 1], [2, 3], [4, 5], [6, 7]],
            ins=[io["ex0in"].opt()], outs=[io["ex0out"].opt()]), inc=cc)
        for e in ENGINES:
            P.wait(e, cc, t_cc)
        stage_exchange(io["ex0out"], None)
        emit_attn0_f(P, nc, es, co, B, (io["qloc"], io["kloc"], io["vloc"]),
                     io["ao0in"])
        t_cc = P.op("gpsimd", lambda e: e.collective_compute(
            "AllGather", ALU.bypass, replica_groups=[[0, 1], [2, 3], [4, 5], [6, 7]],
            ins=[io["ao0in"].opt()], outs=[io["ao0out"].opt()]), inc=cc)
        for e in ENGINES:
            P.wait(e, cc, t_cc)
        stage_ao(io["ao0out"])
        emit_oproj_f(P, nc, es, co, B, io["aoloc"], io["wo_r"],
                     io["bo_row"], io["x0"], io["x1_d"])
        emit_ffn_f(P, nc, es, co, B, io, 0)
        sinT, cosT = emit_rope_f(P, nc, es, B, io, "r1")
        emit_xnT_f(P, nc, es, co, B, io["x2_d"], NT)
        emit_projF_f(P, nc, es, co, B, xnT_ap, io["wq1_t"], ND, io["ex1in"][0:2048],
                     rope=(sinT, cosT), elu1=True)
        emit_projF_f(P, nc, es, co, B, xnT_ap, io["wk1_t"], ND, io["ex1in"][2048:4096],
                     rope=(sinT, cosT), elu1=True)
        emit_projT_f(P, nc, es, co, B, xnT_ap, ND, io["wv1_r"], io["ex1in"][4096:6144])
        t_cc = P.op("gpsimd", lambda e: e.collective_compute(
            "AllGather", ALU.bypass, replica_groups=[[0, 1], [2, 3], [4, 5], [6, 7]],
            ins=[io["ex1in"].opt()], outs=[io["ex1out"].opt()]), inc=cc)
        for e in ENGINES:
            P.wait(e, cc, t_cc)
        stage_exchange(io["ex1out"], None)
        emit_attn1_f(P, nc, es, co, B, (io["qloc"], io["kloc"], io["vloc"]),
                     io["ao1in"])
        t_cc = P.op("gpsimd", lambda e: e.collective_compute(
            "AllGather", ALU.bypass, replica_groups=[[0, 1], [2, 3], [4, 5], [6, 7]],
            ins=[io["ao1in"].opt()], outs=[io["ao1out"].opt()]), inc=cc)
        for e in ENGINES:
            P.wait(e, cc, t_cc)
        stage_ao(io["ao1out"])
        emit_oproj_f(P, nc, es, co, B, io["aoloc"], io["wout_r"],
                     None, io["x2_d"], io["x3_d"])
        emit_tern_f(P, nc, es, co, B, io)
        emit_ffn_f(P, nc, es, co, B, io, 1)
        P.finalize(es)
    return nc


FUSED_SPECS = [
    ("x0", (TC, D), F32, "ExternalInput"),
    ("wq_t", (ND, ND, 128, 128), F32, "ExternalInput"),
    ("wk_t", (ND, ND, 128, 128), F32, "ExternalInput"),
    ("wv_r", (D, D), F32, "ExternalInput"),
    ("wo_r", (D, D), F32, "ExternalInput"),
    ("bq_c", (128, ND), F32, "ExternalInput"),
    ("bk_c", (128, ND), F32, "ExternalInput"),
    ("bv_row", (1, D), F32, "ExternalInput"),
    ("bo_row", (1, D), F32, "ExternalInput"),
    ("gate_t", (NH0, ND, 128, 128), F32, "ExternalInput"),
    ("up_t", (NH0, ND, 128, 128), F32, "ExternalInput"),
    ("down_r", (INTER0, D), F32, "ExternalInput"),
    ("gate_s_c", (128, NH0), F32, "ExternalInput"),
    ("up_s_c", (128, NH0), F32, "ExternalInput"),
    ("down_s_row", (1, D), F32, "ExternalInput"),
    ("wq1_t", (ND, ND, 128, 128), F32, "ExternalInput"),
    ("wk1_t", (ND, ND, 128, 128), F32, "ExternalInput"),
    ("wv1_r", (D, D), F32, "ExternalInput"),
    ("wout_r", (D, D), F32, "ExternalInput"),
    ("w_fc_t", (NH1, ND, 128, 128), F32, "ExternalInput"),
    ("w_proj_r", (H1, D), F32, "ExternalInput"),
    ("r0sh", (128, TC), F32, "ExternalInput"),
    ("r0sl", (128, TC), F32, "ExternalInput"),
    ("r0ch", (128, TC), F32, "ExternalInput"),
    ("r0cl", (128, TC), F32, "ExternalInput"),
    ("r1sh", (128, TC), F32, "ExternalInput"),
    ("r1sl", (128, TC), F32, "ExternalInput"),
    ("r1ch", (128, TC), F32, "ExternalInput"),
    ("r1cl", (128, TC), F32, "ExternalInput"),
    ("pswap", (128, 128), F32, "ExternalInput"),
    ("dyn", (1, 64), mybir.dt.int32, "ExternalInput"),
    ("x4", (TC, D), F32, "ExternalOutput"),
]


def kernel(**inputs):
    f32 = np.float32
    g = {k: np.asarray(v) for k, v in inputs.items()}
    x = np.ascontiguousarray(g["L0_attn_x"].astype(f32, copy=False))

    wq_t = tile_lhsT(np.ascontiguousarray(g["L0_attn_Wq"]), ND, ND)
    wk_t = tile_lhsT(np.ascontiguousarray(g["L0_attn_Wk"]), ND, ND)
    wv_r = np.ascontiguousarray(g["L0_attn_Wv"].T)
    wo_r = np.ascontiguousarray(g["L0_attn_Wo"].T)
    bq_c = col_layout(g["L0_attn_bq"]); bk_c = col_layout(g["L0_attn_bk"])
    bv_row = np.ascontiguousarray(g["L0_attn_bv"].reshape(1, D))
    bo_row = np.ascontiguousarray(g["L0_attn_bo"].reshape(1, D))
    gate_t = tile_lhsT(np.ascontiguousarray(g["L0_ffn_gate_w"]), NH0, ND)
    up_t = tile_lhsT(np.ascontiguousarray(g["L0_ffn_up_w"]), NH0, ND)
    down_r = np.ascontiguousarray(g["L0_ffn_down_w"].T)
    gate_s_c = col_layout(g["L0_ffn_gate_s"]); up_s_c = col_layout(g["L0_ffn_up_s"])
    down_s_row = np.ascontiguousarray(g["L0_ffn_down_s"].reshape(1, D))
    wqkv = np.ascontiguousarray(g["L1_attn_W_qkv"])
    wq1_t = tile_lhsT(np.ascontiguousarray(wqkv[0:D]), ND, ND)
    wk1_t = tile_lhsT(np.ascontiguousarray(wqkv[D:2 * D]), ND, ND)
    wv1_r = np.ascontiguousarray(wqkv[2 * D:3 * D].T)
    wout_r = np.ascontiguousarray(g["L1_attn_W_out"].T)
    w_fc_t = tile_lhsT(np.ascontiguousarray(g["L1_ffn_w_fc"]), NH1, ND)
    w_proj_r = np.ascontiguousarray(g["L1_ffn_w_proj"].T)
    r0 = make_rope_tables(0)
    r1 = make_rope_tables(1)
    pswap_m = np.zeros((128, 128), f32)
    pswap_m[np.arange(128), np.arange(128) ^ 1] = 1.0

    def rope_slices(tabs, p):
        sh, sl, ch, cl = tabs
        cols = np.arange(p * 2048, (p + 1) * 2048)
        return tuple(np.ascontiguousarray(a[:, cols]) for a in tabs)

    nc = bass.Bass("TRN2", target_bir_lowering=False)
    build_fused(nc, FUSED_SPECS)

    ins = []
    for c in CORES:
        bb, p = c // 2, c % 2
        sh0, sl0, ch0, cl0 = rope_slices(r0, p)
        sh1, sl1, ch1, cl1 = rope_slices(r1, p)
        dyn = np.zeros((1, 64), np.int32)
        for r in range(2):
            dyn[0, 0 + r] = (r * 6144 + p * 1024) * 2048
            dyn[0, 2 + r] = (r * 6144 + 2048 + p * 1024) * 2048
            dyn[0, 4 + r] = (r * 6144 + 4096) * 2048 + p * 1024
        dyn[0, 6] = p * 2048
        ins.append(dict(
            x0=np.ascontiguousarray(x[bb, p * 2048:(p + 1) * 2048]),
            wq_t=wq_t, wk_t=wk_t, wv_r=wv_r, wo_r=wo_r, bq_c=bq_c, bk_c=bk_c,
            bv_row=bv_row, bo_row=bo_row, gate_t=gate_t, up_t=up_t, down_r=down_r,
            gate_s_c=gate_s_c, up_s_c=up_s_c, down_s_row=down_s_row,
            wq1_t=wq1_t, wk1_t=wk1_t, wv1_r=wv1_r, wout_r=wout_r,
            w_fc_t=w_fc_t, w_proj_r=w_proj_r,
            r0sh=sh0, r0sl=sl0, r0ch=ch0, r0cl=cl0,
            r1sh=sh1, r1sl=sl1, r1ch=ch1, r1cl=cl1,
            pswap=pswap_m, dyn=dyn))
    res = run_bass_kernel_spmd(nc, ins, core_ids=CORES)
    out = np.zeros((B, S, D), f32)
    for c in CORES:
        bb, p = c // 2, c % 2
        out[bb, p * 2048:(p + 1) * 2048] = res.results[c]["x4"]
    return out
